# version 22
# speedup vs baseline: 1.0237x; 1.0072x over previous
from contextlib import ExitStack
from types import SimpleNamespace

import numpy as np
import ml_dtypes
import concourse.bass as bass
import concourse.mybir as mybir
from concourse.bass_utils import run_bass_kernel_spmd
from concourse.replica_groups import maybe_share_collective_output_space

G4 = [[0, 1, 2, 3], [4, 5, 6, 7]]
G8 = [[0, 1, 2, 3, 4, 5, 6, 7]]

F32 = mybir.dt.float32
BF16 = mybir.dt.bfloat16
I32 = mybir.dt.int32
AF = mybir.ActivationFunctionType
ALU = mybir.AluOpType
AX = mybir.AxisListType

BFNP = ml_dtypes.bfloat16

D = 1024
T = 2048
NT = 4
TTW = 512
NC = 16
IC = 8
H = 16
DH = 64
M = 256
DN = float(DH) ** -0.25
EPS_PERF = 1e-4
LN_EPS = 1e-5
NE = 8
NL = 64
INTER = 2048


class Plan:
    ENGINES = ("sync", "gpsimd", "vector", "scalar", "tensor")

    def __init__(self, nc):
        self.nc = nc
        self.ops = {e: [] for e in self.ENGINES}
        self.sems = {}
        self.counts = {}
        for e in self.ENGINES:
            self.sem("d_" + e)
        for name in ("dma", "pc0", "pc1", "wv", "st", "xs0", "xs1", "sto0", "sto1", "kt0", "kt1", "gth", "res", "wt0", "wt1"):
            self.sem(name)

    def sem(self, name):
        if name not in self.sems:
            self.sems[name] = self.nc.alloc_semaphore(name)
            self.counts[name] = 0
        return self.sems[name]

    def bump(self, name, n=1):
        self.sem(name)
        self.counts[name] += n
        return self.counts[name]

    def cur(self, name):
        self.sem(name)
        return self.counts[name]

    def add(self, engine, fn):
        self.ops[engine].append(fn)

    def op(self, engine, fn):
        name = "d_" + engine
        tgt = self.bump(name)
        s = self.sems[name]

        def _f(e, fn=fn, s=s):
            fn(e).then_inc(s, 1)
        self.ops[engine].append(_f)
        return tgt

    def wait(self, engine, sem_name, val):
        s = self.sem(sem_name)
        self.add(engine, lambda e, s=s, v=val: e.wait_ge(s, v))

    def wait_eng(self, engine, producer_engine, val):
        self.wait(engine, "d_" + producer_engine, val)

    def dma(self, engine, out_ap, in_ap, sem_name="dma"):
        tgt = self.bump(sem_name, 16)
        s = self.sems[sem_name]

        def _f(e, o=out_ap, i=in_ap, s=s):
            e.dma_start(out=o, in_=i).then_inc(s, 16)
        self.add(engine, _f)
        return tgt

    def join(self, engines=None):
        engines = engines or self.ENGINES
        snap = {en: self.cur("d_" + en) for en in engines}
        for en in engines:
            for other in engines:
                if other != en and snap[other] > 0:
                    self.wait(en, "d_" + other, snap[other])

    def emit(self):
        nc = self.nc
        with nc.Block() as block:
            for eng_name in self.ENGINES:
                fns = self.ops[eng_name]
                if not fns:
                    continue
                getattr(block, eng_name)(self._mk(fns))

    @staticmethod
    def _mk(fns):
        def _run(eng):
            for f in fns:
                f(eng)
        return _run


def finish(plan):
    for name in list(plan.counts):
        if name.startswith("st") and plan.counts[name] > 0:
            plan.wait("gpsimd", name, plan.cur(name))
    plan.emit()


class Phase:

    def __init__(self, nc, prefix):
        self.nc = nc
        self.prefix = prefix
        self.stack = ExitStack()

    def sbuf(self, name, shape, dtype):
        return self.stack.enter_context(
            self.nc.sbuf_tensor(self.prefix + name, shape, dtype))

    def psum(self, name, shape, dtype=F32):
        return self.stack.enter_context(
            self.nc.psum_tensor(self.prefix + name, shape, dtype))

    def close(self):
        self.stack.close()


def phase_barrier(plan):
    for name in list(plan.counts):
        if name.startswith("d_") or name.startswith("cc"):
            continue
        if plan.counts[name] > 0:
            plan.wait("gpsimd", name, plan.counts[name])
    plan.op("gpsimd", lambda e: e.nop())
    plan.join()


def emit_collective(plan, name, kind, aluop, groups, in_h, out_h):
    s = plan.sem(name)
    plan.counts[name] = 1

    def _f(e, in_h=in_h, out_h=out_h, s=s):
        e.collective_compute(kind, aluop, replica_groups=groups,
                             ins=[in_h.ap().opt()],
                             outs=[out_h.ap().opt()]).then_inc(s)
    plan.add("gpsimd", _f)


def cc_dram(nc, name, shape, dtype, kind, groups):
    sp = maybe_share_collective_output_space(kind, groups, prefer_local=True)
    return nc.dram_tensor(name, shape, dtype, addr_space=sp)


def emit_ln(plan, x_sb, xbf_sb, sq_sb, ones128b, row_sc, bc_sb, ones1f,
            ps_den, ps_bc, tlen=T):
    ntt = tlen // TTW
    """LayerNorm over feature dim of x_sb (fp32 [128, IC*T]) in place.
    xbf_sb/sq_sb: bf16 scratch [128, IC*T]. row_sc: [1, 3*T+1] f32
    (A=[0:T] sum->meansq->sqrt, B=[T:2T] sumsq->var->rstd, C=[2T:3T] mean).
    bc_sb: [128, 2*T] f32 (mean, rstd broadcasts). ps_bc: psum [128, 2048].
    Caller must join() before; ends unjoined."""
    plan.op("vector", lambda e: e.memset(row_sc[0:1, 3 * tlen:3 * tlen + 1],
                                         LN_EPS))
    for icx in range(IC):
        sl = slice(icx * tlen, (icx + 1) * tlen)
        plan.op("vector", lambda e, sl=sl: e.tensor_copy(xbf_sb[:, sl],
                                                         x_sb[:, sl]))
        plan.op("scalar", lambda e, sl=sl: e.activation(sq_sb[:, sl],
                                                        x_sb[:, sl],
                                                        AF.Square))
    vq = plan.cur("d_vector")
    sq = plan.cur("d_scalar")
    plan.wait_eng("tensor", "vector", vq)
    plan.wait_eng("tensor", "scalar", sq)
    prev_cp = None
    for tt in range(ntt):
        if prev_cp is not None:
            plan.wait_eng("tensor", "vector", prev_cp)
        for which, srcb in ((0, xbf_sb), (1, sq_sb)):
            for icx in range(IC):
                plan.op("tensor", lambda e, which=which, srcb=srcb, icx=icx,
                        tt=tt: e.matmul(
                    ps_bc[0:1, which * TTW:(which + 1) * TTW],
                    ones128b[:, :],
                    srcb[:, icx * tlen + tt * TTW: icx * tlen + (tt + 1) * TTW],
                    start=(icx == 0), stop=(icx == IC - 1)))
        tmm = plan.cur("d_tensor")
        plan.wait_eng("vector", "tensor", tmm)
        plan.op("vector", lambda e, tt=tt: e.tensor_copy(
            row_sc[0:1, tt * TTW:(tt + 1) * TTW], ps_bc[0:1, 0:TTW]))
        prev_cp = plan.op("vector", lambda e, tt=tt: e.tensor_copy(
            row_sc[0:1, tlen + tt * TTW: tlen + (tt + 1) * TTW],
            ps_bc[0:1, TTW:2 * TTW]))
    plan.wait_eng("vector", "vector", prev_cp)
    v1 = plan.op("vector", lambda e: e.tensor_scalar_mul(
        row_sc[0:1, 2 * tlen:3 * tlen], row_sc[0:1, 0:tlen], 1.0 / D))
    plan.wait_eng("vector", "vector", v1)
    v2 = plan.op("vector", lambda e: e.tensor_mul(
        row_sc[0:1, 0:tlen], row_sc[0:1, 2 * tlen:3 * tlen],
        row_sc[0:1, 2 * tlen:3 * tlen]))
    plan.wait_eng("vector", "vector", v2)
    v3 = plan.op("vector", lambda e: e.scalar_tensor_tensor(
        row_sc[0:1, tlen:2 * tlen], row_sc[0:1, tlen:2 * tlen], 1.0 / D,
        row_sc[0:1, 0:tlen], ALU.mult, ALU.subtract))
    plan.wait_eng("scalar", "vector", v3)
    s1 = plan.op("scalar", lambda e: e.activation(
        row_sc[0:1, 0:tlen], row_sc[0:1, tlen:2 * tlen], AF.Sqrt,
        bias=row_sc[0:1, 3 * tlen:3 * tlen + 1]))
    plan.wait_eng("vector", "scalar", s1)
    v4 = plan.op("vector", lambda e: e.reciprocal(
        row_sc[0:1, tlen:2 * tlen], row_sc[0:1, 0:tlen]))
    plan.wait_eng("tensor", "vector", v4)
    for which, off in ((0, 2 * tlen), (1, 1 * tlen)):
        for tt in range(ntt):
            plan.op("tensor", lambda e, which=which, off=off, tt=tt:
                    e.matmul(
                ps_bc[:, tt * TTW:(tt + 1) * TTW], ones1f[0:1, :],
                row_sc[0:1, off + tt * TTW: off + (tt + 1) * TTW],
                start=True, stop=True))
        tmm = plan.cur("d_tensor")
        plan.wait_eng("vector", "tensor", tmm)
        vb = plan.op("vector", lambda e, which=which: e.tensor_copy(
            bc_sb[:, which * tlen:(which + 1) * tlen], ps_bc[:, 0:tlen]))
        plan.wait_eng("tensor", "vector", vb)
    bc_done = plan.cur("d_vector")
    plan.wait_eng("vector", "vector", bc_done)
    for icx in range(IC):
        sl = slice(icx * tlen, (icx + 1) * tlen)
        plan.op("vector", lambda e, sl=sl: e.tensor_sub(
            x_sb[:, sl], x_sb[:, sl], bc_sb[:, 0:tlen]))
    vs = plan.cur("d_vector")
    plan.wait_eng("vector", "vector", vs)
    for icx in range(IC):
        sl = slice(icx * tlen, (icx + 1) * tlen)
        plan.op("vector", lambda e, sl=sl: e.tensor_mul(
            x_sb[:, sl], x_sb[:, sl], bc_sb[:, tlen:2 * tlen]))


def emit_l0attn_p1(plan, C, ph):
    xT = C.xT
    Wq, Wk, Wv, PTm = C.Wq, C.Wk, C.Wv, C.PT
    o_qbf, o_kbf, o_vt = C.qbf_i, C.kbf_i, C.vt_i
    o_negbq, o_diagk = C.negbq_i, C.diagk_i
    eye_f = C.eye_f
    ones128b = C.ones128b

    xbf = ph.sbuf("xbf", [128, IC * T], BF16)
    pbf = ph.sbuf("pbf", [128, M], BF16)
    wbuf = [ph.sbuf(f"wbuf{i}", [128, D], BF16) for i in range(2)]
    wvb = ph.sbuf("wvb", [128, IC * D], BF16)
    abf = ph.sbuf("abf", [128, IC * T], BF16)
    asq = ph.sbuf("asq", [128, IC * T], BF16)
    vt = ph.sbuf("vt_s", [128, NC * D], BF16)
    negbq = ph.sbuf("negbq_s", [128, H * NC], F32)
    diagk = ph.sbuf("diagk_s", [128, H * NC], F32)
    kmaxc = ph.sbuf("kmaxc", [128, H], F32)
    km_t = ph.sbuf("km_t", [128, 1], F32)
    km_s = ph.sbuf("km_s", [1, 128], F32)
    km1 = ph.sbuf("km1", [1, 1], F32)
    km_b = ph.sbuf("km_b", [1, 8], F32)
    mxcol = ph.sbuf("mxcol", [128, NC], F32)

    ps_mm = ph.psum("ps_mm", [128, 2 * TTW], F32)
    ps_dd = ph.psum("ps_dd", [128, 8 * M], F32)
    ps_dg = ph.psum("ps_dg", [128, 512], F32)
    ps_tr = ph.psum("ps_tr", [128, 128], F32)

    plan.dma("gpsimd", xbf[:, :].rearrange("p (i t) -> p i t", i=IC),
             xT.rearrange("(i p) t -> p i t", p=128))
    plan.dma("gpsimd", pbf[0:DH, :], PTm[:, :])
    plan.dma("gpsimd", pbf[DH:128, :], PTm[:, :])
    plan.wait("gpsimd", "dma", plan.cur("dma"))
    plan.op("gpsimd", lambda e: e.memset(km1[0:1, 0:1], 0.0))
    plan.join()

    def proj_side(W_ap):
        last_mm = {}
        ps_pairs = [ps_mm[:, 0:1024], ps_dd[:, 0:1024]]
        for oc in range(IC):
            buf = wbuf[oc % 2]
            psem = "pc%d" % (oc % 2)
            if oc >= 2:
                plan.wait_eng("gpsimd", "tensor", last_mm[oc - 2])
            plan.dma("gpsimd", buf[:, :].rearrange("p (i o) -> p i o", i=IC),
                     W_ap[:, oc * 128:(oc + 1) * 128]
                     .rearrange("(i p) o -> p i o", p=128), psem)
            plan.wait("tensor", psem, plan.cur(psem))
            for pp in range(2):
                psX = ps_pairs[pp]
                prev = plan._psv.get(pp)
                if prev is not None:
                    plan.wait_eng("tensor", "scalar", prev[0])
                    plan.wait_eng("tensor", "vector", prev[1])
                for icx in range(IC):
                    for qq in range(2):
                        tt = pp * 2 + qq
                        tgt = plan.op("tensor", lambda e, buf=buf, icx=icx,
                                      tt=tt, qq=qq, psX=psX: e.matmul(
                            psX[:, qq * TTW:(qq + 1) * TTW],
                            buf[:, icx * 128:(icx + 1) * 128],
                            xbf[:, icx * T + tt * TTW:
                                icx * T + (tt + 1) * TTW],
                            start=(icx == 0), stop=(icx == IC - 1),
                            skip_group_check=True))
                last_mm[oc] = tgt
                plan.wait_eng("scalar", "tensor", tgt)
                for qq in range(2):
                    tt = pp * 2 + qq
                    se_t = plan.op("scalar", lambda e, oc=oc, tt=tt, qq=qq,
                                   psX=psX:
                                   e.copy(abf[:, oc * T + tt * TTW:
                                              oc * T + (tt + 1) * TTW],
                                          psX[:, qq * TTW:(qq + 1) * TTW]))
                    plan.wait_eng("vector", "scalar", se_t)
                    ve_t = plan.op("vector", lambda e, oc=oc, tt=tt:
                                   e.tensor_mul(
                        asq[:, oc * T + tt * TTW: oc * T + (tt + 1) * TTW],
                        abf[:, oc * T + tt * TTW: oc * T + (tt + 1) * TTW],
                        abf[:, oc * T + tt * TTW: oc * T + (tt + 1) * TTW]))
                plan._psv[pp] = (se_t, ve_t)

    def head_stats(side):
        red_c = {}
        last_red = None
        last_cmb = None
        for h in range(H):
            oc = h // 2
            rb = 64 * (h % 2)
            for half in range(2):
                if half == 0 and last_cmb is not None:
                    plan.wait_eng("vector", "vector", last_cmb)
                for c8 in range(8):
                    tc = half * 8 + c8
                    if red_c.get(c8) is not None:
                        plan.wait_eng("tensor", "vector", red_c[c8])
                    tgt = plan.op("tensor", lambda e, oc=oc, rb=rb, tc=tc,
                                  c8=c8: e.matmul(
                        ps_dd[:, c8 * M:(c8 + 1) * M],
                        abf[rb:rb + DH,
                            oc * T + tc * 128: oc * T + (tc + 1) * 128],
                        pbf[rb:rb + DH, :], start=True, stop=True))
                    plan.wait_eng("vector", "tensor", tgt)
                    red_c[c8] = plan.op("vector", lambda e, half=half,
                                        c8=c8: e.reduce_max(
                        mxcol[:, half * 8 + c8:half * 8 + c8 + 1],
                        ps_dd[:, c8 * M:(c8 + 1) * M],
                        axis=AX.X))
                last_red = red_c[7]
            if last_cmb is not None:
                plan.wait_eng("tensor", "vector", last_cmb)
            for tc in range(NC):
                tgt = plan.op("tensor", lambda e, oc=oc, rb=rb, tc=tc:
                              e.matmul(
                    ps_dg[:, tc:tc + 1],
                    asq[rb:rb + DH,
                        oc * T + tc * 128: oc * T + (tc + 1) * 128],
                    ones128b[rb:rb + DH, :], start=True, stop=True))
            plan.wait_eng("vector", "tensor", tgt)
            plan.wait_eng("vector", "vector", last_red)
            if side == 0:
                v1 = plan.op("vector", lambda e: e.tensor_scalar_mul(
                    mxcol[:, :], mxcol[:, :], DN))
                plan.wait_eng("vector", "vector", v1)
                last_cmb = plan.op("vector", lambda e, h=h:
                                   e.scalar_tensor_tensor(
                    negbq[:, h * NC:(h + 1) * NC], ps_dg[:, 0:NC],
                    -0.5 * DN * DN, mxcol[:, :], ALU.mult, ALU.subtract))
            else:
                plan.op("vector", lambda e, h=h: e.tensor_scalar_mul(
                    diagk[:, h * NC:(h + 1) * NC], ps_dg[:, 0:NC],
                    0.5 * DN * DN))
                last_cmb = plan.op("vector", lambda e, h=h: e.reduce_max(
                    kmaxc[:, h:h + 1], mxcol[:, :], axis=AX.X))

    plan._psv = {}
    proj_side(Wq)
    plan.join()
    head_stats(0)
    plan.join()
    plan.dma("sync", o_qbf[:, :], abf[:, :], "st")
    plan.dma("sync", o_negbq[:, :], negbq[:, :], "st")
    plan.wait("sync", "st", plan.cur("st"))
    plan.op("sync", lambda e: e.nop())
    plan.join()

    plan._psv = {}
    proj_side(Wk)
    plan.join()
    head_stats(1)
    plan.join()
    plan.dma("sync", o_kbf[:, :], abf[:, :], "st")
    plan.dma("sync", o_diagk[:, :], diagk[:, :], "st")

    plan.wait_eng("vector", "vector", plan.cur("d_vector"))
    v1 = plan.op("vector", lambda e: e.reduce_max(km_t[:, :], kmaxc[:, :],
                                                  axis=AX.X))
    plan.wait_eng("tensor", "vector", v1)
    t1 = plan.op("tensor", lambda e: e.transpose(ps_tr[0:1, :], km_t[:, :],
                                                 eye_f[:, :]))
    plan.wait_eng("vector", "tensor", t1)
    v2 = plan.op("vector", lambda e: e.tensor_copy(km_s[:, :], ps_tr[0:1, :]))
    plan.wait_eng("vector", "vector", v2)
    v3 = plan.op("vector", lambda e: e.reduce_max(km1[:, :], km_s[:, :],
                                                  axis=AX.X))
    plan.wait_eng("vector", "vector", v3)
    v3b = plan.op("vector", lambda e: e.memset(km_b[:, :], 0.0))
    plan.wait_eng("vector", "vector", v3b)
    v3c = plan.op("vector", lambda e: e.tensor_scalar(
        km_b[:, :], km_b[:, :], km1[0:1, 0:1], None, ALU.add))
    plan.wait_eng("sync", "vector", v3c)
    plan.dma("sync", C.kmax_w.ap(), km_b[:, :], "st")

    plan.dma("gpsimd", wvb[:, :].rearrange("p (i o) -> p i o", i=IC),
             Wv.rearrange("(i p) o -> p i o", p=128), "wv")
    plan.wait("tensor", "wv", plan.cur("wv"))
    plan.join()
    prev_slot = {}
    for tc in range(NC):
        for hp in range(IC):
            slot = (tc * IC + hp) % 4
            if slot in prev_slot:
                plan.wait_eng("tensor", "scalar", prev_slot[slot])
            for icx in range(IC):
                tgt = plan.op("tensor", lambda e, tc=tc, hp=hp, icx=icx,
                              slot=slot: e.matmul(
                    ps_dd[:, slot * TTW: slot * TTW + 128],
                    xbf[:, icx * T + tc * 128: icx * T + (tc + 1) * 128],
                    wvb[:, icx * D + hp * 128: icx * D + (hp + 1) * 128],
                    start=(icx == 0), stop=(icx == IC - 1)))
            plan.wait_eng("scalar", "tensor", tgt)
            prev_slot[slot] = plan.op(
                "scalar", lambda e, tc=tc, hp=hp, slot=slot: e.copy(
                    vt[:, tc * D + hp * 128: tc * D + (hp + 1) * 128],
                    ps_dd[:, slot * TTW: slot * TTW + 128]))
    plan.wait_eng("sync", "scalar", plan.cur("d_scalar"))
    plan.dma("sync", o_vt[:, :], vt[:, :], "st")


def emit_l0attn_p2(plan, C, ph):
    kbf_d, vt_d, diagk_d = C.kbf_i, C.vt_i, C.diagk_i
    kmax8_d = C.kmax_r.ap()
    PTm = C.PT
    o_ctx = C.ctx_w.ap()
    ones1f = C.ones1f
    ones128b = C.ones128b

    kbf = ph.sbuf("kbf_s", [128, IC * T], BF16)
    vt = ph.sbuf("vt_s", [128, NC * D], BF16)
    pbf = ph.sbuf("pbf", [128, M], BF16)
    diagk = ph.sbuf("diagk_s", [128, H * NC], F32)
    negbk = ph.sbuf("negbk_s", [128, H * NC], F32)
    kmax8 = ph.sbuf("kmax8_s", [1, 8], F32)
    mxk1 = ph.sbuf("mxk1", [1, 1], F32)
    mxkc = ph.sbuf("mxkc", [128, 1], F32)
    kp2 = [ph.sbuf(f"kp_s{i}", [128, NC * M], BF16) for i in range(2)]
    expf = ph.sbuf("expf", [128, 2 * M], F32)
    ctx = ph.sbuf("ctx_s", [128, H * 2 * 65], F32)

    ps_dd = ph.psum("ps_dd", [128, 8 * M], F32)
    ps_ctx = ph.psum("ps_ctx", [128, 512], F32)
    ps_one = ph.psum("ps_one", [128, 512], F32)
    ps_b = ph.psum("ps_b", [128, 512], F32)

    plan.dma("gpsimd", kbf[:, :], kbf_d)
    plan.dma("gpsimd", vt[:, :], vt_d)
    plan.dma("gpsimd", diagk[:, :], diagk_d)
    plan.wait("gpsimd", "cc_kmax", 1)
    plan.dma("gpsimd", kmax8[:, :], kmax8_d)
    plan.dma("gpsimd", pbf[0:DH, :], PTm[:, :])
    plan.dma("gpsimd", pbf[DH:128, :], PTm[:, :])
    plan.wait("gpsimd", "dma", plan.cur("dma"))
    plan.op("gpsimd", lambda e: e.memset(mxk1[0:1, 0:1], 0.0))
    plan.join()

    v1 = plan.op("vector", lambda e: e.reduce_max(mxk1[:, :], kmax8[:, :],
                                                  axis=AX.X))
    plan.wait_eng("vector", "vector", v1)
    v2 = plan.op("vector", lambda e: e.tensor_scalar_mul(mxk1[:, :],
                                                         mxk1[:, :], DN))
    plan.wait_eng("tensor", "vector", v2)
    t1 = plan.op("tensor", lambda e: e.matmul(ps_b[:, 0:1], ones1f[0:1, :],
                                              mxk1[0:1, 0:1], start=True,
                                              stop=True))
    plan.wait_eng("vector", "tensor", t1)
    v3 = plan.op("vector", lambda e: e.tensor_copy(mxkc[:, :], ps_b[:, 0:1]))
    plan.wait_eng("vector", "vector", v3)
    v4 = plan.op("vector", lambda e: e.tensor_scalar(
        negbk[:, :], diagk[:, :], mxkc[:, 0:1], -1.0, ALU.add, ALU.mult))
    plan.join()

    st = {"kp_done": {}, "ctx_t": {}, "prev_exp": {}, "expv": {},
          "prev_ctx_ev": None}

    def emit_kp(h):
        oc = h // 2
        rb = 64 * (h % 2)
        kpb = kp2[h % 2]
        if st["ctx_t"].get(h % 2) is not None:
            plan.wait_eng("scalar", "tensor", st["ctx_t"][h % 2])
            plan.wait_eng("vector", "tensor", st["ctx_t"][h % 2])
        for half in range(2):
            for c8 in range(8):
                tc = half * 8 + c8
                slot = tc % 2
                if c8 in st["prev_exp"]:
                    plan.wait_eng("tensor", "vector", st["prev_exp"][c8])
                tgt = plan.op("tensor", lambda e, oc=oc, rb=rb, tc=tc, c8=c8:
                              e.matmul(
                    ps_dd[:, c8 * M:(c8 + 1) * M],
                    kbf[rb:rb + DH,
                        oc * T + tc * 128: oc * T + (tc + 1) * 128],
                    pbf[rb:rb + DH, :], start=True, stop=True))
                plan.wait_eng("scalar", "tensor", tgt)
                if slot in st["expv"]:
                    plan.wait_eng("scalar", "vector", st["expv"][slot])
                s1 = plan.op("scalar", lambda e, h=h, tc=tc, c8=c8, slot=slot:
                             e.activation(
                    expf[:, slot * M:(slot + 1) * M],
                    ps_dd[:, c8 * M:(c8 + 1) * M], AF.Exp,
                    bias=negbk[:, h * NC + tc: h * NC + tc + 1], scale=DN))
                plan.wait_eng("vector", "scalar", s1)
                v5 = plan.op("vector", lambda e, tc=tc, slot=slot, kpb=kpb:
                             e.tensor_scalar_add(
                    kpb[:, tc * M:(tc + 1) * M],
                    expf[:, slot * M:(slot + 1) * M], EPS_PERF))
                st["prev_exp"][c8] = v5
                st["expv"][slot] = v5
        st["kp_done"][h] = plan.cur("d_vector")

    def emit_ctx(h):
        kpb = kp2[h % 2]
        plan.wait_eng("tensor", "vector", st["kp_done"][h])
        for mc in range(2):
            if st["prev_ctx_ev"] is not None:
                plan.wait_eng("tensor", "scalar", st["prev_ctx_ev"])
            for tc in range(NC):
                plan.op("tensor", lambda e, h=h, mc=mc, tc=tc, kpb=kpb:
                        e.matmul(
                    ps_ctx[:, 0:64],
                    kpb[:, tc * M + mc * 128: tc * M + mc * 128 + 128],
                    vt[:, tc * D + h * DH: tc * D + (h + 1) * DH],
                    start=(tc == 0), stop=(tc == NC - 1),
                    skip_group_check=True))
                plan.op("tensor", lambda e, h=h, mc=mc, tc=tc, kpb=kpb:
                        e.matmul(
                    ps_one[:, 0:1],
                    kpb[:, tc * M + mc * 128: tc * M + mc * 128 + 128],
                    ones128b[:, :],
                    start=(tc == 0), stop=(tc == NC - 1),
                    skip_group_check=True))
            tdone = plan.cur("d_tensor")
            plan.wait_eng("scalar", "tensor", tdone)
            plan.op("scalar", lambda e, h=h, mc=mc: e.copy(
                ctx[:, (h * 2 + mc) * 65:(h * 2 + mc) * 65 + 64],
                ps_ctx[:, 0:64]))
            st["prev_ctx_ev"] = plan.op("scalar", lambda e, h=h, mc=mc:
                                        e.copy(
                ctx[:, (h * 2 + mc) * 65 + 64:(h * 2 + mc + 1) * 65],
                ps_one[:, 0:1]))
        st["ctx_t"][h % 2] = tdone

    for h in range(H):
        emit_kp(h)
        if h >= 1:
            emit_ctx(h - 1)
    emit_ctx(H - 1)
    plan.wait_eng("sync", "scalar", plan.cur("d_scalar"))
    plan.dma("sync", o_ctx, ctx[:, :], "st")


def emit_l0attn_p3(plan, C, ph):
    xT_d = C.xT
    qbf_d, negbq_d = C.qbf_i, C.negbq_i
    ctx_r_d = C.ctx_r.ap()
    PTm = C.PT
    Wo_d, bo_d = C.Wo, C.bo
    ones1f = C.ones1f
    eye_f = C.eye_f

    qbf = ph.sbuf("qbf_s", [128, IC * T], BF16)
    attn = ph.sbuf("attn_s", [128, IC * T], BF16)
    xoc = [ph.sbuf(f"xoc{i}", [128, SLAB], F32) for i in range(2)]
    x1slab = ph.sbuf("x1slab", [128, IC * SLAB], F32)
    row_sc1 = ph.sbuf("row_sc1", [1, 3 * SLAB + 1], F32)
    bc_sb1 = ph.sbuf("bc_sb1", [128, 2 * SLAB], F32)
    gw_sb = ph.sbuf("gw_sb", [128, IC * NE], F32)
    gb_sb = ph.sbuf("gb_sb", [NE, 1], F32)
    logit_sb = ph.sbuf("logit_sb", [NE, T], F32)
    usb_g = ph.sbuf("usb_g", [NE, T], F32)
    lmax_sb = ph.sbuf("lmax_sb", [NE, 1], F32)
    sume_sb = ph.sbuf("sume_sb", [NE, 1], F32)
    gmax_sb = ph.sbuf("gmax_sb", [NE, 1], F32)
    tterm_sb = ph.sbuf("tterm_sb", [NE, 1], F32)
    pbf = ph.sbuf("pbf", [128, M], BF16)
    negbq = ph.sbuf("negbq_s", [128, H * NC], F32)
    ctxa = ph.sbuf("ctxa", [128, H * 2 * 65], F32)
    ctxb = ph.sbuf("ctxb", [128, H * 2 * 65], BF16)
    nbb2 = [ph.sbuf(f"nbb{i}", [128, T], F32) for i in range(2)]
    trnb = ph.sbuf("trnb", [128, 256], F32)
    u_sb = ph.sbuf("u_sb", [128, 2 * TTW], F32)
    expf = ph.sbuf("expf", [128, 2 * TTW], F32)
    qp2 = [ph.sbuf(f"qp_s{i}", [128, 2 * T], BF16) for i in range(2)]
    denr2 = [ph.sbuf("denr", [1, TTW], F32), bc_sb1[0:1, TTW:2 * TTW]]
    rbb2 = [ph.sbuf("rbb", [128, TTW], F32), bc_sb1[:, 0:TTW]]
    eps_col = ph.sbuf("eps_col", [128, 1], F32)
    wbuf = [ph.sbuf(f"wbuf{i}", [128, D], BF16) for i in range(2)]
    bo_sb = ph.sbuf("bo_sb", [128, IC], F32)

    ps_dd = ph.psum("ps_dd", [128, 512], F32)
    ps_o = ph.psum("ps_o", [128, 512], F32)
    ps_den = ph.psum("ps_den", [128, 512], F32)
    ps_rb = ph.psum("ps_rb", [128, 512], F32)
    ps_bc = ph.psum("ps_bc", [128, 2048], F32)

    plan.dma("gpsimd", qbf[:, :], qbf_d)
    plan.dma("gpsimd", negbq[:, :], negbq_d)
    plan.dma("gpsimd", pbf[0:DH, :], PTm[:, :])
    plan.dma("gpsimd", pbf[DH:128, :], PTm[:, :])
    plan.wait("gpsimd", "cc_ctx", 1)
    plan.dma("gpsimd", ctxa[:, :], ctx_r_d)
    plan.dma("gpsimd", bo_sb[:, :], bo_d)
    plan.wait("gpsimd", "dma", plan.cur("dma"))
    plan.op("gpsimd", lambda e: e.memset(trnb[0:1, 0:1], 0.0))
    plan.op("gpsimd", lambda e: e.memset(eps_col[:, :], EPS_PERF))
    plan.join()

    vc = plan.op("vector", lambda e: e.tensor_copy(ctxb[:, :], ctxa[:, :]))
    plan.join()

    for blk in range(2):
        tb = plan.op("tensor", lambda e, blk=blk: e.transpose(
            ps_bc[0:128, 0:128], negbq[:, blk * 128:(blk + 1) * 128],
            eye_f[:, :]))
        plan.wait_eng("vector", "tensor", tb)
        vb = plan.op("vector", lambda e, blk=blk: e.tensor_copy(
            trnb[:, blk * 128:(blk + 1) * 128], ps_bc[0:128, 0:128]))
        plan.wait_eng("tensor", "vector", vb)
    plan.wait_eng("sync", "vector", plan.cur("d_vector"))
    for blk in range(2):
        plan.dma("sync",
                 C.nb_d[blk * 8:(blk + 1) * 8, :]
                 .rearrange("h (t i) -> (h t) i", t=16),
                 trnb[:, blk * 128:(blk + 1) * 128], "wv")
    plan.wait("gpsimd", "wv", plan.cur("wv"))
    nbdma = {}
    for h0 in range(2):
        nbdma[h0] = plan.dma("gpsimd", nbb2[h0][:, :],
                             C.nb_d[h0:h0 + 1, :].broadcast_to((128, T)),
                             "nbq")

    st2_t = {}
    qp_dv = {}
    qp_sc = {}
    qst = {}

    def stage1(h):
        oc = h // 2
        rb = 64 * (h % 2)
        qpb = qp2[h % 2]
        nbb = nbb2[h % 2]
        if st2_t.get(h % 2) is not None:
            plan.wait_eng("scalar", "tensor", st2_t[h % 2])
        if h + 1 < H:
            if h >= 1:
                plan.wait_eng("gpsimd", "vector", qp_dv[h - 1])
            nbdma[h + 1] = plan.dma(
                "gpsimd", nbb2[(h + 1) % 2][:, :],
                C.nb_d[h + 1:h + 2, :].broadcast_to((128, T)), "nbq")
        plan.wait("vector", "nbq", nbdma[h])
        for mc in range(2):
            for tt in range(NT):
                par = (mc * NT + tt) % 2
                psX = [ps_dd, ps_rb][par]
                if qst.get(("v3", par)) is not None:
                    plan.wait_eng("tensor", "vector", qst[("v3", par)])
                tgt = plan.op("tensor", lambda e, oc=oc, rb=rb, mc=mc, tt=tt,
                              psX=psX: e.matmul(
                    psX[:, :],
                    pbf[rb:rb + DH, mc * 128:(mc + 1) * 128],
                    qbf[rb:rb + DH, oc * T + tt * TTW: oc * T + (tt + 1) * TTW],
                    start=True, stop=True))
                plan.wait_eng("vector", "tensor", tgt)
                if qst.get(("s1", par)) is not None:
                    plan.wait_eng("vector", "scalar", qst[("s1", par)])
                v3 = plan.op("vector", lambda e, tt=tt, par=par, psX=psX:
                             e.scalar_tensor_tensor(
                    u_sb[:, par * TTW:(par + 1) * TTW], psX[:, :], DN,
                    nbb[:, tt * TTW:(tt + 1) * TTW], ALU.mult, ALU.add))
                plan.wait_eng("scalar", "vector", v3)
                if qst.get(("v4", par)) is not None:
                    plan.wait_eng("scalar", "scalar", qst[("v4", par)])
                s1 = plan.op("scalar", lambda e, par=par: e.activation(
                    expf[:, par * TTW:(par + 1) * TTW],
                    u_sb[:, par * TTW:(par + 1) * TTW], AF.Exp))
                plan.wait_eng("scalar", "scalar", s1)
                v4 = plan.op("scalar", lambda e, mc=mc, tt=tt, par=par,
                             qpb=qpb: e.activation(
                    qpb[:, mc * T + tt * TTW: mc * T + (tt + 1) * TTW],
                    expf[:, par * TTW:(par + 1) * TTW], AF.Identity,
                    bias=eps_col[:, 0:1]))
                qst[("v3", par)] = v3
                qst[("s1", par)] = s1
                qst[("v4", par)] = v4
        qp_dv[h] = plan.cur("d_vector")
        qp_sc[h] = plan.cur("d_scalar")

    pso_s2 = [ps_o[0:DH, 0:TTW], ps_bc[0:DH, TTW:2 * TTW]]
    psd_s2 = [ps_den[0:1, 0:TTW], ps_bc[0:1, 2 * TTW:3 * TTW]]
    bc_s2 = [ps_bc[:, 0:TTW], ps_bc[:, 3 * TTW:4 * TTW]]

    def stage2(h):
        oc = h // 2
        rb = 64 * (h % 2)
        qpb = qp2[h % 2]
        plan.wait_eng("tensor", "vector", qp_dv[h])
        plan.wait_eng("tensor", "scalar", qp_sc[h])

        def tail(tz):
            pz = tz % 2
            plan.wait_eng("vector", "tensor", qst[("t5", pz)])
            v5 = plan.op("vector", lambda e, pz=pz: e.tensor_copy(
                denr2[pz][:, :], psd_s2[pz]))
            plan.wait_eng("tensor", "vector", v5)
            if qst.get(("v6", pz)) is not None:
                plan.wait_eng("tensor", "vector", qst[("v6", pz)])
            t4 = plan.op("tensor", lambda e, pz=pz: e.matmul(
                bc_s2[pz], ones1f[0:1, :], denr2[pz][0:1, :],
                start=True, stop=True))
            plan.wait_eng("vector", "tensor", t4)
            v6 = plan.op("vector", lambda e, pz=pz: e.reciprocal(
                rbb2[pz][:, :], bc_s2[pz]))
            plan.wait_eng("vector", "vector", v6)
            v7 = plan.op("vector", lambda e, oc=oc, rb=rb, tz=tz, pz=pz:
                         e.tensor_mul(
                attn[rb:rb + DH, oc * T + tz * TTW: oc * T + (tz + 1) * TTW],
                pso_s2[pz], rbb2[pz][0:DH, :]))
            qst[("v6", pz)] = v6
            qst[("mul", pz)] = v7

        for tt in range(NT):
            par = tt % 2
            if qst.get(("mul", par)) is not None:
                plan.wait_eng("tensor", "vector", qst[("mul", par)])
            for mc in range(2):
                plan.op("tensor", lambda e, h=h, mc=mc, tt=tt, qpb=qpb,
                        par=par: e.matmul(
                    pso_s2[par],
                    ctxb[:, (h * 2 + mc) * 65:(h * 2 + mc) * 65 + DH],
                    qpb[:, mc * T + tt * TTW: mc * T + (tt + 1) * TTW],
                    start=(mc == 0), stop=(mc == 1)))
            for mc in range(2):
                t5 = plan.op("tensor", lambda e, h=h, mc=mc, tt=tt, qpb=qpb,
                             par=par: e.matmul(
                    psd_s2[par],
                    ctxb[:, (h * 2 + mc) * 65 + DH:(h * 2 + mc) * 65 + 65],
                    qpb[:, mc * T + tt * TTW: mc * T + (tt + 1) * TTW],
                    start=(mc == 0), stop=(mc == 1)))
            qst[("t5", par)] = t5
            if tt > 0:
                tail(tt - 1)
        tail(NT - 1)
        st2_t[h % 2] = plan.cur("d_tensor")

    for h in range(H):
        if h >= 1:
            stage2(h - 1)
        stage1(h)
    stage2(H - 1)
    plan.join()

    plan.dma("gpsimd", gw_sb[:, :].rearrange("p (i en) -> p i en", i=IC),
             C.gw.rearrange("(i p) en -> p i en", p=128), "wv")
    plan.dma("gpsimd", gb_sb[:, :], C.gb, "wv")
    last_mm = {}
    prev_ev = {}
    prev_half = {}
    for sb in range(2):
        for ocx in range(IC):
            idx = sb * IC + ocx
            buf = wbuf[ocx % 2]
            psem = "pc%d" % (ocx % 2)
            xsem = "xs%d" % (ocx % 2)
            if last_mm.get(idx % 2) is not None:
                plan.wait_eng("gpsimd", "tensor", last_mm[idx % 2])
            plan.dma("gpsimd", buf[:, :].rearrange("p (i o) -> p i o", i=IC),
                     Wo_d[:, ocx * 128:(ocx + 1) * 128]
                     .rearrange("(i p) o -> p i o", p=128), psem)
            if idx >= 2:
                plan.wait_eng("sync", "vector", prev_ev[ocx % 2])
            plan.dma("sync", xoc[ocx % 2][:, :],
                     xT_d[ocx * 128:(ocx + 1) * 128,
                          sb * SLAB:(sb + 1) * SLAB], xsem)
            plan.wait("tensor", psem, plan.cur(psem))
            plan.wait("vector", xsem, plan.cur(xsem))
            for tt2 in range(2):
                tt = sb * 2 + tt2
                half = tt2
                if half in prev_half:
                    plan.wait_eng("tensor", "vector", prev_half[half])
                for icx in range(IC):
                    tgt = plan.op("tensor", lambda e, buf=buf, icx=icx,
                                  tt=tt, half=half: e.matmul(
                        ps_bc[:, half * TTW:(half + 1) * TTW],
                        buf[:, icx * 128:(icx + 1) * 128],
                        attn[:, icx * T + tt * TTW: icx * T + (tt + 1) * TTW],
                        start=(icx == 0), stop=(icx == IC - 1)))
                last_mm[idx % 2] = tgt
                plan.wait_eng("vector", "tensor", tgt)
                prev_ev[ocx % 2] = plan.op(
                    "vector", lambda e, ocx=ocx, tt2=tt2, half=half:
                    e.scalar_tensor_tensor(
                        x1slab[:, ocx * SLAB + tt2 * TTW:
                               ocx * SLAB + (tt2 + 1) * TTW],
                        ps_bc[:, half * TTW:(half + 1) * TTW],
                        bo_sb[:, ocx:ocx + 1],
                        xoc[ocx % 2][:, tt2 * TTW:(tt2 + 1) * TTW],
                        ALU.add, ALU.add))
                prev_half[half] = prev_ev[ocx % 2]
        plan.join()
        emit_ln(plan, x1slab, qbf[:, 0:IC * SLAB],
                qbf[:, IC * SLAB:2 * IC * SLAB], C.ones128b, row_sc1,
                bc_sb1, C.ones1f, ps_den, ps_bc, tlen=SLAB)
        plan.join()
        plan.dma("sync",
                 C.x1_i[:, sb * SLAB:(sb + 1) * SLAB]
                 .rearrange("(i p) t -> p i t", p=128),
                 x1slab[:, :].rearrange("p (i t) -> p i t", i=IC), "st")
        plan.wait("tensor", "wv", plan.cur("wv"))
        for tt2 in range(2):
            for icx in range(IC):
                gmm = plan.op("tensor", lambda e, icx=icx, tt2=tt2: e.matmul(
                    ps_bc[0:NE, tt2 * TTW:(tt2 + 1) * TTW],
                    gw_sb[:, icx * NE:(icx + 1) * NE],
                    x1slab[:, icx * SLAB + tt2 * TTW:
                           icx * SLAB + (tt2 + 1) * TTW],
                    start=(icx == 0), stop=(icx == IC - 1),
                    skip_group_check=True))
        plan.wait_eng("vector", "tensor", gmm)
        gv = plan.op("vector", lambda e, sb=sb: e.tensor_scalar(
            logit_sb[:, sb * SLAB:(sb + 1) * SLAB],
            ps_bc[0:NE, 0:SLAB], gb_sb[:, 0:1], None, ALU.add))
        if sb == 0:
            plan.wait("vector", "st", plan.cur("st"))
            plan.wait_eng("vector", "vector", gv)
            plan.wait_eng("tensor", "vector", gv)
            plan.op("vector", lambda e: e.memset(row_sc1[0:1, 0:1], 0.0))
            plan.join()
    plan.join()
    v1g = plan.op("vector", lambda e: e.reduce_max(lmax_sb[:, :],
                                                   logit_sb[:, :], axis=AX.X))
    plan.wait_eng("vector", "vector", v1g)
    v2g = plan.op("vector", lambda e: e.tensor_scalar(
        usb_g[:, :], logit_sb[:, :], lmax_sb[:, 0:1], None, ALU.subtract))
    plan.wait_eng("scalar", "vector", v2g)
    s1g = plan.op("scalar", lambda e: e.activation(usb_g[:, :], usb_g[:, :],
                                                   AF.Exp))
    plan.wait_eng("vector", "scalar", s1g)
    v3g = plan.op("vector", lambda e: e.reduce_sum(sume_sb[:, :], usb_g[:, :],
                                                   axis=AX.X))
    plan.wait_eng("sync", "vector", plan.cur("d_vector"))
    plan.dma("sync", C.logits_i, logit_sb[:, :], "st")
    plan.dma("sync", C.lmax_w.ap(), lmax_sb[:, :], "st")
    phase_barrier(plan)
    emit_collective(plan, "cc_gmax", "AllReduce", ALU.max, G4,
                    C.lmax_w, C.gmax_r)
    plan.wait("gpsimd", "cc_gmax", 1)
    plan.dma("gpsimd", gmax_sb[:, :], C.gmax_r.ap())
    plan.wait("vector", "dma", plan.cur("dma"))
    vt1 = plan.op("vector", lambda e: e.tensor_tensor(
        tterm_sb[:, :], lmax_sb[:, :], gmax_sb[:, :], ALU.subtract))
    plan.wait_eng("scalar", "vector", vt1)
    st1 = plan.op("scalar", lambda e: e.activation(tterm_sb[:, :],
                                                   tterm_sb[:, :], AF.Exp))
    plan.wait_eng("vector", "scalar", st1)
    vt2 = plan.op("vector", lambda e: e.tensor_mul(tterm_sb[:, :],
                                                   tterm_sb[:, :],
                                                   sume_sb[:, :]))
    plan.wait_eng("sync", "vector", vt2)
    plan.dma("sync", C.t_w.ap(), tterm_sb[:, :], "st")
    phase_barrier(plan)
    emit_collective(plan, "cc_rden", "AllReduce", ALU.add, G4,
                    C.t_w, C.rden_r)


def emit_l0ffn_p1(plan, C, ph):
    x1p_d = C.x1p_i
    gw_d, gb_d = C.gw, C.gb
    o_x1 = C.x1_i
    o_logits = C.logits_i
    ones128b = C.ones128b
    ones1f = C.ones1f

    x_sb = ph.sbuf("x_sb", [128, IC * T], F32)
    xbf = ph.sbuf("xbf", [128, IC * T], BF16)
    sqb = ph.sbuf("sqb", [128, IC * T], BF16)
    row_sc = ph.sbuf("row_sc", [1, 3 * T + 1], F32)
    bc_sb = ph.sbuf("bc_sb", [128, 2 * T], F32)
    gw_sb = ph.sbuf("gw_sb", [128, IC * NE], F32)
    gb_sb = ph.sbuf("gb_sb", [NE, 1], F32)
    logit = ph.sbuf("logit", [NE, T], F32)
    lmax = ph.sbuf("lmax_s", [NE, 1], F32)
    usb = ph.sbuf("usb", [NE, T], F32)
    sume = ph.sbuf("sume_s", [NE, 1], F32)
    gmax = ph.sbuf("gmax_s", [NE, 1], F32)
    tterm = ph.sbuf("tterm", [NE, 1], F32)

    ps_den = ph.psum("ps_den", [128, 512], F32)
    ps_bc = ph.psum("ps_bc", [128, 2048], F32)
    ps_g = ph.psum("ps_g", [128, 2 * TTW], F32)

    plan.dma("gpsimd", x_sb[:, :].rearrange("p (i t) -> p i t", i=IC),
             x1p_d.rearrange("(i p) t -> p i t", p=128))
    plan.dma("gpsimd", gw_sb[:, :].rearrange("p (i en) -> p i en", i=IC),
             gw_d.rearrange("(i p) en -> p i en", p=128))
    plan.dma("gpsimd", gb_sb[:, :], gb_d)
    plan.wait("gpsimd", "dma", plan.cur("dma"))
    plan.op("gpsimd", lambda e: e.memset(row_sc[0:1, 0:1], 0.0))
    plan.join()
    emit_ln(plan, x_sb, xbf, sqb, ones128b, row_sc, bc_sb, ones1f,
            ps_den, ps_bc)
    plan.join()
    plan.dma("sync", o_x1.rearrange("(i p) t -> p i t", p=128),
             x_sb[:, :].rearrange("p (i t) -> p i t", i=IC), "st")
    prev_ev = {}
    for tt in range(NT):
        half = tt % 2
        if half in prev_ev:
            plan.wait_eng("tensor", "vector", prev_ev[half])
        for icx in range(IC):
            tgt = plan.op("tensor", lambda e, icx=icx, tt=tt, half=half:
                          e.matmul(
                ps_g[0:NE, half * TTW:(half + 1) * TTW],
                gw_sb[:, icx * NE:(icx + 1) * NE],
                x_sb[:, icx * T + tt * TTW: icx * T + (tt + 1) * TTW],
                start=(icx == 0), stop=(icx == IC - 1)))
        plan.wait_eng("vector", "tensor", tgt)
        prev_ev[half] = plan.op("vector", lambda e, tt=tt, half=half:
                                e.tensor_scalar(
            logit[:, tt * TTW:(tt + 1) * TTW],
            ps_g[0:NE, half * TTW:(half + 1) * TTW],
            gb_sb[:, 0:1], None, ALU.add))
    lg_done = plan.cur("d_vector")
    plan.wait_eng("vector", "vector", lg_done)
    v1 = plan.op("vector", lambda e: e.reduce_max(lmax[:, :], logit[:, :],
                                                  axis=AX.X))
    plan.wait_eng("vector", "vector", v1)
    v2 = plan.op("vector", lambda e: e.tensor_scalar(
        usb[:, :], logit[:, :], lmax[:, 0:1], None, ALU.subtract))
    plan.wait_eng("scalar", "vector", v2)
    s1 = plan.op("scalar", lambda e: e.activation(usb[:, :], usb[:, :],
                                                  AF.Exp))
    plan.wait_eng("vector", "scalar", s1)
    v3 = plan.op("vector", lambda e: e.reduce_sum(sume[:, :], usb[:, :],
                                                  axis=AX.X))
    plan.wait_eng("sync", "vector", plan.cur("d_vector"))
    plan.dma("sync", o_logits, logit[:, :], "st")
    plan.dma("sync", C.lmax_w.ap(), lmax[:, :], "st")
    phase_barrier(plan)
    emit_collective(plan, "cc_gmax", "AllReduce", ALU.max, G4,
                    C.lmax_w, C.gmax_r)
    plan.wait("gpsimd", "cc_gmax", 1)
    plan.dma("gpsimd", gmax[:, :], C.gmax_r.ap())
    plan.wait("vector", "dma", plan.cur("dma"))
    vt1 = plan.op("vector", lambda e: e.tensor_tensor(
        tterm[:, :], lmax[:, :], gmax[:, :], ALU.subtract))
    plan.wait_eng("scalar", "vector", vt1)
    st1 = plan.op("scalar", lambda e: e.activation(tterm[:, :], tterm[:, :],
                                                   AF.Exp))
    plan.wait_eng("vector", "scalar", st1)
    vt2 = plan.op("vector", lambda e: e.tensor_mul(tterm[:, :], tterm[:, :],
                                                   sume[:, :]))
    plan.wait_eng("sync", "vector", vt2)
    plan.dma("sync", C.t_w.ap(), tterm[:, :], "st")
    phase_barrier(plan)
    emit_collective(plan, "cc_rden", "AllReduce", ALU.add, G4,
                    C.t_w, C.rden_r)


def emit_l0ffn_p2(plan, C, ph):
    x1_d = C.x1_i
    logits_d = C.logits_i
    w1_d = C.w1
    w2_d = C.w2
    w3_d = C.w3
    b1_d, b2_d, b3_d = C.b1, C.b2, C.b3
    o_ind = C.ind_i
    eye_f = C.eye_f
    ones1f = C.ones1f

    xbf = ph.sbuf("xbf", [128, IC * T], BF16)
    h1 = ph.sbuf("h1", [128, IC * T], BF16)
    h2 = ph.sbuf("h2", [128, IC * T], BF16)
    ysb = ph.sbuf("ysb", [128, T], F32)
    ydump = ph.sbuf("ydump", [128, T], F32)
    wbuf = [ph.sbuf(f"wbuf{i}", [128, D], BF16) for i in range(4)]
    logit = ph.sbuf("logit", [NE, T], F32)
    gmax = ph.sbuf("gmax", [NE, 1], F32)
    rden = ph.sbuf("rden", [NE, 1], F32)
    probs = ph.sbuf("probs", [NE, T], F32)
    ptm = ph.sbuf("ptm", [128, NC * NE], F32)
    m1c = ph.sbuf("m1c", [128, 1], F32)
    m2c = ph.sbuf("m2c", [128, 1], F32)
    eqc = ph.sbuf("eqc", [128, NE], F32)
    wtok_cols = ph.sbuf("wtok_cols", [128, NE * NC], F32)
    ind_cols = ph.sbuf("ind_cols", [128, NE * NC], F32)
    ind_fm = ph.sbuf("ind_fm", [NE, T], BF16)
    wb2 = [ph.sbuf(f"wb2_{i}", [128, T], F32) for i in range(2)]
    trwt = ph.sbuf("trwt", [128, 128], F32)
    v_sb = ph.sbuf("v_sb", [128, NE * IC], F32)
    vT_sb = ph.sbuf("vT_sb", [NE * IC, 128], F32)
    b1s = ph.sbuf("b1s", [128, NE * IC], F32)
    b2s = ph.sbuf("b2s", [128, NE * IC], F32)
    b3s = ph.sbuf("b3s", [128, NE * IC], F32)

    ps_pair = [ph.psum(f"ps_p{i}", [128, 1024], F32) for i in range(2)]
    ps_t = ph.psum("ps_t", [128, 512], F32)

    plan.dma("gpsimd", xbf[:, :].rearrange("p (i t) -> p i t", i=IC),
             x1_d.rearrange("(i p) t -> p i t", p=128))
    plan.dma("gpsimd", logit[:, :], logits_d)
    plan.wait("gpsimd", "cc_gmax", 1)
    plan.dma("gpsimd", gmax[:, :], C.gmax_r.ap())
    plan.wait("gpsimd", "cc_rden", 1)
    plan.dma("gpsimd", rden[:, :], C.rden_r.ap())
    plan.dma("gpsimd", b1s[:, :], b1_d)
    plan.dma("gpsimd", b2s[:, :], b2_d)
    plan.dma("gpsimd", b3s[:, :], b3_d)
    plan.wait("gpsimd", "dma", plan.cur("dma"))
    plan.op("gpsimd", lambda e: e.memset(trwt[0:1, 0:1], 0.0))
    plan.join()

    v5 = plan.op("vector", lambda e: e.reciprocal(rden[:, :], rden[:, :]))
    plan.wait_eng("vector", "vector", v5)
    v6 = plan.op("vector", lambda e: e.tensor_scalar(
        probs[:, :], logit[:, :], gmax[:, 0:1], None, ALU.subtract))
    plan.wait_eng("scalar", "vector", v6)
    s2 = plan.op("scalar", lambda e: e.activation(probs[:, :], probs[:, :],
                                                  AF.Exp))
    plan.wait_eng("vector", "scalar", s2)
    v7 = plan.op("vector", lambda e: e.tensor_scalar(
        probs[:, :], probs[:, :], rden[:, 0:1], None, ALU.mult))
    plan.wait_eng("tensor", "vector", v7)

    for tc in range(NC):
        t1 = plan.op("tensor", lambda e, tc=tc: e.transpose(
            ps_t[0:128, 0:NE], probs[:, tc * 128:(tc + 1) * 128],
            eye_f[0:NE, 0:NE]))
        plan.wait_eng("vector", "tensor", t1)
        vv = plan.op("vector", lambda e, tc=tc: e.tensor_copy(
            ptm[:, tc * NE:(tc + 1) * NE], ps_t[0:128, 0:NE]))
        plan.wait_eng("tensor", "vector", vv)
    plan.wait_eng("vector", "vector", plan.cur("d_vector"))
    for tc in range(NC):
        sl = slice(tc * NE, (tc + 1) * NE)
        va = plan.op("vector", lambda e, sl=sl: e.reduce_max(
            m1c[:, :], ptm[:, sl], axis=AX.X))
        plan.wait_eng("vector", "vector", va)
        vb = plan.op("vector", lambda e, sl=sl: e.tensor_scalar(
            eqc[:, :], ptm[:, sl], m1c[:, 0:1], -1e30, ALU.is_equal,
            ALU.mult))
        plan.wait_eng("vector", "vector", vb)
        vc = plan.op("vector", lambda e, sl=sl: e.tensor_add(
            eqc[:, :], eqc[:, :], ptm[:, sl]))
        plan.wait_eng("vector", "vector", vc)
        vd = plan.op("vector", lambda e: e.reduce_max(
            m2c[:, :], eqc[:, :], axis=AX.X))
        plan.wait_eng("vector", "vector", vd)
        ve = plan.op("vector", lambda e, sl=sl, tc=tc: e.tensor_scalar(
            ind_cols[:, tc: NE * NC: NC], ptm[:, sl], m2c[:, 0:1], None,
            ALU.is_ge))
        plan.wait_eng("vector", "vector", ve)
        vf = plan.op("vector", lambda e, sl=sl, tc=tc: e.tensor_mul(
            wtok_cols[:, tc: NE * NC: NC], ind_cols[:, tc: NE * NC: NC],
            ptm[:, sl]))
    sel_done = plan.cur("d_vector")
    plan.wait_eng("tensor", "vector", sel_done)
    for tc in range(NC):
        t2 = plan.op("tensor", lambda e, tc=tc: e.transpose(
            ps_t[0:NE, 0:128], ind_cols[:, tc: NE * NC: NC], eye_f[:, :]))
        plan.wait_eng("vector", "tensor", t2)
        vg = plan.op("vector", lambda e, tc=tc: e.tensor_copy(
            ind_fm[:, tc * 128:(tc + 1) * 128], ps_t[0:NE, 0:128]))
        plan.wait_eng("tensor", "vector", vg)
    plan.join()
    plan.dma("sync", o_ind, ind_fm[:, :], "st")

    tw = plan.op("tensor", lambda e: e.transpose(
        ps_t[0:128, 0:128], wtok_cols[:, :], eye_f[:, :]))
    plan.wait_eng("vector", "tensor", tw)
    vtw = plan.op("vector", lambda e: e.tensor_copy(
        trwt[:, :], ps_t[0:128, 0:128]))
    plan.wait_eng("sync", "vector", vtw)
    plan.wait_eng("tensor", "vector", vtw)
    plan.dma("sync",
             C.wt_d[0:NE, :].rearrange("h (t i) -> (h t) i", t=NC),
             trwt[:, :], "wv")
    plan.wait("gpsimd", "wv", plan.cur("wv"))
    wtdma = {0: plan.dma("gpsimd", wb2[0][:, :],
                         C.wt_d[0:1, :].broadcast_to((128, T)), "nbq")}
    plan._vprev_g = None

    def issue_w(W, oc2):
        key = plan._wiss % 4
        plan._wiss += 1
        psem = "pw%d" % key
        if plan._wpar.get(key) is not None:
            plan.wait_eng("gpsimd", "tensor", plan._wpar[key])
        plan._wsem[key] = plan.dma(
            "gpsimd", wbuf[key][:, :].rearrange("p (i o) -> p i o", i=IC),
            W[:, oc2 * 128:(oc2 + 1) * 128]
            .rearrange("(i p) o -> p i o", p=128), psem)

    def fm_layer(src_sb, W_e, bias_cols, e_idx, dst_sb, relu_to_y=None,
                 next_W=None):
        plan.wait_eng("tensor", "scalar", plan.cur("d_scalar"))
        for oc in range(IC):
            if oc + 2 < IC:
                issue_w(W_e, oc + 2)
            elif next_W is not None:
                issue_w(next_W, oc + 2 - IC)
            key = plan._wuse % 4
            plan._wuse += 1
            buf = wbuf[key]
            plan.wait("tensor", "pw%d" % key, plan._wsem[key])
            for pp in range(2):
                psX = ps_pair[pp]
                prev = plan._psv2.get(pp)
                if prev is not None:
                    plan.wait_eng("tensor", "scalar", prev)
                for icx in range(IC):
                    for qq in range(2):
                        tt = pp * 2 + qq
                        tgt = plan.op("tensor", lambda e, buf=buf, icx=icx,
                                      tt=tt, qq=qq, psX=psX, src_sb=src_sb:
                                      e.matmul(
                            psX[:, qq * TTW:(qq + 1) * TTW],
                            buf[:, icx * 128:(icx + 1) * 128],
                            src_sb[:, icx * T + tt * TTW:
                                   icx * T + (tt + 1) * TTW],
                            start=(icx == 0), stop=(icx == IC - 1),
                            skip_group_check=True))
                plan._wpar[key] = tgt
                plan.wait_eng("scalar", "tensor", tgt)
                for qq in range(2):
                    tt = pp * 2 + qq
                    if relu_to_y is not None:
                        sv = plan.op("scalar", lambda e, oc=oc, tt=tt, qq=qq,
                                     psX=psX, e_idx=e_idx: e.activation(
                            ysb[:, tt * TTW:(tt + 1) * TTW],
                            psX[:, qq * TTW:(qq + 1) * TTW], AF.Relu,
                            bias=bias_cols[:, e_idx * IC + oc:
                                           e_idx * IC + oc + 1]))
                        plan._yev[(oc, tt)] = sv
                    else:
                        sv = plan.op("scalar", lambda e, oc=oc, tt=tt, qq=qq,
                                     psX=psX, e_idx=e_idx, dst_sb=dst_sb:
                                     e.activation(
                            dst_sb[:, oc * T + tt * TTW:
                                   oc * T + (tt + 1) * TTW],
                            psX[:, qq * TTW:(qq + 1) * TTW], AF.Relu,
                            bias=bias_cols[:, e_idx * IC + oc:
                                           e_idx * IC + oc + 1]))
                    plan._psv2[pp] = sv
            if relu_to_y is not None:
                sv = plan._yev[(oc, NT - 1)]
                plan.wait_eng("vector", "scalar", sv)
                if plan._vprev is not None:
                    plan.wait_eng("vector", "vector", plan._vprev)
                plan._vprev = plan.op(
                    "vector", lambda e, oc=oc, e_idx=e_idx, wb_sb=wb_sb:
                    e.scalar_tensor_tensor(
                        ydump[:, :], ysb[:, :], 1.0, wb_sb[:, :],
                        ALU.mult, ALU.mult,
                        accum_out=v_sb[:, e_idx * IC + oc:
                                       e_idx * IC + oc + 1]))
                plan.wait_eng("scalar", "vector", plan._vprev)

    plan._wpar = {}
    plan._wsem = {}
    plan._wiss = 0
    plan._wuse = 0
    plan._psv2 = {}
    plan._vprev = None
    issue_w(w1_d[0], 0)
    issue_w(w1_d[0], 1)
    for e_idx in range(NE):
        plan._yev = {}
        wb_sb = wb2[e_idx % 2]
        if e_idx + 1 < NE:
            if plan._vprev_g is not None:
                plan.wait_eng("gpsimd", "vector", plan._vprev_g)
            wtdma[e_idx + 1] = plan.dma(
                "gpsimd", wb2[(e_idx + 1) % 2][:, :],
                C.wt_d[e_idx + 1:e_idx + 2, :].broadcast_to((128, T)),
                "nbq")
        plan.wait("vector", "nbq", wtdma[e_idx])
        nxt = w1_d[e_idx + 1] if e_idx + 1 < NE else None
        fm_layer(xbf, w1_d[e_idx], b1s, e_idx, h1, next_W=w2_d[e_idx])
        fm_layer(h1, w2_d[e_idx], b2s, e_idx, h2, next_W=w3_d[e_idx])
        fm_layer(h2, w3_d[e_idx], b3s, e_idx, None, relu_to_y=True,
                 next_W=nxt)
        plan._vprev_g = plan._vprev
    dvv = plan.cur("d_vector")
    plan.wait_eng("tensor", "vector", dvv)
    tv = plan.op("tensor", lambda e: e.transpose(
        ps_t[0:NE * IC, 0:128], v_sb[:, :], eye_f[:, :]))
    plan.wait_eng("vector", "tensor", tv)
    vv2 = plan.op("vector", lambda e: e.tensor_copy(vT_sb[:, :],
                                                    ps_t[0:NE * IC, 0:128]))
    plan.wait_eng("sync", "vector", vv2)
    plan.dma("sync", C.v_w.ap(), vT_sb[:, :], "st")


def emit_l0ffn_p3(plan, C, ph):
    x1_d = C.x1_i
    ind_d = C.ind_i

    v_acc = ph.sbuf("v_acc", [NE, D], F32)
    v_bf = ph.sbuf("v_bf", [NE, D], BF16)
    ind_fm = ph.sbuf("ind_fm2", [NE, T], BF16)
    xoc = [ph.sbuf(f"xoc{i}", [128, T], F32) for i in range(2)]
    x2full = ph.sbuf("x2full", [128, IC * T], F32)
    ln_xbf = ph.sbuf("ln_xbf", [128, IC * T], BF16)
    ln_sqb = ph.sbuf("ln_sqb", [128, IC * T], BF16)
    row_sc = ph.sbuf("row_sc", [1, 3 * T + 1], F32)
    bc_sb = ph.sbuf("bc_sb", [128, 2 * T], F32)
    ps_mm = ph.psum("ps_mm", [128, 2 * TTW], F32)
    ps_den = ph.psum("ps_den", [128, 512], F32)
    ps_bc = ph.psum("ps_bc", [128, 2048], F32)

    plan.wait("gpsimd", "cc_v", 1)
    plan.dma("gpsimd", v_acc[:, :],
             C.v_r.ap().rearrange("(e r) p -> e (r p)", e=NE))
    plan.dma("gpsimd", ind_fm[:, :], ind_d)
    plan.wait("gpsimd", "dma", plan.cur("dma"))
    plan.wait("vector", "dma", plan.cur("dma"))
    vb = plan.op("vector", lambda e: e.tensor_copy(v_bf[:, :], v_acc[:, :]))
    plan.op("gpsimd", lambda e: e.memset(row_sc[0:1, 1:2], 0.0))
    plan.join()

    prev_half = {}
    for ocx in range(IC):
        xsem = "xs%d" % (ocx % 2)
        if ocx >= 2:
            plan.wait_eng("sync", "vector", prev_half.get((ocx - 2, NT - 1), 0))
        plan.dma("sync", xoc[ocx % 2][:, :],
                 x1_d[ocx * 128:(ocx + 1) * 128, :], xsem)
        plan.wait("vector", xsem, plan.cur(xsem))
        for tt in range(NT):
            half = tt % 2
            if prev_half.get(half) is not None:
                plan.wait_eng("tensor", "vector", prev_half[half])
            tgt = plan.op("tensor", lambda e, ocx=ocx, tt=tt, half=half:
                          e.matmul(
                ps_mm[:, half * TTW:(half + 1) * TTW],
                v_bf[:, ocx * 128:(ocx + 1) * 128],
                ind_fm[:, tt * TTW:(tt + 1) * TTW],
                start=True, stop=True))
            plan.wait_eng("vector", "tensor", tgt)
            done = plan.op("vector", lambda e, ocx=ocx, tt=tt, half=half:
                           e.tensor_add(
                x2full[:, ocx * T + tt * TTW: ocx * T + (tt + 1) * TTW],
                ps_mm[:, half * TTW:(half + 1) * TTW],
                xoc[ocx % 2][:, tt * TTW:(tt + 1) * TTW]))
            prev_half[half] = done
            prev_half[(ocx, tt)] = done
    plan.join()
    emit_ln(plan, x2full, ln_xbf, ln_sqb, C.ones128b, row_sc, bc_sb,
            C.ones1f, ps_den, ps_bc)
    plan.join()
    plan.dma("sync", C.x2_i.rearrange("(i p) t -> p i t", p=128),
             x2full[:, :].rearrange("p (i t) -> p i t", i=IC), "st")


def emit_ln_phase(plan, C, ph, xin, xout):
    ones128b = C.ones128b
    ones1f = C.ones1f
    x_sb = ph.sbuf("x_sb", [128, IC * T], F32)
    xbf = ph.sbuf("xbf", [128, IC * T], BF16)
    sqb = ph.sbuf("sqb", [128, IC * T], BF16)
    row_sc = ph.sbuf("row_sc", [1, 3 * T + 1], F32)
    bc_sb = ph.sbuf("bc_sb", [128, 2 * T], F32)
    ps_den = ph.psum("ps_den", [128, 512], F32)
    ps_bc = ph.psum("ps_bc", [128, 2048], F32)
    plan.dma("gpsimd", x_sb[:, :].rearrange("p (i t) -> p i t", i=IC),
             xin.rearrange("(i p) t -> p i t", p=128))
    plan.wait("gpsimd", "dma", plan.cur("dma"))
    plan.op("gpsimd", lambda e: e.memset(row_sc[0:1, 1:2], 0.0))
    plan.join()
    emit_ln(plan, x_sb, xbf, sqb, ones128b, row_sc, bc_sb, ones1f,
            ps_den, ps_bc)
    plan.join()
    plan.dma("sync", xout.rearrange("(i p) t -> p i t", p=128),
             x_sb[:, :].rearrange("p (i t) -> p i t", i=IC), "st")


def emit_l1attn_p1(plan, C, ph):
    x2_d = C.x2_i
    Wq_d, Wk_d, Wv_d = C.Wq2, C.Wk2, C.Wv2
    bq_d, bk_d, bv_d = C.bq2, C.bk2, C.bv2
    lidx_d, t0c_d = C.lidx, C.t0c
    o_qbf, o_kbf, o_vt = C.qbf2_i, C.kbf2_i, C.vt2_i
    o_qn, o_kn = C.qn_i, C.kn_i
    ktm_i = C.ktm_i
    ones128b = C.ones128b
    ones1f = C.ones1f
    eye_f = C.eye_f
    eye_b = C.eye_b

    xbf = ph.sbuf("xbf", [128, IC * T], BF16)
    abf = ph.sbuf("abf", [128, IC * T], BF16)
    asq = ph.sbuf("asq", [128, IC * T], BF16)
    vt = ph.sbuf("vt_s", [128, NC * D], BF16)
    wbuf = [ph.sbuf(f"wbuf{i}", [128, D], BF16) for i in range(2)]
    wvb = ph.sbuf("wvb", [128, IC * D], BF16)
    bcol = ph.sbuf("bcol", [128, 2 * IC], F32)
    bvrow = ph.sbuf("bvrow", [1, D], F32)
    bvb = ph.sbuf("bvb", [128, D], F32)
    ncols = ph.sbuf("ncols", [128, H * NC], F32)
    ktile = [ph.sbuf(f"ktile{i}", [128, D], BF16)
             for i in range(2)]
    lmp = ph.sbuf("lmp", [NL, D], BF16)
    lidx = ph.sbuf("lidx_s", [NL, 1], I32)
    t0c = ph.sbuf("t0c_s", [NL, 1], I32)
    idxa = ph.sbuf("idxa", [NL, 1], I32)
    vm1 = ph.sbuf("vm1", [NL, 1], I32)
    vm2 = ph.sbuf("vm2", [NL, 1], I32)
    validf = ph.sbuf("validf", [NL, 1], F32)

    ps_mm = ph.psum("ps_mm", [128, 2 * TTW], F32)
    ps_mm2 = ph.psum("ps_mm2", [128, 2 * TTW], F32)
    ps_dg = ph.psum("ps_dg", [128, 512], F32)
    ps_tp = ph.psum("ps_tp", [128, 1024], BF16)
    ps_bc = ph.psum("ps_bc", [128, 1024], F32)

    plan.dma("gpsimd", xbf[:, :].rearrange("p (i t) -> p i t", i=IC),
             x2_d.rearrange("(i p) t -> p i t", p=128))
    plan.dma("gpsimd", bcol[:, 0:IC], bq_d)
    plan.dma("gpsimd", bcol[:, IC:2 * IC], bk_d)
    plan.dma("gpsimd", bvrow[:, :], bv_d)
    plan.dma("gpsimd", lidx[:, :], lidx_d)
    plan.dma("gpsimd", t0c[:, :], t0c_d)
    plan.wait("gpsimd", "dma", plan.cur("dma"))
    plan.op("gpsimd", lambda e: e.memset(lmp[:, :], 0.0))
    plan.join()


    def proj_side(W_ap, bias_off, dst, dst_sq, ncols_dst):
        last_mm = {}
        ps_pairs = [ps_mm[:, 0:1024], ps_mm2[:, 0:1024]]
        for oc in range(IC):
            buf = wbuf[oc % 2]
            psem = "pc%d" % (oc % 2)
            if oc >= 2:
                plan.wait_eng("gpsimd", "tensor", last_mm[oc - 2])
            plan.dma("gpsimd", buf[:, :].rearrange("p (i o) -> p i o", i=IC),
                     W_ap[:, oc * 128:(oc + 1) * 128]
                     .rearrange("(i p) o -> p i o", p=128), psem)
            plan.wait("tensor", psem, plan.cur(psem))
            for pp in range(2):
                psX = ps_pairs[pp]
                prev = plan._psv.get(pp)
                if prev is not None:
                    plan.wait_eng("tensor", "scalar", prev[0])
                    plan.wait_eng("tensor", "vector", prev[1])
                for icx in range(IC):
                    for qq in range(2):
                        tt = pp * 2 + qq
                        tgt = plan.op("tensor", lambda e, buf=buf, icx=icx,
                                      tt=tt, qq=qq, psX=psX: e.matmul(
                            psX[:, qq * TTW:(qq + 1) * TTW],
                            buf[:, icx * 128:(icx + 1) * 128],
                            xbf[:, icx * T + tt * TTW:
                                icx * T + (tt + 1) * TTW],
                            start=(icx == 0), stop=(icx == IC - 1),
                            skip_group_check=True))
                last_mm[oc] = tgt
                plan.wait_eng("scalar", "tensor", tgt)
                for qq in range(2):
                    tt = pp * 2 + qq
                    se_t = plan.op("scalar", lambda e, oc=oc, tt=tt, qq=qq,
                                   psX=psX: e.add(
                        dst[:, oc * T + tt * TTW: oc * T + (tt + 1) * TTW],
                        psX[:, qq * TTW:(qq + 1) * TTW],
                        bcol[:, bias_off + oc: bias_off + oc + 1]))
                    plan.wait_eng("vector", "scalar", se_t)
                    ve_t = plan.op("vector", lambda e, oc=oc, tt=tt:
                                   e.tensor_mul(
                        dst_sq[:, oc * T + tt * TTW: oc * T + (tt + 1) * TTW],
                        dst[:, oc * T + tt * TTW: oc * T + (tt + 1) * TTW],
                        dst[:, oc * T + tt * TTW: oc * T + (tt + 1) * TTW]))
                plan._psv[pp] = (se_t, ve_t)
        sq_done = plan.cur("d_vector")
        plan.wait_eng("tensor", "vector", sq_done)
        last_cp = None
        for h in range(H):
            oc = h // 2
            rb = 64 * (h % 2)
            if last_cp is not None:
                plan.wait_eng("tensor", "vector", last_cp)
            for tc in range(NC):
                plan.op("tensor", lambda e, oc=oc, rb=rb, tc=tc: e.matmul(
                    ps_dg[:, tc:tc + 1],
                    dst_sq[rb:rb + DH,
                           oc * T + tc * 128: oc * T + (tc + 1) * 128],
                    ones128b[rb:rb + DH, :], start=True, stop=True))
            tgt = plan.cur("d_tensor")
            plan.wait_eng("vector", "tensor", tgt)
            last_cp = plan.op("vector", lambda e, h=h: e.tensor_copy(
                ncols[:, h * NC:(h + 1) * NC], ps_dg[:, 0:NC]))
        plan.wait_eng("sync", "vector", last_cp)
        plan.dma("sync", ncols_dst, ncols[:, :], "st")

    plan._psv = {}
    proj_side(Wq_d, 0, abf, asq, o_qn)
    plan.join()
    plan.dma("sync", o_qbf, abf[:, :], "st")
    plan.wait("sync", "st", plan.cur("st"))
    plan.op("sync", lambda e: e.nop())
    plan.join()
    plan._psv = {}
    proj_side(Wk_d, IC, abf, asq, o_kn)
    plan.join()
    plan.dma("sync", o_kbf, abf[:, :], "st")

    prev_cp = None
    prev_st = {}
    for tc in range(NC):
        if prev_cp is not None:
            plan.wait_eng("tensor", "vector", prev_cp)
        if tc >= 2 and (tc % 2) in prev_st:
            plan.wait("vector", "kt%d" % (tc % 2), prev_st[tc % 2])
        for oc in range(IC):
            plan.op("tensor", lambda e, tc=tc, oc=oc: e.transpose(
                ps_tp[:, oc * 128:(oc + 1) * 128] if False else
                ps_tp[:, 0:1024].rearrange("p (a b) -> p a b", a=8)[:, oc, :],
                abf[:, oc * T + tc * 128: oc * T + (tc + 1) * 128],
                eye_b[:, :]))
        tgt = plan.cur("d_tensor")
        plan.wait_eng("vector", "tensor", tgt)
        prev_cp = plan.op("vector", lambda e, tc=tc: e.tensor_copy(
            ktile[tc % 2][:, :], ps_tp[:, 0:1024]))
        plan.wait_eng("sync", "vector", prev_cp)
        prev_st[tc % 2] = plan.dma(
            "sync", ktm_i[tc * 128:(tc + 1) * 128, :], ktile[tc % 2][:, :],
            "kt%d" % (tc % 2))
    plan.wait("vector", "kt0", plan.cur("kt0"))
    plan.wait("vector", "kt1", plan.cur("kt1"))

    va = plan.op("vector", lambda e: e.tensor_tensor(
        idxa[:, :], lidx[:, :], t0c[:, :], ALU.subtract))
    plan.wait_eng("vector", "vector", va)
    vb = plan.op("vector", lambda e: e.tensor_scalar(
        vm1[:, :], idxa[:, :], 0, None, ALU.is_ge))
    vc = plan.op("vector", lambda e: e.tensor_scalar(
        vm2[:, :], idxa[:, :], T, None, ALU.is_lt))
    plan.wait_eng("vector", "vector", vc)
    vd = plan.op("vector", lambda e: e.tensor_tensor(
        vm1[:, :], vm1[:, :], vm2[:, :], ALU.mult))
    plan.wait_eng("vector", "vector", vd)
    ve = plan.op("vector", lambda e: e.tensor_tensor(
        idxa[:, :], idxa[:, :], vm1[:, :], ALU.mult))
    vf = plan.op("vector", lambda e: e.tensor_copy(validf[:, :], vm1[:, :]))
    plan.wait_eng("gpsimd", "vector", plan.cur("d_vector"))
    gt = plan.bump("gth", 16)

    def _gather(e, s=plan.sems["gth"]):
        from concourse.bass_core_rust import IndirectOffsetOnAxis
        e.indirect_dma_start(
            lmp[:, :], None, ktm_i[:, :],
            IndirectOffsetOnAxis(idxa[:, 0:1], 0)).then_inc(s, 16)
    plan.add("gpsimd", _gather)
    plan.wait("vector", "gth", gt)
    vg = plan.op("vector", lambda e: e.tensor_scalar(
        lmp[:, :], lmp[:, :], validf[:, 0:1], None, ALU.mult))
    plan.wait_eng("sync", "vector", vg)
    plan.dma("sync", C.lm_w.ap(), lmp[:, :], "st")

    plan.dma("gpsimd", wvb[:, :].rearrange("p (i o) -> p i o", i=IC),
             Wv_d.rearrange("(i p) o -> p i o", p=128), "wv")
    plan.wait("tensor", "wv", plan.cur("wv"))
    dvv = plan.cur("d_vector")
    plan.wait_eng("tensor", "vector", dvv)
    for hb in range(2):
        plan.op("tensor", lambda e, hb=hb: e.matmul(
            ps_bc[:, hb * TTW:(hb + 1) * TTW], ones1f[0:1, :],
            bvrow[0:1, hb * TTW:(hb + 1) * TTW], start=True, stop=True))
    t5 = plan.cur("d_tensor")
    plan.wait_eng("vector", "tensor", t5)
    v5 = plan.op("vector", lambda e: e.tensor_copy(bvb[:, 0:1024],
                                                   ps_bc[:, 0:1024]))
    plan.wait_eng("vector", "vector", v5)
    plan.join()
    prev_slot = {}
    for tc in range(NC):
        for hp in range(IC):
            slot = (tc * IC + hp) % 2
            if slot in prev_slot:
                plan.wait_eng("tensor", "vector", prev_slot[slot])
            for icx in range(IC):
                tgt = plan.op("tensor", lambda e, tc=tc, hp=hp, icx=icx,
                              slot=slot: e.matmul(
                    ps_mm[:, slot * TTW: slot * TTW + 128],
                    xbf[:, icx * T + tc * 128: icx * T + (tc + 1) * 128],
                    wvb[:, icx * D + hp * 128: icx * D + (hp + 1) * 128],
                    start=(icx == 0), stop=(icx == IC - 1)))
            plan.wait_eng("vector", "tensor", tgt)
            prev_slot[slot] = plan.op(
                "vector", lambda e, tc=tc, hp=hp, slot=slot:
                e.scalar_tensor_tensor(
                    vt[:, tc * D + hp * 128: tc * D + (hp + 1) * 128],
                    ps_mm[:, slot * TTW: slot * TTW + 128], 1.0,
                    bvb[:, hp * 128:(hp + 1) * 128], ALU.mult, ALU.add))
    plan.wait_eng("sync", "vector", plan.cur("d_vector"))
    plan.dma("sync", o_vt, vt[:, :], "st")


NS_ITERS = 14


def emit_lm_common(plan, C, lm_r_d, temp_d, lmp, lmt, lbf, lsq, eye_f, eye_b,
                   ones64b, ones1f, invt, t2c, invtc, lnt_rows, lncol_t,
                   lnbT, ps_a, ps_row, ps_tpb):
    plan.wait("gpsimd", "cc_lm", 1)
    plan.dma("gpsimd", lmp[:, :], lm_r_d)
    plan.dma("gpsimd", invt[0:1, 0:1], temp_d[0:1, 0:1])
    plan.wait("gpsimd", "dma", plan.cur("dma"))
    plan.op("gpsimd", lambda e: e.memset(lmt[0:1, 0:1], 0.0))
    plan.join()
    v1 = plan.op("vector", lambda e: e.reciprocal(invt[0:1, 0:1],
                                                  invt[0:1, 0:1]))
    plan.wait_eng("tensor", "vector", v1)
    t1 = plan.op("tensor", lambda e: e.matmul(
        ps_row[0:128, 0:1], ones1f[0:1, :], invt[0:1, 0:1],
        start=True, stop=True))
    plan.wait_eng("vector", "tensor", t1)
    v2 = plan.op("vector", lambda e: e.tensor_copy(invtc[:, :],
                                                   ps_row[0:128, 0:1]))
    plan.wait_eng("vector", "vector", v2)
    v3 = plan.op("vector", lambda e: e.tensor_scalar_mul(t2c[:, :],
                                                         invtc[:, :], 2.0))
    for h in range(H):
        t2 = plan.op("tensor", lambda e, h=h: e.transpose(
            ps_tpb[0:DH, 0:DH], lmp[:, h * DH:(h + 1) * DH],
            eye_b[0:NL, 0:NL]))
        plan.wait_eng("vector", "tensor", t2)
        va = plan.op("vector", lambda e, h=h: e.tensor_copy(
            lbf[0:DH, h * DH:(h + 1) * DH], ps_tpb[0:DH, 0:DH]))
        vb = plan.op("vector", lambda e, h=h: e.tensor_copy(
            lbf[DH:128, h * DH:(h + 1) * DH], ps_tpb[0:DH, 0:DH]))
        plan.wait_eng("tensor", "vector", vb)
    lb_done = plan.cur("d_vector")
    plan.wait_eng("vector", "vector", lb_done)
    vsq = plan.op("vector", lambda e: e.tensor_mul(lsq[:, :], lbf[:, :],
                                                   lbf[:, :]))
    plan.wait_eng("tensor", "vector", vsq)
    for h in range(H):
        t3 = plan.op("tensor", lambda e, h=h: e.matmul(
            ps_row[0:1, 0:DH], ones64b[0:DH, :],
            lsq[0:DH, h * DH:(h + 1) * DH], start=True, stop=True))
        plan.wait_eng("vector", "tensor", t3)
        vr = plan.op("vector", lambda e, h=h: e.tensor_scalar(
            lnt_rows[0:1, h * DH:(h + 1) * DH], ps_row[0:1, 0:DH],
            invt[0:1, 0:1], None, ALU.mult))
        plan.wait_eng("tensor", "vector", vr)
        t4 = plan.op("tensor", lambda e, h=h: e.transpose(
            ps_row[0:DH, 1:2], lnt_rows[0:1, h * DH:(h + 1) * DH],
            eye_f[0:1, 0:1]))
        plan.wait_eng("vector", "tensor", t4)
        vc = plan.op("vector", lambda e, h=h: e.tensor_copy(
            lncol_t[:, h:h + 1], ps_row[0:DH, 1:2]))
        plan.wait_eng("tensor", "vector", vc)
        t5 = plan.op("tensor", lambda e, h=h: e.matmul(
            ps_a[:, 0:DH], ones1f[0:1, :],
            lnt_rows[0:1, h * DH:(h + 1) * DH], start=True, stop=True))
        plan.wait_eng("vector", "tensor", t5)
        vd = plan.op("vector", lambda e, h=h: e.tensor_copy(
            lnbT[:, h * DH:(h + 1) * DH], ps_a[:, 0:DH]))
        plan.wait_eng("tensor", "vector", vd)
    plan.join()


def emit_l1attn_p2(plan, C, ph):
    lm_r_d = C.lm_r.ap()
    temp_d = C.temp
    kbf_d, vt_d, kn_d = C.kbf2_i, C.vt2_i, C.kn_i
    o_z = C.z_w.ap()
    o_winv = C.winv_i
    eye_f = C.eye_f
    eye_b = C.eye_b
    ones64b = C.ones128b
    ones128b = C.ones128b
    ones1f = C.ones1f

    kbf = ph.sbuf("kbf_s", [128, IC * T], BF16)
    vt = ph.sbuf("vt_s", [128, NC * D], BF16)
    kn = ph.sbuf("kn_s", [128, H * NC], F32)
    knt = ph.sbuf("knt", [128, H * NC], F32)
    lmp = ph.sbuf("lmp", [NL, D], BF16)
    lmt = ph.sbuf("lmt", [NL, D], BF16)
    lbf = ph.sbuf("lbf", [128, H * DH], BF16)
    lsq = ph.sbuf("lsq", [128, H * DH], BF16)
    invt = ph.sbuf("invt", [1, 1], F32)
    t2c = ph.sbuf("t2c", [128, 1], F32)
    invtc = ph.sbuf("invtc", [128, 1], F32)
    lnt_rows = ph.sbuf("lnt_rows", [1, H * DH], F32)
    lncol_t = ph.sbuf("lncol_t", [NL, H], F32)
    lnbT = ph.sbuf("lnbT", [128, H * DH], F32)
    eye2 = ph.sbuf("eye2", [NL, NL], F32)
    usb = ph.sbuf("usb", [128, 2 * DH], F32)
    wsb = ph.sbuf("wsb", [NL, H * NL], F32)
    xsb = ph.sbuf("xsb", [NL, H * NL], F32)
    rsb = ph.sbuf("rsb", [NL, H * NL], F32)
    rowsum = ph.sbuf("rowsum", [NL, 1], F32)
    srow = ph.sbuf("srow", [1, NL], F32)
    s11 = ph.sbuf("s11", [1, 1], F32)
    scol = ph.sbuf("scol", [NL, 1], F32)
    phik = ph.sbuf("phik", [128, NC * DH], BF16)
    zsb = ph.sbuf("zsb", [NL, H * 65], F32)

    ps_a = ph.psum("ps_a", [128, 512], F32)
    ps_row = ph.psum("ps_row", [128, 512], F32)
    ps_ns = [ph.psum(f"ps_ns{i}", [128, 512], F32)
             for i in range(2)]
    ps_z = ph.psum("ps_z", [128, 512], F32)
    ps_zs = ph.psum("ps_zs", [128, 512], F32)
    ps_ql = ph.psum("ps_ql", [128, 512], F32)
    ps_tpb = ph.psum("ps_tpb", [128, 1024], BF16)

    ve0 = plan.op("vector", lambda e: e.tensor_scalar_mul(
        eye2[:, :], eye_f[0:NL, 0:NL], 2.0))
    plan.dma("gpsimd", kbf[:, :], kbf_d)
    plan.dma("gpsimd", vt[:, :], vt_d)
    plan.dma("gpsimd", kn[:, :], kn_d)
    plan.wait("gpsimd", "dma", plan.cur("dma"))
    emit_lm_common(plan, C, lm_r_d, temp_d, lmp, lmt, lbf, lsq, eye_f, eye_b,
                   ones64b, ones1f, invt, t2c, invtc, lnt_rows, lncol_t,
                   lnbT, ps_a, ps_row, ps_tpb)

    vkn = plan.op("vector", lambda e: e.tensor_scalar(
        knt[:, :], kn[:, :], invtc[:, 0:1], None, ALU.mult))
    plan.join()

    for h in range(H):
        t1 = plan.op("tensor", lambda e, h=h: e.matmul(
            ps_a[0:NL, 0:NL], lbf[0:DH, h * DH:(h + 1) * DH],
            lbf[0:DH, h * DH:(h + 1) * DH], start=True, stop=True))
        plan.wait_eng("vector", "tensor", t1)
        v1 = plan.op("vector", lambda e, h=h: e.scalar_tensor_tensor(
            usb[0:NL, 0:NL], ps_a[0:NL, 0:NL], t2c[0:NL, 0:1],
            lnbT[0:NL, h * DH:(h + 1) * DH], ALU.mult, ALU.subtract))
        plan.wait_eng("vector", "vector", v1)
        v2 = plan.op("vector", lambda e, h=h: e.tensor_scalar(
            usb[0:NL, 0:NL], usb[0:NL, 0:NL], lncol_t[:, h:h + 1], 0.0,
            ALU.subtract, ALU.min))
        plan.wait_eng("scalar", "vector", v2)
        s1 = plan.op("scalar", lambda e, h=h: e.activation(
            wsb[:, h * NL:(h + 1) * NL], usb[0:NL, 0:NL], AF.Exp))
        plan.wait_eng("vector", "scalar", s1)
        v3 = plan.op("vector", lambda e, h=h: e.scalar_tensor_tensor(
            wsb[:, h * NL:(h + 1) * NL], eye_f[0:NL, 0:NL], 1e-6,
            wsb[:, h * NL:(h + 1) * NL], ALU.mult, ALU.add))
        plan.wait_eng("vector", "vector", v3)
        v4 = plan.op("vector", lambda e, h=h: e.reduce_sum(
            rowsum[:, :], wsb[:, h * NL:(h + 1) * NL], axis=AX.X,
            apply_absolute_value=True))
        plan.wait_eng("tensor", "vector", v4)
        t2 = plan.op("tensor", lambda e: e.transpose(
            ps_row[0:1, 0:NL], rowsum[:, 0:1], eye_f[0:NL, 0:NL]))
        plan.wait_eng("vector", "tensor", t2)
        v5 = plan.op("vector", lambda e: e.tensor_copy(srow[:, :],
                                                       ps_row[0:1, 0:NL]))
        plan.wait_eng("vector", "vector", v5)
        v6 = plan.op("vector", lambda e: e.reduce_max(s11[:, :], srow[:, :],
                                                      axis=AX.X))
        plan.wait_eng("vector", "vector", v6)
        v7 = plan.op("vector", lambda e: e.reciprocal(s11[:, :], s11[:, :]))
        plan.wait_eng("tensor", "vector", v7)
        t3 = plan.op("tensor", lambda e: e.matmul(
            ps_row[0:NL, 2:3], ones1f[0:1, 0:NL], s11[0:1, 0:1],
            start=True, stop=True))
        plan.wait_eng("vector", "tensor", t3)
        v8 = plan.op("vector", lambda e: e.tensor_copy(scol[:, :],
                                                       ps_row[0:NL, 2:3]))
        plan.wait_eng("vector", "vector", v8)
        v9 = plan.op("vector", lambda e, h=h: e.tensor_scalar(
            xsb[:, h * NL:(h + 1) * NL], eye_f[0:NL, 0:NL], scol[:, 0:1],
            None, ALU.mult))
    plan.join()

    ns_slots = [ps_ns[0], ps_ns[1], ps_z, ps_zs, ps_ql, ps_a]
    slot_ev = {}
    xev = {}
    for it in range(NS_ITERS):
        for h in range(H):
            key = 2 * (it * H + h)
            sA = ns_slots[key % 6]
            sB = ns_slots[(key + 1) % 6]
            if slot_ev.get(key % 6) is not None:
                plan.wait_eng("tensor", "vector", slot_ev[key % 6])
            if xev.get(h) is not None:
                plan.wait_eng("tensor", "vector", xev[h])
            t1 = plan.op("tensor", lambda e, h=h, sA=sA: e.matmul(
                sA[0:NL, 0:NL], wsb[:, h * NL:(h + 1) * NL],
                xsb[:, h * NL:(h + 1) * NL], start=True, stop=True))
            plan.wait_eng("vector", "tensor", t1)
            r_ev = plan.op(
                "vector", lambda e, h=h, sA=sA: e.scalar_tensor_tensor(
                    rsb[:, h * NL:(h + 1) * NL], sA[0:NL, 0:NL],
                    -1.0, eye2[:, :], ALU.mult, ALU.add))
            slot_ev[key % 6] = r_ev
            if slot_ev.get((key + 1) % 6) is not None:
                plan.wait_eng("tensor", "vector", slot_ev[(key + 1) % 6])
            plan.wait_eng("tensor", "vector", r_ev)
            t2 = plan.op("tensor", lambda e, h=h, sB=sB: e.matmul(
                sB[0:NL, 0:NL], xsb[:, h * NL:(h + 1) * NL],
                rsb[:, h * NL:(h + 1) * NL], start=True, stop=True))
            plan.wait_eng("vector", "tensor", t2)
            x_ev = plan.op(
                "vector", lambda e, h=h, sB=sB: e.tensor_copy(
                    xsb[:, h * NL:(h + 1) * NL], sB[0:NL, 0:NL]))
            slot_ev[(key + 1) % 6] = x_ev
            xev[h] = x_ev
    plan.join()
    plan.dma("sync", o_winv, xsb[:, :], "st")

    prev_phi = {}
    prev_zev = None
    for h in range(H):
        oc = h // 2
        rb = 64 * (h % 2)
        for tc in range(NC):
            slot = tc % 2
            if slot in prev_phi:
                plan.wait_eng("tensor", "scalar", prev_phi[slot][1])
            t1 = plan.op("tensor", lambda e, h=h, oc=oc, rb=rb, tc=tc,
                         slot=slot: e.matmul(
                ps_ql[:, slot * 256:slot * 256 + DH],
                kbf[rb:rb + DH, oc * T + tc * 128: oc * T + (tc + 1) * 128],
                lbf[rb:rb + DH, h * DH:(h + 1) * DH],
                start=True, stop=True))
            plan.wait_eng("vector", "tensor", t1)
            if slot in prev_phi:
                plan.wait_eng("vector", "scalar", prev_phi[slot][1])
            v1 = plan.op("vector", lambda e, h=h, slot=slot:
                         e.scalar_tensor_tensor(
                usb[:, slot * DH:(slot + 1) * DH],
                ps_ql[:, slot * 256:slot * 256 + DH],
                t2c[:, 0:1], lnbT[:, h * DH:(h + 1) * DH],
                ALU.mult, ALU.subtract))
            plan.wait_eng("vector", "vector", v1)
            v2 = plan.op("vector", lambda e, h=h, tc=tc, slot=slot:
                         e.tensor_scalar(
                usb[:, slot * DH:(slot + 1) * DH],
                usb[:, slot * DH:(slot + 1) * DH],
                knt[:, h * NC + tc: h * NC + tc + 1],
                0.0, ALU.subtract, ALU.min))
            plan.wait_eng("scalar", "vector", v2)
            s1 = plan.op("scalar", lambda e, tc=tc, slot=slot: e.activation(
                phik[:, tc * DH:(tc + 1) * DH],
                usb[:, slot * DH:(slot + 1) * DH], AF.Exp))
            prev_phi[slot] = (v2, s1)
        phi_done = plan.cur("d_scalar")
        plan.wait_eng("tensor", "scalar", phi_done)
        if prev_zev is not None:
            plan.wait_eng("tensor", "vector", prev_zev)
        for tc in range(NC):
            plan.op("tensor", lambda e, h=h, tc=tc: e.matmul(
                ps_z[0:DH, 0:DH], phik[:, tc * DH:(tc + 1) * DH],
                vt[:, tc * D + h * DH: tc * D + (h + 1) * DH],
                start=(tc == 0), stop=(tc == NC - 1),
                skip_group_check=True))
            plan.op("tensor", lambda e, h=h, tc=tc: e.matmul(
                ps_zs[0:DH, 0:1], phik[:, tc * DH:(tc + 1) * DH],
                ones128b[:, :],
                start=(tc == 0), stop=(tc == NC - 1),
                skip_group_check=True))
        zt = plan.cur("d_tensor")
        plan.wait_eng("vector", "tensor", zt)
        plan.op("vector", lambda e, h=h: e.tensor_copy(
            zsb[:, h * 65:h * 65 + 64], ps_z[0:DH, 0:64]))
        prev_zev = plan.op("vector", lambda e, h=h: e.tensor_copy(
            zsb[:, h * 65 + 64:(h + 1) * 65], ps_zs[0:DH, 0:1]))
    plan.wait_eng("sync", "vector", plan.cur("d_vector"))
    plan.dma("sync", o_z, zsb[:, :], "st")


def emit_l1attn_p3(plan, C, ph):
    winv_d = C.winv_i
    lm_r_d = C.lm_r.ap()
    temp_d = C.temp
    qbf_d, qn_d = C.qbf2_i, C.qn_i
    x2_d = C.x2_i
    Wo_d, bo_d = C.Wo2, C.bo2
    eye_f = C.eye_f
    eye_b = C.eye_b
    ones64b = C.ones128b
    ones1f = C.ones1f

    qbf = ph.sbuf("qbf_s", [128, IC * T], BF16)
    attn = ph.sbuf("attn_s", [128, IC * T], BF16)
    qn = ph.sbuf("qn_s", [128, H * NC], F32)
    zsb = ph.sbuf("zsb", [NL, H * 65], F32)
    winv = ph.sbuf("winv_s", [NL, H * DH], F32)
    ybf = ph.sbuf("ybf", [NL, H * 65], BF16)
    lmp = ph.sbuf("lmp", [NL, D], BF16)
    lmt = ph.sbuf("lmt", [NL, D], BF16)
    lbf = ph.sbuf("lbf", [128, H * DH], BF16)
    lsq = ph.sbuf("lsq", [128, H * DH], BF16)
    invt = ph.sbuf("invt", [1, 1], F32)
    t2c = ph.sbuf("t2c", [128, 1], F32)
    invtc = ph.sbuf("invtc", [128, 1], F32)
    lnt_rows = ph.sbuf("lnt_rows", [1, H * DH], F32)
    lncol_t = ph.sbuf("lncol_t", [NL, H], F32)
    lnbT = ph.sbuf("lnbT", [128, H * DH], F32)
    usb = ph.sbuf("usb", [128, 2 * TTW], F32)
    qnb2 = [ph.sbuf(f"qnb{i}", [128, T], F32) for i in range(2)]
    trqn = ph.sbuf("trqn", [128, 256], F32)
    phiq = ph.sbuf("phiq", [NL, T], BF16)
    nrow2 = [ph.sbuf(f"nrow{i}", [1, TTW], F32) for i in range(2)]
    rbb2 = [ph.sbuf(f"rbb{i}", [128, TTW], F32) for i in range(2)]
    wbuf = [ph.sbuf(f"wbuf{i}", [128, D], BF16) for i in range(2)]
    bo_sb = ph.sbuf("bo_sb", [128, IC], F32)
    xoc = [ph.sbuf(f"xoc{i}", [128, SLAB], F32) for i in range(2)]
    x3slab = ph.sbuf("x3slab", [128, IC * SLAB], F32)
    row_sc3 = ph.sbuf("row_sc3", [1, 3 * SLAB + 1], F32)
    bc_sb3 = ph.sbuf("bc_sb3", [128, 2 * SLAB], F32)

    ps_tpb = ph.psum("ps_tpb", [128, 1024], BF16)
    ps_o = ph.psum("ps_o", [128, 512], F32)
    ps_den = ph.psum("ps_den", [128, 512], F32)
    ps_ph = ph.psum("ps_ph", [128, 512], F32)
    ps_bc = ph.psum("ps_bc", [128, 2048], F32)
    ps_a = ps_o
    ps_row = ps_den

    plan.dma("gpsimd", qbf[:, :], qbf_d)
    plan.dma("gpsimd", qn[:, :], qn_d)
    plan.dma("gpsimd", winv[:, :], winv_d)
    plan.wait("gpsimd", "cc_z", 1)
    plan.dma("gpsimd", zsb[:, :], C.z_r.ap())
    plan.wait("gpsimd", "dma", plan.cur("dma"))
    emit_lm_common(plan, C, lm_r_d, temp_d, lmp, lmt, lbf, lsq, eye_f, eye_b,
                   ones64b, ones1f, invt, t2c, invtc, lnt_rows, lncol_t,
                   lnbT, ps_a, ps_row, ps_tpb)

    for h in range(H):
        slot = h % 2
        t1 = plan.op("tensor", lambda e, h=h, slot=slot: e.matmul(
            ps_a[0:NL, slot * 128:slot * 128 + 65] if False else
            ps_a[0:NL, 0:65],
            winv[:, h * DH:(h + 1) * DH], zsb[:, h * 65:(h + 1) * 65],
            start=True, stop=True))
        plan.wait_eng("vector", "tensor", t1)
        v1 = plan.op("vector", lambda e, h=h: e.tensor_copy(
            ybf[:, h * 65:(h + 1) * 65], ps_a[0:NL, 0:65]))
        plan.wait_eng("tensor", "vector", v1)
    plan.join()

    for blk in range(2):
        tb = plan.op("tensor", lambda e, blk=blk: e.transpose(
            ps_bc[0:128, 0:128], qn[:, blk * 128:(blk + 1) * 128],
            eye_f[:, :]))
        plan.wait_eng("vector", "tensor", tb)
        vb = plan.op("vector", lambda e, blk=blk: e.tensor_scalar(
            trqn[:, blk * 128:(blk + 1) * 128], ps_bc[0:128, 0:128],
            invtc[:, 0:1], None, ALU.mult))
        plan.wait_eng("tensor", "vector", vb)
    plan.wait_eng("sync", "vector", plan.cur("d_vector"))
    for blk in range(2):
        plan.dma("sync",
                 C.qn_d[blk * 8:(blk + 1) * 8, :]
                 .rearrange("h (t i) -> (h t) i", t=16),
                 trqn[:, blk * 128:(blk + 1) * 128], "wv")
    plan.wait("gpsimd", "wv", plan.cur("wv"))
    qndma = {}
    for h0 in range(2):
        qndma[h0] = plan.dma("gpsimd", qnb2[h0][:, :],
                             C.qn_d[h0:h0 + 1, :].broadcast_to((128, T)),
                             "nbq")

    pso_s3 = [ps_o[0:DH, 0:TTW], ps_bc[0:DH, TTW:2 * TTW]]
    psd_s3 = [ps_den[0:1, 0:TTW], ps_bc[0:1, 2 * TTW:3 * TTW]]
    bc_s3 = [ps_bc[:, 0:TTW], ps_bc[:, 3 * TTW:4 * TTW]]
    nst = {}
    for h in range(H):
        oc = h // 2
        rb = 64 * (h % 2)
        qnb = qnb2[h % 2]
        dv = plan.cur("d_vector")
        plan.wait_eng("tensor", "vector", dv)
        if h + 1 < H:
            plan.wait_eng("gpsimd", "vector", dv)
            qndma[h + 1] = plan.dma(
                "gpsimd", qnb2[(h + 1) % 2][:, :],
                C.qn_d[h + 1:h + 2, :].broadcast_to((128, T)), "nbq")
        plan.wait("vector", "nbq", qndma[h])
        pst_ev = {}
        for tt in range(NT):
            par = tt % 2
            psX = [ps_ph, ps_den][par]
            if pst_ev.get(("s", par)) is not None:
                plan.wait_eng("tensor", "scalar", pst_ev[("s", par)])
            t4 = plan.op("tensor", lambda e, h=h, rb=rb, oc=oc, tt=tt,
                         psX=psX: e.matmul(
                psX[0:NL, 0:TTW],
                lbf[rb:rb + DH, h * DH:(h + 1) * DH],
                qbf[rb:rb + DH, oc * T + tt * TTW: oc * T + (tt + 1) * TTW],
                start=True, stop=True))
            plan.wait_eng("vector", "tensor", t4)
            if pst_ev.get(("s", par)) is not None:
                plan.wait_eng("vector", "scalar", pst_ev[("s", par)])
            v4 = plan.op("vector", lambda e, h=h, tt=tt, par=par, psX=psX:
                         e.scalar_tensor_tensor(
                usb[0:NL, par * TTW:(par + 1) * TTW],
                psX[0:NL, 0:TTW], t2c[0:NL, 0:1],
                qnb[0:NL, tt * TTW:(tt + 1) * TTW], ALU.mult, ALU.subtract))
            plan.wait_eng("vector", "vector", v4)
            v5 = plan.op("vector", lambda e, h=h, par=par: e.tensor_scalar(
                usb[0:NL, par * TTW:(par + 1) * TTW],
                usb[0:NL, par * TTW:(par + 1) * TTW],
                lncol_t[:, h:h + 1], 0.0,
                ALU.subtract, ALU.min))
            plan.wait_eng("scalar", "vector", v5)
            s_phi = plan.op("scalar", lambda e, tt=tt, par=par: e.activation(
                phiq[:, tt * TTW:(tt + 1) * TTW],
                usb[0:NL, par * TTW:(par + 1) * TTW], AF.Exp))
            pst_ev[("s", par)] = s_phi
        phi_done = plan.cur("d_scalar")
        plan.wait_eng("tensor", "scalar", phi_done)
        def ntail(tz, oc=oc, rb=rb):
            pz = tz % 2
            plan.wait_eng("vector", "tensor", nst[("t5", pz)])
            v6 = plan.op("vector", lambda e, pz=pz: e.tensor_copy(
                nrow2[pz][:, :], psd_s3[pz]))
            plan.wait_eng("tensor", "vector", v6)
            if nst.get(("v8", pz)) is not None:
                plan.wait_eng("tensor", "vector", nst[("v8", pz)])
            t6 = plan.op("tensor", lambda e, pz=pz: e.matmul(
                bc_s3[pz], ones1f[0:1, :], nrow2[pz][0:1, :],
                start=True, stop=True))
            plan.wait_eng("vector", "tensor", t6)
            v8 = plan.op("vector", lambda e, pz=pz: e.tensor_scalar(
                rbb2[pz][:, :], bc_s3[pz], 1e-10, None, ALU.max))
            plan.wait_eng("vector", "vector", v8)
            v9 = plan.op("vector", lambda e, pz=pz: e.reciprocal(
                rbb2[pz][:, :], rbb2[pz][:, :]))
            plan.wait_eng("vector", "vector", v9)
            mv = plan.op("vector", lambda e, oc=oc, rb=rb, tz=tz, pz=pz:
                         e.tensor_mul(
                attn[rb:rb + DH, oc * T + tz * TTW: oc * T + (tz + 1) * TTW],
                pso_s3[pz], rbb2[pz][0:DH, :]))
            nst[("v8", pz)] = v8
            nst[("mul", pz)] = mv

        for tt in range(NT):
            par = tt % 2
            if nst.get(("mul", par)) is not None:
                plan.wait_eng("tensor", "vector", nst[("mul", par)])
            plan.op("tensor", lambda e, h=h, tt=tt, par=par: e.matmul(
                pso_s3[par], ybf[:, h * 65:h * 65 + DH],
                phiq[:, tt * TTW:(tt + 1) * TTW], start=True, stop=True))
            t5 = plan.op("tensor", lambda e, h=h, tt=tt, par=par: e.matmul(
                psd_s3[par], ybf[:, h * 65 + DH:h * 65 + 65],
                phiq[:, tt * TTW:(tt + 1) * TTW], start=True, stop=True))
            nst[("t5", par)] = t5
            if tt > 0:
                ntail(tt - 1)
        ntail(NT - 1)
    plan.join()

    plan.dma("gpsimd", bo_sb[:, :], bo_d)
    plan.wait("gpsimd", "dma", plan.cur("dma"))
    plan.op("gpsimd", lambda e: e.memset(lmt[0:1, 0:1], 0.0))
    plan.join()
    last_mm = {}
    prev_ev = {}
    prev_half = {}
    for sb in range(2):
        for ocx in range(IC):
            buf = wbuf[ocx % 2]
            psem = "pc%d" % (ocx % 2)
            xsem = "xs%d" % (ocx % 2)
            idx = sb * IC + ocx
            if last_mm.get(idx % 2) is not None:
                plan.wait_eng("gpsimd", "tensor", last_mm[idx % 2])
            plan.dma("gpsimd", buf[:, :].rearrange("p (i o) -> p i o", i=IC),
                     Wo_d[:, ocx * 128:(ocx + 1) * 128]
                     .rearrange("(i p) o -> p i o", p=128), psem)
            if (sb * IC + ocx) >= 2:
                plan.wait_eng("sync", "vector", prev_ev[ocx % 2])
            plan.dma("sync", xoc[ocx % 2][:, :],
                     x2_d[ocx * 128:(ocx + 1) * 128,
                          sb * SLAB:(sb + 1) * SLAB], xsem)
            plan.wait("tensor", psem, plan.cur(psem))
            plan.wait("vector", xsem, plan.cur(xsem))
            for tt2 in range(2):
                tt = sb * 2 + tt2
                half = tt2
                if half in prev_half:
                    plan.wait_eng("tensor", "vector", prev_half[half])
                for icx in range(IC):
                    tgt = plan.op("tensor", lambda e, buf=buf, icx=icx,
                                  tt=tt, half=half: e.matmul(
                        ps_bc[:, half * TTW:(half + 1) * TTW],
                        buf[:, icx * 128:(icx + 1) * 128],
                        attn[:, icx * T + tt * TTW: icx * T + (tt + 1) * TTW],
                        start=(icx == 0), stop=(icx == IC - 1)))
                last_mm[idx % 2] = tgt
                plan.wait_eng("vector", "tensor", tgt)
                prev_ev[ocx % 2] = plan.op(
                    "vector", lambda e, ocx=ocx, tt2=tt2, half=half:
                    e.scalar_tensor_tensor(
                        x3slab[:, ocx * SLAB + tt2 * TTW:
                               ocx * SLAB + (tt2 + 1) * TTW],
                        ps_bc[:, half * TTW:(half + 1) * TTW],
                        bo_sb[:, ocx:ocx + 1],
                        xoc[ocx % 2][:, tt2 * TTW:(tt2 + 1) * TTW],
                        ALU.add, ALU.add))
                prev_half[half] = prev_ev[ocx % 2]
        plan.join()
        emit_ln(plan, x3slab, qbf[:, 0:IC * SLAB],
                qbf[:, IC * SLAB:2 * IC * SLAB], C.ones128b, row_sc3,
                bc_sb3, C.ones1f, ps_den, ps_bc, tlen=SLAB)
        plan.join()
        plan.dma("sync",
                 C.x3_i[:, sb * SLAB:(sb + 1) * SLAB]
                 .rearrange("(i p) t -> p i t", p=128),
                 x3slab[:, :].rearrange("p (i t) -> p i t", i=IC), "st")
        if sb == 0:
            plan.wait("vector", "st", plan.cur("st"))
            plan.op("vector", lambda e: e.memset(row_sc3[0:1, 0:1], 0.0))
            plan.join()


SLAB = 1024
NSL = T // SLAB
NJ = INTER // 128
NG = 4
PG = 2
JG = 4
OCG = 4


def emit_l1ffn(plan, C, ph):
    x3_d = C.x3_i
    gw_d = C.gw2
    w13_d = C.w13
    w2_d = C.w2w
    o_xf = C.xf
    eye_f = C.eye_f
    ones1f = C.ones1f
    ones128b = C.ones128b

    xbf = ph.sbuf("xbf", [128, IC * T], BF16)
    act = ph.sbuf("act", [128, NJ * SLAB], BF16)
    out_acc = ph.sbuf("out_acc", [128, IC * SLAB], F32)
    xoc = [ph.sbuf(f"xoc{i}", [128, T], F32) for i in range(2)]
    gw_sb = ph.sbuf("gw_sb", [128, IC * NE], F32)
    logit = ph.sbuf("logit", [NE, T], F32)
    sc_tm = ph.sbuf("sc_tm", [128, NE * NC], F32)
    comb = ph.sbuf("comb", [128, NE * NC], F32)
    m1c = ph.sbuf("m1c", [128, 1], F32)
    m2c = ph.sbuf("m2c", [128, 1], F32)
    g1c = ph.sbuf("g1c", [128, 1], F32)
    g2c = ph.sbuf("g2c", [128, 1], F32)
    gsc = ph.sbuf("gsc", [128, NG], F32)
    geq = ph.sbuf("geq", [128, NG], F32)
    gsel = ph.sbuf("gsel", [128, NG], F32)
    smask = ph.sbuf("smask", [128, NE], F32)
    msked = ph.sbuf("msked", [128, NE], F32)
    meq = ph.sbuf("meq", [128, NE], F32)
    sel = ph.sbuf("sel", [128, NE], F32)
    wsum = ph.sbuf("wsum", [128, 1], F32)
    sgl = ph.sbuf("sgl", [128, 4 * TTW], F32)
    tmp5 = ph.sbuf("tmp5", [128, TTW], F32)
    cbb2 = [ph.sbuf(f"cbb{i}", [128, SLAB], F32) for i in range(2)]
    trcb = ph.sbuf("trcb", [128, 128], F32)
    wbuf = [ph.sbuf(f"wbuf{i}", [128, 2 * IC * 128], BF16)
            for i in range(4)]
    w2buf = [ph.sbuf(f"w2buf{i}", [128, NJ * 128], BF16)
             for i in range(4)]
    row_sc = ph.sbuf("row_sc", [1, 3 * SLAB + 1], F32)
    bc_sb = ph.sbuf("bc_sb", [128, 2 * SLAB], F32)

    ps_g = ph.psum("ps_g", [128, 2048], F32)
    ps_t = ph.psum("ps_t", [128, 512], F32)
    ps_o = ph.psum("ps_o", [128, 512], F32)
    ps_h = ph.psum("ps_h", [128, 1024], F32)
    gu_slots = [
        [ps_g[:, 0:512], ps_g[:, 512:1024],
         ps_g[:, 1024:1536], ps_g[:, 1536:2048]],
        [ps_t[:, 0:512], ps_o[:, 0:512],
         ps_h[:, 0:512], ps_h[:, 512:1024]],
    ]
    w2_slots = [ps_g[:, 0:1024], ps_g[:, 1024:2048]]

    plan._wj = {}
    plan._wsem2 = {}
    plan._wiss2 = 0
    plan._wuse2 = 0
    plan._w2j = {}
    plan._w2sem = {}
    plan._w2iss = 0
    plan._w2use = 0
    plan._actv = {}
    plan._oev = {}
    plan._guv = {}
    plan._gus = {}
    plan._sglv = {}
    plan._tmpv = None

    def issue_w13(e2, j2):
        key = plan._wiss2 % 4
        plan._wiss2 += 1
        psem = "pu%d" % key
        if plan._wj.get(key) is not None:
            plan.wait_eng("gpsimd", "tensor", plan._wj[key])
        plan.dma("gpsimd",
                 wbuf[key][:, 0:1024].rearrange("p (i o) -> p i o", i=IC),
                 w13_d[e2, :, j2 * 128:(j2 + 1) * 128]
                 .rearrange("(i p) o -> p i o", p=128), psem)
        plan._wsem2[key] = plan.dma(
            "gpsimd",
            wbuf[key][:, 1024:2048].rearrange("p (i o) -> p i o", i=IC),
            w13_d[e2, :, INTER + j2 * 128: INTER + (j2 + 1) * 128]
            .rearrange("(i p) o -> p i o", p=128), psem)

    def issue_w2(e2, oc2):
        key = plan._w2iss % 4
        plan._w2iss += 1
        w2s = "pv%d" % key
        if plan._w2j.get(key) is not None:
            plan.wait_eng("gpsimd", "tensor", plan._w2j[key])
        plan._w2sem[key] = plan.dma(
            "gpsimd", w2buf[key][:, :].rearrange("p (j o) -> p j o", j=NJ),
            w2_d[e2, :, oc2 * 128:(oc2 + 1) * 128]
            .rearrange("(j p) o -> p j o", p=128), w2s)
    plan.dma("gpsimd", gw_sb[:, :].rearrange("p (i en) -> p i en", i=IC),
             gw_d.rearrange("(i p) en -> p i en", p=128))
    plan.wait("gpsimd", "dma", plan.cur("dma"))
    plan.op("gpsimd", lambda e: e.memset(row_sc[0:1, 0:1], 0.0))
    plan.join()

    prev_cast = {}
    prev_gmm = {}
    for icx in range(IC):
        xsem = "xs%d" % (icx % 2)
        if icx >= 2:
            plan.wait_eng("sync", "vector", prev_cast[icx - 2])
            plan.wait_eng("sync", "tensor", prev_gmm[icx - 2])
        plan.dma("sync", xoc[icx % 2][:, :],
                 x3_d[icx * 128:(icx + 1) * 128, :], xsem)
        plan.wait("tensor", xsem, plan.cur(xsem))
        plan.wait("vector", xsem, plan.cur(xsem))
        for tt in range(NT):
            tgt = plan.op("tensor", lambda e, icx=icx, tt=tt: e.matmul(
                ps_g[0:NE, tt * TTW:(tt + 1) * TTW],
                gw_sb[:, icx * NE:(icx + 1) * NE],
                xoc[icx % 2][:, tt * TTW:(tt + 1) * TTW],
                start=(icx == 0), stop=(icx == IC - 1)))
        prev_gmm[icx] = tgt
        prev_cast[icx] = plan.op("vector", lambda e, icx=icx: e.tensor_copy(
            xbf[:, icx * T:(icx + 1) * T], xoc[icx % 2][:, :]))
    gmm = plan.cur("d_tensor")
    plan.wait_eng("vector", "tensor", gmm)
    vl = plan.op("vector", lambda e: e.tensor_copy(logit[:, :],
                                                   ps_g[0:NE, 0:T]))
    plan.join()

    for tc in range(NC):
        t1 = plan.op("tensor", lambda e, tc=tc: e.transpose(
            ps_t[0:128, 0:NE], logit[:, tc * 128:(tc + 1) * 128],
            eye_f[0:NE, 0:NE]))
        plan.wait_eng("vector", "tensor", t1)
        va = plan.op("vector", lambda e, tc=tc: e.tensor_copy(
            sc_tm[:, tc * NE:(tc + 1) * NE], ps_t[0:128, 0:NE]))
        plan.wait_eng("tensor", "vector", va)
        plan.wait_eng("vector", "vector", va)
        sl = slice(tc * NE, (tc + 1) * NE)
        vb = plan.op("vector", lambda e, sl=sl: e.reduce_max(
            m1c[:, :], sc_tm[:, sl], axis=AX.X))
        plan.wait_eng("vector", "vector", vb)
        vc = plan.op("vector", lambda e, sl=sl: e.tensor_scalar(
            sc_tm[:, sl], sc_tm[:, sl], m1c[:, 0:1], None, ALU.subtract))
        plan.wait_eng("scalar", "vector", vc)
        s1 = plan.op("scalar", lambda e, sl=sl: e.activation(
            sc_tm[:, sl], sc_tm[:, sl], AF.Exp))
        plan.wait_eng("vector", "scalar", s1)
        vd = plan.op("vector", lambda e, sl=sl: e.reduce_sum(
            m1c[:, :], sc_tm[:, sl], axis=AX.X))
        plan.wait_eng("vector", "vector", vd)
        ve = plan.op("vector", lambda e: e.reciprocal(m1c[:, :], m1c[:, :]))
        plan.wait_eng("vector", "vector", ve)
        vf = plan.op("vector", lambda e, sl=sl: e.tensor_scalar(
            sc_tm[:, sl], sc_tm[:, sl], m1c[:, 0:1], None, ALU.mult))
        plan.wait_eng("vector", "vector", vf)
        vg = plan.op("vector", lambda e, sl=sl: e.reduce_max(
            gsc[:, :], sc_tm[:, sl].rearrange("p (g k) -> p g k", g=NG),
            axis=AX.X))
        plan.wait_eng("vector", "vector", vg)
        vh = plan.op("vector", lambda e: e.reduce_max(g1c[:, :], gsc[:, :],
                                                      axis=AX.X))
        plan.wait_eng("vector", "vector", vh)
        vi = plan.op("vector", lambda e: e.tensor_scalar(
            geq[:, :], gsc[:, :], g1c[:, 0:1], -1e30, ALU.is_equal,
            ALU.mult))
        plan.wait_eng("vector", "vector", vi)
        vj = plan.op("vector", lambda e: e.tensor_add(geq[:, :], geq[:, :],
                                                      gsc[:, :]))
        plan.wait_eng("vector", "vector", vj)
        vk = plan.op("vector", lambda e: e.reduce_max(g2c[:, :], geq[:, :],
                                                      axis=AX.X))
        plan.wait_eng("vector", "vector", vk)
        vm = plan.op("vector", lambda e: e.tensor_scalar(
            gsel[:, :], gsc[:, :], g2c[:, 0:1], None, ALU.is_ge))
        plan.wait_eng("vector", "vector", vm)
        vn = plan.op("vector", lambda e: e.tensor_copy(
            smask[:, :].rearrange("p (g k) -> p g k", g=NG),
            gsel[:, :].rearrange("p (g o) -> p g o", o=1)
            .broadcast_to((128, NG, PG))))
        plan.wait_eng("vector", "vector", vn)
        vo = plan.op("vector", lambda e, sl=sl: e.tensor_mul(
            msked[:, :], sc_tm[:, sl], smask[:, :]))
        plan.wait_eng("vector", "vector", vo)
        vp = plan.op("vector", lambda e: e.reduce_max(m1c[:, :], msked[:, :],
                                                      axis=AX.X))
        plan.wait_eng("vector", "vector", vp)
        vq = plan.op("vector", lambda e: e.tensor_scalar(
            meq[:, :], msked[:, :], m1c[:, 0:1], -1e30, ALU.is_equal,
            ALU.mult))
        plan.wait_eng("vector", "vector", vq)
        vr = plan.op("vector", lambda e: e.tensor_add(meq[:, :], meq[:, :],
                                                      msked[:, :]))
        plan.wait_eng("vector", "vector", vr)
        vs = plan.op("vector", lambda e: e.reduce_max(m2c[:, :], meq[:, :],
                                                      axis=AX.X))
        plan.wait_eng("vector", "vector", vs)
        vt_ = plan.op("vector", lambda e: e.tensor_scalar(
            sel[:, :], msked[:, :], m2c[:, 0:1], None, ALU.is_ge))
        plan.wait_eng("vector", "vector", vt_)
        vu = plan.op("vector", lambda e: e.tensor_mul(sel[:, :], sel[:, :],
                                                      msked[:, :]))
        plan.wait_eng("vector", "vector", vu)
        vv = plan.op("vector", lambda e: e.reduce_sum(wsum[:, :], sel[:, :],
                                                      axis=AX.X))
        plan.wait_eng("vector", "vector", vv)
        vw = plan.op("vector", lambda e: e.reciprocal(wsum[:, :],
                                                      wsum[:, :]))
        plan.wait_eng("vector", "vector", vw)
        vx = plan.op("vector", lambda e, tc=tc: e.tensor_scalar(
            comb[:, tc: NE * NC: NC], sel[:, :], wsum[:, 0:1], None,
            ALU.mult))
    plan.join()

    tcb = plan.op("tensor", lambda e: e.transpose(
        ps_t[0:128, 0:128], comb[:, :], eye_f[:, :]))
    plan.wait_eng("vector", "tensor", tcb)
    vcb = plan.op("vector", lambda e: e.tensor_copy(
        trcb[:, :], ps_t[0:128, 0:128]))
    plan.wait_eng("sync", "vector", vcb)
    plan.wait_eng("tensor", "vector", vcb)
    plan.dma("sync",
             C.cb_d[0:NE, :].rearrange("h (t i) -> (h t) i", t=NC),
             trcb[:, :], "wv")
    plan.wait("gpsimd", "wv", plan.cur("wv"))
    cbdma = {0: plan.dma("gpsimd", cbb2[0][:, :],
                         C.cb_d[0:1, 0:SLAB].broadcast_to((128, SLAB)),
                         "nbq")}

    for sb in range(NSL):
        sl_tt = [sb * 2, sb * 2 + 1]
        rhs0 = sb * SLAB
        first = True
        issue_w13(0, 0)
        issue_w13(0, 1)
        issue_w2(0, 0)
        issue_w2(0, 1)
        for e_i in range(NE):
            kse = sb * NE + e_i
            cbb = cbb2[kse % 2]
            dv = plan.cur("d_vector")
            plan.wait_eng("tensor", "vector", dv)
            if kse + 1 < 2 * NE:
                sb2, e2 = divmod(kse + 1, NE)
                plan.wait_eng("gpsimd", "vector", dv)
                cbdma[kse + 1] = plan.dma(
                    "gpsimd", cbb2[(kse + 1) % 2][:, :],
                    C.cb_d[e2:e2 + 1, sb2 * SLAB:(sb2 + 1) * SLAB]
                    .broadcast_to((128, SLAB)), "nbq")
            plan.wait("vector", "nbq", cbdma[kse])
            for j in range(NJ):
                k = e_i * NJ + j
                if k + 2 < NE * NJ:
                    e2, j2 = divmod(k + 2, NJ)
                    issue_w13(e2, j2)
                key = plan._wuse2 % 4
                plan._wuse2 += 1
                buf = wbuf[key]
                plan.wait("tensor", "pu%d" % key, plan._wsem2[key])
                par = j % 2
                slots = gu_slots[par]
                if plan._guv.get(par) is not None:
                    plan.wait_eng("tensor", "vector", plan._guv[par])
                    plan.wait_eng("tensor", "scalar", plan._gus[par])
                for icx in range(IC):
                    go = icx * 128
                    for qq in range(2):
                        tt = sl_tt[qq]
                        plan.op("tensor", lambda e, buf=buf, go=go,
                                icx=icx, tt=tt, qq=qq, slots=slots:
                                e.matmul(
                            slots[qq],
                            buf[:, go:go + 128],
                            xbf[:, icx * T + tt * TTW:
                                icx * T + (tt + 1) * TTW],
                            start=(icx == 0), stop=(icx == IC - 1),
                            skip_group_check=True))
                for icx in range(IC):
                    uo = 1024 + icx * 128
                    for qq in range(2):
                        tt = sl_tt[qq]
                        tgt = plan.op("tensor", lambda e, buf=buf, uo=uo,
                                      icx=icx, tt=tt, qq=qq, slots=slots:
                                      e.matmul(
                            slots[2 + qq],
                            buf[:, uo:uo + 128],
                            xbf[:, icx * T + tt * TTW:
                                icx * T + (tt + 1) * TTW],
                            start=(icx == 0), stop=(icx == IC - 1),
                            skip_group_check=True))
                plan._wj[key] = tgt
                gu_done = plan.cur("d_tensor")
                for qq in range(2):
                    plan.wait_eng("scalar", "tensor", gu_done)
                    if plan._sglv.get((par, qq)) is not None:
                        plan.wait_eng("scalar", "vector",
                                      plan._sglv[(par, qq)])
                    qs = slice((par * 2 + qq) * TTW,
                               (par * 2 + qq + 1) * TTW)
                    s1 = plan.op("scalar", lambda e, qs=qs, qq=qq,
                                 slots=slots: e.activation(
                        sgl[:, qs], slots[qq], AF.Silu))
                    plan.wait_eng("vector", "scalar", s1)
                    v1 = plan.op("vector", lambda e, j=j, qq=qq, qs=qs,
                                 slots=slots: e.tensor_mul(
                        act[:, j * SLAB + qq * TTW:
                            j * SLAB + (qq + 1) * TTW],
                        sgl[:, qs], slots[2 + qq]))
                    plan._sglv[(par, qq)] = v1
                    plan._gus[par] = s1
                plan._guv[par] = v1
            act_done = plan.cur("d_vector")
            plan.wait_eng("tensor", "vector", act_done)
            plan.wait_eng("tensor", "scalar", plan.cur("d_scalar"))
            for oc in range(IC):
                kw = e_i * IC + oc
                if kw + 2 < NE * IC:
                    e2w, oc2 = divmod(kw + 2, IC)
                    issue_w2(e2w, oc2)
                key2 = plan._w2use % 4
                plan._w2use += 1
                b2 = w2buf[key2]
                plan.wait("tensor", "pv%d" % key2, plan._w2sem[key2])
                if True:
                    par = oc % 2
                    pso = w2_slots[par]
                    if plan._oev.get(par) is not None:
                        plan.wait_eng("tensor", "vector", plan._oev[par])
                    for j in range(NJ):
                        lo = j * 128
                        for qq in range(2):
                            tgt = plan.op("tensor", lambda e, b2=b2, lo=lo,
                                          j=j, qq=qq, pso=pso: e.matmul(
                                pso[:, qq * TTW:(qq + 1) * TTW],
                                b2[:, lo:lo + 128],
                                act[:, j * SLAB + qq * TTW:
                                    j * SLAB + (qq + 1) * TTW],
                                start=(j == 0), stop=(j == NJ - 1),
                                skip_group_check=True))
                    plan._w2j[key2] = tgt
                    plan.wait_eng("vector", "tensor", tgt)
                    for qq in range(2):
                        if first:
                            ov = plan.op("vector", lambda e, oc=oc, qq=qq,
                                         pso=pso, cbb=cbb: e.tensor_mul(
                                out_acc[:, oc * SLAB + qq * TTW:
                                        oc * SLAB + (qq + 1) * TTW],
                                pso[:, qq * TTW:(qq + 1) * TTW],
                                cbb[:, qq * TTW:(qq + 1) * TTW]))
                        else:
                            if plan._tmpv is not None:
                                plan.wait_eng("vector", "vector", plan._tmpv)
                            vm = plan.op("vector", lambda e, qq=qq, pso=pso,
                                         cbb=cbb: e.tensor_mul(
                                tmp5[:, :],
                                pso[:, qq * TTW:(qq + 1) * TTW],
                                cbb[:, qq * TTW:(qq + 1) * TTW]))
                            plan.wait_eng("vector", "vector", vm)
                            ov = plan.op("vector", lambda e, oc=oc, qq=qq:
                                         e.tensor_add(
                                out_acc[:, oc * SLAB + qq * TTW:
                                        oc * SLAB + (qq + 1) * TTW],
                                out_acc[:, oc * SLAB + qq * TTW:
                                        oc * SLAB + (qq + 1) * TTW],
                                tmp5[:, :]))
                            plan._tmpv = ov
                    plan._oev[par] = ov
            first = False
        plan.wait_eng("gpsimd", "vector", plan.cur("d_vector"))
        tgt = plan.bump("res", 16)

        def _acc(e, sb=sb, s=plan.sems["res"]):
            e.dma_start(out=out_acc[:, :].rearrange("p (i t) -> p i t", i=IC),
                        in_=x3_d[:, sb * SLAB:(sb + 1) * SLAB]
                        .rearrange("(i p) t -> p i t", p=128),
                        accum_op=ALU.add).then_inc(s, 16)
        plan.add("gpsimd", _acc)
        plan.wait("gpsimd", "res", tgt)
        plan.op("gpsimd", lambda e: e.memset(trcb[0:1, 0:1], 0.0))
        plan.join()
        ln_xbf = act[:, 0:IC * SLAB]
        ln_sqb = act[:, IC * SLAB:2 * IC * SLAB]
        emit_ln(plan, out_acc, ln_xbf, ln_sqb, ones128b, row_sc, bc_sb,
                ones1f, ps_t, ps_g, tlen=SLAB)
        plan.join()
        plan.dma("sync",
                 o_xf[:, sb * SLAB:(sb + 1) * SLAB]
                 .rearrange("(i p) t -> p i t", p=128),
                 out_acc[:, :].rearrange("p (i t) -> p i t", i=IC),
                 "sto%d" % sb)
        if sb == 0:
            plan.wait("vector", "sto0", plan.cur("sto0"))
            plan.op("vector", lambda e: e.memset(m1c[:, :], 0.0))
            plan.join()


def build_fused():
    nc = bass.Bass("TRN2", target_bir_lowering=False)
    C = SimpleNamespace()

    def ein(name, shape, dtype=F32):
        return nc.dram_tensor(name, shape, dtype, kind="ExternalInput").ap()

    C.xT = ein("xT", [D, T])
    C.Wq = ein("Wq", [D, D])
    C.Wk = ein("Wk", [D, D])
    C.Wv = ein("Wv", [D, D])
    C.Wo = ein("Wo", [D, D])
    C.bo = ein("bo", [128, IC])
    C.PT = ein("PT", [DH, M])
    C.gw = ein("gw", [D, NE])
    C.gb = ein("gb", [NE, 1])
    C.w1 = ein("w1", [NE, D, D])
    C.w2 = ein("w2", [NE, D, D])
    C.w3 = ein("w3", [NE, D, D])
    C.b1 = ein("b1", [128, NE * IC])
    C.b2 = ein("b2", [128, NE * IC])
    C.b3 = ein("b3", [128, NE * IC])
    C.Wq2 = ein("Wq2", [D, D])
    C.Wk2 = ein("Wk2", [D, D])
    C.Wv2 = ein("Wv2", [D, D])
    C.bq2 = ein("bq2", [128, IC])
    C.bk2 = ein("bk2", [128, IC])
    C.bv2 = ein("bv2", [1, D])
    C.lidx = ein("lidx", [NL, 1], I32)
    C.t0c = ein("t0c", [NL, 1], I32)
    C.Wo2 = ein("Wo2", [D, D])
    C.bo2 = ein("bo2", [128, IC])
    C.temp = ein("temp", [1, 1])
    C.gw2 = ein("gw2", [D, NE])
    C.w13 = ein("w13", [NE, D, 2 * INTER])
    C.w2w = ein("w2w", [NE, INTER, D])
    C.xf = nc.dram_tensor("xf", [D, T], F32, kind="ExternalOutput").ap()

    def din(name, shape, dtype=F32):
        return nc.dram_tensor(name, shape, dtype).ap()

    C.qbf_i = din("qbf_i", [128, IC * T], BF16)
    C.kbf_i = din("kbf_i", [128, IC * T], BF16)
    C.vt_i = din("vt_i", [128, NC * D], BF16)
    C.negbq_i = din("negbq_i", [128, H * NC])
    C.diagk_i = din("diagk_i", [128, H * NC])
    C.x1p_i = din("x1p_i", [D, T])
    C.x1_i = din("x1_i", [D, T])
    C.logits_i = din("logits_i", [NE, T])
    C.ind_i = din("ind_i", [NE, T], BF16)
    C.x2_i = din("x2_i", [D, T])
    C.qbf2_i = din("qbf2_i", [128, IC * T], BF16)
    C.kbf2_i = din("kbf2_i", [128, IC * T], BF16)
    C.vt2_i = din("vt2_i", [128, NC * D], BF16)
    C.qn_i = din("qn_i", [128, H * NC])
    C.kn_i = din("kn_i", [128, H * NC])
    C.ktm_i = din("ktm_i", [T, D], BF16)
    C.winv_i = din("winv_i", [NL, H * DH])
    C.x3_i = din("x3_i", [D, T])
    C.nb_d = din("nb_d", [H, T])
    C.qn_d = din("qn_d", [H, T])
    C.wt_d = din("wt_d", [NE, T])
    C.cb_d = din("cb_d", [NE, T])

    C.kmax_w = nc.dram_tensor("kmax_w", [1, 8], F32)
    C.kmax_r = cc_dram(nc, "kmax_r", [1, 8], F32, "AllReduce", G8)
    C.ctx_w = nc.dram_tensor("ctx_w", [128, H * 2 * 65], F32)
    C.ctx_r = cc_dram(nc, "ctx_r", [128, H * 2 * 65], F32, "AllReduce", G4)
    C.lmax_w = nc.dram_tensor("lmax_w", [NE, 1], F32)
    C.gmax_r = cc_dram(nc, "gmax_r", [NE, 1], F32, "AllReduce", G4)
    C.t_w = nc.dram_tensor("t_w", [NE, 1], F32)
    C.rden_r = cc_dram(nc, "rden_r", [NE, 1], F32, "AllReduce", G4)
    C.v_w = nc.dram_tensor("v_w", [NE * IC, 128], F32)
    C.v_r = cc_dram(nc, "v_r", [NE * IC, 128], F32, "AllReduce", G8)
    C.lm_w = nc.dram_tensor("lm_w", [NL, D], BF16)
    C.lm_r = cc_dram(nc, "lm_r", [NL, D], BF16, "AllReduce", G4)
    C.z_w = nc.dram_tensor("z_w", [NL, H * 65], F32)
    C.z_r = cc_dram(nc, "z_r", [NL, H * 65], F32, "AllReduce", G4)

    C.eye_f = nc.alloc_sbuf_tensor("eye_f", [128, 128], F32)
    C.eye_b = nc.alloc_sbuf_tensor("eye_b", [128, 128], BF16)
    C.ones1f = nc.alloc_sbuf_tensor("ones1f", [1, 128], F32)
    C.ones128b = nc.alloc_sbuf_tensor("ones128b", [128, 1], BF16)

    plan = Plan(nc)
    g1 = plan.op("gpsimd", lambda e: e.iota(
        C.eye_f[:, :], pattern=[[1, 128]], base=0, channel_multiplier=-1,
        allow_small_or_imprecise_dtypes=True))
    plan.wait_eng("vector", "gpsimd", g1)
    veq = plan.op("vector", lambda e: e.tensor_scalar(
        C.eye_f[:, :], C.eye_f[:, :], 0.0, None, ALU.is_equal))
    plan.wait_eng("vector", "vector", veq)
    plan.op("vector", lambda e: e.tensor_scalar(
        C.eye_b[:, :], C.eye_f[:, :], 0.5, None, ALU.is_gt))
    plan.op("vector", lambda e: e.memset(C.ones1f[:, :], 1.0))
    plan.op("vector", lambda e: e.memset(C.ones128b[:, :], 1.0))
    plan.join()

    def phase(name, emitter, *args):
        ph = Phase(nc, name + "_")
        emitter(plan, C, ph, *args)
        phase_barrier(plan)
        ph.close()

    phase("a1", emit_l0attn_p1)
    emit_collective(plan, "cc_kmax", "AllReduce", ALU.max, G8,
                    C.kmax_w, C.kmax_r)
    phase("a2", emit_l0attn_p2)
    emit_collective(plan, "cc_ctx", "AllReduce", ALU.add, G4,
                    C.ctx_w, C.ctx_r)
    phase("a3", emit_l0attn_p3)
    phase("f2", emit_l0ffn_p2)
    emit_collective(plan, "cc_v", "AllReduce", ALU.add, G8,
                    C.v_w, C.v_r)
    phase("f3", emit_l0ffn_p3)
    phase("b1", emit_l1attn_p1)
    emit_collective(plan, "cc_lm", "AllReduce", ALU.add, G4,
                    C.lm_w, C.lm_r)
    phase("b2", emit_l1attn_p2)
    emit_collective(plan, "cc_z", "AllReduce", ALU.add, G4,
                    C.z_w, C.z_r)
    phase("b3", emit_l1attn_p3)
    phase("m1", emit_l1ffn)
    finish(plan)
    return nc


def _shard_tokens(x):
    out = []
    for c in range(8):
        b, t0 = c // 4, (c % 4) * T
        out.append(np.ascontiguousarray(x[b, t0:t0 + T, :].T))
    return out


def _bias_cols(b):
    return np.ascontiguousarray(b.reshape(IC, 128).T).astype(np.float32)


def _bias_cols_e(b):
    return np.ascontiguousarray(
        b.reshape(NE, IC, 128).transpose(2, 0, 1).reshape(128, NE * IC)
    ).astype(np.float32)


def kernel(**inp):
    f32 = np.float32
    x = np.asarray(inp['L0_attn_x'], f32)
    xs = _shard_tokens(x)
    PT = np.ascontiguousarray(np.asarray(inp['L0_attn_P'], f32).T)
    Wq0 = np.asarray(inp['L0_attn_Wq'], f32)
    Wk0 = np.asarray(inp['L0_attn_Wk'], f32)
    Wv0 = np.asarray(inp['L0_attn_Wv'], f32)
    Wo0 = np.asarray(inp['L0_attn_Wo'], f32)
    bo0 = _bias_cols(np.asarray(inp['L0_attn_bo'], f32))
    gw0 = np.asarray(inp['L0_ffn_gate_w'], f32)
    gb0 = np.ascontiguousarray(np.asarray(inp['L0_ffn_gate_b'],
                                          f32).reshape(NE, 1))
    w1 = np.asarray(inp['L0_ffn_w1'], f32)
    w2 = np.asarray(inp['L0_ffn_w2'], f32)
    w3 = np.asarray(inp['L0_ffn_w3'], f32)
    b1 = _bias_cols_e(np.asarray(inp['L0_ffn_b1'], f32))
    b2 = _bias_cols_e(np.asarray(inp['L0_ffn_b2'], f32))
    b3 = _bias_cols_e(np.asarray(inp['L0_ffn_b3'], f32))
    Wq1T = np.ascontiguousarray(np.asarray(inp['L1_attn_Wq'], f32).T)
    Wk1T = np.ascontiguousarray(np.asarray(inp['L1_attn_Wk'], f32).T)
    Wv1T = np.ascontiguousarray(np.asarray(inp['L1_attn_Wv'], f32).T)
    Wo1T = np.ascontiguousarray(np.asarray(inp['L1_attn_Wo'], f32).T)
    bq1 = _bias_cols(np.asarray(inp['L1_attn_bq'], f32))
    bk1 = _bias_cols(np.asarray(inp['L1_attn_bk'], f32))
    bv1 = np.ascontiguousarray(np.asarray(inp['L1_attn_bv'],
                                          f32).reshape(1, D))
    bo1 = _bias_cols(np.asarray(inp['L1_attn_bo'], f32))
    temp = np.asarray(inp['L1_attn_temperature'], f32).reshape(1, 1)
    lidx = np.ascontiguousarray(np.asarray(inp['L1_attn_landmark_idx'],
                                           np.int32).reshape(NL, 1))
    gw1T = np.ascontiguousarray(np.asarray(inp['L1_ffn_gate_w'], f32).T)
    w13T = np.ascontiguousarray(
        np.asarray(inp['L1_ffn_w13_weight'], f32).transpose(0, 2, 1))
    w2T = np.ascontiguousarray(
        np.asarray(inp['L1_ffn_w2_weight'], f32).transpose(0, 2, 1))

    cores = list(range(8))
    t0c = [np.full((NL, 1), (c % 4) * T, np.int32) for c in cores]

    nc = build_fused()
    maps = [dict(
        xT=xs[c], Wq=Wq0, Wk=Wk0, Wv=Wv0, Wo=Wo0, bo=bo0, PT=PT,
        gw=gw0, gb=gb0, w1=w1, w2=w2, w3=w3, b1=b1, b2=b2, b3=b3,
        Wq2=Wq1T, Wk2=Wk1T, Wv2=Wv1T, bq2=bq1, bk2=bk1, bv2=bv1,
        lidx=lidx, t0c=t0c[c], Wo2=Wo1T, bo2=bo1, temp=temp,
        gw2=gw1T, w13=w13T, w2w=w2T) for c in cores]
    rf = run_bass_kernel_spmd(nc, maps, core_ids=cores).results
    out = np.zeros((2, 8192, D), f32)
    for c in cores:
        b, t0 = c // 4, (c % 4) * T
        out[b, t0:t0 + T, :] = np.asarray(rf[c]['xf'], f32).T
    return out


if __name__ == "__main__":
    build_fused()
    print("build ok")


# version 23
# speedup vs baseline: 1.0252x; 1.0015x over previous
from contextlib import ExitStack
from types import SimpleNamespace

import numpy as np
import ml_dtypes
import concourse.bass as bass
import concourse.mybir as mybir
from concourse.bass_utils import run_bass_kernel_spmd
from concourse.replica_groups import maybe_share_collective_output_space

G4 = [[0, 1, 2, 3], [4, 5, 6, 7]]
G8 = [[0, 1, 2, 3, 4, 5, 6, 7]]

F32 = mybir.dt.float32
BF16 = mybir.dt.bfloat16
I32 = mybir.dt.int32
AF = mybir.ActivationFunctionType
ALU = mybir.AluOpType
AX = mybir.AxisListType

BFNP = ml_dtypes.bfloat16

D = 1024
T = 2048
NT = 4
TTW = 512
NC = 16
IC = 8
H = 16
DH = 64
M = 256
DN = float(DH) ** -0.25
EPS_PERF = 1e-4
LN_EPS = 1e-5
NE = 8
NL = 64
INTER = 2048


class Plan:
    ENGINES = ("sync", "gpsimd", "vector", "scalar", "tensor")

    def __init__(self, nc):
        self.nc = nc
        self.ops = {e: [] for e in self.ENGINES}
        self.sems = {}
        self.counts = {}
        for e in self.ENGINES:
            self.sem("d_" + e)
        for name in ("dma", "pc0", "pc1", "wv", "st", "xs0", "xs1", "sto0", "sto1", "kt0", "kt1", "gth", "res", "wt0", "wt1"):
            self.sem(name)

    def sem(self, name):
        if name not in self.sems:
            self.sems[name] = self.nc.alloc_semaphore(name)
            self.counts[name] = 0
        return self.sems[name]

    def bump(self, name, n=1):
        self.sem(name)
        self.counts[name] += n
        return self.counts[name]

    def cur(self, name):
        self.sem(name)
        return self.counts[name]

    def add(self, engine, fn):
        self.ops[engine].append(fn)

    def op(self, engine, fn):
        name = "d_" + engine
        tgt = self.bump(name)
        s = self.sems[name]

        def _f(e, fn=fn, s=s):
            fn(e).then_inc(s, 1)
        self.ops[engine].append(_f)
        return tgt

    def wait(self, engine, sem_name, val):
        s = self.sem(sem_name)
        self.add(engine, lambda e, s=s, v=val: e.wait_ge(s, v))

    def wait_eng(self, engine, producer_engine, val):
        self.wait(engine, "d_" + producer_engine, val)

    def dma(self, engine, out_ap, in_ap, sem_name="dma"):
        tgt = self.bump(sem_name, 16)
        s = self.sems[sem_name]

        def _f(e, o=out_ap, i=in_ap, s=s):
            e.dma_start(out=o, in_=i).then_inc(s, 16)
        self.add(engine, _f)
        return tgt

    def join(self, engines=None):
        engines = engines or self.ENGINES
        snap = {en: self.cur("d_" + en) for en in engines}
        for en in engines:
            for other in engines:
                if other != en and snap[other] > 0:
                    self.wait(en, "d_" + other, snap[other])

    def emit(self):
        nc = self.nc
        with nc.Block() as block:
            for eng_name in self.ENGINES:
                fns = self.ops[eng_name]
                if not fns:
                    continue
                getattr(block, eng_name)(self._mk(fns))

    @staticmethod
    def _mk(fns):
        def _run(eng):
            for f in fns:
                f(eng)
        return _run


def finish(plan):
    for name in list(plan.counts):
        if name.startswith("st") and plan.counts[name] > 0:
            plan.wait("gpsimd", name, plan.cur(name))
    plan.emit()


class Phase:

    def __init__(self, nc, prefix):
        self.nc = nc
        self.prefix = prefix
        self.stack = ExitStack()

    def sbuf(self, name, shape, dtype):
        return self.stack.enter_context(
            self.nc.sbuf_tensor(self.prefix + name, shape, dtype))

    def psum(self, name, shape, dtype=F32):
        return self.stack.enter_context(
            self.nc.psum_tensor(self.prefix + name, shape, dtype))

    def close(self):
        self.stack.close()


def phase_barrier(plan):
    for name in list(plan.counts):
        if name.startswith("d_") or name.startswith("cc"):
            continue
        if plan.counts[name] > 0:
            plan.wait("gpsimd", name, plan.counts[name])
    plan.op("gpsimd", lambda e: e.nop())
    plan.join()


def emit_collective(plan, name, kind, aluop, groups, in_h, out_h):
    s = plan.sem(name)
    plan.counts[name] = 1

    def _f(e, in_h=in_h, out_h=out_h, s=s):
        e.collective_compute(kind, aluop, replica_groups=groups,
                             ins=[in_h.ap().opt()],
                             outs=[out_h.ap().opt()]).then_inc(s)
    plan.add("gpsimd", _f)


def cc_dram(nc, name, shape, dtype, kind, groups):
    sp = maybe_share_collective_output_space(kind, groups, prefer_local=True)
    return nc.dram_tensor(name, shape, dtype, addr_space=sp)


def emit_ln(plan, x_sb, xbf_sb, sq_sb, ones128b, row_sc, bc_sb, ones1f,
            ps_den, ps_bc, tlen=T):
    ntt = tlen // TTW
    """LayerNorm over feature dim of x_sb (fp32 [128, IC*T]) in place.
    xbf_sb/sq_sb: bf16 scratch [128, IC*T]. row_sc: [1, 3*T+1] f32
    (A=[0:T] sum->meansq->sqrt, B=[T:2T] sumsq->var->rstd, C=[2T:3T] mean).
    bc_sb: [128, 2*T] f32 (mean, rstd broadcasts). ps_bc: psum [128, 2048].
    Caller must join() before; ends unjoined."""
    plan.op("vector", lambda e: e.memset(row_sc[0:1, 3 * tlen:3 * tlen + 1],
                                         LN_EPS))
    for icx in range(IC):
        sl = slice(icx * tlen, (icx + 1) * tlen)
        plan.op("vector", lambda e, sl=sl: e.tensor_copy(xbf_sb[:, sl],
                                                         x_sb[:, sl]))
        plan.op("scalar", lambda e, sl=sl: e.activation(sq_sb[:, sl],
                                                        x_sb[:, sl],
                                                        AF.Square))
    vq = plan.cur("d_vector")
    sq = plan.cur("d_scalar")
    plan.wait_eng("tensor", "vector", vq)
    plan.wait_eng("tensor", "scalar", sq)
    prev_cp = None
    for tt in range(ntt):
        if prev_cp is not None:
            plan.wait_eng("tensor", "vector", prev_cp)
        for which, srcb in ((0, xbf_sb), (1, sq_sb)):
            for icx in range(IC):
                plan.op("tensor", lambda e, which=which, srcb=srcb, icx=icx,
                        tt=tt: e.matmul(
                    ps_bc[0:1, which * TTW:(which + 1) * TTW],
                    ones128b[:, :],
                    srcb[:, icx * tlen + tt * TTW: icx * tlen + (tt + 1) * TTW],
                    start=(icx == 0), stop=(icx == IC - 1)))
        tmm = plan.cur("d_tensor")
        plan.wait_eng("vector", "tensor", tmm)
        plan.op("vector", lambda e, tt=tt: e.tensor_copy(
            row_sc[0:1, tt * TTW:(tt + 1) * TTW], ps_bc[0:1, 0:TTW]))
        prev_cp = plan.op("vector", lambda e, tt=tt: e.tensor_copy(
            row_sc[0:1, tlen + tt * TTW: tlen + (tt + 1) * TTW],
            ps_bc[0:1, TTW:2 * TTW]))
    plan.wait_eng("vector", "vector", prev_cp)
    v1 = plan.op("vector", lambda e: e.tensor_scalar_mul(
        row_sc[0:1, 2 * tlen:3 * tlen], row_sc[0:1, 0:tlen], 1.0 / D))
    plan.wait_eng("vector", "vector", v1)
    v2 = plan.op("vector", lambda e: e.tensor_mul(
        row_sc[0:1, 0:tlen], row_sc[0:1, 2 * tlen:3 * tlen],
        row_sc[0:1, 2 * tlen:3 * tlen]))
    plan.wait_eng("vector", "vector", v2)
    v3 = plan.op("vector", lambda e: e.scalar_tensor_tensor(
        row_sc[0:1, tlen:2 * tlen], row_sc[0:1, tlen:2 * tlen], 1.0 / D,
        row_sc[0:1, 0:tlen], ALU.mult, ALU.subtract))
    plan.wait_eng("scalar", "vector", v3)
    s1 = plan.op("scalar", lambda e: e.activation(
        row_sc[0:1, 0:tlen], row_sc[0:1, tlen:2 * tlen], AF.Sqrt,
        bias=row_sc[0:1, 3 * tlen:3 * tlen + 1]))
    plan.wait_eng("vector", "scalar", s1)
    v4 = plan.op("vector", lambda e: e.reciprocal(
        row_sc[0:1, tlen:2 * tlen], row_sc[0:1, 0:tlen]))
    plan.wait_eng("tensor", "vector", v4)
    for which, off in ((0, 2 * tlen), (1, 1 * tlen)):
        for tt in range(ntt):
            plan.op("tensor", lambda e, which=which, off=off, tt=tt:
                    e.matmul(
                ps_bc[:, tt * TTW:(tt + 1) * TTW], ones1f[0:1, :],
                row_sc[0:1, off + tt * TTW: off + (tt + 1) * TTW],
                start=True, stop=True))
        tmm = plan.cur("d_tensor")
        plan.wait_eng("vector", "tensor", tmm)
        vb = plan.op("vector", lambda e, which=which: e.tensor_copy(
            bc_sb[:, which * tlen:(which + 1) * tlen], ps_bc[:, 0:tlen]))
        plan.wait_eng("tensor", "vector", vb)
    bc_done = plan.cur("d_vector")
    plan.wait_eng("vector", "vector", bc_done)
    for icx in range(IC):
        sl = slice(icx * tlen, (icx + 1) * tlen)
        plan.op("vector", lambda e, sl=sl: e.tensor_sub(
            x_sb[:, sl], x_sb[:, sl], bc_sb[:, 0:tlen]))
    vs = plan.cur("d_vector")
    plan.wait_eng("vector", "vector", vs)
    for icx in range(IC):
        sl = slice(icx * tlen, (icx + 1) * tlen)
        plan.op("vector", lambda e, sl=sl: e.tensor_mul(
            x_sb[:, sl], x_sb[:, sl], bc_sb[:, tlen:2 * tlen]))


def emit_l0attn_p1(plan, C, ph):
    xT = C.xT
    Wq, Wk, Wv, PTm = C.Wq, C.Wk, C.Wv, C.PT
    o_qbf, o_kbf, o_vt = C.qbf_i, C.kbf_i, C.vt_i
    o_negbq, o_diagk = C.negbq_i, C.diagk_i
    eye_f = C.eye_f
    ones128b = C.ones128b

    xbf = ph.sbuf("xbf", [128, IC * T], BF16)
    pbf = ph.sbuf("pbf", [128, M], BF16)
    wbuf = [ph.sbuf(f"wbuf{i}", [128, D], BF16) for i in range(2)]
    wvb = ph.sbuf("wvb", [128, IC * D], BF16)
    abf = ph.sbuf("abf", [128, IC * T], BF16)
    asq = ph.sbuf("asq", [128, IC * T], BF16)
    vt = ph.sbuf("vt_s", [128, NC * D], BF16)
    negbq = ph.sbuf("negbq_s", [128, H * NC], F32)
    diagk = ph.sbuf("diagk_s", [128, H * NC], F32)
    kmaxc = ph.sbuf("kmaxc", [128, H], F32)
    km_t = ph.sbuf("km_t", [128, 1], F32)
    km_s = ph.sbuf("km_s", [1, 128], F32)
    km1 = ph.sbuf("km1", [1, 1], F32)
    km_b = ph.sbuf("km_b", [1, 8], F32)
    mxcol = ph.sbuf("mxcol", [128, NC], F32)

    ps_mm = ph.psum("ps_mm", [128, 2 * TTW], F32)
    ps_dd = ph.psum("ps_dd", [128, 8 * M], F32)
    ps_dg = ph.psum("ps_dg", [128, 512], F32)
    ps_tr = ph.psum("ps_tr", [128, 128], F32)

    plan.dma("gpsimd", xbf[:, :].rearrange("p (i t) -> p i t", i=IC),
             xT.rearrange("(i p) t -> p i t", p=128))
    plan.dma("gpsimd", pbf[0:DH, :], PTm[:, :])
    plan.dma("gpsimd", pbf[DH:128, :], PTm[:, :])
    plan.wait("gpsimd", "dma", plan.cur("dma"))
    plan.op("gpsimd", lambda e: e.memset(km1[0:1, 0:1], 0.0))
    plan.join()

    def proj_side(W_ap):
        last_mm = {}
        ps_pairs = [ps_mm[:, 0:1024], ps_dd[:, 0:1024]]
        for oc in range(IC):
            buf = wbuf[oc % 2]
            psem = "pc%d" % (oc % 2)
            if oc >= 2:
                plan.wait_eng("gpsimd", "tensor", last_mm[oc - 2])
            plan.dma("gpsimd", buf[:, :].rearrange("p (i o) -> p i o", i=IC),
                     W_ap[:, oc * 128:(oc + 1) * 128]
                     .rearrange("(i p) o -> p i o", p=128), psem)
            plan.wait("tensor", psem, plan.cur(psem))
            for pp in range(2):
                psX = ps_pairs[pp]
                prev = plan._psv.get(pp)
                if prev is not None:
                    plan.wait_eng("tensor", "scalar", prev[0])
                    plan.wait_eng("tensor", "vector", prev[1])
                for icx in range(IC):
                    for qq in range(2):
                        tt = pp * 2 + qq
                        tgt = plan.op("tensor", lambda e, buf=buf, icx=icx,
                                      tt=tt, qq=qq, psX=psX: e.matmul(
                            psX[:, qq * TTW:(qq + 1) * TTW],
                            buf[:, icx * 128:(icx + 1) * 128],
                            xbf[:, icx * T + tt * TTW:
                                icx * T + (tt + 1) * TTW],
                            start=(icx == 0), stop=(icx == IC - 1),
                            skip_group_check=True))
                last_mm[oc] = tgt
                plan.wait_eng("scalar", "tensor", tgt)
                for qq in range(2):
                    tt = pp * 2 + qq
                    se_t = plan.op("scalar", lambda e, oc=oc, tt=tt, qq=qq,
                                   psX=psX:
                                   e.copy(abf[:, oc * T + tt * TTW:
                                              oc * T + (tt + 1) * TTW],
                                          psX[:, qq * TTW:(qq + 1) * TTW]))
                    plan.wait_eng("vector", "scalar", se_t)
                    ve_t = plan.op("vector", lambda e, oc=oc, tt=tt:
                                   e.tensor_mul(
                        asq[:, oc * T + tt * TTW: oc * T + (tt + 1) * TTW],
                        abf[:, oc * T + tt * TTW: oc * T + (tt + 1) * TTW],
                        abf[:, oc * T + tt * TTW: oc * T + (tt + 1) * TTW]))
                plan._psv[pp] = (se_t, ve_t)

    def head_stats(side):
        red_c = {}
        last_red = None
        last_cmb = None
        for h in range(H):
            oc = h // 2
            rb = 64 * (h % 2)
            for half in range(2):
                if half == 0 and last_cmb is not None:
                    plan.wait_eng("vector", "vector", last_cmb)
                for c8 in range(8):
                    tc = half * 8 + c8
                    if red_c.get(c8) is not None:
                        plan.wait_eng("tensor", "vector", red_c[c8])
                    tgt = plan.op("tensor", lambda e, oc=oc, rb=rb, tc=tc,
                                  c8=c8: e.matmul(
                        ps_dd[:, c8 * M:(c8 + 1) * M],
                        abf[rb:rb + DH,
                            oc * T + tc * 128: oc * T + (tc + 1) * 128],
                        pbf[rb:rb + DH, :], start=True, stop=True))
                    plan.wait_eng("vector", "tensor", tgt)
                    red_c[c8] = plan.op("vector", lambda e, half=half,
                                        c8=c8: e.reduce_max(
                        mxcol[:, half * 8 + c8:half * 8 + c8 + 1],
                        ps_dd[:, c8 * M:(c8 + 1) * M],
                        axis=AX.X))
                last_red = red_c[7]
            if last_cmb is not None:
                plan.wait_eng("tensor", "vector", last_cmb)
            for tc in range(NC):
                tgt = plan.op("tensor", lambda e, oc=oc, rb=rb, tc=tc:
                              e.matmul(
                    ps_dg[:, tc:tc + 1],
                    asq[rb:rb + DH,
                        oc * T + tc * 128: oc * T + (tc + 1) * 128],
                    ones128b[rb:rb + DH, :], start=True, stop=True))
            plan.wait_eng("vector", "tensor", tgt)
            plan.wait_eng("vector", "vector", last_red)
            if side == 0:
                v1 = plan.op("vector", lambda e: e.tensor_scalar_mul(
                    mxcol[:, :], mxcol[:, :], DN))
                plan.wait_eng("vector", "vector", v1)
                last_cmb = plan.op("vector", lambda e, h=h:
                                   e.scalar_tensor_tensor(
                    negbq[:, h * NC:(h + 1) * NC], ps_dg[:, 0:NC],
                    -0.5 * DN * DN, mxcol[:, :], ALU.mult, ALU.subtract))
            else:
                plan.op("vector", lambda e, h=h: e.tensor_scalar_mul(
                    diagk[:, h * NC:(h + 1) * NC], ps_dg[:, 0:NC],
                    0.5 * DN * DN))
                last_cmb = plan.op("vector", lambda e, h=h: e.reduce_max(
                    kmaxc[:, h:h + 1], mxcol[:, :], axis=AX.X))

    plan._psv = {}
    proj_side(Wq)
    plan.join()
    head_stats(0)
    plan.join()
    plan.dma("sync", o_qbf[:, :], abf[:, :], "st")
    plan.dma("sync", o_negbq[:, :], negbq[:, :], "st")
    plan.wait("sync", "st", plan.cur("st"))
    plan.op("sync", lambda e: e.nop())
    plan.join()

    plan._psv = {}
    proj_side(Wk)
    plan.join()
    head_stats(1)
    plan.join()
    plan.dma("sync", o_kbf[:, :], abf[:, :], "st")
    plan.dma("sync", o_diagk[:, :], diagk[:, :], "st")

    plan.wait_eng("vector", "vector", plan.cur("d_vector"))
    v1 = plan.op("vector", lambda e: e.reduce_max(km_t[:, :], kmaxc[:, :],
                                                  axis=AX.X))
    plan.wait_eng("tensor", "vector", v1)
    t1 = plan.op("tensor", lambda e: e.transpose(ps_tr[0:1, :], km_t[:, :],
                                                 eye_f[:, :]))
    plan.wait_eng("vector", "tensor", t1)
    v2 = plan.op("vector", lambda e: e.tensor_copy(km_s[:, :], ps_tr[0:1, :]))
    plan.wait_eng("vector", "vector", v2)
    v3 = plan.op("vector", lambda e: e.reduce_max(km1[:, :], km_s[:, :],
                                                  axis=AX.X))
    plan.wait_eng("vector", "vector", v3)
    v3b = plan.op("vector", lambda e: e.memset(km_b[:, :], 0.0))
    plan.wait_eng("vector", "vector", v3b)
    v3c = plan.op("vector", lambda e: e.tensor_scalar(
        km_b[:, :], km_b[:, :], km1[0:1, 0:1], None, ALU.add))
    plan.wait_eng("sync", "vector", v3c)
    plan.dma("sync", C.kmax_w.ap(), km_b[:, :], "st")

    plan.dma("gpsimd", wvb[:, :].rearrange("p (i o) -> p i o", i=IC),
             Wv.rearrange("(i p) o -> p i o", p=128), "wv")
    plan.wait("tensor", "wv", plan.cur("wv"))
    plan.join()
    prev_slot = {}
    for tc in range(NC):
        for hp in range(IC):
            slot = (tc * IC + hp) % 4
            if slot in prev_slot:
                plan.wait_eng("tensor", "scalar", prev_slot[slot])
            for icx in range(IC):
                tgt = plan.op("tensor", lambda e, tc=tc, hp=hp, icx=icx,
                              slot=slot: e.matmul(
                    ps_dd[:, slot * TTW: slot * TTW + 128],
                    xbf[:, icx * T + tc * 128: icx * T + (tc + 1) * 128],
                    wvb[:, icx * D + hp * 128: icx * D + (hp + 1) * 128],
                    start=(icx == 0), stop=(icx == IC - 1)))
            plan.wait_eng("scalar", "tensor", tgt)
            prev_slot[slot] = plan.op(
                "scalar", lambda e, tc=tc, hp=hp, slot=slot: e.copy(
                    vt[:, tc * D + hp * 128: tc * D + (hp + 1) * 128],
                    ps_dd[:, slot * TTW: slot * TTW + 128]))
    plan.wait_eng("sync", "scalar", plan.cur("d_scalar"))
    plan.dma("sync", o_vt[:, :], vt[:, :], "st")


def emit_l0attn_p2(plan, C, ph):
    kbf_d, vt_d, diagk_d = C.kbf_i, C.vt_i, C.diagk_i
    kmax8_d = C.kmax_r.ap()
    PTm = C.PT
    o_ctx = C.ctx_w.ap()
    ones1f = C.ones1f
    ones128b = C.ones128b

    kbf = ph.sbuf("kbf_s", [128, IC * T], BF16)
    vt = ph.sbuf("vt_s", [128, NC * D], BF16)
    pbf = ph.sbuf("pbf", [128, M], BF16)
    diagk = ph.sbuf("diagk_s", [128, H * NC], F32)
    negbk = ph.sbuf("negbk_s", [128, H * NC], F32)
    kmax8 = ph.sbuf("kmax8_s", [1, 8], F32)
    mxk1 = ph.sbuf("mxk1", [1, 1], F32)
    mxkc = ph.sbuf("mxkc", [128, 1], F32)
    kp2 = [ph.sbuf(f"kp_s{i}", [128, NC * M], BF16) for i in range(2)]
    expf = ph.sbuf("expf", [128, 2 * M], F32)
    ctx = ph.sbuf("ctx_s", [128, H * 2 * 65], F32)

    ps_dd = ph.psum("ps_dd", [128, 8 * M], F32)
    ps_ctx = ph.psum("ps_ctx", [128, 512], F32)
    ps_one = ph.psum("ps_one", [128, 512], F32)
    ps_b = ph.psum("ps_b", [128, 512], F32)

    plan.dma("gpsimd", kbf[:, :], kbf_d)
    plan.dma("gpsimd", vt[:, :], vt_d)
    plan.dma("gpsimd", diagk[:, :], diagk_d)
    plan.wait("gpsimd", "cc_kmax", 1)
    plan.dma("gpsimd", kmax8[:, :], kmax8_d)
    plan.dma("gpsimd", pbf[0:DH, :], PTm[:, :])
    plan.dma("gpsimd", pbf[DH:128, :], PTm[:, :])
    plan.wait("gpsimd", "dma", plan.cur("dma"))
    plan.op("gpsimd", lambda e: e.memset(mxk1[0:1, 0:1], 0.0))
    plan.join()

    v1 = plan.op("vector", lambda e: e.reduce_max(mxk1[:, :], kmax8[:, :],
                                                  axis=AX.X))
    plan.wait_eng("vector", "vector", v1)
    v2 = plan.op("vector", lambda e: e.tensor_scalar_mul(mxk1[:, :],
                                                         mxk1[:, :], DN))
    plan.wait_eng("tensor", "vector", v2)
    t1 = plan.op("tensor", lambda e: e.matmul(ps_b[:, 0:1], ones1f[0:1, :],
                                              mxk1[0:1, 0:1], start=True,
                                              stop=True))
    plan.wait_eng("vector", "tensor", t1)
    v3 = plan.op("vector", lambda e: e.tensor_copy(mxkc[:, :], ps_b[:, 0:1]))
    plan.wait_eng("vector", "vector", v3)
    v4 = plan.op("vector", lambda e: e.tensor_scalar(
        negbk[:, :], diagk[:, :], mxkc[:, 0:1], -1.0, ALU.add, ALU.mult))
    plan.join()

    st = {"kp_done": {}, "ctx_t": {}, "prev_exp": {}, "expv": {},
          "prev_ctx_ev": None}

    def emit_kp(h):
        oc = h // 2
        rb = 64 * (h % 2)
        kpb = kp2[h % 2]
        if st["ctx_t"].get(h % 2) is not None:
            plan.wait_eng("scalar", "tensor", st["ctx_t"][h % 2])
            plan.wait_eng("vector", "tensor", st["ctx_t"][h % 2])
        for half in range(2):
            for c8 in range(8):
                tc = half * 8 + c8
                slot = tc % 2
                if c8 in st["prev_exp"]:
                    plan.wait_eng("tensor", "vector", st["prev_exp"][c8])
                tgt = plan.op("tensor", lambda e, oc=oc, rb=rb, tc=tc, c8=c8:
                              e.matmul(
                    ps_dd[:, c8 * M:(c8 + 1) * M],
                    kbf[rb:rb + DH,
                        oc * T + tc * 128: oc * T + (tc + 1) * 128],
                    pbf[rb:rb + DH, :], start=True, stop=True))
                plan.wait_eng("scalar", "tensor", tgt)
                if slot in st["expv"]:
                    plan.wait_eng("scalar", "vector", st["expv"][slot])
                s1 = plan.op("scalar", lambda e, h=h, tc=tc, c8=c8, slot=slot:
                             e.activation(
                    expf[:, slot * M:(slot + 1) * M],
                    ps_dd[:, c8 * M:(c8 + 1) * M], AF.Exp,
                    bias=negbk[:, h * NC + tc: h * NC + tc + 1], scale=DN))
                plan.wait_eng("vector", "scalar", s1)
                v5 = plan.op("vector", lambda e, tc=tc, slot=slot, kpb=kpb:
                             e.tensor_scalar_add(
                    kpb[:, tc * M:(tc + 1) * M],
                    expf[:, slot * M:(slot + 1) * M], EPS_PERF))
                st["prev_exp"][c8] = v5
                st["expv"][slot] = v5
        st["kp_done"][h] = plan.cur("d_vector")

    def emit_ctx(h):
        kpb = kp2[h % 2]
        plan.wait_eng("tensor", "vector", st["kp_done"][h])
        for mc in range(2):
            if st["prev_ctx_ev"] is not None:
                plan.wait_eng("tensor", "scalar", st["prev_ctx_ev"])
            for tc in range(NC):
                plan.op("tensor", lambda e, h=h, mc=mc, tc=tc, kpb=kpb:
                        e.matmul(
                    ps_ctx[:, 0:64],
                    kpb[:, tc * M + mc * 128: tc * M + mc * 128 + 128],
                    vt[:, tc * D + h * DH: tc * D + (h + 1) * DH],
                    start=(tc == 0), stop=(tc == NC - 1),
                    skip_group_check=True))
                plan.op("tensor", lambda e, h=h, mc=mc, tc=tc, kpb=kpb:
                        e.matmul(
                    ps_one[:, 0:1],
                    kpb[:, tc * M + mc * 128: tc * M + mc * 128 + 128],
                    ones128b[:, :],
                    start=(tc == 0), stop=(tc == NC - 1),
                    skip_group_check=True))
            tdone = plan.cur("d_tensor")
            plan.wait_eng("scalar", "tensor", tdone)
            plan.op("scalar", lambda e, h=h, mc=mc: e.copy(
                ctx[:, (h * 2 + mc) * 65:(h * 2 + mc) * 65 + 64],
                ps_ctx[:, 0:64]))
            st["prev_ctx_ev"] = plan.op("scalar", lambda e, h=h, mc=mc:
                                        e.copy(
                ctx[:, (h * 2 + mc) * 65 + 64:(h * 2 + mc + 1) * 65],
                ps_one[:, 0:1]))
        st["ctx_t"][h % 2] = tdone

    for h in range(H):
        emit_kp(h)
        if h >= 1:
            emit_ctx(h - 1)
    emit_ctx(H - 1)
    plan.wait_eng("sync", "scalar", plan.cur("d_scalar"))
    plan.dma("sync", o_ctx, ctx[:, :], "st")


def emit_l0attn_p3(plan, C, ph):
    xT_d = C.xT
    qbf_d, negbq_d = C.qbf_i, C.negbq_i
    ctx_r_d = C.ctx_r.ap()
    PTm = C.PT
    Wo_d, bo_d = C.Wo, C.bo
    ones1f = C.ones1f
    eye_f = C.eye_f

    qbf = ph.sbuf("qbf_s", [128, IC * T], BF16)
    attn = ph.sbuf("attn_s", [128, IC * T], BF16)
    xoc = [ph.sbuf(f"xoc{i}", [128, SLAB], F32) for i in range(2)]
    x1slab = ph.sbuf("x1slab", [128, IC * SLAB], F32)
    row_sc1 = ph.sbuf("row_sc1", [1, 3 * SLAB + 1], F32)
    bc_sb1 = ph.sbuf("bc_sb1", [128, 2 * SLAB], F32)
    gw_sb = ph.sbuf("gw_sb", [128, IC * NE], F32)
    gb_sb = ph.sbuf("gb_sb", [NE, 1], F32)
    logit_sb = ph.sbuf("logit_sb", [NE, T], F32)
    usb_g = ph.sbuf("usb_g", [NE, T], F32)
    lmax_sb = ph.sbuf("lmax_sb", [NE, 1], F32)
    sume_sb = ph.sbuf("sume_sb", [NE, 1], F32)
    gmax_sb = ph.sbuf("gmax_sb", [NE, 1], F32)
    tterm_sb = ph.sbuf("tterm_sb", [NE, 1], F32)
    pbf = ph.sbuf("pbf", [128, M], BF16)
    negbq = ph.sbuf("negbq_s", [128, H * NC], F32)
    ctxa = ph.sbuf("ctxa", [128, H * 2 * 65], F32)
    ctxb = ph.sbuf("ctxb", [128, H * 2 * 65], BF16)
    nbb2 = [ph.sbuf(f"nbb{i}", [128, T], F32) for i in range(2)]
    trnb = ph.sbuf("trnb", [128, 256], F32)
    u_sb = ph.sbuf("u_sb", [128, 2 * TTW], F32)
    expf = ph.sbuf("expf", [128, 2 * TTW], F32)
    qp2 = [ph.sbuf(f"qp_s{i}", [128, 2 * T], BF16) for i in range(2)]
    denr2 = [ph.sbuf("denr", [1, TTW], F32), bc_sb1[0:1, TTW:2 * TTW]]
    rbb2 = [ph.sbuf("rbb", [128, TTW], F32), bc_sb1[:, 0:TTW]]
    eps_col = ph.sbuf("eps_col", [128, 1], F32)
    wbuf = [ph.sbuf(f"wbuf{i}", [128, D], BF16) for i in range(2)]
    bo_sb = ph.sbuf("bo_sb", [128, IC], F32)

    ps_dd = ph.psum("ps_dd", [128, 512], F32)
    ps_o = ph.psum("ps_o", [128, 512], F32)
    ps_den = ph.psum("ps_den", [128, 512], F32)
    ps_rb = ph.psum("ps_rb", [128, 512], F32)
    ps_bc = ph.psum("ps_bc", [128, 2048], F32)

    plan.dma("gpsimd", qbf[:, :], qbf_d)
    plan.dma("gpsimd", negbq[:, :], negbq_d)
    plan.dma("gpsimd", pbf[0:DH, :], PTm[:, :])
    plan.dma("gpsimd", pbf[DH:128, :], PTm[:, :])
    plan.wait("gpsimd", "cc_ctx", 1)
    plan.dma("gpsimd", ctxa[:, :], ctx_r_d)
    plan.dma("gpsimd", bo_sb[:, :], bo_d)
    plan.wait("gpsimd", "dma", plan.cur("dma"))
    plan.op("gpsimd", lambda e: e.memset(trnb[0:1, 0:1], 0.0))
    plan.op("gpsimd", lambda e: e.memset(eps_col[:, :], EPS_PERF))
    plan.join()

    vc = plan.op("vector", lambda e: e.tensor_copy(ctxb[:, :], ctxa[:, :]))
    plan.join()

    for blk in range(2):
        tb = plan.op("tensor", lambda e, blk=blk: e.transpose(
            ps_bc[0:128, 0:128], negbq[:, blk * 128:(blk + 1) * 128],
            eye_f[:, :]))
        plan.wait_eng("vector", "tensor", tb)
        vb = plan.op("vector", lambda e, blk=blk: e.tensor_copy(
            trnb[:, blk * 128:(blk + 1) * 128], ps_bc[0:128, 0:128]))
        plan.wait_eng("tensor", "vector", vb)
    plan.wait_eng("sync", "vector", plan.cur("d_vector"))
    for blk in range(2):
        plan.dma("sync",
                 C.nb_d[blk * 8:(blk + 1) * 8, :]
                 .rearrange("h (t i) -> (h t) i", t=16),
                 trnb[:, blk * 128:(blk + 1) * 128], "wv")
    plan.wait("gpsimd", "wv", plan.cur("wv"))
    nbdma = {}
    for h0 in range(2):
        nbdma[h0] = plan.dma("gpsimd", nbb2[h0][:, :],
                             C.nb_d[h0:h0 + 1, :].broadcast_to((128, T)),
                             "nbq")

    st2_t = {}
    qp_dv = {}
    qp_sc = {}
    qst = {}

    def stage1(h):
        oc = h // 2
        rb = 64 * (h % 2)
        qpb = qp2[h % 2]
        nbb = nbb2[h % 2]
        if st2_t.get(h % 2) is not None:
            plan.wait_eng("scalar", "tensor", st2_t[h % 2])
        if h + 1 < H:
            if h >= 1:
                plan.wait_eng("gpsimd", "vector", qp_dv[h - 1])
            nbdma[h + 1] = plan.dma(
                "gpsimd", nbb2[(h + 1) % 2][:, :],
                C.nb_d[h + 1:h + 2, :].broadcast_to((128, T)), "nbq")
        plan.wait("vector", "nbq", nbdma[h])
        for mc in range(2):
            for tt in range(NT):
                par = (mc * NT + tt) % 2
                psX = [ps_dd, ps_rb][par]
                if qst.get(("v3", par)) is not None:
                    plan.wait_eng("tensor", "vector", qst[("v3", par)])
                tgt = plan.op("tensor", lambda e, oc=oc, rb=rb, mc=mc, tt=tt,
                              psX=psX: e.matmul(
                    psX[:, :],
                    pbf[rb:rb + DH, mc * 128:(mc + 1) * 128],
                    qbf[rb:rb + DH, oc * T + tt * TTW: oc * T + (tt + 1) * TTW],
                    start=True, stop=True))
                plan.wait_eng("vector", "tensor", tgt)
                if qst.get(("s1", par)) is not None:
                    plan.wait_eng("vector", "scalar", qst[("s1", par)])
                v3 = plan.op("vector", lambda e, tt=tt, par=par, psX=psX:
                             e.scalar_tensor_tensor(
                    u_sb[:, par * TTW:(par + 1) * TTW], psX[:, :], DN,
                    nbb[:, tt * TTW:(tt + 1) * TTW], ALU.mult, ALU.add))
                plan.wait_eng("scalar", "vector", v3)
                if qst.get(("v4", par)) is not None:
                    plan.wait_eng("scalar", "scalar", qst[("v4", par)])
                s1 = plan.op("scalar", lambda e, par=par: e.activation(
                    expf[:, par * TTW:(par + 1) * TTW],
                    u_sb[:, par * TTW:(par + 1) * TTW], AF.Exp))
                plan.wait_eng("scalar", "scalar", s1)
                v4 = plan.op("scalar", lambda e, mc=mc, tt=tt, par=par,
                             qpb=qpb: e.activation(
                    qpb[:, mc * T + tt * TTW: mc * T + (tt + 1) * TTW],
                    expf[:, par * TTW:(par + 1) * TTW], AF.Identity,
                    bias=eps_col[:, 0:1]))
                qst[("v3", par)] = v3
                qst[("s1", par)] = s1
                qst[("v4", par)] = v4
        qp_dv[h] = plan.cur("d_vector")
        qp_sc[h] = plan.cur("d_scalar")

    pso_s2 = [ps_o[0:DH, 0:TTW], ps_bc[0:DH, TTW:2 * TTW]]
    psd_s2 = [ps_den[0:1, 0:TTW], ps_bc[0:1, 2 * TTW:3 * TTW]]
    bc_s2 = [ps_bc[:, 0:TTW], ps_bc[:, 3 * TTW:4 * TTW]]

    def stage2(h):
        oc = h // 2
        rb = 64 * (h % 2)
        qpb = qp2[h % 2]
        plan.wait_eng("tensor", "vector", qp_dv[h])
        plan.wait_eng("tensor", "scalar", qp_sc[h])

        def tail(tz):
            pz = tz % 2
            plan.wait_eng("vector", "tensor", qst[("t5", pz)])
            v5 = plan.op("vector", lambda e, pz=pz: e.tensor_copy(
                denr2[pz][:, :], psd_s2[pz]))
            plan.wait_eng("tensor", "vector", v5)
            if qst.get(("v6", pz)) is not None:
                plan.wait_eng("tensor", "vector", qst[("v6", pz)])
            t4 = plan.op("tensor", lambda e, pz=pz: e.matmul(
                bc_s2[pz], ones1f[0:1, :], denr2[pz][0:1, :],
                start=True, stop=True))
            plan.wait_eng("vector", "tensor", t4)
            v6 = plan.op("vector", lambda e, pz=pz: e.reciprocal(
                rbb2[pz][:, :], bc_s2[pz]))
            plan.wait_eng("vector", "vector", v6)
            v7 = plan.op("vector", lambda e, oc=oc, rb=rb, tz=tz, pz=pz:
                         e.tensor_mul(
                attn[rb:rb + DH, oc * T + tz * TTW: oc * T + (tz + 1) * TTW],
                pso_s2[pz], rbb2[pz][0:DH, :]))
            qst[("v6", pz)] = v6
            qst[("mul", pz)] = v7

        for tt in range(NT):
            par = tt % 2
            if qst.get(("mul", par)) is not None:
                plan.wait_eng("tensor", "vector", qst[("mul", par)])
            for mc in range(2):
                plan.op("tensor", lambda e, h=h, mc=mc, tt=tt, qpb=qpb,
                        par=par: e.matmul(
                    pso_s2[par],
                    ctxb[:, (h * 2 + mc) * 65:(h * 2 + mc) * 65 + DH],
                    qpb[:, mc * T + tt * TTW: mc * T + (tt + 1) * TTW],
                    start=(mc == 0), stop=(mc == 1)))
            for mc in range(2):
                t5 = plan.op("tensor", lambda e, h=h, mc=mc, tt=tt, qpb=qpb,
                             par=par: e.matmul(
                    psd_s2[par],
                    ctxb[:, (h * 2 + mc) * 65 + DH:(h * 2 + mc) * 65 + 65],
                    qpb[:, mc * T + tt * TTW: mc * T + (tt + 1) * TTW],
                    start=(mc == 0), stop=(mc == 1)))
            qst[("t5", par)] = t5
            if tt > 0:
                tail(tt - 1)
        tail(NT - 1)
        st2_t[h % 2] = plan.cur("d_tensor")

    for h in range(H):
        if h >= 1:
            stage2(h - 1)
        stage1(h)
    stage2(H - 1)
    plan.join()

    plan.dma("gpsimd", gw_sb[:, :].rearrange("p (i en) -> p i en", i=IC),
             C.gw.rearrange("(i p) en -> p i en", p=128), "wv")
    plan.dma("gpsimd", gb_sb[:, :], C.gb, "wv")
    last_mm = {}
    prev_ev = {}
    prev_half = {}
    for sb in range(2):
        for ocx in range(IC):
            idx = sb * IC + ocx
            buf = wbuf[ocx % 2]
            psem = "pc%d" % (ocx % 2)
            xsem = "xs%d" % (ocx % 2)
            if last_mm.get(idx % 2) is not None:
                plan.wait_eng("gpsimd", "tensor", last_mm[idx % 2])
            plan.dma("gpsimd", buf[:, :].rearrange("p (i o) -> p i o", i=IC),
                     Wo_d[:, ocx * 128:(ocx + 1) * 128]
                     .rearrange("(i p) o -> p i o", p=128), psem)
            if idx >= 2:
                plan.wait_eng("sync", "vector", prev_ev[ocx % 2])
            plan.dma("sync", xoc[ocx % 2][:, :],
                     xT_d[ocx * 128:(ocx + 1) * 128,
                          sb * SLAB:(sb + 1) * SLAB], xsem)
            plan.wait("tensor", psem, plan.cur(psem))
            plan.wait("vector", xsem, plan.cur(xsem))
            for tt2 in range(2):
                tt = sb * 2 + tt2
                half = tt2
                if half in prev_half:
                    plan.wait_eng("tensor", "vector", prev_half[half])
                for icx in range(IC):
                    tgt = plan.op("tensor", lambda e, buf=buf, icx=icx,
                                  tt=tt, half=half: e.matmul(
                        ps_bc[:, half * TTW:(half + 1) * TTW],
                        buf[:, icx * 128:(icx + 1) * 128],
                        attn[:, icx * T + tt * TTW: icx * T + (tt + 1) * TTW],
                        start=(icx == 0), stop=(icx == IC - 1)))
                last_mm[idx % 2] = tgt
                plan.wait_eng("vector", "tensor", tgt)
                prev_ev[ocx % 2] = plan.op(
                    "vector", lambda e, ocx=ocx, tt2=tt2, half=half:
                    e.scalar_tensor_tensor(
                        x1slab[:, ocx * SLAB + tt2 * TTW:
                               ocx * SLAB + (tt2 + 1) * TTW],
                        ps_bc[:, half * TTW:(half + 1) * TTW],
                        bo_sb[:, ocx:ocx + 1],
                        xoc[ocx % 2][:, tt2 * TTW:(tt2 + 1) * TTW],
                        ALU.add, ALU.add))
                prev_half[half] = prev_ev[ocx % 2]
        plan.join()
        emit_ln(plan, x1slab, qbf[:, 0:IC * SLAB],
                qbf[:, IC * SLAB:2 * IC * SLAB], C.ones128b, row_sc1,
                bc_sb1, C.ones1f, ps_den, ps_bc, tlen=SLAB)
        plan.join()
        plan.dma("sync",
                 C.x1_i[:, sb * SLAB:(sb + 1) * SLAB]
                 .rearrange("(i p) t -> p i t", p=128),
                 x1slab[:, :].rearrange("p (i t) -> p i t", i=IC), "st")
        plan.wait("tensor", "wv", plan.cur("wv"))
        for tt2 in range(2):
            for icx in range(IC):
                gmm = plan.op("tensor", lambda e, icx=icx, tt2=tt2: e.matmul(
                    ps_bc[0:NE, tt2 * TTW:(tt2 + 1) * TTW],
                    gw_sb[:, icx * NE:(icx + 1) * NE],
                    x1slab[:, icx * SLAB + tt2 * TTW:
                           icx * SLAB + (tt2 + 1) * TTW],
                    start=(icx == 0), stop=(icx == IC - 1),
                    skip_group_check=True))
        plan.wait_eng("vector", "tensor", gmm)
        gv = plan.op("vector", lambda e, sb=sb: e.tensor_scalar(
            logit_sb[:, sb * SLAB:(sb + 1) * SLAB],
            ps_bc[0:NE, 0:SLAB], gb_sb[:, 0:1], None, ALU.add))
        if sb == 0:
            plan.wait("vector", "st", plan.cur("st"))
            plan.wait_eng("vector", "vector", gv)
            plan.wait_eng("tensor", "vector", gv)
            plan.op("vector", lambda e: e.memset(row_sc1[0:1, 0:1], 0.0))
            plan.join()
    plan.join()
    v1g = plan.op("vector", lambda e: e.reduce_max(lmax_sb[:, :],
                                                   logit_sb[:, :], axis=AX.X))
    plan.wait_eng("vector", "vector", v1g)
    v2g = plan.op("vector", lambda e: e.tensor_scalar(
        usb_g[:, :], logit_sb[:, :], lmax_sb[:, 0:1], None, ALU.subtract))
    plan.wait_eng("scalar", "vector", v2g)
    s1g = plan.op("scalar", lambda e: e.activation(usb_g[:, :], usb_g[:, :],
                                                   AF.Exp))
    plan.wait_eng("vector", "scalar", s1g)
    v3g = plan.op("vector", lambda e: e.reduce_sum(sume_sb[:, :], usb_g[:, :],
                                                   axis=AX.X))
    plan.wait_eng("sync", "vector", plan.cur("d_vector"))
    plan.dma("sync", C.logits_i, logit_sb[:, :], "st")
    plan.dma("sync", C.lmax_w.ap(), lmax_sb[:, :], "st")
    phase_barrier(plan)
    emit_collective(plan, "cc_gmax", "AllReduce", ALU.max, G4,
                    C.lmax_w, C.gmax_r)
    plan.wait("gpsimd", "cc_gmax", 1)
    plan.dma("gpsimd", gmax_sb[:, :], C.gmax_r.ap())
    plan.wait("vector", "dma", plan.cur("dma"))
    vt1 = plan.op("vector", lambda e: e.tensor_tensor(
        tterm_sb[:, :], lmax_sb[:, :], gmax_sb[:, :], ALU.subtract))
    plan.wait_eng("scalar", "vector", vt1)
    st1 = plan.op("scalar", lambda e: e.activation(tterm_sb[:, :],
                                                   tterm_sb[:, :], AF.Exp))
    plan.wait_eng("vector", "scalar", st1)
    vt2 = plan.op("vector", lambda e: e.tensor_mul(tterm_sb[:, :],
                                                   tterm_sb[:, :],
                                                   sume_sb[:, :]))
    plan.wait_eng("sync", "vector", vt2)
    plan.dma("sync", C.t_w.ap(), tterm_sb[:, :], "st")
    phase_barrier(plan)
    emit_collective(plan, "cc_rden", "AllReduce", ALU.add, G4,
                    C.t_w, C.rden_r)


def emit_l0ffn_p1(plan, C, ph):
    x1p_d = C.x1p_i
    gw_d, gb_d = C.gw, C.gb
    o_x1 = C.x1_i
    o_logits = C.logits_i
    ones128b = C.ones128b
    ones1f = C.ones1f

    x_sb = ph.sbuf("x_sb", [128, IC * T], F32)
    xbf = ph.sbuf("xbf", [128, IC * T], BF16)
    sqb = ph.sbuf("sqb", [128, IC * T], BF16)
    row_sc = ph.sbuf("row_sc", [1, 3 * T + 1], F32)
    bc_sb = ph.sbuf("bc_sb", [128, 2 * T], F32)
    gw_sb = ph.sbuf("gw_sb", [128, IC * NE], F32)
    gb_sb = ph.sbuf("gb_sb", [NE, 1], F32)
    logit = ph.sbuf("logit", [NE, T], F32)
    lmax = ph.sbuf("lmax_s", [NE, 1], F32)
    usb = ph.sbuf("usb", [NE, T], F32)
    sume = ph.sbuf("sume_s", [NE, 1], F32)
    gmax = ph.sbuf("gmax_s", [NE, 1], F32)
    tterm = ph.sbuf("tterm", [NE, 1], F32)

    ps_den = ph.psum("ps_den", [128, 512], F32)
    ps_bc = ph.psum("ps_bc", [128, 2048], F32)
    ps_g = ph.psum("ps_g", [128, 2 * TTW], F32)

    plan.dma("gpsimd", x_sb[:, :].rearrange("p (i t) -> p i t", i=IC),
             x1p_d.rearrange("(i p) t -> p i t", p=128))
    plan.dma("gpsimd", gw_sb[:, :].rearrange("p (i en) -> p i en", i=IC),
             gw_d.rearrange("(i p) en -> p i en", p=128))
    plan.dma("gpsimd", gb_sb[:, :], gb_d)
    plan.wait("gpsimd", "dma", plan.cur("dma"))
    plan.op("gpsimd", lambda e: e.memset(row_sc[0:1, 0:1], 0.0))
    plan.join()
    emit_ln(plan, x_sb, xbf, sqb, ones128b, row_sc, bc_sb, ones1f,
            ps_den, ps_bc)
    plan.join()
    plan.dma("sync", o_x1.rearrange("(i p) t -> p i t", p=128),
             x_sb[:, :].rearrange("p (i t) -> p i t", i=IC), "st")
    prev_ev = {}
    for tt in range(NT):
        half = tt % 2
        if half in prev_ev:
            plan.wait_eng("tensor", "vector", prev_ev[half])
        for icx in range(IC):
            tgt = plan.op("tensor", lambda e, icx=icx, tt=tt, half=half:
                          e.matmul(
                ps_g[0:NE, half * TTW:(half + 1) * TTW],
                gw_sb[:, icx * NE:(icx + 1) * NE],
                x_sb[:, icx * T + tt * TTW: icx * T + (tt + 1) * TTW],
                start=(icx == 0), stop=(icx == IC - 1)))
        plan.wait_eng("vector", "tensor", tgt)
        prev_ev[half] = plan.op("vector", lambda e, tt=tt, half=half:
                                e.tensor_scalar(
            logit[:, tt * TTW:(tt + 1) * TTW],
            ps_g[0:NE, half * TTW:(half + 1) * TTW],
            gb_sb[:, 0:1], None, ALU.add))
    lg_done = plan.cur("d_vector")
    plan.wait_eng("vector", "vector", lg_done)
    v1 = plan.op("vector", lambda e: e.reduce_max(lmax[:, :], logit[:, :],
                                                  axis=AX.X))
    plan.wait_eng("vector", "vector", v1)
    v2 = plan.op("vector", lambda e: e.tensor_scalar(
        usb[:, :], logit[:, :], lmax[:, 0:1], None, ALU.subtract))
    plan.wait_eng("scalar", "vector", v2)
    s1 = plan.op("scalar", lambda e: e.activation(usb[:, :], usb[:, :],
                                                  AF.Exp))
    plan.wait_eng("vector", "scalar", s1)
    v3 = plan.op("vector", lambda e: e.reduce_sum(sume[:, :], usb[:, :],
                                                  axis=AX.X))
    plan.wait_eng("sync", "vector", plan.cur("d_vector"))
    plan.dma("sync", o_logits, logit[:, :], "st")
    plan.dma("sync", C.lmax_w.ap(), lmax[:, :], "st")
    phase_barrier(plan)
    emit_collective(plan, "cc_gmax", "AllReduce", ALU.max, G4,
                    C.lmax_w, C.gmax_r)
    plan.wait("gpsimd", "cc_gmax", 1)
    plan.dma("gpsimd", gmax[:, :], C.gmax_r.ap())
    plan.wait("vector", "dma", plan.cur("dma"))
    vt1 = plan.op("vector", lambda e: e.tensor_tensor(
        tterm[:, :], lmax[:, :], gmax[:, :], ALU.subtract))
    plan.wait_eng("scalar", "vector", vt1)
    st1 = plan.op("scalar", lambda e: e.activation(tterm[:, :], tterm[:, :],
                                                   AF.Exp))
    plan.wait_eng("vector", "scalar", st1)
    vt2 = plan.op("vector", lambda e: e.tensor_mul(tterm[:, :], tterm[:, :],
                                                   sume[:, :]))
    plan.wait_eng("sync", "vector", vt2)
    plan.dma("sync", C.t_w.ap(), tterm[:, :], "st")
    phase_barrier(plan)
    emit_collective(plan, "cc_rden", "AllReduce", ALU.add, G4,
                    C.t_w, C.rden_r)


def emit_l0ffn_p2(plan, C, ph):
    x1_d = C.x1_i
    logits_d = C.logits_i
    w1_d = C.w1
    w2_d = C.w2
    w3_d = C.w3
    b1_d, b2_d, b3_d = C.b1, C.b2, C.b3
    o_ind = C.ind_i
    eye_f = C.eye_f
    ones1f = C.ones1f

    xbf = ph.sbuf("xbf", [128, IC * T], BF16)
    h1 = ph.sbuf("h1", [128, IC * T], BF16)
    h2 = ph.sbuf("h2", [128, IC * T], BF16)
    ysb = ph.sbuf("ysb", [128, T], F32)
    ydump = ph.sbuf("ydump", [128, T], F32)
    wbuf = [ph.sbuf(f"wbuf{i}", [128, D], BF16) for i in range(4)]
    logit = ph.sbuf("logit", [NE, T], F32)
    gmax = ph.sbuf("gmax", [NE, 1], F32)
    rden = ph.sbuf("rden", [NE, 1], F32)
    probs = ph.sbuf("probs", [NE, T], F32)
    ptm = ph.sbuf("ptm", [128, NC * NE], F32)
    m1c = ph.sbuf("m1c", [128, 1], F32)
    m2c = ph.sbuf("m2c", [128, 1], F32)
    eqc = ph.sbuf("eqc", [128, NE], F32)
    wtok_cols = ph.sbuf("wtok_cols", [128, NE * NC], F32)
    ind_cols = ph.sbuf("ind_cols", [128, NE * NC], F32)
    ind_fm = ph.sbuf("ind_fm", [NE, T], BF16)
    wb2 = [ph.sbuf(f"wb2_{i}", [128, T], F32) for i in range(2)]
    trwt = ph.sbuf("trwt", [128, 128], F32)
    v_sb = ph.sbuf("v_sb", [128, NE * IC], F32)
    vT_sb = ph.sbuf("vT_sb", [NE * IC, 128], F32)
    b1s = ph.sbuf("b1s", [128, NE * IC], F32)
    b2s = ph.sbuf("b2s", [128, NE * IC], F32)
    b3s = ph.sbuf("b3s", [128, NE * IC], F32)

    ps_pair = [ph.psum(f"ps_p{i}", [128, 1024], F32) for i in range(2)]
    ps_t = ph.psum("ps_t", [128, 512], F32)

    plan.dma("gpsimd", xbf[:, :].rearrange("p (i t) -> p i t", i=IC),
             x1_d.rearrange("(i p) t -> p i t", p=128))
    plan.dma("gpsimd", logit[:, :], logits_d)
    plan.wait("gpsimd", "cc_gmax", 1)
    plan.dma("gpsimd", gmax[:, :], C.gmax_r.ap())
    plan.wait("gpsimd", "cc_rden", 1)
    plan.dma("gpsimd", rden[:, :], C.rden_r.ap())
    plan.dma("gpsimd", b1s[:, :], b1_d)
    plan.dma("gpsimd", b2s[:, :], b2_d)
    plan.dma("gpsimd", b3s[:, :], b3_d)
    plan.wait("gpsimd", "dma", plan.cur("dma"))
    plan.op("gpsimd", lambda e: e.memset(trwt[0:1, 0:1], 0.0))
    plan.join()

    v5 = plan.op("vector", lambda e: e.reciprocal(rden[:, :], rden[:, :]))
    plan.wait_eng("vector", "vector", v5)
    v6 = plan.op("vector", lambda e: e.tensor_scalar(
        probs[:, :], logit[:, :], gmax[:, 0:1], None, ALU.subtract))
    plan.wait_eng("scalar", "vector", v6)
    s2 = plan.op("scalar", lambda e: e.activation(probs[:, :], probs[:, :],
                                                  AF.Exp))
    plan.wait_eng("vector", "scalar", s2)
    v7 = plan.op("vector", lambda e: e.tensor_scalar(
        probs[:, :], probs[:, :], rden[:, 0:1], None, ALU.mult))
    plan.wait_eng("tensor", "vector", v7)

    for tc in range(NC):
        t1 = plan.op("tensor", lambda e, tc=tc: e.transpose(
            ps_t[0:128, 0:NE], probs[:, tc * 128:(tc + 1) * 128],
            eye_f[0:NE, 0:NE]))
        plan.wait_eng("vector", "tensor", t1)
        vv = plan.op("vector", lambda e, tc=tc: e.tensor_copy(
            ptm[:, tc * NE:(tc + 1) * NE], ps_t[0:128, 0:NE]))
        plan.wait_eng("tensor", "vector", vv)
    plan.wait_eng("vector", "vector", plan.cur("d_vector"))
    for tc in range(NC):
        sl = slice(tc * NE, (tc + 1) * NE)
        va = plan.op("vector", lambda e, sl=sl: e.reduce_max(
            m1c[:, :], ptm[:, sl], axis=AX.X))
        plan.wait_eng("vector", "vector", va)
        vb = plan.op("vector", lambda e, sl=sl: e.tensor_scalar(
            eqc[:, :], ptm[:, sl], m1c[:, 0:1], -1e30, ALU.is_equal,
            ALU.mult))
        plan.wait_eng("vector", "vector", vb)
        vc = plan.op("vector", lambda e, sl=sl: e.tensor_add(
            eqc[:, :], eqc[:, :], ptm[:, sl]))
        plan.wait_eng("vector", "vector", vc)
        vd = plan.op("vector", lambda e: e.reduce_max(
            m2c[:, :], eqc[:, :], axis=AX.X))
        plan.wait_eng("vector", "vector", vd)
        ve = plan.op("vector", lambda e, sl=sl, tc=tc: e.tensor_scalar(
            ind_cols[:, tc: NE * NC: NC], ptm[:, sl], m2c[:, 0:1], None,
            ALU.is_ge))
        plan.wait_eng("vector", "vector", ve)
        vf = plan.op("vector", lambda e, sl=sl, tc=tc: e.tensor_mul(
            wtok_cols[:, tc: NE * NC: NC], ind_cols[:, tc: NE * NC: NC],
            ptm[:, sl]))
    sel_done = plan.cur("d_vector")
    plan.wait_eng("tensor", "vector", sel_done)
    for tc in range(NC):
        t2 = plan.op("tensor", lambda e, tc=tc: e.transpose(
            ps_t[0:NE, 0:128], ind_cols[:, tc: NE * NC: NC], eye_f[:, :]))
        plan.wait_eng("vector", "tensor", t2)
        vg = plan.op("vector", lambda e, tc=tc: e.tensor_copy(
            ind_fm[:, tc * 128:(tc + 1) * 128], ps_t[0:NE, 0:128]))
        plan.wait_eng("tensor", "vector", vg)
    plan.join()
    plan.dma("sync", o_ind, ind_fm[:, :], "st")

    tw = plan.op("tensor", lambda e: e.transpose(
        ps_t[0:128, 0:128], wtok_cols[:, :], eye_f[:, :]))
    plan.wait_eng("vector", "tensor", tw)
    vtw = plan.op("vector", lambda e: e.tensor_copy(
        trwt[:, :], ps_t[0:128, 0:128]))
    plan.wait_eng("sync", "vector", vtw)
    plan.wait_eng("tensor", "vector", vtw)
    plan.dma("sync",
             C.wt_d[0:NE, :].rearrange("h (t i) -> (h t) i", t=NC),
             trwt[:, :], "wv")
    plan.wait("gpsimd", "wv", plan.cur("wv"))
    wtdma = {0: plan.dma("gpsimd", wb2[0][:, :],
                         C.wt_d[0:1, :].broadcast_to((128, T)), "nbq")}
    plan._vprev_g = None

    def issue_w(W, oc2):
        key = plan._wiss % 4
        plan._wiss += 1
        psem = "pw%d" % key
        if plan._wpar.get(key) is not None:
            plan.wait_eng("gpsimd", "tensor", plan._wpar[key])
        plan._wsem[key] = plan.dma(
            "gpsimd", wbuf[key][:, :].rearrange("p (i o) -> p i o", i=IC),
            W[:, oc2 * 128:(oc2 + 1) * 128]
            .rearrange("(i p) o -> p i o", p=128), psem)

    def fm_layer(src_sb, W_e, bias_cols, e_idx, dst_sb, relu_to_y=None,
                 next_W=None):
        plan.wait_eng("tensor", "scalar", plan.cur("d_scalar"))
        for oc in range(IC):
            if oc + 2 < IC:
                issue_w(W_e, oc + 2)
            elif next_W is not None:
                issue_w(next_W, oc + 2 - IC)
            key = plan._wuse % 4
            plan._wuse += 1
            buf = wbuf[key]
            plan.wait("tensor", "pw%d" % key, plan._wsem[key])
            for pp in range(2):
                psX = ps_pair[pp]
                prev = plan._psv2.get(pp)
                if prev is not None:
                    plan.wait_eng("tensor", "scalar", prev)
                for icx in range(IC):
                    for qq in range(2):
                        tt = pp * 2 + qq
                        tgt = plan.op("tensor", lambda e, buf=buf, icx=icx,
                                      tt=tt, qq=qq, psX=psX, src_sb=src_sb:
                                      e.matmul(
                            psX[:, qq * TTW:(qq + 1) * TTW],
                            buf[:, icx * 128:(icx + 1) * 128],
                            src_sb[:, icx * T + tt * TTW:
                                   icx * T + (tt + 1) * TTW],
                            start=(icx == 0), stop=(icx == IC - 1),
                            skip_group_check=True))
                plan._wpar[key] = tgt
                plan.wait_eng("scalar", "tensor", tgt)
                for qq in range(2):
                    tt = pp * 2 + qq
                    if relu_to_y is not None:
                        sv = plan.op("scalar", lambda e, oc=oc, tt=tt, qq=qq,
                                     psX=psX, e_idx=e_idx: e.activation(
                            ysb[:, tt * TTW:(tt + 1) * TTW],
                            psX[:, qq * TTW:(qq + 1) * TTW], AF.Relu,
                            bias=bias_cols[:, e_idx * IC + oc:
                                           e_idx * IC + oc + 1]))
                        plan._yev[(oc, tt)] = sv
                    else:
                        sv = plan.op("scalar", lambda e, oc=oc, tt=tt, qq=qq,
                                     psX=psX, e_idx=e_idx, dst_sb=dst_sb:
                                     e.activation(
                            dst_sb[:, oc * T + tt * TTW:
                                   oc * T + (tt + 1) * TTW],
                            psX[:, qq * TTW:(qq + 1) * TTW], AF.Relu,
                            bias=bias_cols[:, e_idx * IC + oc:
                                           e_idx * IC + oc + 1]))
                    plan._psv2[pp] = sv
            if relu_to_y is not None:
                sv = plan._yev[(oc, NT - 1)]
                plan.wait_eng("vector", "scalar", sv)
                if plan._vprev is not None:
                    plan.wait_eng("vector", "vector", plan._vprev)
                plan._vprev = plan.op(
                    "vector", lambda e, oc=oc, e_idx=e_idx, wb_sb=wb_sb:
                    e.scalar_tensor_tensor(
                        ydump[:, :], ysb[:, :], 1.0, wb_sb[:, :],
                        ALU.mult, ALU.mult,
                        accum_out=v_sb[:, e_idx * IC + oc:
                                       e_idx * IC + oc + 1]))
                plan.wait_eng("scalar", "vector", plan._vprev)

    plan._wpar = {}
    plan._wsem = {}
    plan._wiss = 0
    plan._wuse = 0
    plan._psv2 = {}
    plan._vprev = None
    issue_w(w1_d[0], 0)
    issue_w(w1_d[0], 1)
    for e_idx in range(NE):
        plan._yev = {}
        wb_sb = wb2[e_idx % 2]
        if e_idx + 1 < NE:
            if plan._vprev_g is not None:
                plan.wait_eng("gpsimd", "vector", plan._vprev_g)
            wtdma[e_idx + 1] = plan.dma(
                "gpsimd", wb2[(e_idx + 1) % 2][:, :],
                C.wt_d[e_idx + 1:e_idx + 2, :].broadcast_to((128, T)),
                "nbq")
        plan.wait("vector", "nbq", wtdma[e_idx])
        nxt = w1_d[e_idx + 1] if e_idx + 1 < NE else None
        fm_layer(xbf, w1_d[e_idx], b1s, e_idx, h1, next_W=w2_d[e_idx])
        fm_layer(h1, w2_d[e_idx], b2s, e_idx, h2, next_W=w3_d[e_idx])
        fm_layer(h2, w3_d[e_idx], b3s, e_idx, None, relu_to_y=True,
                 next_W=nxt)
        plan._vprev_g = plan._vprev
    dvv = plan.cur("d_vector")
    plan.wait_eng("tensor", "vector", dvv)
    tv = plan.op("tensor", lambda e: e.transpose(
        ps_t[0:NE * IC, 0:128], v_sb[:, :], eye_f[:, :]))
    plan.wait_eng("vector", "tensor", tv)
    vv2 = plan.op("vector", lambda e: e.tensor_copy(vT_sb[:, :],
                                                    ps_t[0:NE * IC, 0:128]))
    plan.wait_eng("sync", "vector", vv2)
    plan.dma("sync", C.v_w.ap(), vT_sb[:, :], "st")


def emit_l0ffn_p3(plan, C, ph):
    x1_d = C.x1_i
    ind_d = C.ind_i

    v_acc = ph.sbuf("v_acc", [NE, D], F32)
    v_bf = ph.sbuf("v_bf", [NE, D], BF16)
    ind_fm = ph.sbuf("ind_fm2", [NE, T], BF16)
    xoc = [ph.sbuf(f"xoc{i}", [128, T], F32) for i in range(2)]
    x2full = ph.sbuf("x2full", [128, IC * T], F32)
    ln_xbf = ph.sbuf("ln_xbf", [128, IC * T], BF16)
    ln_sqb = ph.sbuf("ln_sqb", [128, IC * T], BF16)
    row_sc = ph.sbuf("row_sc", [1, 3 * T + 1], F32)
    bc_sb = ph.sbuf("bc_sb", [128, 2 * T], F32)
    ps_mm = ph.psum("ps_mm", [128, 2 * TTW], F32)
    ps_den = ph.psum("ps_den", [128, 512], F32)
    ps_bc = ph.psum("ps_bc", [128, 2048], F32)

    plan.wait("gpsimd", "cc_v", 1)
    plan.dma("gpsimd", v_acc[:, :],
             C.v_r.ap().rearrange("(e r) p -> e (r p)", e=NE))
    plan.dma("gpsimd", ind_fm[:, :], ind_d)
    plan.wait("gpsimd", "dma", plan.cur("dma"))
    plan.wait("vector", "dma", plan.cur("dma"))
    vb = plan.op("vector", lambda e: e.tensor_copy(v_bf[:, :], v_acc[:, :]))
    plan.op("gpsimd", lambda e: e.memset(row_sc[0:1, 1:2], 0.0))
    plan.join()

    prev_half = {}
    for ocx in range(IC):
        xsem = "xs%d" % (ocx % 2)
        if ocx >= 2:
            plan.wait_eng("sync", "vector", prev_half.get((ocx - 2, NT - 1), 0))
        plan.dma("sync", xoc[ocx % 2][:, :],
                 x1_d[ocx * 128:(ocx + 1) * 128, :], xsem)
        plan.wait("vector", xsem, plan.cur(xsem))
        for tt in range(NT):
            half = tt % 2
            if prev_half.get(half) is not None:
                plan.wait_eng("tensor", "vector", prev_half[half])
            tgt = plan.op("tensor", lambda e, ocx=ocx, tt=tt, half=half:
                          e.matmul(
                ps_mm[:, half * TTW:(half + 1) * TTW],
                v_bf[:, ocx * 128:(ocx + 1) * 128],
                ind_fm[:, tt * TTW:(tt + 1) * TTW],
                start=True, stop=True))
            plan.wait_eng("vector", "tensor", tgt)
            done = plan.op("vector", lambda e, ocx=ocx, tt=tt, half=half:
                           e.tensor_add(
                x2full[:, ocx * T + tt * TTW: ocx * T + (tt + 1) * TTW],
                ps_mm[:, half * TTW:(half + 1) * TTW],
                xoc[ocx % 2][:, tt * TTW:(tt + 1) * TTW]))
            prev_half[half] = done
            prev_half[(ocx, tt)] = done
    plan.join()
    emit_ln(plan, x2full, ln_xbf, ln_sqb, C.ones128b, row_sc, bc_sb,
            C.ones1f, ps_den, ps_bc)
    plan.join()
    plan.dma("sync", C.x2_i.rearrange("(i p) t -> p i t", p=128),
             x2full[:, :].rearrange("p (i t) -> p i t", i=IC), "st")


def emit_ln_phase(plan, C, ph, xin, xout):
    ones128b = C.ones128b
    ones1f = C.ones1f
    x_sb = ph.sbuf("x_sb", [128, IC * T], F32)
    xbf = ph.sbuf("xbf", [128, IC * T], BF16)
    sqb = ph.sbuf("sqb", [128, IC * T], BF16)
    row_sc = ph.sbuf("row_sc", [1, 3 * T + 1], F32)
    bc_sb = ph.sbuf("bc_sb", [128, 2 * T], F32)
    ps_den = ph.psum("ps_den", [128, 512], F32)
    ps_bc = ph.psum("ps_bc", [128, 2048], F32)
    plan.dma("gpsimd", x_sb[:, :].rearrange("p (i t) -> p i t", i=IC),
             xin.rearrange("(i p) t -> p i t", p=128))
    plan.wait("gpsimd", "dma", plan.cur("dma"))
    plan.op("gpsimd", lambda e: e.memset(row_sc[0:1, 1:2], 0.0))
    plan.join()
    emit_ln(plan, x_sb, xbf, sqb, ones128b, row_sc, bc_sb, ones1f,
            ps_den, ps_bc)
    plan.join()
    plan.dma("sync", xout.rearrange("(i p) t -> p i t", p=128),
             x_sb[:, :].rearrange("p (i t) -> p i t", i=IC), "st")


def emit_l1attn_p1(plan, C, ph):
    x2_d = C.x2_i
    Wq_d, Wk_d, Wv_d = C.Wq2, C.Wk2, C.Wv2
    bq_d, bk_d, bv_d = C.bq2, C.bk2, C.bv2
    lidx_d, t0c_d = C.lidx, C.t0c
    o_qbf, o_kbf, o_vt = C.qbf2_i, C.kbf2_i, C.vt2_i
    o_qn, o_kn = C.qn_i, C.kn_i
    ktm_i = C.ktm_i
    ones128b = C.ones128b
    ones1f = C.ones1f
    eye_f = C.eye_f
    eye_b = C.eye_b

    xbf = ph.sbuf("xbf", [128, IC * T], BF16)
    abf = ph.sbuf("abf", [128, IC * T], BF16)
    asq = ph.sbuf("asq", [128, IC * T], BF16)
    vt = ph.sbuf("vt_s", [128, NC * D], BF16)
    wbuf = [ph.sbuf(f"wbuf{i}", [128, D], BF16) for i in range(2)]
    wvb = ph.sbuf("wvb", [128, IC * D], BF16)
    bcol = ph.sbuf("bcol", [128, 2 * IC], F32)
    bvrow = ph.sbuf("bvrow", [1, D], F32)
    bvb = ph.sbuf("bvb", [128, D], F32)
    ncols = ph.sbuf("ncols", [128, H * NC], F32)
    ktile = [ph.sbuf(f"ktile{i}", [128, D], BF16)
             for i in range(2)]
    lmp = ph.sbuf("lmp", [NL, D], BF16)
    lidx = ph.sbuf("lidx_s", [NL, 1], I32)
    t0c = ph.sbuf("t0c_s", [NL, 1], I32)
    idxa = ph.sbuf("idxa", [NL, 1], I32)
    vm1 = ph.sbuf("vm1", [NL, 1], I32)
    vm2 = ph.sbuf("vm2", [NL, 1], I32)
    validf = ph.sbuf("validf", [NL, 1], F32)

    ps_mm = ph.psum("ps_mm", [128, 2 * TTW], F32)
    ps_mm2 = ph.psum("ps_mm2", [128, 2 * TTW], F32)
    ps_dg = ph.psum("ps_dg", [128, 512], F32)
    ps_tp = ph.psum("ps_tp", [128, 1024], BF16)
    ps_bc = ph.psum("ps_bc", [128, 1024], F32)

    plan.dma("gpsimd", xbf[:, :].rearrange("p (i t) -> p i t", i=IC),
             x2_d.rearrange("(i p) t -> p i t", p=128))
    plan.dma("gpsimd", bcol[:, 0:IC], bq_d)
    plan.dma("gpsimd", bcol[:, IC:2 * IC], bk_d)
    plan.dma("gpsimd", bvrow[:, :], bv_d)
    plan.dma("gpsimd", lidx[:, :], lidx_d)
    plan.dma("gpsimd", t0c[:, :], t0c_d)
    plan.wait("gpsimd", "dma", plan.cur("dma"))
    plan.op("gpsimd", lambda e: e.memset(lmp[:, :], 0.0))
    plan.join()


    def proj_side(W_ap, bias_off, dst, dst_sq, ncols_dst):
        last_mm = {}
        ps_pairs = [ps_mm[:, 0:1024], ps_mm2[:, 0:1024]]
        for oc in range(IC):
            buf = wbuf[oc % 2]
            psem = "pc%d" % (oc % 2)
            if oc >= 2:
                plan.wait_eng("gpsimd", "tensor", last_mm[oc - 2])
            plan.dma("gpsimd", buf[:, :].rearrange("p (i o) -> p i o", i=IC),
                     W_ap[:, oc * 128:(oc + 1) * 128]
                     .rearrange("(i p) o -> p i o", p=128), psem)
            plan.wait("tensor", psem, plan.cur(psem))
            for pp in range(2):
                psX = ps_pairs[pp]
                prev = plan._psv.get(pp)
                if prev is not None:
                    plan.wait_eng("tensor", "scalar", prev[0])
                    plan.wait_eng("tensor", "vector", prev[1])
                for icx in range(IC):
                    for qq in range(2):
                        tt = pp * 2 + qq
                        tgt = plan.op("tensor", lambda e, buf=buf, icx=icx,
                                      tt=tt, qq=qq, psX=psX: e.matmul(
                            psX[:, qq * TTW:(qq + 1) * TTW],
                            buf[:, icx * 128:(icx + 1) * 128],
                            xbf[:, icx * T + tt * TTW:
                                icx * T + (tt + 1) * TTW],
                            start=(icx == 0), stop=(icx == IC - 1),
                            skip_group_check=True))
                last_mm[oc] = tgt
                plan.wait_eng("scalar", "tensor", tgt)
                for qq in range(2):
                    tt = pp * 2 + qq
                    se_t = plan.op("scalar", lambda e, oc=oc, tt=tt, qq=qq,
                                   psX=psX: e.add(
                        dst[:, oc * T + tt * TTW: oc * T + (tt + 1) * TTW],
                        psX[:, qq * TTW:(qq + 1) * TTW],
                        bcol[:, bias_off + oc: bias_off + oc + 1]))
                    plan.wait_eng("vector", "scalar", se_t)
                    ve_t = plan.op("vector", lambda e, oc=oc, tt=tt:
                                   e.tensor_mul(
                        dst_sq[:, oc * T + tt * TTW: oc * T + (tt + 1) * TTW],
                        dst[:, oc * T + tt * TTW: oc * T + (tt + 1) * TTW],
                        dst[:, oc * T + tt * TTW: oc * T + (tt + 1) * TTW]))
                plan._psv[pp] = (se_t, ve_t)
        sq_done = plan.cur("d_vector")
        plan.wait_eng("tensor", "vector", sq_done)
        last_cp = None
        for h in range(H):
            oc = h // 2
            rb = 64 * (h % 2)
            if last_cp is not None:
                plan.wait_eng("tensor", "vector", last_cp)
            for tc in range(NC):
                plan.op("tensor", lambda e, oc=oc, rb=rb, tc=tc: e.matmul(
                    ps_dg[:, tc:tc + 1],
                    dst_sq[rb:rb + DH,
                           oc * T + tc * 128: oc * T + (tc + 1) * 128],
                    ones128b[rb:rb + DH, :], start=True, stop=True))
            tgt = plan.cur("d_tensor")
            plan.wait_eng("vector", "tensor", tgt)
            last_cp = plan.op("vector", lambda e, h=h: e.tensor_copy(
                ncols[:, h * NC:(h + 1) * NC], ps_dg[:, 0:NC]))
        plan.wait_eng("sync", "vector", last_cp)
        plan.dma("sync", ncols_dst, ncols[:, :], "st")

    plan._psv = {}
    proj_side(Wq_d, 0, abf, asq, o_qn)
    plan.join()
    plan.dma("sync", o_qbf, abf[:, :], "st")
    plan.wait("sync", "st", plan.cur("st"))
    plan.op("sync", lambda e: e.nop())
    plan.join()
    plan._psv = {}
    proj_side(Wk_d, IC, abf, asq, o_kn)
    plan.join()
    plan.dma("sync", o_kbf, abf[:, :], "st")

    prev_cp = None
    prev_st = {}
    for tc in range(NC):
        if prev_cp is not None:
            plan.wait_eng("tensor", "vector", prev_cp)
        if tc >= 2 and (tc % 2) in prev_st:
            plan.wait("vector", "kt%d" % (tc % 2), prev_st[tc % 2])
        for oc in range(IC):
            plan.op("tensor", lambda e, tc=tc, oc=oc: e.transpose(
                ps_tp[:, oc * 128:(oc + 1) * 128] if False else
                ps_tp[:, 0:1024].rearrange("p (a b) -> p a b", a=8)[:, oc, :],
                abf[:, oc * T + tc * 128: oc * T + (tc + 1) * 128],
                eye_b[:, :]))
        tgt = plan.cur("d_tensor")
        plan.wait_eng("vector", "tensor", tgt)
        prev_cp = plan.op("vector", lambda e, tc=tc: e.tensor_copy(
            ktile[tc % 2][:, :], ps_tp[:, 0:1024]))
        plan.wait_eng("sync", "vector", prev_cp)
        prev_st[tc % 2] = plan.dma(
            "sync", ktm_i[tc * 128:(tc + 1) * 128, :], ktile[tc % 2][:, :],
            "kt%d" % (tc % 2))
    plan.wait("vector", "kt0", plan.cur("kt0"))
    plan.wait("vector", "kt1", plan.cur("kt1"))

    va = plan.op("vector", lambda e: e.tensor_tensor(
        idxa[:, :], lidx[:, :], t0c[:, :], ALU.subtract))
    plan.wait_eng("vector", "vector", va)
    vb = plan.op("vector", lambda e: e.tensor_scalar(
        vm1[:, :], idxa[:, :], 0, None, ALU.is_ge))
    vc = plan.op("vector", lambda e: e.tensor_scalar(
        vm2[:, :], idxa[:, :], T, None, ALU.is_lt))
    plan.wait_eng("vector", "vector", vc)
    vd = plan.op("vector", lambda e: e.tensor_tensor(
        vm1[:, :], vm1[:, :], vm2[:, :], ALU.mult))
    plan.wait_eng("vector", "vector", vd)
    ve = plan.op("vector", lambda e: e.tensor_tensor(
        idxa[:, :], idxa[:, :], vm1[:, :], ALU.mult))
    vf = plan.op("vector", lambda e: e.tensor_copy(validf[:, :], vm1[:, :]))
    plan.wait_eng("gpsimd", "vector", plan.cur("d_vector"))
    gt = plan.bump("gth", 16)

    def _gather(e, s=plan.sems["gth"]):
        from concourse.bass_core_rust import IndirectOffsetOnAxis
        e.indirect_dma_start(
            lmp[:, :], None, ktm_i[:, :],
            IndirectOffsetOnAxis(idxa[:, 0:1], 0)).then_inc(s, 16)
    plan.add("gpsimd", _gather)
    plan.wait("vector", "gth", gt)
    vg = plan.op("vector", lambda e: e.tensor_scalar(
        lmp[:, :], lmp[:, :], validf[:, 0:1], None, ALU.mult))
    plan.wait_eng("sync", "vector", vg)
    plan.dma("sync", C.lm_w.ap(), lmp[:, :], "st")

    plan.dma("gpsimd", wvb[:, :].rearrange("p (i o) -> p i o", i=IC),
             Wv_d.rearrange("(i p) o -> p i o", p=128), "wv")
    plan.wait("tensor", "wv", plan.cur("wv"))
    dvv = plan.cur("d_vector")
    plan.wait_eng("tensor", "vector", dvv)
    for hb in range(2):
        plan.op("tensor", lambda e, hb=hb: e.matmul(
            ps_bc[:, hb * TTW:(hb + 1) * TTW], ones1f[0:1, :],
            bvrow[0:1, hb * TTW:(hb + 1) * TTW], start=True, stop=True))
    t5 = plan.cur("d_tensor")
    plan.wait_eng("vector", "tensor", t5)
    v5 = plan.op("vector", lambda e: e.tensor_copy(bvb[:, 0:1024],
                                                   ps_bc[:, 0:1024]))
    plan.wait_eng("vector", "vector", v5)
    plan.join()
    prev_slot = {}
    for tc in range(NC):
        for hp in range(IC):
            slot = (tc * IC + hp) % 2
            if slot in prev_slot:
                plan.wait_eng("tensor", "vector", prev_slot[slot])
            for icx in range(IC):
                tgt = plan.op("tensor", lambda e, tc=tc, hp=hp, icx=icx,
                              slot=slot: e.matmul(
                    ps_mm[:, slot * TTW: slot * TTW + 128],
                    xbf[:, icx * T + tc * 128: icx * T + (tc + 1) * 128],
                    wvb[:, icx * D + hp * 128: icx * D + (hp + 1) * 128],
                    start=(icx == 0), stop=(icx == IC - 1)))
            plan.wait_eng("vector", "tensor", tgt)
            prev_slot[slot] = plan.op(
                "vector", lambda e, tc=tc, hp=hp, slot=slot:
                e.scalar_tensor_tensor(
                    vt[:, tc * D + hp * 128: tc * D + (hp + 1) * 128],
                    ps_mm[:, slot * TTW: slot * TTW + 128], 1.0,
                    bvb[:, hp * 128:(hp + 1) * 128], ALU.mult, ALU.add))
    plan.wait_eng("sync", "vector", plan.cur("d_vector"))
    plan.dma("sync", o_vt, vt[:, :], "st")


NS_ITERS = 14


def emit_lm_common(plan, C, lm_r_d, temp_d, lmp, lmt, lbf, lsq, eye_f, eye_b,
                   ones64b, ones1f, invt, t2c, invtc, lnt_rows, lncol_t,
                   lnbT, ps_a, ps_row, ps_tpb):
    plan.wait("gpsimd", "cc_lm", 1)
    plan.dma("gpsimd", lmp[:, :], lm_r_d)
    plan.dma("gpsimd", invt[0:1, 0:1], temp_d[0:1, 0:1])
    plan.wait("gpsimd", "dma", plan.cur("dma"))
    plan.op("gpsimd", lambda e: e.memset(lmt[0:1, 0:1], 0.0))
    plan.join()
    v1 = plan.op("vector", lambda e: e.reciprocal(invt[0:1, 0:1],
                                                  invt[0:1, 0:1]))
    plan.wait_eng("tensor", "vector", v1)
    t1 = plan.op("tensor", lambda e: e.matmul(
        ps_row[0:128, 0:1], ones1f[0:1, :], invt[0:1, 0:1],
        start=True, stop=True))
    plan.wait_eng("vector", "tensor", t1)
    v2 = plan.op("vector", lambda e: e.tensor_copy(invtc[:, :],
                                                   ps_row[0:128, 0:1]))
    plan.wait_eng("vector", "vector", v2)
    v3 = plan.op("vector", lambda e: e.tensor_scalar_mul(t2c[:, :],
                                                         invtc[:, :], 2.0))
    for h in range(H):
        t2 = plan.op("tensor", lambda e, h=h: e.transpose(
            ps_tpb[0:DH, 0:DH], lmp[:, h * DH:(h + 1) * DH],
            eye_b[0:NL, 0:NL]))
        plan.wait_eng("vector", "tensor", t2)
        va = plan.op("vector", lambda e, h=h: e.tensor_copy(
            lbf[0:DH, h * DH:(h + 1) * DH], ps_tpb[0:DH, 0:DH]))
        vb = plan.op("vector", lambda e, h=h: e.tensor_copy(
            lbf[DH:128, h * DH:(h + 1) * DH], ps_tpb[0:DH, 0:DH]))
        plan.wait_eng("tensor", "vector", vb)
    lb_done = plan.cur("d_vector")
    plan.wait_eng("vector", "vector", lb_done)
    vsq = plan.op("vector", lambda e: e.tensor_mul(lsq[:, :], lbf[:, :],
                                                   lbf[:, :]))
    plan.wait_eng("tensor", "vector", vsq)
    for h in range(H):
        t3 = plan.op("tensor", lambda e, h=h: e.matmul(
            ps_row[0:1, 0:DH], ones64b[0:DH, :],
            lsq[0:DH, h * DH:(h + 1) * DH], start=True, stop=True))
        plan.wait_eng("vector", "tensor", t3)
        vr = plan.op("vector", lambda e, h=h: e.tensor_scalar(
            lnt_rows[0:1, h * DH:(h + 1) * DH], ps_row[0:1, 0:DH],
            invt[0:1, 0:1], None, ALU.mult))
        plan.wait_eng("tensor", "vector", vr)
        t4 = plan.op("tensor", lambda e, h=h: e.transpose(
            ps_row[0:DH, 1:2], lnt_rows[0:1, h * DH:(h + 1) * DH],
            eye_f[0:1, 0:1]))
        plan.wait_eng("vector", "tensor", t4)
        vc = plan.op("vector", lambda e, h=h: e.tensor_copy(
            lncol_t[:, h:h + 1], ps_row[0:DH, 1:2]))
        plan.wait_eng("tensor", "vector", vc)
        t5 = plan.op("tensor", lambda e, h=h: e.matmul(
            ps_a[:, 0:DH], ones1f[0:1, :],
            lnt_rows[0:1, h * DH:(h + 1) * DH], start=True, stop=True))
        plan.wait_eng("vector", "tensor", t5)
        vd = plan.op("vector", lambda e, h=h: e.tensor_copy(
            lnbT[:, h * DH:(h + 1) * DH], ps_a[:, 0:DH]))
        plan.wait_eng("tensor", "vector", vd)
    plan.join()


def emit_l1attn_p2(plan, C, ph):
    lm_r_d = C.lm_r.ap()
    temp_d = C.temp
    kbf_d, vt_d, kn_d = C.kbf2_i, C.vt2_i, C.kn_i
    o_z = C.z_w.ap()
    o_winv = C.winv_i
    eye_f = C.eye_f
    eye_b = C.eye_b
    ones64b = C.ones128b
    ones128b = C.ones128b
    ones1f = C.ones1f

    kbf = ph.sbuf("kbf_s", [128, IC * T], BF16)
    vt = ph.sbuf("vt_s", [128, NC * D], BF16)
    kn = ph.sbuf("kn_s", [128, H * NC], F32)
    knt = ph.sbuf("knt", [128, H * NC], F32)
    lmp = ph.sbuf("lmp", [NL, D], BF16)
    lmt = ph.sbuf("lmt", [NL, D], BF16)
    lbf = ph.sbuf("lbf", [128, H * DH], BF16)
    lsq = ph.sbuf("lsq", [128, H * DH], BF16)
    invt = ph.sbuf("invt", [1, 1], F32)
    t2c = ph.sbuf("t2c", [128, 1], F32)
    invtc = ph.sbuf("invtc", [128, 1], F32)
    lnt_rows = ph.sbuf("lnt_rows", [1, H * DH], F32)
    lncol_t = ph.sbuf("lncol_t", [NL, H], F32)
    lnbT = ph.sbuf("lnbT", [128, H * DH], F32)
    eye2 = ph.sbuf("eye2", [NL, NL], F32)
    usb = ph.sbuf("usb", [128, 2 * DH], F32)
    wsb = ph.sbuf("wsb", [NL, H * NL], F32)
    xsb = ph.sbuf("xsb", [NL, H * NL], F32)
    rsb = ph.sbuf("rsb", [NL, H * NL], F32)
    rowsum = ph.sbuf("rowsum", [NL, 1], F32)
    srow = ph.sbuf("srow", [1, NL], F32)
    s11 = ph.sbuf("s11", [1, 1], F32)
    scol = ph.sbuf("scol", [NL, 1], F32)
    phik = ph.sbuf("phik", [128, NC * DH], BF16)
    zsb = ph.sbuf("zsb", [NL, H * 65], F32)

    ps_a = ph.psum("ps_a", [128, 512], F32)
    ps_row = ph.psum("ps_row", [128, 512], F32)
    ps_ns = [ph.psum(f"ps_ns{i}", [128, 512], F32)
             for i in range(2)]
    ps_z = ph.psum("ps_z", [128, 512], F32)
    ps_zs = ph.psum("ps_zs", [128, 512], F32)
    ps_ql = ph.psum("ps_ql", [128, 512], F32)
    ps_tpb = ph.psum("ps_tpb", [128, 1024], BF16)

    ve0 = plan.op("vector", lambda e: e.tensor_scalar_mul(
        eye2[:, :], eye_f[0:NL, 0:NL], 2.0))
    plan.dma("gpsimd", kbf[:, :], kbf_d)
    plan.dma("gpsimd", vt[:, :], vt_d)
    plan.dma("gpsimd", kn[:, :], kn_d)
    plan.wait("gpsimd", "dma", plan.cur("dma"))
    emit_lm_common(plan, C, lm_r_d, temp_d, lmp, lmt, lbf, lsq, eye_f, eye_b,
                   ones64b, ones1f, invt, t2c, invtc, lnt_rows, lncol_t,
                   lnbT, ps_a, ps_row, ps_tpb)

    vkn = plan.op("vector", lambda e: e.tensor_scalar(
        knt[:, :], kn[:, :], invtc[:, 0:1], None, ALU.mult))
    plan.join()

    for h in range(H):
        t1 = plan.op("tensor", lambda e, h=h: e.matmul(
            ps_a[0:NL, 0:NL], lbf[0:DH, h * DH:(h + 1) * DH],
            lbf[0:DH, h * DH:(h + 1) * DH], start=True, stop=True))
        plan.wait_eng("vector", "tensor", t1)
        v1 = plan.op("vector", lambda e, h=h: e.scalar_tensor_tensor(
            usb[0:NL, 0:NL], ps_a[0:NL, 0:NL], t2c[0:NL, 0:1],
            lnbT[0:NL, h * DH:(h + 1) * DH], ALU.mult, ALU.subtract))
        plan.wait_eng("vector", "vector", v1)
        v2 = plan.op("vector", lambda e, h=h: e.tensor_scalar(
            usb[0:NL, 0:NL], usb[0:NL, 0:NL], lncol_t[:, h:h + 1], 0.0,
            ALU.subtract, ALU.min))
        plan.wait_eng("scalar", "vector", v2)
        s1 = plan.op("scalar", lambda e, h=h: e.activation(
            wsb[:, h * NL:(h + 1) * NL], usb[0:NL, 0:NL], AF.Exp))
        plan.wait_eng("vector", "scalar", s1)
        v3 = plan.op("vector", lambda e, h=h: e.scalar_tensor_tensor(
            wsb[:, h * NL:(h + 1) * NL], eye_f[0:NL, 0:NL], 1e-6,
            wsb[:, h * NL:(h + 1) * NL], ALU.mult, ALU.add))
        plan.wait_eng("vector", "vector", v3)
        v4 = plan.op("vector", lambda e, h=h: e.reduce_sum(
            rowsum[:, :], wsb[:, h * NL:(h + 1) * NL], axis=AX.X,
            apply_absolute_value=True))
        plan.wait_eng("tensor", "vector", v4)
        t2 = plan.op("tensor", lambda e: e.transpose(
            ps_row[0:1, 0:NL], rowsum[:, 0:1], eye_f[0:NL, 0:NL]))
        plan.wait_eng("vector", "tensor", t2)
        v5 = plan.op("vector", lambda e: e.tensor_copy(srow[:, :],
                                                       ps_row[0:1, 0:NL]))
        plan.wait_eng("vector", "vector", v5)
        v6 = plan.op("vector", lambda e: e.reduce_max(s11[:, :], srow[:, :],
                                                      axis=AX.X))
        plan.wait_eng("vector", "vector", v6)
        v7 = plan.op("vector", lambda e: e.reciprocal(s11[:, :], s11[:, :]))
        plan.wait_eng("tensor", "vector", v7)
        t3 = plan.op("tensor", lambda e: e.matmul(
            ps_row[0:NL, 2:3], ones1f[0:1, 0:NL], s11[0:1, 0:1],
            start=True, stop=True))
        plan.wait_eng("vector", "tensor", t3)
        v8 = plan.op("vector", lambda e: e.tensor_copy(scol[:, :],
                                                       ps_row[0:NL, 2:3]))
        plan.wait_eng("vector", "vector", v8)
        v9 = plan.op("vector", lambda e, h=h: e.tensor_scalar(
            xsb[:, h * NL:(h + 1) * NL], eye_f[0:NL, 0:NL], scol[:, 0:1],
            None, ALU.mult))
    plan.join()

    ns_slots = [ps_ns[0], ps_ns[1], ps_z, ps_zs, ps_ql, ps_a]
    slot_ev = {}
    xev = {}
    for it in range(NS_ITERS):
        for h in range(H):
            key = 2 * (it * H + h)
            sA = ns_slots[key % 6]
            sB = ns_slots[(key + 1) % 6]
            if slot_ev.get(key % 6) is not None:
                plan.wait_eng("tensor", "vector", slot_ev[key % 6])
            if xev.get(h) is not None:
                plan.wait_eng("tensor", "vector", xev[h])
            t1 = plan.op("tensor", lambda e, h=h, sA=sA: e.matmul(
                sA[0:NL, 0:NL], wsb[:, h * NL:(h + 1) * NL],
                xsb[:, h * NL:(h + 1) * NL], start=True, stop=True))
            plan.wait_eng("vector", "tensor", t1)
            r_ev = plan.op(
                "vector", lambda e, h=h, sA=sA: e.scalar_tensor_tensor(
                    rsb[:, h * NL:(h + 1) * NL], sA[0:NL, 0:NL],
                    -1.0, eye2[:, :], ALU.mult, ALU.add))
            slot_ev[key % 6] = r_ev
            if slot_ev.get((key + 1) % 6) is not None:
                plan.wait_eng("tensor", "vector", slot_ev[(key + 1) % 6])
            plan.wait_eng("tensor", "vector", r_ev)
            t2 = plan.op("tensor", lambda e, h=h, sB=sB: e.matmul(
                sB[0:NL, 0:NL], xsb[:, h * NL:(h + 1) * NL],
                rsb[:, h * NL:(h + 1) * NL], start=True, stop=True))
            plan.wait_eng("vector", "tensor", t2)
            x_ev = plan.op(
                "vector", lambda e, h=h, sB=sB: e.tensor_copy(
                    xsb[:, h * NL:(h + 1) * NL], sB[0:NL, 0:NL]))
            slot_ev[(key + 1) % 6] = x_ev
            xev[h] = x_ev
    plan.join()
    plan.dma("sync", o_winv, xsb[:, :], "st")

    prev_phi = {}
    prev_zev = None
    for h in range(H):
        oc = h // 2
        rb = 64 * (h % 2)
        for tc in range(NC):
            slot = tc % 2
            if slot in prev_phi:
                plan.wait_eng("tensor", "scalar", prev_phi[slot][1])
            t1 = plan.op("tensor", lambda e, h=h, oc=oc, rb=rb, tc=tc,
                         slot=slot: e.matmul(
                ps_ql[:, slot * 256:slot * 256 + DH],
                kbf[rb:rb + DH, oc * T + tc * 128: oc * T + (tc + 1) * 128],
                lbf[rb:rb + DH, h * DH:(h + 1) * DH],
                start=True, stop=True))
            plan.wait_eng("vector", "tensor", t1)
            if slot in prev_phi:
                plan.wait_eng("vector", "scalar", prev_phi[slot][1])
            v1 = plan.op("vector", lambda e, h=h, slot=slot:
                         e.scalar_tensor_tensor(
                usb[:, slot * DH:(slot + 1) * DH],
                ps_ql[:, slot * 256:slot * 256 + DH],
                t2c[:, 0:1], lnbT[:, h * DH:(h + 1) * DH],
                ALU.mult, ALU.subtract))
            plan.wait_eng("vector", "vector", v1)
            v2 = plan.op("vector", lambda e, h=h, tc=tc, slot=slot:
                         e.tensor_scalar(
                usb[:, slot * DH:(slot + 1) * DH],
                usb[:, slot * DH:(slot + 1) * DH],
                knt[:, h * NC + tc: h * NC + tc + 1],
                0.0, ALU.subtract, ALU.min))
            plan.wait_eng("scalar", "vector", v2)
            s1 = plan.op("scalar", lambda e, tc=tc, slot=slot: e.activation(
                phik[:, tc * DH:(tc + 1) * DH],
                usb[:, slot * DH:(slot + 1) * DH], AF.Exp))
            prev_phi[slot] = (v2, s1)
        phi_done = plan.cur("d_scalar")
        plan.wait_eng("tensor", "scalar", phi_done)
        if prev_zev is not None:
            plan.wait_eng("tensor", "vector", prev_zev)
        for tc in range(NC):
            plan.op("tensor", lambda e, h=h, tc=tc: e.matmul(
                ps_z[0:DH, 0:DH], phik[:, tc * DH:(tc + 1) * DH],
                vt[:, tc * D + h * DH: tc * D + (h + 1) * DH],
                start=(tc == 0), stop=(tc == NC - 1),
                skip_group_check=True))
            plan.op("tensor", lambda e, h=h, tc=tc: e.matmul(
                ps_zs[0:DH, 0:1], phik[:, tc * DH:(tc + 1) * DH],
                ones128b[:, :],
                start=(tc == 0), stop=(tc == NC - 1),
                skip_group_check=True))
        zt = plan.cur("d_tensor")
        plan.wait_eng("vector", "tensor", zt)
        plan.op("vector", lambda e, h=h: e.tensor_copy(
            zsb[:, h * 65:h * 65 + 64], ps_z[0:DH, 0:64]))
        prev_zev = plan.op("vector", lambda e, h=h: e.tensor_copy(
            zsb[:, h * 65 + 64:(h + 1) * 65], ps_zs[0:DH, 0:1]))
    plan.wait_eng("sync", "vector", plan.cur("d_vector"))
    plan.dma("sync", o_z, zsb[:, :], "st")


def emit_l1attn_p3(plan, C, ph):
    winv_d = C.winv_i
    lm_r_d = C.lm_r.ap()
    temp_d = C.temp
    qbf_d, qn_d = C.qbf2_i, C.qn_i
    x2_d = C.x2_i
    Wo_d, bo_d = C.Wo2, C.bo2
    eye_f = C.eye_f
    eye_b = C.eye_b
    ones64b = C.ones128b
    ones1f = C.ones1f

    qbf = ph.sbuf("qbf_s", [128, IC * T], BF16)
    attn = ph.sbuf("attn_s", [128, IC * T], BF16)
    qn = ph.sbuf("qn_s", [128, H * NC], F32)
    zsb = ph.sbuf("zsb", [NL, H * 65], F32)
    winv = ph.sbuf("winv_s", [NL, H * DH], F32)
    ybf = ph.sbuf("ybf", [NL, H * 65], BF16)
    lmp = ph.sbuf("lmp", [NL, D], BF16)
    lmt = ph.sbuf("lmt", [NL, D], BF16)
    lbf = ph.sbuf("lbf", [128, H * DH], BF16)
    lsq = ph.sbuf("lsq", [128, H * DH], BF16)
    invt = ph.sbuf("invt", [1, 1], F32)
    t2c = ph.sbuf("t2c", [128, 1], F32)
    invtc = ph.sbuf("invtc", [128, 1], F32)
    lnt_rows = ph.sbuf("lnt_rows", [1, H * DH], F32)
    lncol_t = ph.sbuf("lncol_t", [NL, H], F32)
    lnbT = ph.sbuf("lnbT", [128, H * DH], F32)
    usb = ph.sbuf("usb", [128, 2 * TTW], F32)
    qnb2 = [ph.sbuf(f"qnb{i}", [128, T], F32) for i in range(2)]
    trqn = ph.sbuf("trqn", [128, 256], F32)
    phiq = ph.sbuf("phiq", [NL, T], BF16)
    nrow2 = [ph.sbuf(f"nrow{i}", [1, TTW], F32) for i in range(2)]
    rbb2 = [ph.sbuf(f"rbb{i}", [128, TTW], F32) for i in range(2)]
    wbuf = [ph.sbuf(f"wbuf{i}", [128, D], BF16) for i in range(2)]
    bo_sb = ph.sbuf("bo_sb", [128, IC], F32)
    xoc = [ph.sbuf(f"xoc{i}", [128, SLAB], F32) for i in range(2)]
    x3slab = ph.sbuf("x3slab", [128, IC * SLAB], F32)
    row_sc3 = ph.sbuf("row_sc3", [1, 3 * SLAB + 1], F32)
    bc_sb3 = ph.sbuf("bc_sb3", [128, 2 * SLAB], F32)

    ps_tpb = ph.psum("ps_tpb", [128, 1024], BF16)
    ps_o = ph.psum("ps_o", [128, 512], F32)
    ps_den = ph.psum("ps_den", [128, 512], F32)
    ps_ph = ph.psum("ps_ph", [128, 512], F32)
    ps_bc = ph.psum("ps_bc", [128, 2048], F32)
    ps_a = ps_o
    ps_row = ps_den

    plan.dma("gpsimd", qbf[:, :], qbf_d)
    plan.dma("gpsimd", qn[:, :], qn_d)
    plan.dma("gpsimd", winv[:, :], winv_d)
    plan.wait("gpsimd", "cc_z", 1)
    plan.dma("gpsimd", zsb[:, :], C.z_r.ap())
    plan.wait("gpsimd", "dma", plan.cur("dma"))
    emit_lm_common(plan, C, lm_r_d, temp_d, lmp, lmt, lbf, lsq, eye_f, eye_b,
                   ones64b, ones1f, invt, t2c, invtc, lnt_rows, lncol_t,
                   lnbT, ps_a, ps_row, ps_tpb)

    for h in range(H):
        slot = h % 2
        t1 = plan.op("tensor", lambda e, h=h, slot=slot: e.matmul(
            ps_a[0:NL, slot * 128:slot * 128 + 65] if False else
            ps_a[0:NL, 0:65],
            winv[:, h * DH:(h + 1) * DH], zsb[:, h * 65:(h + 1) * 65],
            start=True, stop=True))
        plan.wait_eng("vector", "tensor", t1)
        v1 = plan.op("vector", lambda e, h=h: e.tensor_copy(
            ybf[:, h * 65:(h + 1) * 65], ps_a[0:NL, 0:65]))
        plan.wait_eng("tensor", "vector", v1)
    plan.join()

    for blk in range(2):
        tb = plan.op("tensor", lambda e, blk=blk: e.transpose(
            ps_bc[0:128, 0:128], qn[:, blk * 128:(blk + 1) * 128],
            eye_f[:, :]))
        plan.wait_eng("vector", "tensor", tb)
        vb = plan.op("vector", lambda e, blk=blk: e.tensor_scalar(
            trqn[:, blk * 128:(blk + 1) * 128], ps_bc[0:128, 0:128],
            invtc[:, 0:1], None, ALU.mult))
        plan.wait_eng("tensor", "vector", vb)
    plan.wait_eng("sync", "vector", plan.cur("d_vector"))
    for blk in range(2):
        plan.dma("sync",
                 C.qn_d[blk * 8:(blk + 1) * 8, :]
                 .rearrange("h (t i) -> (h t) i", t=16),
                 trqn[:, blk * 128:(blk + 1) * 128], "wv")
    plan.wait("gpsimd", "wv", plan.cur("wv"))
    qndma = {}
    for h0 in range(2):
        qndma[h0] = plan.dma("gpsimd", qnb2[h0][:, :],
                             C.qn_d[h0:h0 + 1, :].broadcast_to((128, T)),
                             "nbq")

    pso_s3 = [ps_o[0:DH, 0:TTW], ps_bc[0:DH, TTW:2 * TTW]]
    psd_s3 = [ps_den[0:1, 0:TTW], ps_bc[0:1, 2 * TTW:3 * TTW]]
    bc_s3 = [ps_bc[:, 0:TTW], ps_bc[:, 3 * TTW:4 * TTW]]
    nst = {}
    phst = {}
    for h in range(H):
        oc = h // 2
        rb = 64 * (h % 2)
        qnb = qnb2[h % 2]
        dv = plan.cur("d_vector")
        plan.wait_eng("tensor", "vector", dv)
        if h + 1 < H:
            plan.wait_eng("gpsimd", "vector", dv)
            qndma[h + 1] = plan.dma(
                "gpsimd", qnb2[(h + 1) % 2][:, :],
                C.qn_d[h + 1:h + 2, :].broadcast_to((128, T)), "nbq")
        plan.wait("vector", "nbq", qndma[h])
        pst_ev = phst
        for tt in range(NT):
            par = tt % 2
            psX = [ps_ph, ps_den,
                   ps_bc[:, 0:TTW], ps_bc[:, TTW:2 * TTW]][tt]
            if pst_ev.get(("v", tt)) is not None:
                plan.wait_eng("tensor", "vector", pst_ev[("v", tt)])
            t4 = plan.op("tensor", lambda e, h=h, rb=rb, oc=oc, tt=tt,
                         psX=psX: e.matmul(
                psX[0:NL, 0:TTW],
                lbf[rb:rb + DH, h * DH:(h + 1) * DH],
                qbf[rb:rb + DH, oc * T + tt * TTW: oc * T + (tt + 1) * TTW],
                start=True, stop=True))
            plan.wait_eng("vector", "tensor", t4)
            if pst_ev.get(("s", par)) is not None:
                plan.wait_eng("vector", "scalar", pst_ev[("s", par)])
            v4 = plan.op("vector", lambda e, h=h, tt=tt, par=par, psX=psX:
                         e.scalar_tensor_tensor(
                usb[0:NL, par * TTW:(par + 1) * TTW],
                psX[0:NL, 0:TTW], t2c[0:NL, 0:1],
                qnb[0:NL, tt * TTW:(tt + 1) * TTW], ALU.mult, ALU.subtract))
            pst_ev[("v", tt)] = v4
            plan.wait_eng("vector", "vector", v4)
            v5 = plan.op("vector", lambda e, h=h, par=par: e.tensor_scalar(
                usb[0:NL, par * TTW:(par + 1) * TTW],
                usb[0:NL, par * TTW:(par + 1) * TTW],
                lncol_t[:, h:h + 1], 0.0,
                ALU.subtract, ALU.min))
            plan.wait_eng("scalar", "vector", v5)
            s_phi = plan.op("scalar", lambda e, tt=tt, par=par: e.activation(
                phiq[:, tt * TTW:(tt + 1) * TTW],
                usb[0:NL, par * TTW:(par + 1) * TTW], AF.Exp))
            pst_ev[("s", par)] = s_phi
        phi_done = plan.cur("d_scalar")
        plan.wait_eng("tensor", "scalar", phi_done)
        def ntail(tz, oc=oc, rb=rb):
            pz = tz % 2
            plan.wait_eng("vector", "tensor", nst[("t5", pz)])
            v6 = plan.op("vector", lambda e, pz=pz: e.tensor_copy(
                nrow2[pz][:, :], psd_s3[pz]))
            plan.wait_eng("tensor", "vector", v6)
            if nst.get(("v8", pz)) is not None:
                plan.wait_eng("tensor", "vector", nst[("v8", pz)])
            t6 = plan.op("tensor", lambda e, pz=pz: e.matmul(
                bc_s3[pz], ones1f[0:1, :], nrow2[pz][0:1, :],
                start=True, stop=True))
            plan.wait_eng("vector", "tensor", t6)
            v8 = plan.op("vector", lambda e, pz=pz: e.tensor_scalar(
                rbb2[pz][:, :], bc_s3[pz], 1e-10, None, ALU.max))
            plan.wait_eng("vector", "vector", v8)
            v9 = plan.op("vector", lambda e, pz=pz: e.reciprocal(
                rbb2[pz][:, :], rbb2[pz][:, :]))
            plan.wait_eng("vector", "vector", v9)
            mv = plan.op("vector", lambda e, oc=oc, rb=rb, tz=tz, pz=pz:
                         e.tensor_mul(
                attn[rb:rb + DH, oc * T + tz * TTW: oc * T + (tz + 1) * TTW],
                pso_s3[pz], rbb2[pz][0:DH, :]))
            nst[("v8", pz)] = v8
            nst[("mul", pz)] = mv

        for tt in range(NT):
            par = tt % 2
            if nst.get(("mul", par)) is not None:
                plan.wait_eng("tensor", "vector", nst[("mul", par)])
            plan.op("tensor", lambda e, h=h, tt=tt, par=par: e.matmul(
                pso_s3[par], ybf[:, h * 65:h * 65 + DH],
                phiq[:, tt * TTW:(tt + 1) * TTW], start=True, stop=True))
            t5 = plan.op("tensor", lambda e, h=h, tt=tt, par=par: e.matmul(
                psd_s3[par], ybf[:, h * 65 + DH:h * 65 + 65],
                phiq[:, tt * TTW:(tt + 1) * TTW], start=True, stop=True))
            nst[("t5", par)] = t5
            if tt > 0:
                ntail(tt - 1)
        ntail(NT - 1)
    plan.join()

    plan.dma("gpsimd", bo_sb[:, :], bo_d)
    plan.wait("gpsimd", "dma", plan.cur("dma"))
    plan.op("gpsimd", lambda e: e.memset(lmt[0:1, 0:1], 0.0))
    plan.join()
    last_mm = {}
    prev_ev = {}
    prev_half = {}
    for sb in range(2):
        for ocx in range(IC):
            buf = wbuf[ocx % 2]
            psem = "pc%d" % (ocx % 2)
            xsem = "xs%d" % (ocx % 2)
            idx = sb * IC + ocx
            if last_mm.get(idx % 2) is not None:
                plan.wait_eng("gpsimd", "tensor", last_mm[idx % 2])
            plan.dma("gpsimd", buf[:, :].rearrange("p (i o) -> p i o", i=IC),
                     Wo_d[:, ocx * 128:(ocx + 1) * 128]
                     .rearrange("(i p) o -> p i o", p=128), psem)
            if (sb * IC + ocx) >= 2:
                plan.wait_eng("sync", "vector", prev_ev[ocx % 2])
            plan.dma("sync", xoc[ocx % 2][:, :],
                     x2_d[ocx * 128:(ocx + 1) * 128,
                          sb * SLAB:(sb + 1) * SLAB], xsem)
            plan.wait("tensor", psem, plan.cur(psem))
            plan.wait("vector", xsem, plan.cur(xsem))
            for tt2 in range(2):
                tt = sb * 2 + tt2
                half = tt2
                if half in prev_half:
                    plan.wait_eng("tensor", "vector", prev_half[half])
                for icx in range(IC):
                    tgt = plan.op("tensor", lambda e, buf=buf, icx=icx,
                                  tt=tt, half=half: e.matmul(
                        ps_bc[:, half * TTW:(half + 1) * TTW],
                        buf[:, icx * 128:(icx + 1) * 128],
                        attn[:, icx * T + tt * TTW: icx * T + (tt + 1) * TTW],
                        start=(icx == 0), stop=(icx == IC - 1)))
                last_mm[idx % 2] = tgt
                plan.wait_eng("vector", "tensor", tgt)
                prev_ev[ocx % 2] = plan.op(
                    "vector", lambda e, ocx=ocx, tt2=tt2, half=half:
                    e.scalar_tensor_tensor(
                        x3slab[:, ocx * SLAB + tt2 * TTW:
                               ocx * SLAB + (tt2 + 1) * TTW],
                        ps_bc[:, half * TTW:(half + 1) * TTW],
                        bo_sb[:, ocx:ocx + 1],
                        xoc[ocx % 2][:, tt2 * TTW:(tt2 + 1) * TTW],
                        ALU.add, ALU.add))
                prev_half[half] = prev_ev[ocx % 2]
        plan.join()
        emit_ln(plan, x3slab, qbf[:, 0:IC * SLAB],
                qbf[:, IC * SLAB:2 * IC * SLAB], C.ones128b, row_sc3,
                bc_sb3, C.ones1f, ps_den, ps_bc, tlen=SLAB)
        plan.join()
        plan.dma("sync",
                 C.x3_i[:, sb * SLAB:(sb + 1) * SLAB]
                 .rearrange("(i p) t -> p i t", p=128),
                 x3slab[:, :].rearrange("p (i t) -> p i t", i=IC), "st")
        if sb == 0:
            plan.wait("vector", "st", plan.cur("st"))
            plan.op("vector", lambda e: e.memset(row_sc3[0:1, 0:1], 0.0))
            plan.join()


SLAB = 1024
NSL = T // SLAB
NJ = INTER // 128
NG = 4
PG = 2
JG = 4
OCG = 4


def emit_l1ffn(plan, C, ph):
    x3_d = C.x3_i
    gw_d = C.gw2
    w13_d = C.w13
    w2_d = C.w2w
    o_xf = C.xf
    eye_f = C.eye_f
    ones1f = C.ones1f
    ones128b = C.ones128b

    xbf = ph.sbuf("xbf", [128, IC * T], BF16)
    act = ph.sbuf("act", [128, NJ * SLAB], BF16)
    out_acc = ph.sbuf("out_acc", [128, IC * SLAB], F32)
    xoc = [ph.sbuf(f"xoc{i}", [128, T], F32) for i in range(2)]
    gw_sb = ph.sbuf("gw_sb", [128, IC * NE], F32)
    logit = ph.sbuf("logit", [NE, T], F32)
    sc_tm = ph.sbuf("sc_tm", [128, NE * NC], F32)
    comb = ph.sbuf("comb", [128, NE * NC], F32)
    m1c = ph.sbuf("m1c", [128, 1], F32)
    m2c = ph.sbuf("m2c", [128, 1], F32)
    g1c = ph.sbuf("g1c", [128, 1], F32)
    g2c = ph.sbuf("g2c", [128, 1], F32)
    gsc = ph.sbuf("gsc", [128, NG], F32)
    geq = ph.sbuf("geq", [128, NG], F32)
    gsel = ph.sbuf("gsel", [128, NG], F32)
    smask = ph.sbuf("smask", [128, NE], F32)
    msked = ph.sbuf("msked", [128, NE], F32)
    meq = ph.sbuf("meq", [128, NE], F32)
    sel = ph.sbuf("sel", [128, NE], F32)
    wsum = ph.sbuf("wsum", [128, 1], F32)
    sgl = ph.sbuf("sgl", [128, 4 * TTW], F32)
    tmp5 = ph.sbuf("tmp5", [128, TTW], F32)
    cbb2 = [ph.sbuf(f"cbb{i}", [128, SLAB], F32) for i in range(2)]
    trcb = ph.sbuf("trcb", [128, 128], F32)
    wbuf = [ph.sbuf(f"wbuf{i}", [128, 2 * IC * 128], BF16)
            for i in range(4)]
    w2buf = [ph.sbuf(f"w2buf{i}", [128, NJ * 128], BF16)
             for i in range(4)]
    row_sc = ph.sbuf("row_sc", [1, 3 * SLAB + 1], F32)
    bc_sb = ph.sbuf("bc_sb", [128, 2 * SLAB], F32)

    ps_g = ph.psum("ps_g", [128, 2048], F32)
    ps_t = ph.psum("ps_t", [128, 512], F32)
    ps_o = ph.psum("ps_o", [128, 512], F32)
    ps_h = ph.psum("ps_h", [128, 1024], F32)
    gu_slots = [
        [ps_g[:, 0:512], ps_g[:, 512:1024],
         ps_g[:, 1024:1536], ps_g[:, 1536:2048]],
        [ps_t[:, 0:512], ps_o[:, 0:512],
         ps_h[:, 0:512], ps_h[:, 512:1024]],
    ]
    w2_slots = [ps_g[:, 0:1024], ps_g[:, 1024:2048]]

    plan._wj = {}
    plan._wsem2 = {}
    plan._wiss2 = 0
    plan._wuse2 = 0
    plan._w2j = {}
    plan._w2sem = {}
    plan._w2iss = 0
    plan._w2use = 0
    plan._actv = {}
    plan._oev = {}
    plan._guv = {}
    plan._gus = {}
    plan._sglv = {}
    plan._tmpv = None

    def issue_w13(e2, j2):
        key = plan._wiss2 % 4
        plan._wiss2 += 1
        psem = "pu%d" % key
        if plan._wj.get(key) is not None:
            plan.wait_eng("gpsimd", "tensor", plan._wj[key])
        plan.dma("gpsimd",
                 wbuf[key][:, 0:1024].rearrange("p (i o) -> p i o", i=IC),
                 w13_d[e2, :, j2 * 128:(j2 + 1) * 128]
                 .rearrange("(i p) o -> p i o", p=128), psem)
        plan._wsem2[key] = plan.dma(
            "gpsimd",
            wbuf[key][:, 1024:2048].rearrange("p (i o) -> p i o", i=IC),
            w13_d[e2, :, INTER + j2 * 128: INTER + (j2 + 1) * 128]
            .rearrange("(i p) o -> p i o", p=128), psem)

    def issue_w2(e2, oc2):
        key = plan._w2iss % 4
        plan._w2iss += 1
        w2s = "pv%d" % key
        if plan._w2j.get(key) is not None:
            plan.wait_eng("gpsimd", "tensor", plan._w2j[key])
        plan._w2sem[key] = plan.dma(
            "gpsimd", w2buf[key][:, :].rearrange("p (j o) -> p j o", j=NJ),
            w2_d[e2, :, oc2 * 128:(oc2 + 1) * 128]
            .rearrange("(j p) o -> p j o", p=128), w2s)
    plan.dma("gpsimd", gw_sb[:, :].rearrange("p (i en) -> p i en", i=IC),
             gw_d.rearrange("(i p) en -> p i en", p=128))
    plan.wait("gpsimd", "dma", plan.cur("dma"))
    plan.op("gpsimd", lambda e: e.memset(row_sc[0:1, 0:1], 0.0))
    plan.join()

    prev_cast = {}
    prev_gmm = {}
    for icx in range(IC):
        xsem = "xs%d" % (icx % 2)
        if icx >= 2:
            plan.wait_eng("sync", "vector", prev_cast[icx - 2])
            plan.wait_eng("sync", "tensor", prev_gmm[icx - 2])
        plan.dma("sync", xoc[icx % 2][:, :],
                 x3_d[icx * 128:(icx + 1) * 128, :], xsem)
        plan.wait("tensor", xsem, plan.cur(xsem))
        plan.wait("vector", xsem, plan.cur(xsem))
        for tt in range(NT):
            tgt = plan.op("tensor", lambda e, icx=icx, tt=tt: e.matmul(
                ps_g[0:NE, tt * TTW:(tt + 1) * TTW],
                gw_sb[:, icx * NE:(icx + 1) * NE],
                xoc[icx % 2][:, tt * TTW:(tt + 1) * TTW],
                start=(icx == 0), stop=(icx == IC - 1)))
        prev_gmm[icx] = tgt
        prev_cast[icx] = plan.op("vector", lambda e, icx=icx: e.tensor_copy(
            xbf[:, icx * T:(icx + 1) * T], xoc[icx % 2][:, :]))
    gmm = plan.cur("d_tensor")
    plan.wait_eng("vector", "tensor", gmm)
    vl = plan.op("vector", lambda e: e.tensor_copy(logit[:, :],
                                                   ps_g[0:NE, 0:T]))
    plan.join()

    for tc in range(NC):
        t1 = plan.op("tensor", lambda e, tc=tc: e.transpose(
            ps_t[0:128, 0:NE], logit[:, tc * 128:(tc + 1) * 128],
            eye_f[0:NE, 0:NE]))
        plan.wait_eng("vector", "tensor", t1)
        va = plan.op("vector", lambda e, tc=tc: e.tensor_copy(
            sc_tm[:, tc * NE:(tc + 1) * NE], ps_t[0:128, 0:NE]))
        plan.wait_eng("tensor", "vector", va)
        plan.wait_eng("vector", "vector", va)
        sl = slice(tc * NE, (tc + 1) * NE)
        vb = plan.op("vector", lambda e, sl=sl: e.reduce_max(
            m1c[:, :], sc_tm[:, sl], axis=AX.X))
        plan.wait_eng("vector", "vector", vb)
        vc = plan.op("vector", lambda e, sl=sl: e.tensor_scalar(
            sc_tm[:, sl], sc_tm[:, sl], m1c[:, 0:1], None, ALU.subtract))
        plan.wait_eng("scalar", "vector", vc)
        s1 = plan.op("scalar", lambda e, sl=sl: e.activation(
            sc_tm[:, sl], sc_tm[:, sl], AF.Exp))
        plan.wait_eng("vector", "scalar", s1)
        vd = plan.op("vector", lambda e, sl=sl: e.reduce_sum(
            m1c[:, :], sc_tm[:, sl], axis=AX.X))
        plan.wait_eng("vector", "vector", vd)
        ve = plan.op("vector", lambda e: e.reciprocal(m1c[:, :], m1c[:, :]))
        plan.wait_eng("vector", "vector", ve)
        vf = plan.op("vector", lambda e, sl=sl: e.tensor_scalar(
            sc_tm[:, sl], sc_tm[:, sl], m1c[:, 0:1], None, ALU.mult))
        plan.wait_eng("vector", "vector", vf)
        vg = plan.op("vector", lambda e, sl=sl: e.reduce_max(
            gsc[:, :], sc_tm[:, sl].rearrange("p (g k) -> p g k", g=NG),
            axis=AX.X))
        plan.wait_eng("vector", "vector", vg)
        vh = plan.op("vector", lambda e: e.reduce_max(g1c[:, :], gsc[:, :],
                                                      axis=AX.X))
        plan.wait_eng("vector", "vector", vh)
        vi = plan.op("vector", lambda e: e.tensor_scalar(
            geq[:, :], gsc[:, :], g1c[:, 0:1], -1e30, ALU.is_equal,
            ALU.mult))
        plan.wait_eng("vector", "vector", vi)
        vj = plan.op("vector", lambda e: e.tensor_add(geq[:, :], geq[:, :],
                                                      gsc[:, :]))
        plan.wait_eng("vector", "vector", vj)
        vk = plan.op("vector", lambda e: e.reduce_max(g2c[:, :], geq[:, :],
                                                      axis=AX.X))
        plan.wait_eng("vector", "vector", vk)
        vm = plan.op("vector", lambda e: e.tensor_scalar(
            gsel[:, :], gsc[:, :], g2c[:, 0:1], None, ALU.is_ge))
        plan.wait_eng("vector", "vector", vm)
        vn = plan.op("vector", lambda e: e.tensor_copy(
            smask[:, :].rearrange("p (g k) -> p g k", g=NG),
            gsel[:, :].rearrange("p (g o) -> p g o", o=1)
            .broadcast_to((128, NG, PG))))
        plan.wait_eng("vector", "vector", vn)
        vo = plan.op("vector", lambda e, sl=sl: e.tensor_mul(
            msked[:, :], sc_tm[:, sl], smask[:, :]))
        plan.wait_eng("vector", "vector", vo)
        vp = plan.op("vector", lambda e: e.reduce_max(m1c[:, :], msked[:, :],
                                                      axis=AX.X))
        plan.wait_eng("vector", "vector", vp)
        vq = plan.op("vector", lambda e: e.tensor_scalar(
            meq[:, :], msked[:, :], m1c[:, 0:1], -1e30, ALU.is_equal,
            ALU.mult))
        plan.wait_eng("vector", "vector", vq)
        vr = plan.op("vector", lambda e: e.tensor_add(meq[:, :], meq[:, :],
                                                      msked[:, :]))
        plan.wait_eng("vector", "vector", vr)
        vs = plan.op("vector", lambda e: e.reduce_max(m2c[:, :], meq[:, :],
                                                      axis=AX.X))
        plan.wait_eng("vector", "vector", vs)
        vt_ = plan.op("vector", lambda e: e.tensor_scalar(
            sel[:, :], msked[:, :], m2c[:, 0:1], None, ALU.is_ge))
        plan.wait_eng("vector", "vector", vt_)
        vu = plan.op("vector", lambda e: e.tensor_mul(sel[:, :], sel[:, :],
                                                      msked[:, :]))
        plan.wait_eng("vector", "vector", vu)
        vv = plan.op("vector", lambda e: e.reduce_sum(wsum[:, :], sel[:, :],
                                                      axis=AX.X))
        plan.wait_eng("vector", "vector", vv)
        vw = plan.op("vector", lambda e: e.reciprocal(wsum[:, :],
                                                      wsum[:, :]))
        plan.wait_eng("vector", "vector", vw)
        vx = plan.op("vector", lambda e, tc=tc: e.tensor_scalar(
            comb[:, tc: NE * NC: NC], sel[:, :], wsum[:, 0:1], None,
            ALU.mult))
    plan.join()

    tcb = plan.op("tensor", lambda e: e.transpose(
        ps_t[0:128, 0:128], comb[:, :], eye_f[:, :]))
    plan.wait_eng("vector", "tensor", tcb)
    vcb = plan.op("vector", lambda e: e.tensor_copy(
        trcb[:, :], ps_t[0:128, 0:128]))
    plan.wait_eng("sync", "vector", vcb)
    plan.wait_eng("tensor", "vector", vcb)
    plan.dma("sync",
             C.cb_d[0:NE, :].rearrange("h (t i) -> (h t) i", t=NC),
             trcb[:, :], "wv")
    plan.wait("gpsimd", "wv", plan.cur("wv"))
    cbdma = {0: plan.dma("gpsimd", cbb2[0][:, :],
                         C.cb_d[0:1, 0:SLAB].broadcast_to((128, SLAB)),
                         "nbq")}

    for sb in range(NSL):
        sl_tt = [sb * 2, sb * 2 + 1]
        rhs0 = sb * SLAB
        first = True
        issue_w13(0, 0)
        issue_w13(0, 1)
        issue_w2(0, 0)
        issue_w2(0, 1)
        for e_i in range(NE):
            kse = sb * NE + e_i
            cbb = cbb2[kse % 2]
            dv = plan.cur("d_vector")
            plan.wait_eng("tensor", "vector", dv)
            if kse + 1 < 2 * NE:
                sb2, e2 = divmod(kse + 1, NE)
                plan.wait_eng("gpsimd", "vector", dv)
                cbdma[kse + 1] = plan.dma(
                    "gpsimd", cbb2[(kse + 1) % 2][:, :],
                    C.cb_d[e2:e2 + 1, sb2 * SLAB:(sb2 + 1) * SLAB]
                    .broadcast_to((128, SLAB)), "nbq")
            plan.wait("vector", "nbq", cbdma[kse])
            for j in range(NJ):
                k = e_i * NJ + j
                if k + 2 < NE * NJ:
                    e2, j2 = divmod(k + 2, NJ)
                    issue_w13(e2, j2)
                key = plan._wuse2 % 4
                plan._wuse2 += 1
                buf = wbuf[key]
                plan.wait("tensor", "pu%d" % key, plan._wsem2[key])
                par = j % 2
                slots = gu_slots[par]
                if plan._guv.get(par) is not None:
                    plan.wait_eng("tensor", "vector", plan._guv[par])
                    plan.wait_eng("tensor", "scalar", plan._gus[par])
                for icx in range(IC):
                    go = icx * 128
                    for qq in range(2):
                        tt = sl_tt[qq]
                        plan.op("tensor", lambda e, buf=buf, go=go,
                                icx=icx, tt=tt, qq=qq, slots=slots:
                                e.matmul(
                            slots[qq],
                            buf[:, go:go + 128],
                            xbf[:, icx * T + tt * TTW:
                                icx * T + (tt + 1) * TTW],
                            start=(icx == 0), stop=(icx == IC - 1),
                            skip_group_check=True))
                for icx in range(IC):
                    uo = 1024 + icx * 128
                    for qq in range(2):
                        tt = sl_tt[qq]
                        tgt = plan.op("tensor", lambda e, buf=buf, uo=uo,
                                      icx=icx, tt=tt, qq=qq, slots=slots:
                                      e.matmul(
                            slots[2 + qq],
                            buf[:, uo:uo + 128],
                            xbf[:, icx * T + tt * TTW:
                                icx * T + (tt + 1) * TTW],
                            start=(icx == 0), stop=(icx == IC - 1),
                            skip_group_check=True))
                plan._wj[key] = tgt
                gu_done = plan.cur("d_tensor")
                for qq in range(2):
                    plan.wait_eng("scalar", "tensor", gu_done)
                    if plan._sglv.get((par, qq)) is not None:
                        plan.wait_eng("scalar", "vector",
                                      plan._sglv[(par, qq)])
                    qs = slice((par * 2 + qq) * TTW,
                               (par * 2 + qq + 1) * TTW)
                    s1 = plan.op("scalar", lambda e, qs=qs, qq=qq,
                                 slots=slots: e.activation(
                        sgl[:, qs], slots[qq], AF.Silu))
                    plan.wait_eng("vector", "scalar", s1)
                    v1 = plan.op("vector", lambda e, j=j, qq=qq, qs=qs,
                                 slots=slots: e.tensor_mul(
                        act[:, j * SLAB + qq * TTW:
                            j * SLAB + (qq + 1) * TTW],
                        sgl[:, qs], slots[2 + qq]))
                    plan._sglv[(par, qq)] = v1
                    plan._gus[par] = s1
                plan._guv[par] = v1
            act_done = plan.cur("d_vector")
            plan.wait_eng("tensor", "vector", act_done)
            plan.wait_eng("tensor", "scalar", plan.cur("d_scalar"))
            for oc in range(IC):
                kw = e_i * IC + oc
                if kw + 2 < NE * IC:
                    e2w, oc2 = divmod(kw + 2, IC)
                    issue_w2(e2w, oc2)
                key2 = plan._w2use % 4
                plan._w2use += 1
                b2 = w2buf[key2]
                plan.wait("tensor", "pv%d" % key2, plan._w2sem[key2])
                if True:
                    par = oc % 2
                    pso = w2_slots[par]
                    if plan._oev.get(par) is not None:
                        plan.wait_eng("tensor", "vector", plan._oev[par])
                    for j in range(NJ):
                        lo = j * 128
                        for qq in range(2):
                            tgt = plan.op("tensor", lambda e, b2=b2, lo=lo,
                                          j=j, qq=qq, pso=pso: e.matmul(
                                pso[:, qq * TTW:(qq + 1) * TTW],
                                b2[:, lo:lo + 128],
                                act[:, j * SLAB + qq * TTW:
                                    j * SLAB + (qq + 1) * TTW],
                                start=(j == 0), stop=(j == NJ - 1),
                                skip_group_check=True))
                    plan._w2j[key2] = tgt
                    plan.wait_eng("vector", "tensor", tgt)
                    for qq in range(2):
                        if first:
                            ov = plan.op("vector", lambda e, oc=oc, qq=qq,
                                         pso=pso, cbb=cbb: e.tensor_mul(
                                out_acc[:, oc * SLAB + qq * TTW:
                                        oc * SLAB + (qq + 1) * TTW],
                                pso[:, qq * TTW:(qq + 1) * TTW],
                                cbb[:, qq * TTW:(qq + 1) * TTW]))
                        else:
                            if plan._tmpv is not None:
                                plan.wait_eng("vector", "vector", plan._tmpv)
                            vm = plan.op("vector", lambda e, qq=qq, pso=pso,
                                         cbb=cbb: e.tensor_mul(
                                tmp5[:, :],
                                pso[:, qq * TTW:(qq + 1) * TTW],
                                cbb[:, qq * TTW:(qq + 1) * TTW]))
                            plan.wait_eng("vector", "vector", vm)
                            ov = plan.op("vector", lambda e, oc=oc, qq=qq:
                                         e.tensor_add(
                                out_acc[:, oc * SLAB + qq * TTW:
                                        oc * SLAB + (qq + 1) * TTW],
                                out_acc[:, oc * SLAB + qq * TTW:
                                        oc * SLAB + (qq + 1) * TTW],
                                tmp5[:, :]))
                            plan._tmpv = ov
                    plan._oev[par] = ov
            first = False
        plan.wait_eng("gpsimd", "vector", plan.cur("d_vector"))
        tgt = plan.bump("res", 16)

        def _acc(e, sb=sb, s=plan.sems["res"]):
            e.dma_start(out=out_acc[:, :].rearrange("p (i t) -> p i t", i=IC),
                        in_=x3_d[:, sb * SLAB:(sb + 1) * SLAB]
                        .rearrange("(i p) t -> p i t", p=128),
                        accum_op=ALU.add).then_inc(s, 16)
        plan.add("gpsimd", _acc)
        plan.wait("gpsimd", "res", tgt)
        plan.op("gpsimd", lambda e: e.memset(trcb[0:1, 0:1], 0.0))
        plan.join()
        ln_xbf = act[:, 0:IC * SLAB]
        ln_sqb = act[:, IC * SLAB:2 * IC * SLAB]
        emit_ln(plan, out_acc, ln_xbf, ln_sqb, ones128b, row_sc, bc_sb,
                ones1f, ps_t, ps_g, tlen=SLAB)
        plan.join()
        plan.dma("sync",
                 o_xf[:, sb * SLAB:(sb + 1) * SLAB]
                 .rearrange("(i p) t -> p i t", p=128),
                 out_acc[:, :].rearrange("p (i t) -> p i t", i=IC),
                 "sto%d" % sb)
        if sb == 0:
            plan.wait("vector", "sto0", plan.cur("sto0"))
            plan.op("vector", lambda e: e.memset(m1c[:, :], 0.0))
            plan.join()


def build_fused():
    nc = bass.Bass("TRN2", target_bir_lowering=False)
    C = SimpleNamespace()

    def ein(name, shape, dtype=F32):
        return nc.dram_tensor(name, shape, dtype, kind="ExternalInput").ap()

    C.xT = ein("xT", [D, T])
    C.Wq = ein("Wq", [D, D])
    C.Wk = ein("Wk", [D, D])
    C.Wv = ein("Wv", [D, D])
    C.Wo = ein("Wo", [D, D])
    C.bo = ein("bo", [128, IC])
    C.PT = ein("PT", [DH, M])
    C.gw = ein("gw", [D, NE])
    C.gb = ein("gb", [NE, 1])
    C.w1 = ein("w1", [NE, D, D])
    C.w2 = ein("w2", [NE, D, D])
    C.w3 = ein("w3", [NE, D, D])
    C.b1 = ein("b1", [128, NE * IC])
    C.b2 = ein("b2", [128, NE * IC])
    C.b3 = ein("b3", [128, NE * IC])
    C.Wq2 = ein("Wq2", [D, D])
    C.Wk2 = ein("Wk2", [D, D])
    C.Wv2 = ein("Wv2", [D, D])
    C.bq2 = ein("bq2", [128, IC])
    C.bk2 = ein("bk2", [128, IC])
    C.bv2 = ein("bv2", [1, D])
    C.lidx = ein("lidx", [NL, 1], I32)
    C.t0c = ein("t0c", [NL, 1], I32)
    C.Wo2 = ein("Wo2", [D, D])
    C.bo2 = ein("bo2", [128, IC])
    C.temp = ein("temp", [1, 1])
    C.gw2 = ein("gw2", [D, NE])
    C.w13 = ein("w13", [NE, D, 2 * INTER])
    C.w2w = ein("w2w", [NE, INTER, D])
    C.xf = nc.dram_tensor("xf", [D, T], F32, kind="ExternalOutput").ap()

    def din(name, shape, dtype=F32):
        return nc.dram_tensor(name, shape, dtype).ap()

    C.qbf_i = din("qbf_i", [128, IC * T], BF16)
    C.kbf_i = din("kbf_i", [128, IC * T], BF16)
    C.vt_i = din("vt_i", [128, NC * D], BF16)
    C.negbq_i = din("negbq_i", [128, H * NC])
    C.diagk_i = din("diagk_i", [128, H * NC])
    C.x1p_i = din("x1p_i", [D, T])
    C.x1_i = din("x1_i", [D, T])
    C.logits_i = din("logits_i", [NE, T])
    C.ind_i = din("ind_i", [NE, T], BF16)
    C.x2_i = din("x2_i", [D, T])
    C.qbf2_i = din("qbf2_i", [128, IC * T], BF16)
    C.kbf2_i = din("kbf2_i", [128, IC * T], BF16)
    C.vt2_i = din("vt2_i", [128, NC * D], BF16)
    C.qn_i = din("qn_i", [128, H * NC])
    C.kn_i = din("kn_i", [128, H * NC])
    C.ktm_i = din("ktm_i", [T, D], BF16)
    C.winv_i = din("winv_i", [NL, H * DH])
    C.x3_i = din("x3_i", [D, T])
    C.nb_d = din("nb_d", [H, T])
    C.qn_d = din("qn_d", [H, T])
    C.wt_d = din("wt_d", [NE, T])
    C.cb_d = din("cb_d", [NE, T])

    C.kmax_w = nc.dram_tensor("kmax_w", [1, 8], F32)
    C.kmax_r = cc_dram(nc, "kmax_r", [1, 8], F32, "AllReduce", G8)
    C.ctx_w = nc.dram_tensor("ctx_w", [128, H * 2 * 65], F32)
    C.ctx_r = cc_dram(nc, "ctx_r", [128, H * 2 * 65], F32, "AllReduce", G4)
    C.lmax_w = nc.dram_tensor("lmax_w", [NE, 1], F32)
    C.gmax_r = cc_dram(nc, "gmax_r", [NE, 1], F32, "AllReduce", G4)
    C.t_w = nc.dram_tensor("t_w", [NE, 1], F32)
    C.rden_r = cc_dram(nc, "rden_r", [NE, 1], F32, "AllReduce", G4)
    C.v_w = nc.dram_tensor("v_w", [NE * IC, 128], F32)
    C.v_r = cc_dram(nc, "v_r", [NE * IC, 128], F32, "AllReduce", G8)
    C.lm_w = nc.dram_tensor("lm_w", [NL, D], BF16)
    C.lm_r = cc_dram(nc, "lm_r", [NL, D], BF16, "AllReduce", G4)
    C.z_w = nc.dram_tensor("z_w", [NL, H * 65], F32)
    C.z_r = cc_dram(nc, "z_r", [NL, H * 65], F32, "AllReduce", G4)

    C.eye_f = nc.alloc_sbuf_tensor("eye_f", [128, 128], F32)
    C.eye_b = nc.alloc_sbuf_tensor("eye_b", [128, 128], BF16)
    C.ones1f = nc.alloc_sbuf_tensor("ones1f", [1, 128], F32)
    C.ones128b = nc.alloc_sbuf_tensor("ones128b", [128, 1], BF16)

    plan = Plan(nc)
    g1 = plan.op("gpsimd", lambda e: e.iota(
        C.eye_f[:, :], pattern=[[1, 128]], base=0, channel_multiplier=-1,
        allow_small_or_imprecise_dtypes=True))
    plan.wait_eng("vector", "gpsimd", g1)
    veq = plan.op("vector", lambda e: e.tensor_scalar(
        C.eye_f[:, :], C.eye_f[:, :], 0.0, None, ALU.is_equal))
    plan.wait_eng("vector", "vector", veq)
    plan.op("vector", lambda e: e.tensor_scalar(
        C.eye_b[:, :], C.eye_f[:, :], 0.5, None, ALU.is_gt))
    plan.op("vector", lambda e: e.memset(C.ones1f[:, :], 1.0))
    plan.op("vector", lambda e: e.memset(C.ones128b[:, :], 1.0))
    plan.join()

    def phase(name, emitter, *args):
        ph = Phase(nc, name + "_")
        emitter(plan, C, ph, *args)
        phase_barrier(plan)
        ph.close()

    phase("a1", emit_l0attn_p1)
    emit_collective(plan, "cc_kmax", "AllReduce", ALU.max, G8,
                    C.kmax_w, C.kmax_r)
    phase("a2", emit_l0attn_p2)
    emit_collective(plan, "cc_ctx", "AllReduce", ALU.add, G4,
                    C.ctx_w, C.ctx_r)
    phase("a3", emit_l0attn_p3)
    phase("f2", emit_l0ffn_p2)
    emit_collective(plan, "cc_v", "AllReduce", ALU.add, G8,
                    C.v_w, C.v_r)
    phase("f3", emit_l0ffn_p3)
    phase("b1", emit_l1attn_p1)
    emit_collective(plan, "cc_lm", "AllReduce", ALU.add, G4,
                    C.lm_w, C.lm_r)
    phase("b2", emit_l1attn_p2)
    emit_collective(plan, "cc_z", "AllReduce", ALU.add, G4,
                    C.z_w, C.z_r)
    phase("b3", emit_l1attn_p3)
    phase("m1", emit_l1ffn)
    finish(plan)
    return nc


def _shard_tokens(x):
    out = []
    for c in range(8):
        b, t0 = c // 4, (c % 4) * T
        out.append(np.ascontiguousarray(x[b, t0:t0 + T, :].T))
    return out


def _bias_cols(b):
    return np.ascontiguousarray(b.reshape(IC, 128).T).astype(np.float32)


def _bias_cols_e(b):
    return np.ascontiguousarray(
        b.reshape(NE, IC, 128).transpose(2, 0, 1).reshape(128, NE * IC)
    ).astype(np.float32)


def kernel(**inp):
    f32 = np.float32
    x = np.asarray(inp['L0_attn_x'], f32)
    xs = _shard_tokens(x)
    PT = np.ascontiguousarray(np.asarray(inp['L0_attn_P'], f32).T)
    Wq0 = np.asarray(inp['L0_attn_Wq'], f32)
    Wk0 = np.asarray(inp['L0_attn_Wk'], f32)
    Wv0 = np.asarray(inp['L0_attn_Wv'], f32)
    Wo0 = np.asarray(inp['L0_attn_Wo'], f32)
    bo0 = _bias_cols(np.asarray(inp['L0_attn_bo'], f32))
    gw0 = np.asarray(inp['L0_ffn_gate_w'], f32)
    gb0 = np.ascontiguousarray(np.asarray(inp['L0_ffn_gate_b'],
                                          f32).reshape(NE, 1))
    w1 = np.asarray(inp['L0_ffn_w1'], f32)
    w2 = np.asarray(inp['L0_ffn_w2'], f32)
    w3 = np.asarray(inp['L0_ffn_w3'], f32)
    b1 = _bias_cols_e(np.asarray(inp['L0_ffn_b1'], f32))
    b2 = _bias_cols_e(np.asarray(inp['L0_ffn_b2'], f32))
    b3 = _bias_cols_e(np.asarray(inp['L0_ffn_b3'], f32))
    Wq1T = np.ascontiguousarray(np.asarray(inp['L1_attn_Wq'], f32).T)
    Wk1T = np.ascontiguousarray(np.asarray(inp['L1_attn_Wk'], f32).T)
    Wv1T = np.ascontiguousarray(np.asarray(inp['L1_attn_Wv'], f32).T)
    Wo1T = np.ascontiguousarray(np.asarray(inp['L1_attn_Wo'], f32).T)
    bq1 = _bias_cols(np.asarray(inp['L1_attn_bq'], f32))
    bk1 = _bias_cols(np.asarray(inp['L1_attn_bk'], f32))
    bv1 = np.ascontiguousarray(np.asarray(inp['L1_attn_bv'],
                                          f32).reshape(1, D))
    bo1 = _bias_cols(np.asarray(inp['L1_attn_bo'], f32))
    temp = np.asarray(inp['L1_attn_temperature'], f32).reshape(1, 1)
    lidx = np.ascontiguousarray(np.asarray(inp['L1_attn_landmark_idx'],
                                           np.int32).reshape(NL, 1))
    gw1T = np.ascontiguousarray(np.asarray(inp['L1_ffn_gate_w'], f32).T)
    w13T = np.ascontiguousarray(
        np.asarray(inp['L1_ffn_w13_weight'], f32).transpose(0, 2, 1))
    w2T = np.ascontiguousarray(
        np.asarray(inp['L1_ffn_w2_weight'], f32).transpose(0, 2, 1))

    cores = list(range(8))
    t0c = [np.full((NL, 1), (c % 4) * T, np.int32) for c in cores]

    nc = build_fused()
    maps = [dict(
        xT=xs[c], Wq=Wq0, Wk=Wk0, Wv=Wv0, Wo=Wo0, bo=bo0, PT=PT,
        gw=gw0, gb=gb0, w1=w1, w2=w2, w3=w3, b1=b1, b2=b2, b3=b3,
        Wq2=Wq1T, Wk2=Wk1T, Wv2=Wv1T, bq2=bq1, bk2=bk1, bv2=bv1,
        lidx=lidx, t0c=t0c[c], Wo2=Wo1T, bo2=bo1, temp=temp,
        gw2=gw1T, w13=w13T, w2w=w2T) for c in cores]
    rf = run_bass_kernel_spmd(nc, maps, core_ids=cores).results
    out = np.zeros((2, 8192, D), f32)
    for c in cores:
        b, t0 = c // 4, (c % 4) * T
        out[b, t0:t0 + T, :] = np.asarray(rf[c]['xf'], f32).T
    return out


if __name__ == "__main__":
    build_fused()
    print("build ok")


# version 24 (final)
# speedup vs baseline: 1.0283x; 1.0031x over previous
from contextlib import ExitStack
from types import SimpleNamespace

import numpy as np
import ml_dtypes
import concourse.bass as bass
import concourse.mybir as mybir
from concourse.bass_utils import run_bass_kernel_spmd
from concourse.replica_groups import maybe_share_collective_output_space

G4 = [[0, 1, 2, 3], [4, 5, 6, 7]]
G8 = [[0, 1, 2, 3, 4, 5, 6, 7]]

F32 = mybir.dt.float32
BF16 = mybir.dt.bfloat16
I32 = mybir.dt.int32
AF = mybir.ActivationFunctionType
ALU = mybir.AluOpType
AX = mybir.AxisListType

BFNP = ml_dtypes.bfloat16

D = 1024
T = 2048
NT = 4
TTW = 512
NC = 16
IC = 8
H = 16
DH = 64
M = 256
DN = float(DH) ** -0.25
EPS_PERF = 1e-4
LN_EPS = 1e-5
NE = 8
NL = 64
INTER = 2048


class Plan:
    ENGINES = ("sync", "gpsimd", "vector", "scalar", "tensor")

    def __init__(self, nc):
        self.nc = nc
        self.ops = {e: [] for e in self.ENGINES}
        self.sems = {}
        self.counts = {}
        for e in self.ENGINES:
            self.sem("d_" + e)
        for name in ("dma", "pc0", "pc1", "wv", "st", "xs0", "xs1", "sto0", "sto1", "kt0", "kt1", "gth", "res", "wt0", "wt1"):
            self.sem(name)

    def sem(self, name):
        if name not in self.sems:
            self.sems[name] = self.nc.alloc_semaphore(name)
            self.counts[name] = 0
        return self.sems[name]

    def bump(self, name, n=1):
        self.sem(name)
        self.counts[name] += n
        return self.counts[name]

    def cur(self, name):
        self.sem(name)
        return self.counts[name]

    def add(self, engine, fn):
        self.ops[engine].append(fn)

    def op(self, engine, fn):
        name = "d_" + engine
        tgt = self.bump(name)
        s = self.sems[name]

        def _f(e, fn=fn, s=s):
            fn(e).then_inc(s, 1)
        self.ops[engine].append(_f)
        return tgt

    def wait(self, engine, sem_name, val):
        s = self.sem(sem_name)
        self.add(engine, lambda e, s=s, v=val: e.wait_ge(s, v))

    def wait_eng(self, engine, producer_engine, val):
        self.wait(engine, "d_" + producer_engine, val)

    def dma(self, engine, out_ap, in_ap, sem_name="dma"):
        tgt = self.bump(sem_name, 16)
        s = self.sems[sem_name]

        def _f(e, o=out_ap, i=in_ap, s=s):
            e.dma_start(out=o, in_=i).then_inc(s, 16)
        self.add(engine, _f)
        return tgt

    def join(self, engines=None):
        engines = engines or self.ENGINES
        snap = {en: self.cur("d_" + en) for en in engines}
        for en in engines:
            for other in engines:
                if other != en and snap[other] > 0:
                    self.wait(en, "d_" + other, snap[other])

    def emit(self):
        nc = self.nc
        with nc.Block() as block:
            for eng_name in self.ENGINES:
                fns = self.ops[eng_name]
                if not fns:
                    continue
                getattr(block, eng_name)(self._mk(fns))

    @staticmethod
    def _mk(fns):
        def _run(eng):
            for f in fns:
                f(eng)
        return _run


def finish(plan):
    for name in list(plan.counts):
        if name.startswith("st") and plan.counts[name] > 0:
            plan.wait("gpsimd", name, plan.cur(name))
    plan.emit()


class Phase:

    def __init__(self, nc, prefix):
        self.nc = nc
        self.prefix = prefix
        self.stack = ExitStack()

    def sbuf(self, name, shape, dtype):
        return self.stack.enter_context(
            self.nc.sbuf_tensor(self.prefix + name, shape, dtype))

    def psum(self, name, shape, dtype=F32):
        return self.stack.enter_context(
            self.nc.psum_tensor(self.prefix + name, shape, dtype))

    def close(self):
        self.stack.close()


def phase_barrier(plan):
    for name in list(plan.counts):
        if name.startswith("d_") or name.startswith("cc"):
            continue
        if plan.counts[name] > 0:
            plan.wait("gpsimd", name, plan.counts[name])
    plan.op("gpsimd", lambda e: e.nop())
    plan.join()


def emit_collective(plan, name, kind, aluop, groups, in_h, out_h):
    s = plan.sem(name)
    plan.counts[name] = 1

    def _f(e, in_h=in_h, out_h=out_h, s=s):
        e.collective_compute(kind, aluop, replica_groups=groups,
                             ins=[in_h.ap().opt()],
                             outs=[out_h.ap().opt()]).then_inc(s)
    plan.add("gpsimd", _f)


def cc_dram(nc, name, shape, dtype, kind, groups):
    sp = maybe_share_collective_output_space(kind, groups, prefer_local=True)
    return nc.dram_tensor(name, shape, dtype, addr_space=sp)


def emit_ln(plan, x_sb, xbf_sb, sq_sb, ones128b, row_sc, bc_sb, ones1f,
            ps_den, ps_bc, tlen=T):
    ntt = tlen // TTW
    """LayerNorm over feature dim of x_sb (fp32 [128, IC*T]) in place.
    xbf_sb/sq_sb: bf16 scratch [128, IC*T]. row_sc: [1, 3*T+1] f32
    (A=[0:T] sum->meansq->sqrt, B=[T:2T] sumsq->var->rstd, C=[2T:3T] mean).
    bc_sb: [128, 2*T] f32 (mean, rstd broadcasts). ps_bc: psum [128, 2048].
    Caller must join() before; ends unjoined."""
    plan.op("vector", lambda e: e.memset(row_sc[0:1, 3 * tlen:3 * tlen + 1],
                                         LN_EPS))
    for icx in range(IC):
        sl = slice(icx * tlen, (icx + 1) * tlen)
        plan.op("vector", lambda e, sl=sl: e.tensor_copy(xbf_sb[:, sl],
                                                         x_sb[:, sl]))
        plan.op("scalar", lambda e, sl=sl: e.activation(sq_sb[:, sl],
                                                        x_sb[:, sl],
                                                        AF.Square))
    vq = plan.cur("d_vector")
    sq = plan.cur("d_scalar")
    plan.wait_eng("tensor", "vector", vq)
    plan.wait_eng("tensor", "scalar", sq)
    prev_cp = None
    for tt in range(ntt):
        if prev_cp is not None:
            plan.wait_eng("tensor", "vector", prev_cp)
        for which, srcb in ((0, xbf_sb), (1, sq_sb)):
            for icx in range(IC):
                plan.op("tensor", lambda e, which=which, srcb=srcb, icx=icx,
                        tt=tt: e.matmul(
                    ps_bc[0:1, which * TTW:(which + 1) * TTW],
                    ones128b[:, :],
                    srcb[:, icx * tlen + tt * TTW: icx * tlen + (tt + 1) * TTW],
                    start=(icx == 0), stop=(icx == IC - 1)))
        tmm = plan.cur("d_tensor")
        plan.wait_eng("vector", "tensor", tmm)
        plan.op("vector", lambda e, tt=tt: e.tensor_copy(
            row_sc[0:1, tt * TTW:(tt + 1) * TTW], ps_bc[0:1, 0:TTW]))
        prev_cp = plan.op("vector", lambda e, tt=tt: e.tensor_copy(
            row_sc[0:1, tlen + tt * TTW: tlen + (tt + 1) * TTW],
            ps_bc[0:1, TTW:2 * TTW]))
    plan.wait_eng("vector", "vector", prev_cp)
    v1 = plan.op("vector", lambda e: e.tensor_scalar_mul(
        row_sc[0:1, 2 * tlen:3 * tlen], row_sc[0:1, 0:tlen], 1.0 / D))
    plan.wait_eng("vector", "vector", v1)
    v2 = plan.op("vector", lambda e: e.tensor_mul(
        row_sc[0:1, 0:tlen], row_sc[0:1, 2 * tlen:3 * tlen],
        row_sc[0:1, 2 * tlen:3 * tlen]))
    plan.wait_eng("vector", "vector", v2)
    v3 = plan.op("vector", lambda e: e.scalar_tensor_tensor(
        row_sc[0:1, tlen:2 * tlen], row_sc[0:1, tlen:2 * tlen], 1.0 / D,
        row_sc[0:1, 0:tlen], ALU.mult, ALU.subtract))
    plan.wait_eng("scalar", "vector", v3)
    s1 = plan.op("scalar", lambda e: e.activation(
        row_sc[0:1, 0:tlen], row_sc[0:1, tlen:2 * tlen], AF.Sqrt,
        bias=row_sc[0:1, 3 * tlen:3 * tlen + 1]))
    plan.wait_eng("vector", "scalar", s1)
    v4 = plan.op("vector", lambda e: e.reciprocal(
        row_sc[0:1, tlen:2 * tlen], row_sc[0:1, 0:tlen]))
    plan.wait_eng("tensor", "vector", v4)
    for which, off in ((0, 2 * tlen), (1, 1 * tlen)):
        for tt in range(ntt):
            plan.op("tensor", lambda e, which=which, off=off, tt=tt:
                    e.matmul(
                ps_bc[:, tt * TTW:(tt + 1) * TTW], ones1f[0:1, :],
                row_sc[0:1, off + tt * TTW: off + (tt + 1) * TTW],
                start=True, stop=True))
        tmm = plan.cur("d_tensor")
        plan.wait_eng("vector", "tensor", tmm)
        vb = plan.op("vector", lambda e, which=which: e.tensor_copy(
            bc_sb[:, which * tlen:(which + 1) * tlen], ps_bc[:, 0:tlen]))
        plan.wait_eng("tensor", "vector", vb)
    bc_done = plan.cur("d_vector")
    plan.wait_eng("vector", "vector", bc_done)
    for icx in range(IC):
        sl = slice(icx * tlen, (icx + 1) * tlen)
        plan.op("vector", lambda e, sl=sl: e.tensor_sub(
            x_sb[:, sl], x_sb[:, sl], bc_sb[:, 0:tlen]))
    vs = plan.cur("d_vector")
    plan.wait_eng("vector", "vector", vs)
    for icx in range(IC):
        sl = slice(icx * tlen, (icx + 1) * tlen)
        plan.op("vector", lambda e, sl=sl: e.tensor_mul(
            x_sb[:, sl], x_sb[:, sl], bc_sb[:, tlen:2 * tlen]))


def emit_l0attn_p1(plan, C, ph):
    xT = C.xT
    Wq, Wk, Wv, PTm = C.Wq, C.Wk, C.Wv, C.PT
    o_qbf, o_kbf, o_vt = C.qbf_i, C.kbf_i, C.vt_i
    o_negbq, o_diagk = C.negbq_i, C.diagk_i
    eye_f = C.eye_f
    ones128b = C.ones128b

    xbf = ph.sbuf("xbf", [128, IC * T], BF16)
    pbf = ph.sbuf("pbf", [128, M], BF16)
    wbuf = [ph.sbuf(f"wbuf{i}", [128, D], BF16) for i in range(2)]
    wvb = ph.sbuf("wvb", [128, IC * D], BF16)
    abf = ph.sbuf("abf", [128, IC * T], BF16)
    asq = ph.sbuf("asq", [128, IC * T], BF16)
    vt = ph.sbuf("vt_s", [128, NC * D], BF16)
    negbq = ph.sbuf("negbq_s", [128, H * NC], F32)
    diagk = ph.sbuf("diagk_s", [128, H * NC], F32)
    kmaxc = ph.sbuf("kmaxc", [128, H], F32)
    km_t = ph.sbuf("km_t", [128, 1], F32)
    km_s = ph.sbuf("km_s", [1, 128], F32)
    km1 = ph.sbuf("km1", [1, 1], F32)
    km_b = ph.sbuf("km_b", [1, 8], F32)
    mxcol = ph.sbuf("mxcol", [128, NC], F32)

    ps_mm = ph.psum("ps_mm", [128, 2 * TTW], F32)
    ps_dd = ph.psum("ps_dd", [128, 8 * M], F32)
    ps_dg = ph.psum("ps_dg", [128, 512], F32)
    ps_tr = ph.psum("ps_tr", [128, 128], F32)

    plan.dma("gpsimd", xbf[:, :].rearrange("p (i t) -> p i t", i=IC),
             xT.rearrange("(i p) t -> p i t", p=128))
    plan.dma("gpsimd", pbf[0:DH, :], PTm[:, :])
    plan.dma("gpsimd", pbf[DH:128, :], PTm[:, :])
    plan.wait("gpsimd", "dma", plan.cur("dma"))
    plan.op("gpsimd", lambda e: e.memset(km1[0:1, 0:1], 0.0))
    plan.join()

    def proj_side(W_ap):
        last_mm = {}
        ps_pairs = [ps_mm[:, 0:1024], ps_dd[:, 0:1024]]
        for oc in range(IC):
            buf = wbuf[oc % 2]
            psem = "pc%d" % (oc % 2)
            if oc >= 2:
                plan.wait_eng("gpsimd", "tensor", last_mm[oc - 2])
            plan.dma("gpsimd", buf[:, :].rearrange("p (i o) -> p i o", i=IC),
                     W_ap[:, oc * 128:(oc + 1) * 128]
                     .rearrange("(i p) o -> p i o", p=128), psem)
            plan.wait("tensor", psem, plan.cur(psem))
            for pp in range(2):
                psX = ps_pairs[pp]
                prev = plan._psv.get(pp)
                if prev is not None:
                    plan.wait_eng("tensor", "scalar", prev[0])
                    plan.wait_eng("tensor", "vector", prev[1])
                for icx in range(IC):
                    for qq in range(2):
                        tt = pp * 2 + qq
                        tgt = plan.op("tensor", lambda e, buf=buf, icx=icx,
                                      tt=tt, qq=qq, psX=psX: e.matmul(
                            psX[:, qq * TTW:(qq + 1) * TTW],
                            buf[:, icx * 128:(icx + 1) * 128],
                            xbf[:, icx * T + tt * TTW:
                                icx * T + (tt + 1) * TTW],
                            start=(icx == 0), stop=(icx == IC - 1),
                            skip_group_check=True))
                last_mm[oc] = tgt
                plan.wait_eng("scalar", "tensor", tgt)
                for qq in range(2):
                    tt = pp * 2 + qq
                    se_t = plan.op("scalar", lambda e, oc=oc, tt=tt, qq=qq,
                                   psX=psX:
                                   e.copy(abf[:, oc * T + tt * TTW:
                                              oc * T + (tt + 1) * TTW],
                                          psX[:, qq * TTW:(qq + 1) * TTW]))
                    plan.wait_eng("vector", "scalar", se_t)
                    ve_t = plan.op("vector", lambda e, oc=oc, tt=tt:
                                   e.tensor_mul(
                        asq[:, oc * T + tt * TTW: oc * T + (tt + 1) * TTW],
                        abf[:, oc * T + tt * TTW: oc * T + (tt + 1) * TTW],
                        abf[:, oc * T + tt * TTW: oc * T + (tt + 1) * TTW]))
                plan._psv[pp] = (se_t, ve_t)

    def head_stats(side):
        red_c = {}
        last_red = None
        last_cmb = None
        for h in range(H):
            oc = h // 2
            rb = 64 * (h % 2)
            for half in range(2):
                if half == 0 and last_cmb is not None:
                    plan.wait_eng("vector", "vector", last_cmb)
                for c8 in range(8):
                    tc = half * 8 + c8
                    if red_c.get(c8) is not None:
                        plan.wait_eng("tensor", "vector", red_c[c8])
                    tgt = plan.op("tensor", lambda e, oc=oc, rb=rb, tc=tc,
                                  c8=c8: e.matmul(
                        ps_dd[:, c8 * M:(c8 + 1) * M],
                        abf[rb:rb + DH,
                            oc * T + tc * 128: oc * T + (tc + 1) * 128],
                        pbf[rb:rb + DH, :], start=True, stop=True))
                    plan.wait_eng("vector", "tensor", tgt)
                    red_c[c8] = plan.op("vector", lambda e, half=half,
                                        c8=c8: e.reduce_max(
                        mxcol[:, half * 8 + c8:half * 8 + c8 + 1],
                        ps_dd[:, c8 * M:(c8 + 1) * M],
                        axis=AX.X))
                last_red = red_c[7]
            if last_cmb is not None:
                plan.wait_eng("tensor", "vector", last_cmb)
            for tc in range(NC):
                tgt = plan.op("tensor", lambda e, oc=oc, rb=rb, tc=tc:
                              e.matmul(
                    ps_dg[:, tc:tc + 1],
                    asq[rb:rb + DH,
                        oc * T + tc * 128: oc * T + (tc + 1) * 128],
                    ones128b[rb:rb + DH, :], start=True, stop=True))
            plan.wait_eng("vector", "tensor", tgt)
            plan.wait_eng("vector", "vector", last_red)
            if side == 0:
                v1 = plan.op("vector", lambda e: e.tensor_scalar_mul(
                    mxcol[:, :], mxcol[:, :], DN))
                plan.wait_eng("vector", "vector", v1)
                last_cmb = plan.op("vector", lambda e, h=h:
                                   e.scalar_tensor_tensor(
                    negbq[:, h * NC:(h + 1) * NC], ps_dg[:, 0:NC],
                    -0.5 * DN * DN, mxcol[:, :], ALU.mult, ALU.subtract))
            else:
                plan.op("vector", lambda e, h=h: e.tensor_scalar_mul(
                    diagk[:, h * NC:(h + 1) * NC], ps_dg[:, 0:NC],
                    0.5 * DN * DN))
                last_cmb = plan.op("vector", lambda e, h=h: e.reduce_max(
                    kmaxc[:, h:h + 1], mxcol[:, :], axis=AX.X))

    plan._psv = {}
    proj_side(Wq)
    plan.join()
    head_stats(0)
    plan.join()
    plan.dma("sync", o_qbf[:, :], abf[:, :], "st")
    plan.dma("sync", o_negbq[:, :], negbq[:, :], "st")
    plan.wait("sync", "st", plan.cur("st"))
    plan.op("sync", lambda e: e.nop())
    plan.join()

    plan._psv = {}
    proj_side(Wk)
    plan.join()
    head_stats(1)
    plan.join()
    plan.dma("sync", o_kbf[:, :], abf[:, :], "st")
    plan.dma("sync", o_diagk[:, :], diagk[:, :], "st")

    plan.wait_eng("vector", "vector", plan.cur("d_vector"))
    v1 = plan.op("vector", lambda e: e.reduce_max(km_t[:, :], kmaxc[:, :],
                                                  axis=AX.X))
    plan.wait_eng("tensor", "vector", v1)
    t1 = plan.op("tensor", lambda e: e.transpose(ps_tr[0:1, :], km_t[:, :],
                                                 eye_f[:, :]))
    plan.wait_eng("vector", "tensor", t1)
    v2 = plan.op("vector", lambda e: e.tensor_copy(km_s[:, :], ps_tr[0:1, :]))
    plan.wait_eng("vector", "vector", v2)
    v3 = plan.op("vector", lambda e: e.reduce_max(km1[:, :], km_s[:, :],
                                                  axis=AX.X))
    plan.wait_eng("vector", "vector", v3)
    v3b = plan.op("vector", lambda e: e.memset(km_b[:, :], 0.0))
    plan.wait_eng("vector", "vector", v3b)
    v3c = plan.op("vector", lambda e: e.tensor_scalar(
        km_b[:, :], km_b[:, :], km1[0:1, 0:1], None, ALU.add))
    plan.wait_eng("sync", "vector", v3c)
    plan.dma("sync", C.kmax_w.ap(), km_b[:, :], "st")

    plan.dma("gpsimd", wvb[:, :].rearrange("p (i o) -> p i o", i=IC),
             Wv.rearrange("(i p) o -> p i o", p=128), "wv")
    plan.wait("tensor", "wv", plan.cur("wv"))
    plan.join()
    prev_slot = {}
    for tc in range(NC):
        for hp in range(IC):
            slot = (tc * IC + hp) % 4
            if slot in prev_slot:
                plan.wait_eng("tensor", "scalar", prev_slot[slot])
            for icx in range(IC):
                tgt = plan.op("tensor", lambda e, tc=tc, hp=hp, icx=icx,
                              slot=slot: e.matmul(
                    ps_dd[:, slot * TTW: slot * TTW + 128],
                    xbf[:, icx * T + tc * 128: icx * T + (tc + 1) * 128],
                    wvb[:, icx * D + hp * 128: icx * D + (hp + 1) * 128],
                    start=(icx == 0), stop=(icx == IC - 1)))
            plan.wait_eng("scalar", "tensor", tgt)
            prev_slot[slot] = plan.op(
                "scalar", lambda e, tc=tc, hp=hp, slot=slot: e.copy(
                    vt[:, tc * D + hp * 128: tc * D + (hp + 1) * 128],
                    ps_dd[:, slot * TTW: slot * TTW + 128]))
    plan.wait_eng("sync", "scalar", plan.cur("d_scalar"))
    plan.dma("sync", o_vt[:, :], vt[:, :], "st")


def emit_l0attn_p2(plan, C, ph):
    kbf_d, vt_d, diagk_d = C.kbf_i, C.vt_i, C.diagk_i
    kmax8_d = C.kmax_r.ap()
    PTm = C.PT
    o_ctx = C.ctx_w.ap()
    ones1f = C.ones1f
    ones128b = C.ones128b

    kbf = ph.sbuf("kbf_s", [128, IC * T], BF16)
    vt = ph.sbuf("vt_s", [128, NC * D], BF16)
    pbf = ph.sbuf("pbf", [128, M], BF16)
    diagk = ph.sbuf("diagk_s", [128, H * NC], F32)
    negbk = ph.sbuf("negbk_s", [128, H * NC], F32)
    kmax8 = ph.sbuf("kmax8_s", [1, 8], F32)
    mxk1 = ph.sbuf("mxk1", [1, 1], F32)
    mxkc = ph.sbuf("mxkc", [128, 1], F32)
    kp2 = [ph.sbuf(f"kp_s{i}", [128, NC * M], BF16) for i in range(2)]
    expf = ph.sbuf("expf", [128, 2 * M], F32)
    ctx = ph.sbuf("ctx_s", [128, H * 2 * 65], F32)

    ps_dd = ph.psum("ps_dd", [128, 8 * M], F32)
    ps_ctx = ph.psum("ps_ctx", [128, 512], F32)
    ps_one = ph.psum("ps_one", [128, 512], F32)
    ps_b = ph.psum("ps_b", [128, 512], F32)

    plan.dma("gpsimd", kbf[:, :], kbf_d)
    plan.dma("gpsimd", vt[:, :], vt_d)
    plan.dma("gpsimd", diagk[:, :], diagk_d)
    plan.wait("gpsimd", "cc_kmax", 1)
    plan.dma("gpsimd", kmax8[:, :], kmax8_d)
    plan.dma("gpsimd", pbf[0:DH, :], PTm[:, :])
    plan.dma("gpsimd", pbf[DH:128, :], PTm[:, :])
    plan.wait("gpsimd", "dma", plan.cur("dma"))
    plan.op("gpsimd", lambda e: e.memset(mxk1[0:1, 0:1], 0.0))
    plan.join()

    v1 = plan.op("vector", lambda e: e.reduce_max(mxk1[:, :], kmax8[:, :],
                                                  axis=AX.X))
    plan.wait_eng("vector", "vector", v1)
    v2 = plan.op("vector", lambda e: e.tensor_scalar_mul(mxk1[:, :],
                                                         mxk1[:, :], DN))
    plan.wait_eng("tensor", "vector", v2)
    t1 = plan.op("tensor", lambda e: e.matmul(ps_b[:, 0:1], ones1f[0:1, :],
                                              mxk1[0:1, 0:1], start=True,
                                              stop=True))
    plan.wait_eng("vector", "tensor", t1)
    v3 = plan.op("vector", lambda e: e.tensor_copy(mxkc[:, :], ps_b[:, 0:1]))
    plan.wait_eng("vector", "vector", v3)
    v4 = plan.op("vector", lambda e: e.tensor_scalar(
        negbk[:, :], diagk[:, :], mxkc[:, 0:1], -1.0, ALU.add, ALU.mult))
    plan.join()

    st = {"kp_done": {}, "ctx_t": {}, "prev_exp": {}, "expv": {},
          "prev_ctx_ev": None}

    def emit_kp(h):
        oc = h // 2
        rb = 64 * (h % 2)
        kpb = kp2[h % 2]
        if st["ctx_t"].get(h % 2) is not None:
            plan.wait_eng("scalar", "tensor", st["ctx_t"][h % 2])
            plan.wait_eng("vector", "tensor", st["ctx_t"][h % 2])
        for half in range(2):
            for c8 in range(8):
                tc = half * 8 + c8
                slot = tc % 2
                if c8 in st["prev_exp"]:
                    plan.wait_eng("tensor", "vector", st["prev_exp"][c8])
                tgt = plan.op("tensor", lambda e, oc=oc, rb=rb, tc=tc, c8=c8:
                              e.matmul(
                    ps_dd[:, c8 * M:(c8 + 1) * M],
                    kbf[rb:rb + DH,
                        oc * T + tc * 128: oc * T + (tc + 1) * 128],
                    pbf[rb:rb + DH, :], start=True, stop=True))
                plan.wait_eng("scalar", "tensor", tgt)
                if slot in st["expv"]:
                    plan.wait_eng("scalar", "vector", st["expv"][slot])
                s1 = plan.op("scalar", lambda e, h=h, tc=tc, c8=c8, slot=slot:
                             e.activation(
                    expf[:, slot * M:(slot + 1) * M],
                    ps_dd[:, c8 * M:(c8 + 1) * M], AF.Exp,
                    bias=negbk[:, h * NC + tc: h * NC + tc + 1], scale=DN))
                plan.wait_eng("vector", "scalar", s1)
                v5 = plan.op("vector", lambda e, tc=tc, slot=slot, kpb=kpb:
                             e.tensor_scalar_add(
                    kpb[:, tc * M:(tc + 1) * M],
                    expf[:, slot * M:(slot + 1) * M], EPS_PERF))
                st["prev_exp"][c8] = v5
                st["expv"][slot] = v5
        st["kp_done"][h] = plan.cur("d_vector")

    def emit_ctx(h):
        kpb = kp2[h % 2]
        plan.wait_eng("tensor", "vector", st["kp_done"][h])
        for mc in range(2):
            if st["prev_ctx_ev"] is not None:
                plan.wait_eng("tensor", "scalar", st["prev_ctx_ev"])
            for tc in range(NC):
                plan.op("tensor", lambda e, h=h, mc=mc, tc=tc, kpb=kpb:
                        e.matmul(
                    ps_ctx[:, 0:64],
                    kpb[:, tc * M + mc * 128: tc * M + mc * 128 + 128],
                    vt[:, tc * D + h * DH: tc * D + (h + 1) * DH],
                    start=(tc == 0), stop=(tc == NC - 1),
                    skip_group_check=True))
                plan.op("tensor", lambda e, h=h, mc=mc, tc=tc, kpb=kpb:
                        e.matmul(
                    ps_one[:, 0:1],
                    kpb[:, tc * M + mc * 128: tc * M + mc * 128 + 128],
                    ones128b[:, :],
                    start=(tc == 0), stop=(tc == NC - 1),
                    skip_group_check=True))
            tdone = plan.cur("d_tensor")
            plan.wait_eng("scalar", "tensor", tdone)
            plan.op("scalar", lambda e, h=h, mc=mc: e.copy(
                ctx[:, (h * 2 + mc) * 65:(h * 2 + mc) * 65 + 64],
                ps_ctx[:, 0:64]))
            st["prev_ctx_ev"] = plan.op("scalar", lambda e, h=h, mc=mc:
                                        e.copy(
                ctx[:, (h * 2 + mc) * 65 + 64:(h * 2 + mc + 1) * 65],
                ps_one[:, 0:1]))
        st["ctx_t"][h % 2] = tdone

    for h in range(H):
        emit_kp(h)
        if h >= 1:
            emit_ctx(h - 1)
    emit_ctx(H - 1)
    plan.wait_eng("sync", "scalar", plan.cur("d_scalar"))
    plan.dma("sync", o_ctx, ctx[:, :], "st")


def emit_l0attn_p3(plan, C, ph):
    xT_d = C.xT
    qbf_d, negbq_d = C.qbf_i, C.negbq_i
    ctx_r_d = C.ctx_r.ap()
    PTm = C.PT
    Wo_d, bo_d = C.Wo, C.bo
    ones1f = C.ones1f
    eye_f = C.eye_f

    qbf = ph.sbuf("qbf_s", [128, IC * T], BF16)
    attn = ph.sbuf("attn_s", [128, IC * T], BF16)
    xoc = [ph.sbuf(f"xoc{i}", [128, SLAB], F32) for i in range(2)]
    x1slab = ph.sbuf("x1slab", [128, IC * SLAB], F32)
    row_sc1 = ph.sbuf("row_sc1", [1, 3 * SLAB + 1], F32)
    bc_sb1 = ph.sbuf("bc_sb1", [128, 2 * SLAB], F32)
    gw_sb = ph.sbuf("gw_sb", [128, IC * NE], F32)
    gb_sb = ph.sbuf("gb_sb", [NE, 1], F32)
    logit_sb = ph.sbuf("logit_sb", [NE, T], F32)
    usb_g = ph.sbuf("usb_g", [NE, T], F32)
    lmax_sb = ph.sbuf("lmax_sb", [NE, 1], F32)
    sume_sb = ph.sbuf("sume_sb", [NE, 1], F32)
    gmax_sb = ph.sbuf("gmax_sb", [NE, 1], F32)
    tterm_sb = ph.sbuf("tterm_sb", [NE, 1], F32)
    pbf = ph.sbuf("pbf", [128, M], BF16)
    negbq = ph.sbuf("negbq_s", [128, H * NC], F32)
    ctxa = ph.sbuf("ctxa", [128, H * 2 * 65], F32)
    ctxb = ph.sbuf("ctxb", [128, H * 2 * 65], BF16)
    nbb2 = [ph.sbuf(f"nbb{i}", [128, T], F32) for i in range(2)]
    trnb = ph.sbuf("trnb", [128, 256], F32)
    u_sb = ph.sbuf("u_sb", [128, 2 * TTW], F32)
    expf = ph.sbuf("expf", [128, 2 * TTW], F32)
    qp2 = [ph.sbuf(f"qp_s{i}", [128, 2 * T], BF16) for i in range(2)]
    denr2 = [ph.sbuf("denr", [1, TTW], F32), bc_sb1[0:1, TTW:2 * TTW]]
    rbb2 = [ph.sbuf("rbb", [128, TTW], F32), bc_sb1[:, 0:TTW]]
    eps_col = ph.sbuf("eps_col", [128, 1], F32)
    wbuf = [ph.sbuf(f"wbuf{i}", [128, D], BF16) for i in range(2)]
    bo_sb = ph.sbuf("bo_sb", [128, IC], F32)

    ps_dd = ph.psum("ps_dd", [128, 512], F32)
    ps_o = ph.psum("ps_o", [128, 512], F32)
    ps_den = ph.psum("ps_den", [128, 512], F32)
    ps_rb = ph.psum("ps_rb", [128, 512], F32)
    ps_bc = ph.psum("ps_bc", [128, 2048], F32)

    plan.dma("gpsimd", qbf[:, :], qbf_d)
    plan.dma("gpsimd", negbq[:, :], negbq_d)
    plan.dma("gpsimd", pbf[0:DH, :], PTm[:, :])
    plan.dma("gpsimd", pbf[DH:128, :], PTm[:, :])
    plan.wait("gpsimd", "cc_ctx", 1)
    plan.dma("gpsimd", ctxa[:, :], ctx_r_d)
    plan.dma("gpsimd", bo_sb[:, :], bo_d)
    plan.wait("gpsimd", "dma", plan.cur("dma"))
    plan.op("gpsimd", lambda e: e.memset(trnb[0:1, 0:1], 0.0))
    plan.op("gpsimd", lambda e: e.memset(eps_col[:, :], EPS_PERF))
    plan.join()

    vc = plan.op("vector", lambda e: e.tensor_copy(ctxb[:, :], ctxa[:, :]))
    plan.join()

    for blk in range(2):
        tb = plan.op("tensor", lambda e, blk=blk: e.transpose(
            ps_bc[0:128, 0:128], negbq[:, blk * 128:(blk + 1) * 128],
            eye_f[:, :]))
        plan.wait_eng("vector", "tensor", tb)
        vb = plan.op("vector", lambda e, blk=blk: e.tensor_copy(
            trnb[:, blk * 128:(blk + 1) * 128], ps_bc[0:128, 0:128]))
        plan.wait_eng("tensor", "vector", vb)
    plan.wait_eng("sync", "vector", plan.cur("d_vector"))
    for blk in range(2):
        plan.dma("sync",
                 C.nb_d[blk * 8:(blk + 1) * 8, :]
                 .rearrange("h (t i) -> (h t) i", t=16),
                 trnb[:, blk * 128:(blk + 1) * 128], "wvs")
    plan.wait("gpsimd", "wvs", plan.cur("wvs"))
    nbdma = {}
    for h0 in range(2):
        nbdma[h0] = plan.dma("gpsimd", nbb2[h0][:, :],
                             C.nb_d[h0:h0 + 1, :].broadcast_to((128, T)),
                             "nbq")

    st2_t = {}
    qp_dv = {}
    qp_sc = {}
    qst = {}

    def stage1(h):
        oc = h // 2
        rb = 64 * (h % 2)
        qpb = qp2[h % 2]
        nbb = nbb2[h % 2]
        if st2_t.get(h % 2) is not None:
            plan.wait_eng("scalar", "tensor", st2_t[h % 2])
        if h + 1 < H:
            if h >= 1:
                plan.wait_eng("gpsimd", "vector", qp_dv[h - 1])
            nbdma[h + 1] = plan.dma(
                "gpsimd", nbb2[(h + 1) % 2][:, :],
                C.nb_d[h + 1:h + 2, :].broadcast_to((128, T)), "nbq")
        plan.wait("vector", "nbq", nbdma[h])
        for mc in range(2):
            for tt in range(NT):
                par = (mc * NT + tt) % 2
                psX = [ps_dd, ps_rb][par]
                if qst.get(("v3", par)) is not None:
                    plan.wait_eng("tensor", "vector", qst[("v3", par)])
                tgt = plan.op("tensor", lambda e, oc=oc, rb=rb, mc=mc, tt=tt,
                              psX=psX: e.matmul(
                    psX[:, :],
                    pbf[rb:rb + DH, mc * 128:(mc + 1) * 128],
                    qbf[rb:rb + DH, oc * T + tt * TTW: oc * T + (tt + 1) * TTW],
                    start=True, stop=True))
                plan.wait_eng("vector", "tensor", tgt)
                if qst.get(("s1", par)) is not None:
                    plan.wait_eng("vector", "scalar", qst[("s1", par)])
                v3 = plan.op("vector", lambda e, tt=tt, par=par, psX=psX:
                             e.scalar_tensor_tensor(
                    u_sb[:, par * TTW:(par + 1) * TTW], psX[:, :], DN,
                    nbb[:, tt * TTW:(tt + 1) * TTW], ALU.mult, ALU.add))
                plan.wait_eng("scalar", "vector", v3)
                if qst.get(("v4", par)) is not None:
                    plan.wait_eng("scalar", "scalar", qst[("v4", par)])
                s1 = plan.op("scalar", lambda e, par=par: e.activation(
                    expf[:, par * TTW:(par + 1) * TTW],
                    u_sb[:, par * TTW:(par + 1) * TTW], AF.Exp))
                plan.wait_eng("scalar", "scalar", s1)
                v4 = plan.op("scalar", lambda e, mc=mc, tt=tt, par=par,
                             qpb=qpb: e.activation(
                    qpb[:, mc * T + tt * TTW: mc * T + (tt + 1) * TTW],
                    expf[:, par * TTW:(par + 1) * TTW], AF.Identity,
                    bias=eps_col[:, 0:1]))
                qst[("v3", par)] = v3
                qst[("s1", par)] = s1
                qst[("v4", par)] = v4
        qp_dv[h] = plan.cur("d_vector")
        qp_sc[h] = plan.cur("d_scalar")

    pso_s2 = [ps_o[0:DH, 0:TTW], ps_bc[0:DH, TTW:2 * TTW]]
    psd_s2 = [ps_den[0:1, 0:TTW], ps_bc[0:1, 2 * TTW:3 * TTW]]
    bc_s2 = [ps_bc[:, 0:TTW], ps_bc[:, 3 * TTW:4 * TTW]]

    def stage2(h):
        oc = h // 2
        rb = 64 * (h % 2)
        qpb = qp2[h % 2]
        plan.wait_eng("tensor", "vector", qp_dv[h])
        plan.wait_eng("tensor", "scalar", qp_sc[h])

        def tail(tz):
            pz = tz % 2
            plan.wait_eng("vector", "tensor", qst[("t5", pz)])
            v5 = plan.op("vector", lambda e, pz=pz: e.tensor_copy(
                denr2[pz][:, :], psd_s2[pz]))
            plan.wait_eng("tensor", "vector", v5)
            if qst.get(("v6", pz)) is not None:
                plan.wait_eng("tensor", "vector", qst[("v6", pz)])
            t4 = plan.op("tensor", lambda e, pz=pz: e.matmul(
                bc_s2[pz], ones1f[0:1, :], denr2[pz][0:1, :],
                start=True, stop=True))
            plan.wait_eng("vector", "tensor", t4)
            v6 = plan.op("vector", lambda e, pz=pz: e.reciprocal(
                rbb2[pz][:, :], bc_s2[pz]))
            plan.wait_eng("vector", "vector", v6)
            v7 = plan.op("vector", lambda e, oc=oc, rb=rb, tz=tz, pz=pz:
                         e.tensor_mul(
                attn[rb:rb + DH, oc * T + tz * TTW: oc * T + (tz + 1) * TTW],
                pso_s2[pz], rbb2[pz][0:DH, :]))
            qst[("v6", pz)] = v6
            qst[("mul", pz)] = v7

        for tt in range(NT):
            par = tt % 2
            if qst.get(("mul", par)) is not None:
                plan.wait_eng("tensor", "vector", qst[("mul", par)])
            for mc in range(2):
                plan.op("tensor", lambda e, h=h, mc=mc, tt=tt, qpb=qpb,
                        par=par: e.matmul(
                    pso_s2[par],
                    ctxb[:, (h * 2 + mc) * 65:(h * 2 + mc) * 65 + DH],
                    qpb[:, mc * T + tt * TTW: mc * T + (tt + 1) * TTW],
                    start=(mc == 0), stop=(mc == 1)))
            for mc in range(2):
                t5 = plan.op("tensor", lambda e, h=h, mc=mc, tt=tt, qpb=qpb,
                             par=par: e.matmul(
                    psd_s2[par],
                    ctxb[:, (h * 2 + mc) * 65 + DH:(h * 2 + mc) * 65 + 65],
                    qpb[:, mc * T + tt * TTW: mc * T + (tt + 1) * TTW],
                    start=(mc == 0), stop=(mc == 1)))
            qst[("t5", par)] = t5
            if tt > 0:
                tail(tt - 1)
        tail(NT - 1)
        st2_t[h % 2] = plan.cur("d_tensor")

    for h in range(H):
        if h >= 1:
            stage2(h - 1)
        stage1(h)
    stage2(H - 1)
    plan.join()

    plan.dma("gpsimd", gw_sb[:, :].rearrange("p (i en) -> p i en", i=IC),
             C.gw.rearrange("(i p) en -> p i en", p=128), "wv")
    plan.dma("gpsimd", gb_sb[:, :], C.gb, "wv")
    last_mm = {}
    prev_ev = {}
    prev_half = {}
    for sb in range(2):
        for ocx in range(IC):
            idx = sb * IC + ocx
            buf = wbuf[ocx % 2]
            psem = "pc%d" % (ocx % 2)
            xsem = "xs%d" % (ocx % 2)
            if last_mm.get(idx % 2) is not None:
                plan.wait_eng("gpsimd", "tensor", last_mm[idx % 2])
            plan.dma("gpsimd", buf[:, :].rearrange("p (i o) -> p i o", i=IC),
                     Wo_d[:, ocx * 128:(ocx + 1) * 128]
                     .rearrange("(i p) o -> p i o", p=128), psem)
            if idx >= 2:
                plan.wait_eng("sync", "vector", prev_ev[ocx % 2])
            plan.dma("sync", xoc[ocx % 2][:, :],
                     xT_d[ocx * 128:(ocx + 1) * 128,
                          sb * SLAB:(sb + 1) * SLAB], xsem)
            plan.wait("tensor", psem, plan.cur(psem))
            plan.wait("vector", xsem, plan.cur(xsem))
            for tt2 in range(2):
                tt = sb * 2 + tt2
                half = tt2
                if half in prev_half:
                    plan.wait_eng("tensor", "vector", prev_half[half])
                for icx in range(IC):
                    tgt = plan.op("tensor", lambda e, buf=buf, icx=icx,
                                  tt=tt, half=half: e.matmul(
                        ps_bc[:, half * TTW:(half + 1) * TTW],
                        buf[:, icx * 128:(icx + 1) * 128],
                        attn[:, icx * T + tt * TTW: icx * T + (tt + 1) * TTW],
                        start=(icx == 0), stop=(icx == IC - 1)))
                last_mm[idx % 2] = tgt
                plan.wait_eng("vector", "tensor", tgt)
                prev_ev[ocx % 2] = plan.op(
                    "vector", lambda e, ocx=ocx, tt2=tt2, half=half:
                    e.scalar_tensor_tensor(
                        x1slab[:, ocx * SLAB + tt2 * TTW:
                               ocx * SLAB + (tt2 + 1) * TTW],
                        ps_bc[:, half * TTW:(half + 1) * TTW],
                        bo_sb[:, ocx:ocx + 1],
                        xoc[ocx % 2][:, tt2 * TTW:(tt2 + 1) * TTW],
                        ALU.add, ALU.add))
                prev_half[half] = prev_ev[ocx % 2]
        plan.join()
        emit_ln(plan, x1slab, qbf[:, 0:IC * SLAB],
                qbf[:, IC * SLAB:2 * IC * SLAB], C.ones128b, row_sc1,
                bc_sb1, C.ones1f, ps_den, ps_bc, tlen=SLAB)
        plan.join()
        plan.dma("sync",
                 C.x1_i[:, sb * SLAB:(sb + 1) * SLAB]
                 .rearrange("(i p) t -> p i t", p=128),
                 x1slab[:, :].rearrange("p (i t) -> p i t", i=IC), "st")
        plan.wait("tensor", "wv", plan.cur("wv"))
        for tt2 in range(2):
            for icx in range(IC):
                gmm = plan.op("tensor", lambda e, icx=icx, tt2=tt2: e.matmul(
                    ps_bc[0:NE, tt2 * TTW:(tt2 + 1) * TTW],
                    gw_sb[:, icx * NE:(icx + 1) * NE],
                    x1slab[:, icx * SLAB + tt2 * TTW:
                           icx * SLAB + (tt2 + 1) * TTW],
                    start=(icx == 0), stop=(icx == IC - 1),
                    skip_group_check=True))
        plan.wait_eng("vector", "tensor", gmm)
        gv = plan.op("vector", lambda e, sb=sb: e.tensor_scalar(
            logit_sb[:, sb * SLAB:(sb + 1) * SLAB],
            ps_bc[0:NE, 0:SLAB], gb_sb[:, 0:1], None, ALU.add))
        if sb == 0:
            plan.wait("vector", "st", plan.cur("st"))
            plan.wait_eng("vector", "vector", gv)
            plan.wait_eng("tensor", "vector", gv)
            plan.op("vector", lambda e: e.memset(row_sc1[0:1, 0:1], 0.0))
            plan.join()
    plan.join()
    v1g = plan.op("vector", lambda e: e.reduce_max(lmax_sb[:, :],
                                                   logit_sb[:, :], axis=AX.X))
    plan.wait_eng("vector", "vector", v1g)
    v2g = plan.op("vector", lambda e: e.tensor_scalar(
        usb_g[:, :], logit_sb[:, :], lmax_sb[:, 0:1], None, ALU.subtract))
    plan.wait_eng("scalar", "vector", v2g)
    s1g = plan.op("scalar", lambda e: e.activation(usb_g[:, :], usb_g[:, :],
                                                   AF.Exp))
    plan.wait_eng("vector", "scalar", s1g)
    v3g = plan.op("vector", lambda e: e.reduce_sum(sume_sb[:, :], usb_g[:, :],
                                                   axis=AX.X))
    plan.wait_eng("sync", "vector", plan.cur("d_vector"))
    plan.dma("sync", C.logits_i, logit_sb[:, :], "st")
    plan.dma("sync", C.lmax_w.ap(), lmax_sb[:, :], "st")
    phase_barrier(plan)
    emit_collective(plan, "cc_gmax", "AllReduce", ALU.max, G4,
                    C.lmax_w, C.gmax_r)
    plan.wait("gpsimd", "cc_gmax", 1)
    plan.dma("gpsimd", gmax_sb[:, :], C.gmax_r.ap())
    plan.wait("vector", "dma", plan.cur("dma"))
    vt1 = plan.op("vector", lambda e: e.tensor_tensor(
        tterm_sb[:, :], lmax_sb[:, :], gmax_sb[:, :], ALU.subtract))
    plan.wait_eng("scalar", "vector", vt1)
    st1 = plan.op("scalar", lambda e: e.activation(tterm_sb[:, :],
                                                   tterm_sb[:, :], AF.Exp))
    plan.wait_eng("vector", "scalar", st1)
    vt2 = plan.op("vector", lambda e: e.tensor_mul(tterm_sb[:, :],
                                                   tterm_sb[:, :],
                                                   sume_sb[:, :]))
    plan.wait_eng("sync", "vector", vt2)
    plan.dma("sync", C.t_w.ap(), tterm_sb[:, :], "st")
    phase_barrier(plan)
    emit_collective(plan, "cc_rden", "AllReduce", ALU.add, G4,
                    C.t_w, C.rden_r)


def emit_l0ffn_p1(plan, C, ph):
    x1p_d = C.x1p_i
    gw_d, gb_d = C.gw, C.gb
    o_x1 = C.x1_i
    o_logits = C.logits_i
    ones128b = C.ones128b
    ones1f = C.ones1f

    x_sb = ph.sbuf("x_sb", [128, IC * T], F32)
    xbf = ph.sbuf("xbf", [128, IC * T], BF16)
    sqb = ph.sbuf("sqb", [128, IC * T], BF16)
    row_sc = ph.sbuf("row_sc", [1, 3 * T + 1], F32)
    bc_sb = ph.sbuf("bc_sb", [128, 2 * T], F32)
    gw_sb = ph.sbuf("gw_sb", [128, IC * NE], F32)
    gb_sb = ph.sbuf("gb_sb", [NE, 1], F32)
    logit = ph.sbuf("logit", [NE, T], F32)
    lmax = ph.sbuf("lmax_s", [NE, 1], F32)
    usb = ph.sbuf("usb", [NE, T], F32)
    sume = ph.sbuf("sume_s", [NE, 1], F32)
    gmax = ph.sbuf("gmax_s", [NE, 1], F32)
    tterm = ph.sbuf("tterm", [NE, 1], F32)

    ps_den = ph.psum("ps_den", [128, 512], F32)
    ps_bc = ph.psum("ps_bc", [128, 2048], F32)
    ps_g = ph.psum("ps_g", [128, 2 * TTW], F32)

    plan.dma("gpsimd", x_sb[:, :].rearrange("p (i t) -> p i t", i=IC),
             x1p_d.rearrange("(i p) t -> p i t", p=128))
    plan.dma("gpsimd", gw_sb[:, :].rearrange("p (i en) -> p i en", i=IC),
             gw_d.rearrange("(i p) en -> p i en", p=128))
    plan.dma("gpsimd", gb_sb[:, :], gb_d)
    plan.wait("gpsimd", "dma", plan.cur("dma"))
    plan.op("gpsimd", lambda e: e.memset(row_sc[0:1, 0:1], 0.0))
    plan.join()
    emit_ln(plan, x_sb, xbf, sqb, ones128b, row_sc, bc_sb, ones1f,
            ps_den, ps_bc)
    plan.join()
    plan.dma("sync", o_x1.rearrange("(i p) t -> p i t", p=128),
             x_sb[:, :].rearrange("p (i t) -> p i t", i=IC), "st")
    prev_ev = {}
    for tt in range(NT):
        half = tt % 2
        if half in prev_ev:
            plan.wait_eng("tensor", "vector", prev_ev[half])
        for icx in range(IC):
            tgt = plan.op("tensor", lambda e, icx=icx, tt=tt, half=half:
                          e.matmul(
                ps_g[0:NE, half * TTW:(half + 1) * TTW],
                gw_sb[:, icx * NE:(icx + 1) * NE],
                x_sb[:, icx * T + tt * TTW: icx * T + (tt + 1) * TTW],
                start=(icx == 0), stop=(icx == IC - 1)))
        plan.wait_eng("vector", "tensor", tgt)
        prev_ev[half] = plan.op("vector", lambda e, tt=tt, half=half:
                                e.tensor_scalar(
            logit[:, tt * TTW:(tt + 1) * TTW],
            ps_g[0:NE, half * TTW:(half + 1) * TTW],
            gb_sb[:, 0:1], None, ALU.add))
    lg_done = plan.cur("d_vector")
    plan.wait_eng("vector", "vector", lg_done)
    v1 = plan.op("vector", lambda e: e.reduce_max(lmax[:, :], logit[:, :],
                                                  axis=AX.X))
    plan.wait_eng("vector", "vector", v1)
    v2 = plan.op("vector", lambda e: e.tensor_scalar(
        usb[:, :], logit[:, :], lmax[:, 0:1], None, ALU.subtract))
    plan.wait_eng("scalar", "vector", v2)
    s1 = plan.op("scalar", lambda e: e.activation(usb[:, :], usb[:, :],
                                                  AF.Exp))
    plan.wait_eng("vector", "scalar", s1)
    v3 = plan.op("vector", lambda e: e.reduce_sum(sume[:, :], usb[:, :],
                                                  axis=AX.X))
    plan.wait_eng("sync", "vector", plan.cur("d_vector"))
    plan.dma("sync", o_logits, logit[:, :], "st")
    plan.dma("sync", C.lmax_w.ap(), lmax[:, :], "st")
    phase_barrier(plan)
    emit_collective(plan, "cc_gmax", "AllReduce", ALU.max, G4,
                    C.lmax_w, C.gmax_r)
    plan.wait("gpsimd", "cc_gmax", 1)
    plan.dma("gpsimd", gmax[:, :], C.gmax_r.ap())
    plan.wait("vector", "dma", plan.cur("dma"))
    vt1 = plan.op("vector", lambda e: e.tensor_tensor(
        tterm[:, :], lmax[:, :], gmax[:, :], ALU.subtract))
    plan.wait_eng("scalar", "vector", vt1)
    st1 = plan.op("scalar", lambda e: e.activation(tterm[:, :], tterm[:, :],
                                                   AF.Exp))
    plan.wait_eng("vector", "scalar", st1)
    vt2 = plan.op("vector", lambda e: e.tensor_mul(tterm[:, :], tterm[:, :],
                                                   sume[:, :]))
    plan.wait_eng("sync", "vector", vt2)
    plan.dma("sync", C.t_w.ap(), tterm[:, :], "st")
    phase_barrier(plan)
    emit_collective(plan, "cc_rden", "AllReduce", ALU.add, G4,
                    C.t_w, C.rden_r)


def emit_l0ffn_p2(plan, C, ph):
    x1_d = C.x1_i
    logits_d = C.logits_i
    w1_d = C.w1
    w2_d = C.w2
    w3_d = C.w3
    b1_d, b2_d, b3_d = C.b1, C.b2, C.b3
    o_ind = C.ind_i
    eye_f = C.eye_f
    ones1f = C.ones1f

    xbf = ph.sbuf("xbf", [128, IC * T], BF16)
    h1 = ph.sbuf("h1", [128, IC * T], BF16)
    h2 = ph.sbuf("h2", [128, IC * T], BF16)
    ysb = ph.sbuf("ysb", [128, T], F32)
    ydump = ph.sbuf("ydump", [128, T], F32)
    wbuf = [ph.sbuf(f"wbuf{i}", [128, D], BF16) for i in range(4)]
    logit = ph.sbuf("logit", [NE, T], F32)
    gmax = ph.sbuf("gmax", [NE, 1], F32)
    rden = ph.sbuf("rden", [NE, 1], F32)
    probs = ph.sbuf("probs", [NE, T], F32)
    ptm = ph.sbuf("ptm", [128, NC * NE], F32)
    m1c = ph.sbuf("m1c", [128, 1], F32)
    m2c = ph.sbuf("m2c", [128, 1], F32)
    eqc = ph.sbuf("eqc", [128, NE], F32)
    wtok_cols = ph.sbuf("wtok_cols", [128, NE * NC], F32)
    ind_cols = ph.sbuf("ind_cols", [128, NE * NC], F32)
    ind_fm = ph.sbuf("ind_fm", [NE, T], BF16)
    wb2 = [ph.sbuf(f"wb2_{i}", [128, T], F32) for i in range(2)]
    trwt = ph.sbuf("trwt", [128, 128], F32)
    v_sb = ph.sbuf("v_sb", [128, NE * IC], F32)
    vT_sb = ph.sbuf("vT_sb", [NE * IC, 128], F32)
    b1s = ph.sbuf("b1s", [128, NE * IC], F32)
    b2s = ph.sbuf("b2s", [128, NE * IC], F32)
    b3s = ph.sbuf("b3s", [128, NE * IC], F32)

    ps_pair = [ph.psum(f"ps_p{i}", [128, 1024], F32) for i in range(2)]
    ps_t = ph.psum("ps_t", [128, 512], F32)

    plan.dma("gpsimd", xbf[:, :].rearrange("p (i t) -> p i t", i=IC),
             x1_d.rearrange("(i p) t -> p i t", p=128))
    plan.dma("gpsimd", logit[:, :], logits_d)
    plan.wait("gpsimd", "cc_gmax", 1)
    plan.dma("gpsimd", gmax[:, :], C.gmax_r.ap())
    plan.wait("gpsimd", "cc_rden", 1)
    plan.dma("gpsimd", rden[:, :], C.rden_r.ap())
    plan.dma("gpsimd", b1s[:, :], b1_d)
    plan.dma("gpsimd", b2s[:, :], b2_d)
    plan.dma("gpsimd", b3s[:, :], b3_d)
    plan.wait("gpsimd", "dma", plan.cur("dma"))
    plan.op("gpsimd", lambda e: e.memset(trwt[0:1, 0:1], 0.0))
    plan.join()

    v5 = plan.op("vector", lambda e: e.reciprocal(rden[:, :], rden[:, :]))
    plan.wait_eng("vector", "vector", v5)
    v6 = plan.op("vector", lambda e: e.tensor_scalar(
        probs[:, :], logit[:, :], gmax[:, 0:1], None, ALU.subtract))
    plan.wait_eng("scalar", "vector", v6)
    s2 = plan.op("scalar", lambda e: e.activation(probs[:, :], probs[:, :],
                                                  AF.Exp))
    plan.wait_eng("vector", "scalar", s2)
    v7 = plan.op("vector", lambda e: e.tensor_scalar(
        probs[:, :], probs[:, :], rden[:, 0:1], None, ALU.mult))
    plan.wait_eng("tensor", "vector", v7)

    for tc in range(NC):
        t1 = plan.op("tensor", lambda e, tc=tc: e.transpose(
            ps_t[0:128, 0:NE], probs[:, tc * 128:(tc + 1) * 128],
            eye_f[0:NE, 0:NE]))
        plan.wait_eng("vector", "tensor", t1)
        vv = plan.op("vector", lambda e, tc=tc: e.tensor_copy(
            ptm[:, tc * NE:(tc + 1) * NE], ps_t[0:128, 0:NE]))
        plan.wait_eng("tensor", "vector", vv)
    plan.wait_eng("vector", "vector", plan.cur("d_vector"))
    for tc in range(NC):
        sl = slice(tc * NE, (tc + 1) * NE)
        va = plan.op("vector", lambda e, sl=sl: e.reduce_max(
            m1c[:, :], ptm[:, sl], axis=AX.X))
        plan.wait_eng("vector", "vector", va)
        vb = plan.op("vector", lambda e, sl=sl: e.tensor_scalar(
            eqc[:, :], ptm[:, sl], m1c[:, 0:1], -1e30, ALU.is_equal,
            ALU.mult))
        plan.wait_eng("vector", "vector", vb)
        vc = plan.op("vector", lambda e, sl=sl: e.tensor_add(
            eqc[:, :], eqc[:, :], ptm[:, sl]))
        plan.wait_eng("vector", "vector", vc)
        vd = plan.op("vector", lambda e: e.reduce_max(
            m2c[:, :], eqc[:, :], axis=AX.X))
        plan.wait_eng("vector", "vector", vd)
        ve = plan.op("vector", lambda e, sl=sl, tc=tc: e.tensor_scalar(
            ind_cols[:, tc: NE * NC: NC], ptm[:, sl], m2c[:, 0:1], None,
            ALU.is_ge))
        plan.wait_eng("vector", "vector", ve)
        vf = plan.op("vector", lambda e, sl=sl, tc=tc: e.tensor_mul(
            wtok_cols[:, tc: NE * NC: NC], ind_cols[:, tc: NE * NC: NC],
            ptm[:, sl]))
    sel_done = plan.cur("d_vector")
    plan.wait_eng("tensor", "vector", sel_done)
    for tc in range(NC):
        t2 = plan.op("tensor", lambda e, tc=tc: e.transpose(
            ps_t[0:NE, 0:128], ind_cols[:, tc: NE * NC: NC], eye_f[:, :]))
        plan.wait_eng("vector", "tensor", t2)
        vg = plan.op("vector", lambda e, tc=tc: e.tensor_copy(
            ind_fm[:, tc * 128:(tc + 1) * 128], ps_t[0:NE, 0:128]))
        plan.wait_eng("tensor", "vector", vg)
    plan.join()
    plan.dma("sync", o_ind, ind_fm[:, :], "st")

    tw = plan.op("tensor", lambda e: e.transpose(
        ps_t[0:128, 0:128], wtok_cols[:, :], eye_f[:, :]))
    plan.wait_eng("vector", "tensor", tw)
    vtw = plan.op("vector", lambda e: e.tensor_copy(
        trwt[:, :], ps_t[0:128, 0:128]))
    plan.wait_eng("sync", "vector", vtw)
    plan.wait_eng("tensor", "vector", vtw)
    plan.dma("sync",
             C.wt_d[0:NE, :].rearrange("h (t i) -> (h t) i", t=NC),
             trwt[:, :], "wvs")
    plan.wait("gpsimd", "wvs", plan.cur("wvs"))
    wtdma = {0: plan.dma("gpsimd", wb2[0][:, :],
                         C.wt_d[0:1, :].broadcast_to((128, T)), "nbq")}
    plan._vprev_g = None

    def issue_w(W, oc2):
        key = plan._wiss % 4
        plan._wiss += 1
        psem = "pw%d" % key
        if plan._wpar.get(key) is not None:
            plan.wait_eng("gpsimd", "tensor", plan._wpar[key])
        plan._wsem[key] = plan.dma(
            "gpsimd", wbuf[key][:, :].rearrange("p (i o) -> p i o", i=IC),
            W[:, oc2 * 128:(oc2 + 1) * 128]
            .rearrange("(i p) o -> p i o", p=128), psem)

    def fm_layer(src_sb, W_e, bias_cols, e_idx, dst_sb, relu_to_y=None,
                 next_W=None):
        plan.wait_eng("tensor", "scalar", plan.cur("d_scalar"))
        for oc in range(IC):
            if oc + 2 < IC:
                issue_w(W_e, oc + 2)
            elif next_W is not None:
                issue_w(next_W, oc + 2 - IC)
            key = plan._wuse % 4
            plan._wuse += 1
            buf = wbuf[key]
            plan.wait("tensor", "pw%d" % key, plan._wsem[key])
            for pp in range(2):
                psX = ps_pair[pp]
                prev = plan._psv2.get(pp)
                if prev is not None:
                    plan.wait_eng("tensor", "scalar", prev)
                for icx in range(IC):
                    for qq in range(2):
                        tt = pp * 2 + qq
                        tgt = plan.op("tensor", lambda e, buf=buf, icx=icx,
                                      tt=tt, qq=qq, psX=psX, src_sb=src_sb:
                                      e.matmul(
                            psX[:, qq * TTW:(qq + 1) * TTW],
                            buf[:, icx * 128:(icx + 1) * 128],
                            src_sb[:, icx * T + tt * TTW:
                                   icx * T + (tt + 1) * TTW],
                            start=(icx == 0), stop=(icx == IC - 1),
                            skip_group_check=True))
                plan._wpar[key] = tgt
                plan.wait_eng("scalar", "tensor", tgt)
                for qq in range(2):
                    tt = pp * 2 + qq
                    if relu_to_y is not None:
                        sv = plan.op("scalar", lambda e, oc=oc, tt=tt, qq=qq,
                                     psX=psX, e_idx=e_idx: e.activation(
                            ysb[:, tt * TTW:(tt + 1) * TTW],
                            psX[:, qq * TTW:(qq + 1) * TTW], AF.Relu,
                            bias=bias_cols[:, e_idx * IC + oc:
                                           e_idx * IC + oc + 1]))
                        plan._yev[(oc, tt)] = sv
                    else:
                        sv = plan.op("scalar", lambda e, oc=oc, tt=tt, qq=qq,
                                     psX=psX, e_idx=e_idx, dst_sb=dst_sb:
                                     e.activation(
                            dst_sb[:, oc * T + tt * TTW:
                                   oc * T + (tt + 1) * TTW],
                            psX[:, qq * TTW:(qq + 1) * TTW], AF.Relu,
                            bias=bias_cols[:, e_idx * IC + oc:
                                           e_idx * IC + oc + 1]))
                    plan._psv2[pp] = sv
            if relu_to_y is not None:
                sv = plan._yev[(oc, NT - 1)]
                plan.wait_eng("vector", "scalar", sv)
                if plan._vprev is not None:
                    plan.wait_eng("vector", "vector", plan._vprev)
                plan._vprev = plan.op(
                    "vector", lambda e, oc=oc, e_idx=e_idx, wb_sb=wb_sb:
                    e.scalar_tensor_tensor(
                        ydump[:, :], ysb[:, :], 1.0, wb_sb[:, :],
                        ALU.mult, ALU.mult,
                        accum_out=v_sb[:, e_idx * IC + oc:
                                       e_idx * IC + oc + 1]))
                plan.wait_eng("scalar", "vector", plan._vprev)

    plan._wpar = {}
    plan._wsem = {}
    plan._wiss = 0
    plan._wuse = 0
    plan._psv2 = {}
    plan._vprev = None
    issue_w(w1_d[0], 0)
    issue_w(w1_d[0], 1)
    for e_idx in range(NE):
        plan._yev = {}
        wb_sb = wb2[e_idx % 2]
        if e_idx + 1 < NE:
            if plan._vprev_g is not None:
                plan.wait_eng("gpsimd", "vector", plan._vprev_g)
            wtdma[e_idx + 1] = plan.dma(
                "gpsimd", wb2[(e_idx + 1) % 2][:, :],
                C.wt_d[e_idx + 1:e_idx + 2, :].broadcast_to((128, T)),
                "nbq")
        plan.wait("vector", "nbq", wtdma[e_idx])
        nxt = w1_d[e_idx + 1] if e_idx + 1 < NE else None
        fm_layer(xbf, w1_d[e_idx], b1s, e_idx, h1, next_W=w2_d[e_idx])
        fm_layer(h1, w2_d[e_idx], b2s, e_idx, h2, next_W=w3_d[e_idx])
        fm_layer(h2, w3_d[e_idx], b3s, e_idx, None, relu_to_y=True,
                 next_W=nxt)
        plan._vprev_g = plan._vprev
    dvv = plan.cur("d_vector")
    plan.wait_eng("tensor", "vector", dvv)
    tv = plan.op("tensor", lambda e: e.transpose(
        ps_t[0:NE * IC, 0:128], v_sb[:, :], eye_f[:, :]))
    plan.wait_eng("vector", "tensor", tv)
    vv2 = plan.op("vector", lambda e: e.tensor_copy(vT_sb[:, :],
                                                    ps_t[0:NE * IC, 0:128]))
    plan.wait_eng("sync", "vector", vv2)
    plan.dma("sync", C.v_w.ap(), vT_sb[:, :], "st")


def emit_l0ffn_p3(plan, C, ph):
    x1_d = C.x1_i
    ind_d = C.ind_i

    v_acc = ph.sbuf("v_acc", [NE, D], F32)
    v_bf = ph.sbuf("v_bf", [NE, D], BF16)
    ind_fm = ph.sbuf("ind_fm2", [NE, T], BF16)
    xoc = [ph.sbuf(f"xoc{i}", [128, T], F32) for i in range(2)]
    x2full = ph.sbuf("x2full", [128, IC * T], F32)
    ln_xbf = ph.sbuf("ln_xbf", [128, IC * T], BF16)
    ln_sqb = ph.sbuf("ln_sqb", [128, IC * T], BF16)
    row_sc = ph.sbuf("row_sc", [1, 3 * T + 1], F32)
    bc_sb = ph.sbuf("bc_sb", [128, 2 * T], F32)
    ps_mm = ph.psum("ps_mm", [128, 2 * TTW], F32)
    ps_den = ph.psum("ps_den", [128, 512], F32)
    ps_bc = ph.psum("ps_bc", [128, 2048], F32)

    plan.wait("gpsimd", "cc_v", 1)
    plan.dma("gpsimd", v_acc[:, :],
             C.v_r.ap().rearrange("(e r) p -> e (r p)", e=NE))
    plan.dma("gpsimd", ind_fm[:, :], ind_d)
    plan.wait("gpsimd", "dma", plan.cur("dma"))
    plan.wait("vector", "dma", plan.cur("dma"))
    vb = plan.op("vector", lambda e: e.tensor_copy(v_bf[:, :], v_acc[:, :]))
    plan.op("gpsimd", lambda e: e.memset(row_sc[0:1, 1:2], 0.0))
    plan.join()

    prev_half = {}
    for ocx in range(IC):
        xsem = "xs%d" % (ocx % 2)
        if ocx >= 2:
            plan.wait_eng("sync", "vector", prev_half.get((ocx - 2, NT - 1), 0))
        plan.dma("sync", xoc[ocx % 2][:, :],
                 x1_d[ocx * 128:(ocx + 1) * 128, :], xsem)
        plan.wait("vector", xsem, plan.cur(xsem))
        for tt in range(NT):
            half = tt % 2
            if prev_half.get(half) is not None:
                plan.wait_eng("tensor", "vector", prev_half[half])
            tgt = plan.op("tensor", lambda e, ocx=ocx, tt=tt, half=half:
                          e.matmul(
                ps_mm[:, half * TTW:(half + 1) * TTW],
                v_bf[:, ocx * 128:(ocx + 1) * 128],
                ind_fm[:, tt * TTW:(tt + 1) * TTW],
                start=True, stop=True))
            plan.wait_eng("vector", "tensor", tgt)
            done = plan.op("vector", lambda e, ocx=ocx, tt=tt, half=half:
                           e.tensor_add(
                x2full[:, ocx * T + tt * TTW: ocx * T + (tt + 1) * TTW],
                ps_mm[:, half * TTW:(half + 1) * TTW],
                xoc[ocx % 2][:, tt * TTW:(tt + 1) * TTW]))
            prev_half[half] = done
            prev_half[(ocx, tt)] = done
    plan.join()
    emit_ln(plan, x2full, ln_xbf, ln_sqb, C.ones128b, row_sc, bc_sb,
            C.ones1f, ps_den, ps_bc)
    plan.join()
    plan.dma("sync", C.x2_i.rearrange("(i p) t -> p i t", p=128),
             x2full[:, :].rearrange("p (i t) -> p i t", i=IC), "st")


def emit_ln_phase(plan, C, ph, xin, xout):
    ones128b = C.ones128b
    ones1f = C.ones1f
    x_sb = ph.sbuf("x_sb", [128, IC * T], F32)
    xbf = ph.sbuf("xbf", [128, IC * T], BF16)
    sqb = ph.sbuf("sqb", [128, IC * T], BF16)
    row_sc = ph.sbuf("row_sc", [1, 3 * T + 1], F32)
    bc_sb = ph.sbuf("bc_sb", [128, 2 * T], F32)
    ps_den = ph.psum("ps_den", [128, 512], F32)
    ps_bc = ph.psum("ps_bc", [128, 2048], F32)
    plan.dma("gpsimd", x_sb[:, :].rearrange("p (i t) -> p i t", i=IC),
             xin.rearrange("(i p) t -> p i t", p=128))
    plan.wait("gpsimd", "dma", plan.cur("dma"))
    plan.op("gpsimd", lambda e: e.memset(row_sc[0:1, 1:2], 0.0))
    plan.join()
    emit_ln(plan, x_sb, xbf, sqb, ones128b, row_sc, bc_sb, ones1f,
            ps_den, ps_bc)
    plan.join()
    plan.dma("sync", xout.rearrange("(i p) t -> p i t", p=128),
             x_sb[:, :].rearrange("p (i t) -> p i t", i=IC), "st")


def emit_l1attn_p1(plan, C, ph):
    x2_d = C.x2_i
    Wq_d, Wk_d, Wv_d = C.Wq2, C.Wk2, C.Wv2
    bq_d, bk_d, bv_d = C.bq2, C.bk2, C.bv2
    lidx_d, t0c_d = C.lidx, C.t0c
    o_qbf, o_kbf, o_vt = C.qbf2_i, C.kbf2_i, C.vt2_i
    o_qn, o_kn = C.qn_i, C.kn_i
    ktm_i = C.ktm_i
    ones128b = C.ones128b
    ones1f = C.ones1f
    eye_f = C.eye_f
    eye_b = C.eye_b

    xbf = ph.sbuf("xbf", [128, IC * T], BF16)
    abf = ph.sbuf("abf", [128, IC * T], BF16)
    asq = ph.sbuf("asq", [128, IC * T], BF16)
    vt = ph.sbuf("vt_s", [128, NC * D], BF16)
    wbuf = [ph.sbuf(f"wbuf{i}", [128, D], BF16) for i in range(2)]
    wvb = ph.sbuf("wvb", [128, IC * D], BF16)
    bcol = ph.sbuf("bcol", [128, 2 * IC], F32)
    bvrow = ph.sbuf("bvrow", [1, D], F32)
    bvb = ph.sbuf("bvb", [128, D], F32)
    ncols = ph.sbuf("ncols", [128, H * NC], F32)
    ktile = [ph.sbuf(f"ktile{i}", [128, D], BF16)
             for i in range(2)]
    lmp = ph.sbuf("lmp", [NL, D], BF16)
    lidx = ph.sbuf("lidx_s", [NL, 1], I32)
    t0c = ph.sbuf("t0c_s", [NL, 1], I32)
    idxa = ph.sbuf("idxa", [NL, 1], I32)
    vm1 = ph.sbuf("vm1", [NL, 1], I32)
    vm2 = ph.sbuf("vm2", [NL, 1], I32)
    validf = ph.sbuf("validf", [NL, 1], F32)

    ps_mm = ph.psum("ps_mm", [128, 2 * TTW], F32)
    ps_mm2 = ph.psum("ps_mm2", [128, 2 * TTW], F32)
    ps_dg = ph.psum("ps_dg", [128, 512], F32)
    ps_tp = ph.psum("ps_tp", [128, 1024], BF16)
    ps_bc = ph.psum("ps_bc", [128, 1024], F32)

    plan.dma("gpsimd", xbf[:, :].rearrange("p (i t) -> p i t", i=IC),
             x2_d.rearrange("(i p) t -> p i t", p=128))
    plan.dma("gpsimd", bcol[:, 0:IC], bq_d)
    plan.dma("gpsimd", bcol[:, IC:2 * IC], bk_d)
    plan.dma("gpsimd", bvrow[:, :], bv_d)
    plan.dma("gpsimd", lidx[:, :], lidx_d)
    plan.dma("gpsimd", t0c[:, :], t0c_d)
    plan.wait("gpsimd", "dma", plan.cur("dma"))
    plan.op("gpsimd", lambda e: e.memset(lmp[:, :], 0.0))
    plan.join()


    def proj_side(W_ap, bias_off, dst, dst_sq, ncols_dst):
        last_mm = {}
        ps_pairs = [ps_mm[:, 0:1024], ps_mm2[:, 0:1024]]
        for oc in range(IC):
            buf = wbuf[oc % 2]
            psem = "pc%d" % (oc % 2)
            if oc >= 2:
                plan.wait_eng("gpsimd", "tensor", last_mm[oc - 2])
            plan.dma("gpsimd", buf[:, :].rearrange("p (i o) -> p i o", i=IC),
                     W_ap[:, oc * 128:(oc + 1) * 128]
                     .rearrange("(i p) o -> p i o", p=128), psem)
            plan.wait("tensor", psem, plan.cur(psem))
            for pp in range(2):
                psX = ps_pairs[pp]
                prev = plan._psv.get(pp)
                if prev is not None:
                    plan.wait_eng("tensor", "scalar", prev[0])
                    plan.wait_eng("tensor", "vector", prev[1])
                for icx in range(IC):
                    for qq in range(2):
                        tt = pp * 2 + qq
                        tgt = plan.op("tensor", lambda e, buf=buf, icx=icx,
                                      tt=tt, qq=qq, psX=psX: e.matmul(
                            psX[:, qq * TTW:(qq + 1) * TTW],
                            buf[:, icx * 128:(icx + 1) * 128],
                            xbf[:, icx * T + tt * TTW:
                                icx * T + (tt + 1) * TTW],
                            start=(icx == 0), stop=(icx == IC - 1),
                            skip_group_check=True))
                last_mm[oc] = tgt
                plan.wait_eng("scalar", "tensor", tgt)
                for qq in range(2):
                    tt = pp * 2 + qq
                    se_t = plan.op("scalar", lambda e, oc=oc, tt=tt, qq=qq,
                                   psX=psX: e.add(
                        dst[:, oc * T + tt * TTW: oc * T + (tt + 1) * TTW],
                        psX[:, qq * TTW:(qq + 1) * TTW],
                        bcol[:, bias_off + oc: bias_off + oc + 1]))
                    plan.wait_eng("vector", "scalar", se_t)
                    ve_t = plan.op("vector", lambda e, oc=oc, tt=tt:
                                   e.tensor_mul(
                        dst_sq[:, oc * T + tt * TTW: oc * T + (tt + 1) * TTW],
                        dst[:, oc * T + tt * TTW: oc * T + (tt + 1) * TTW],
                        dst[:, oc * T + tt * TTW: oc * T + (tt + 1) * TTW]))
                plan._psv[pp] = (se_t, ve_t)
        sq_done = plan.cur("d_vector")
        plan.wait_eng("tensor", "vector", sq_done)
        last_cp = None
        for h in range(H):
            oc = h // 2
            rb = 64 * (h % 2)
            if last_cp is not None:
                plan.wait_eng("tensor", "vector", last_cp)
            for tc in range(NC):
                plan.op("tensor", lambda e, oc=oc, rb=rb, tc=tc: e.matmul(
                    ps_dg[:, tc:tc + 1],
                    dst_sq[rb:rb + DH,
                           oc * T + tc * 128: oc * T + (tc + 1) * 128],
                    ones128b[rb:rb + DH, :], start=True, stop=True))
            tgt = plan.cur("d_tensor")
            plan.wait_eng("vector", "tensor", tgt)
            last_cp = plan.op("vector", lambda e, h=h: e.tensor_copy(
                ncols[:, h * NC:(h + 1) * NC], ps_dg[:, 0:NC]))
        plan.wait_eng("sync", "vector", last_cp)
        plan.dma("sync", ncols_dst, ncols[:, :], "st")

    plan._psv = {}
    proj_side(Wq_d, 0, abf, asq, o_qn)
    plan.join()
    plan.dma("sync", o_qbf, abf[:, :], "st")
    plan.wait("sync", "st", plan.cur("st"))
    plan.op("sync", lambda e: e.nop())
    plan.join()
    plan._psv = {}
    proj_side(Wk_d, IC, abf, asq, o_kn)
    plan.join()
    plan.dma("sync", o_kbf, abf[:, :], "st")

    prev_cp = None
    prev_st = {}
    for tc in range(NC):
        if prev_cp is not None:
            plan.wait_eng("tensor", "vector", prev_cp)
        if tc >= 2 and (tc % 2) in prev_st:
            plan.wait("vector", "kt%d" % (tc % 2), prev_st[tc % 2])
        for oc in range(IC):
            plan.op("tensor", lambda e, tc=tc, oc=oc: e.transpose(
                ps_tp[:, oc * 128:(oc + 1) * 128] if False else
                ps_tp[:, 0:1024].rearrange("p (a b) -> p a b", a=8)[:, oc, :],
                abf[:, oc * T + tc * 128: oc * T + (tc + 1) * 128],
                eye_b[:, :]))
        tgt = plan.cur("d_tensor")
        plan.wait_eng("vector", "tensor", tgt)
        prev_cp = plan.op("vector", lambda e, tc=tc: e.tensor_copy(
            ktile[tc % 2][:, :], ps_tp[:, 0:1024]))
        plan.wait_eng("sync", "vector", prev_cp)
        prev_st[tc % 2] = plan.dma(
            "sync", ktm_i[tc * 128:(tc + 1) * 128, :], ktile[tc % 2][:, :],
            "kt%d" % (tc % 2))
    plan.wait("vector", "kt0", plan.cur("kt0"))
    plan.wait("vector", "kt1", plan.cur("kt1"))

    va = plan.op("vector", lambda e: e.tensor_tensor(
        idxa[:, :], lidx[:, :], t0c[:, :], ALU.subtract))
    plan.wait_eng("vector", "vector", va)
    vb = plan.op("vector", lambda e: e.tensor_scalar(
        vm1[:, :], idxa[:, :], 0, None, ALU.is_ge))
    vc = plan.op("vector", lambda e: e.tensor_scalar(
        vm2[:, :], idxa[:, :], T, None, ALU.is_lt))
    plan.wait_eng("vector", "vector", vc)
    vd = plan.op("vector", lambda e: e.tensor_tensor(
        vm1[:, :], vm1[:, :], vm2[:, :], ALU.mult))
    plan.wait_eng("vector", "vector", vd)
    ve = plan.op("vector", lambda e: e.tensor_tensor(
        idxa[:, :], idxa[:, :], vm1[:, :], ALU.mult))
    vf = plan.op("vector", lambda e: e.tensor_copy(validf[:, :], vm1[:, :]))
    plan.wait_eng("gpsimd", "vector", plan.cur("d_vector"))
    gt = plan.bump("gth", 16)

    def _gather(e, s=plan.sems["gth"]):
        from concourse.bass_core_rust import IndirectOffsetOnAxis
        e.indirect_dma_start(
            lmp[:, :], None, ktm_i[:, :],
            IndirectOffsetOnAxis(idxa[:, 0:1], 0)).then_inc(s, 16)
    plan.add("gpsimd", _gather)
    plan.wait("vector", "gth", gt)
    vg = plan.op("vector", lambda e: e.tensor_scalar(
        lmp[:, :], lmp[:, :], validf[:, 0:1], None, ALU.mult))
    plan.wait_eng("sync", "vector", vg)
    plan.dma("sync", C.lm_w.ap(), lmp[:, :], "st")

    plan.dma("gpsimd", wvb[:, :].rearrange("p (i o) -> p i o", i=IC),
             Wv_d.rearrange("(i p) o -> p i o", p=128), "wv")
    plan.wait("tensor", "wv", plan.cur("wv"))
    dvv = plan.cur("d_vector")
    plan.wait_eng("tensor", "vector", dvv)
    for hb in range(2):
        plan.op("tensor", lambda e, hb=hb: e.matmul(
            ps_bc[:, hb * TTW:(hb + 1) * TTW], ones1f[0:1, :],
            bvrow[0:1, hb * TTW:(hb + 1) * TTW], start=True, stop=True))
    t5 = plan.cur("d_tensor")
    plan.wait_eng("vector", "tensor", t5)
    v5 = plan.op("vector", lambda e: e.tensor_copy(bvb[:, 0:1024],
                                                   ps_bc[:, 0:1024]))
    plan.wait_eng("vector", "vector", v5)
    plan.join()
    prev_slot = {}
    for tc in range(NC):
        for hp in range(IC):
            slot = (tc * IC + hp) % 2
            if slot in prev_slot:
                plan.wait_eng("tensor", "vector", prev_slot[slot])
            for icx in range(IC):
                tgt = plan.op("tensor", lambda e, tc=tc, hp=hp, icx=icx,
                              slot=slot: e.matmul(
                    ps_mm[:, slot * TTW: slot * TTW + 128],
                    xbf[:, icx * T + tc * 128: icx * T + (tc + 1) * 128],
                    wvb[:, icx * D + hp * 128: icx * D + (hp + 1) * 128],
                    start=(icx == 0), stop=(icx == IC - 1)))
            plan.wait_eng("vector", "tensor", tgt)
            prev_slot[slot] = plan.op(
                "vector", lambda e, tc=tc, hp=hp, slot=slot:
                e.scalar_tensor_tensor(
                    vt[:, tc * D + hp * 128: tc * D + (hp + 1) * 128],
                    ps_mm[:, slot * TTW: slot * TTW + 128], 1.0,
                    bvb[:, hp * 128:(hp + 1) * 128], ALU.mult, ALU.add))
    plan.wait_eng("sync", "vector", plan.cur("d_vector"))
    plan.dma("sync", o_vt, vt[:, :], "st")


NS_ITERS = 14


def emit_lm_common(plan, C, lm_r_d, temp_d, lmp, lmt, lbf, lsq, eye_f, eye_b,
                   ones64b, ones1f, invt, t2c, invtc, lnt_rows, lncol_t,
                   lnbT, ps_a, ps_row, ps_tpb):
    plan.wait("gpsimd", "cc_lm", 1)
    plan.dma("gpsimd", lmp[:, :], lm_r_d)
    plan.dma("gpsimd", invt[0:1, 0:1], temp_d[0:1, 0:1])
    plan.wait("gpsimd", "dma", plan.cur("dma"))
    plan.op("gpsimd", lambda e: e.memset(lmt[0:1, 0:1], 0.0))
    plan.join()
    v1 = plan.op("vector", lambda e: e.reciprocal(invt[0:1, 0:1],
                                                  invt[0:1, 0:1]))
    plan.wait_eng("tensor", "vector", v1)
    t1 = plan.op("tensor", lambda e: e.matmul(
        ps_row[0:128, 0:1], ones1f[0:1, :], invt[0:1, 0:1],
        start=True, stop=True))
    plan.wait_eng("vector", "tensor", t1)
    v2 = plan.op("vector", lambda e: e.tensor_copy(invtc[:, :],
                                                   ps_row[0:128, 0:1]))
    plan.wait_eng("vector", "vector", v2)
    v3 = plan.op("vector", lambda e: e.tensor_scalar_mul(t2c[:, :],
                                                         invtc[:, :], 2.0))
    for h in range(H):
        t2 = plan.op("tensor", lambda e, h=h: e.transpose(
            ps_tpb[0:DH, 0:DH], lmp[:, h * DH:(h + 1) * DH],
            eye_b[0:NL, 0:NL]))
        plan.wait_eng("vector", "tensor", t2)
        va = plan.op("vector", lambda e, h=h: e.tensor_copy(
            lbf[0:DH, h * DH:(h + 1) * DH], ps_tpb[0:DH, 0:DH]))
        vb = plan.op("vector", lambda e, h=h: e.tensor_copy(
            lbf[DH:128, h * DH:(h + 1) * DH], ps_tpb[0:DH, 0:DH]))
        plan.wait_eng("tensor", "vector", vb)
    lb_done = plan.cur("d_vector")
    plan.wait_eng("vector", "vector", lb_done)
    vsq = plan.op("vector", lambda e: e.tensor_mul(lsq[:, :], lbf[:, :],
                                                   lbf[:, :]))
    plan.wait_eng("tensor", "vector", vsq)
    for h in range(H):
        t3 = plan.op("tensor", lambda e, h=h: e.matmul(
            ps_row[0:1, 0:DH], ones64b[0:DH, :],
            lsq[0:DH, h * DH:(h + 1) * DH], start=True, stop=True))
        plan.wait_eng("vector", "tensor", t3)
        vr = plan.op("vector", lambda e, h=h: e.tensor_scalar(
            lnt_rows[0:1, h * DH:(h + 1) * DH], ps_row[0:1, 0:DH],
            invt[0:1, 0:1], None, ALU.mult))
        plan.wait_eng("tensor", "vector", vr)
        t4 = plan.op("tensor", lambda e, h=h: e.transpose(
            ps_row[0:DH, 1:2], lnt_rows[0:1, h * DH:(h + 1) * DH],
            eye_f[0:1, 0:1]))
        plan.wait_eng("vector", "tensor", t4)
        vc = plan.op("vector", lambda e, h=h: e.tensor_copy(
            lncol_t[:, h:h + 1], ps_row[0:DH, 1:2]))
        plan.wait_eng("tensor", "vector", vc)
        t5 = plan.op("tensor", lambda e, h=h: e.matmul(
            ps_a[:, 0:DH], ones1f[0:1, :],
            lnt_rows[0:1, h * DH:(h + 1) * DH], start=True, stop=True))
        plan.wait_eng("vector", "tensor", t5)
        vd = plan.op("vector", lambda e, h=h: e.tensor_copy(
            lnbT[:, h * DH:(h + 1) * DH], ps_a[:, 0:DH]))
        plan.wait_eng("tensor", "vector", vd)
    plan.join()


def emit_l1attn_p2(plan, C, ph):
    lm_r_d = C.lm_r.ap()
    temp_d = C.temp
    kbf_d, vt_d, kn_d = C.kbf2_i, C.vt2_i, C.kn_i
    o_z = C.z_w.ap()
    o_winv = C.winv_i
    eye_f = C.eye_f
    eye_b = C.eye_b
    ones64b = C.ones128b
    ones128b = C.ones128b
    ones1f = C.ones1f

    kbf = ph.sbuf("kbf_s", [128, IC * T], BF16)
    vt = ph.sbuf("vt_s", [128, NC * D], BF16)
    kn = ph.sbuf("kn_s", [128, H * NC], F32)
    knt = ph.sbuf("knt", [128, H * NC], F32)
    lmp = ph.sbuf("lmp", [NL, D], BF16)
    lmt = ph.sbuf("lmt", [NL, D], BF16)
    lbf = ph.sbuf("lbf", [128, H * DH], BF16)
    lsq = ph.sbuf("lsq", [128, H * DH], BF16)
    invt = ph.sbuf("invt", [1, 1], F32)
    t2c = ph.sbuf("t2c", [128, 1], F32)
    invtc = ph.sbuf("invtc", [128, 1], F32)
    lnt_rows = ph.sbuf("lnt_rows", [1, H * DH], F32)
    lncol_t = ph.sbuf("lncol_t", [NL, H], F32)
    lnbT = ph.sbuf("lnbT", [128, H * DH], F32)
    eye2 = ph.sbuf("eye2", [NL, NL], F32)
    usb = ph.sbuf("usb", [128, 2 * DH], F32)
    wsb = ph.sbuf("wsb", [NL, H * NL], F32)
    xsb = ph.sbuf("xsb", [NL, H * NL], F32)
    rsb = ph.sbuf("rsb", [NL, H * NL], F32)
    rowsum = ph.sbuf("rowsum", [NL, 1], F32)
    srow = ph.sbuf("srow", [1, NL], F32)
    s11 = ph.sbuf("s11", [1, 1], F32)
    scol = ph.sbuf("scol", [NL, 1], F32)
    phik = ph.sbuf("phik", [128, NC * DH], BF16)
    zsb = ph.sbuf("zsb", [NL, H * 65], F32)

    ps_a = ph.psum("ps_a", [128, 512], F32)
    ps_row = ph.psum("ps_row", [128, 512], F32)
    ps_ns = [ph.psum(f"ps_ns{i}", [128, 512], F32)
             for i in range(2)]
    ps_z = ph.psum("ps_z", [128, 512], F32)
    ps_zs = ph.psum("ps_zs", [128, 512], F32)
    ps_ql = ph.psum("ps_ql", [128, 512], F32)
    ps_tpb = ph.psum("ps_tpb", [128, 1024], BF16)

    ve0 = plan.op("vector", lambda e: e.tensor_scalar_mul(
        eye2[:, :], eye_f[0:NL, 0:NL], 2.0))
    plan.dma("gpsimd", kbf[:, :], kbf_d)
    plan.dma("gpsimd", vt[:, :], vt_d)
    plan.dma("gpsimd", kn[:, :], kn_d)
    plan.wait("gpsimd", "dma", plan.cur("dma"))
    emit_lm_common(plan, C, lm_r_d, temp_d, lmp, lmt, lbf, lsq, eye_f, eye_b,
                   ones64b, ones1f, invt, t2c, invtc, lnt_rows, lncol_t,
                   lnbT, ps_a, ps_row, ps_tpb)

    vkn = plan.op("vector", lambda e: e.tensor_scalar(
        knt[:, :], kn[:, :], invtc[:, 0:1], None, ALU.mult))
    plan.join()

    for h in range(H):
        t1 = plan.op("tensor", lambda e, h=h: e.matmul(
            ps_a[0:NL, 0:NL], lbf[0:DH, h * DH:(h + 1) * DH],
            lbf[0:DH, h * DH:(h + 1) * DH], start=True, stop=True))
        plan.wait_eng("vector", "tensor", t1)
        v1 = plan.op("vector", lambda e, h=h: e.scalar_tensor_tensor(
            usb[0:NL, 0:NL], ps_a[0:NL, 0:NL], t2c[0:NL, 0:1],
            lnbT[0:NL, h * DH:(h + 1) * DH], ALU.mult, ALU.subtract))
        plan.wait_eng("vector", "vector", v1)
        v2 = plan.op("vector", lambda e, h=h: e.tensor_scalar(
            usb[0:NL, 0:NL], usb[0:NL, 0:NL], lncol_t[:, h:h + 1], 0.0,
            ALU.subtract, ALU.min))
        plan.wait_eng("scalar", "vector", v2)
        s1 = plan.op("scalar", lambda e, h=h: e.activation(
            wsb[:, h * NL:(h + 1) * NL], usb[0:NL, 0:NL], AF.Exp))
        plan.wait_eng("vector", "scalar", s1)
        v3 = plan.op("vector", lambda e, h=h: e.scalar_tensor_tensor(
            wsb[:, h * NL:(h + 1) * NL], eye_f[0:NL, 0:NL], 1e-6,
            wsb[:, h * NL:(h + 1) * NL], ALU.mult, ALU.add))
        plan.wait_eng("vector", "vector", v3)
        v4 = plan.op("vector", lambda e, h=h: e.reduce_sum(
            rowsum[:, :], wsb[:, h * NL:(h + 1) * NL], axis=AX.X,
            apply_absolute_value=True))
        plan.wait_eng("tensor", "vector", v4)
        t2 = plan.op("tensor", lambda e: e.transpose(
            ps_row[0:1, 0:NL], rowsum[:, 0:1], eye_f[0:NL, 0:NL]))
        plan.wait_eng("vector", "tensor", t2)
        v5 = plan.op("vector", lambda e: e.tensor_copy(srow[:, :],
                                                       ps_row[0:1, 0:NL]))
        plan.wait_eng("vector", "vector", v5)
        v6 = plan.op("vector", lambda e: e.reduce_max(s11[:, :], srow[:, :],
                                                      axis=AX.X))
        plan.wait_eng("vector", "vector", v6)
        v7 = plan.op("vector", lambda e: e.reciprocal(s11[:, :], s11[:, :]))
        plan.wait_eng("tensor", "vector", v7)
        t3 = plan.op("tensor", lambda e: e.matmul(
            ps_row[0:NL, 2:3], ones1f[0:1, 0:NL], s11[0:1, 0:1],
            start=True, stop=True))
        plan.wait_eng("vector", "tensor", t3)
        v8 = plan.op("vector", lambda e: e.tensor_copy(scol[:, :],
                                                       ps_row[0:NL, 2:3]))
        plan.wait_eng("vector", "vector", v8)
        v9 = plan.op("vector", lambda e, h=h: e.tensor_scalar(
            xsb[:, h * NL:(h + 1) * NL], eye_f[0:NL, 0:NL], scol[:, 0:1],
            None, ALU.mult))
    plan.join()

    ns_slots = [ps_ns[0], ps_ns[1], ps_z, ps_zs, ps_ql, ps_a]
    slot_ev = {}
    xev = {}
    for it in range(NS_ITERS):
        for h in range(H):
            key = 2 * (it * H + h)
            sA = ns_slots[key % 6]
            sB = ns_slots[(key + 1) % 6]
            if slot_ev.get(key % 6) is not None:
                plan.wait_eng("tensor", "vector", slot_ev[key % 6])
            if xev.get(h) is not None:
                plan.wait_eng("tensor", "vector", xev[h])
            t1 = plan.op("tensor", lambda e, h=h, sA=sA: e.matmul(
                sA[0:NL, 0:NL], wsb[:, h * NL:(h + 1) * NL],
                xsb[:, h * NL:(h + 1) * NL], start=True, stop=True))
            plan.wait_eng("vector", "tensor", t1)
            r_ev = plan.op(
                "vector", lambda e, h=h, sA=sA: e.scalar_tensor_tensor(
                    rsb[:, h * NL:(h + 1) * NL], sA[0:NL, 0:NL],
                    -1.0, eye2[:, :], ALU.mult, ALU.add))
            slot_ev[key % 6] = r_ev
            if slot_ev.get((key + 1) % 6) is not None:
                plan.wait_eng("tensor", "vector", slot_ev[(key + 1) % 6])
            plan.wait_eng("tensor", "vector", r_ev)
            t2 = plan.op("tensor", lambda e, h=h, sB=sB: e.matmul(
                sB[0:NL, 0:NL], xsb[:, h * NL:(h + 1) * NL],
                rsb[:, h * NL:(h + 1) * NL], start=True, stop=True))
            plan.wait_eng("vector", "tensor", t2)
            x_ev = plan.op(
                "vector", lambda e, h=h, sB=sB: e.tensor_copy(
                    xsb[:, h * NL:(h + 1) * NL], sB[0:NL, 0:NL]))
            slot_ev[(key + 1) % 6] = x_ev
            xev[h] = x_ev
    plan.join()
    plan.dma("sync", o_winv, xsb[:, :], "st")

    prev_phi = {}
    prev_zev = None
    for h in range(H):
        oc = h // 2
        rb = 64 * (h % 2)
        for tc in range(NC):
            slot = tc % 2
            if slot in prev_phi:
                plan.wait_eng("tensor", "scalar", prev_phi[slot][1])
            t1 = plan.op("tensor", lambda e, h=h, oc=oc, rb=rb, tc=tc,
                         slot=slot: e.matmul(
                ps_ql[:, slot * 256:slot * 256 + DH],
                kbf[rb:rb + DH, oc * T + tc * 128: oc * T + (tc + 1) * 128],
                lbf[rb:rb + DH, h * DH:(h + 1) * DH],
                start=True, stop=True))
            plan.wait_eng("vector", "tensor", t1)
            if slot in prev_phi:
                plan.wait_eng("vector", "scalar", prev_phi[slot][1])
            v1 = plan.op("vector", lambda e, h=h, slot=slot:
                         e.scalar_tensor_tensor(
                usb[:, slot * DH:(slot + 1) * DH],
                ps_ql[:, slot * 256:slot * 256 + DH],
                t2c[:, 0:1], lnbT[:, h * DH:(h + 1) * DH],
                ALU.mult, ALU.subtract))
            plan.wait_eng("vector", "vector", v1)
            v2 = plan.op("vector", lambda e, h=h, tc=tc, slot=slot:
                         e.tensor_scalar(
                usb[:, slot * DH:(slot + 1) * DH],
                usb[:, slot * DH:(slot + 1) * DH],
                knt[:, h * NC + tc: h * NC + tc + 1],
                0.0, ALU.subtract, ALU.min))
            plan.wait_eng("scalar", "vector", v2)
            s1 = plan.op("scalar", lambda e, tc=tc, slot=slot: e.activation(
                phik[:, tc * DH:(tc + 1) * DH],
                usb[:, slot * DH:(slot + 1) * DH], AF.Exp))
            prev_phi[slot] = (v2, s1)
        phi_done = plan.cur("d_scalar")
        plan.wait_eng("tensor", "scalar", phi_done)
        if prev_zev is not None:
            plan.wait_eng("tensor", "vector", prev_zev)
        for tc in range(NC):
            plan.op("tensor", lambda e, h=h, tc=tc: e.matmul(
                ps_z[0:DH, 0:DH], phik[:, tc * DH:(tc + 1) * DH],
                vt[:, tc * D + h * DH: tc * D + (h + 1) * DH],
                start=(tc == 0), stop=(tc == NC - 1),
                skip_group_check=True))
            plan.op("tensor", lambda e, h=h, tc=tc: e.matmul(
                ps_zs[0:DH, 0:1], phik[:, tc * DH:(tc + 1) * DH],
                ones128b[:, :],
                start=(tc == 0), stop=(tc == NC - 1),
                skip_group_check=True))
        zt = plan.cur("d_tensor")
        plan.wait_eng("vector", "tensor", zt)
        plan.op("vector", lambda e, h=h: e.tensor_copy(
            zsb[:, h * 65:h * 65 + 64], ps_z[0:DH, 0:64]))
        prev_zev = plan.op("vector", lambda e, h=h: e.tensor_copy(
            zsb[:, h * 65 + 64:(h + 1) * 65], ps_zs[0:DH, 0:1]))
    plan.wait_eng("sync", "vector", plan.cur("d_vector"))
    plan.dma("sync", o_z, zsb[:, :], "st")


def emit_l1attn_p3(plan, C, ph):
    winv_d = C.winv_i
    lm_r_d = C.lm_r.ap()
    temp_d = C.temp
    qbf_d, qn_d = C.qbf2_i, C.qn_i
    x2_d = C.x2_i
    Wo_d, bo_d = C.Wo2, C.bo2
    eye_f = C.eye_f
    eye_b = C.eye_b
    ones64b = C.ones128b
    ones1f = C.ones1f

    qbf = ph.sbuf("qbf_s", [128, IC * T], BF16)
    attn = ph.sbuf("attn_s", [128, IC * T], BF16)
    qn = ph.sbuf("qn_s", [128, H * NC], F32)
    zsb = ph.sbuf("zsb", [NL, H * 65], F32)
    winv = ph.sbuf("winv_s", [NL, H * DH], F32)
    ybf = ph.sbuf("ybf", [NL, H * 65], BF16)
    lmp = ph.sbuf("lmp", [NL, D], BF16)
    lmt = ph.sbuf("lmt", [NL, D], BF16)
    lbf = ph.sbuf("lbf", [128, H * DH], BF16)
    lsq = ph.sbuf("lsq", [128, H * DH], BF16)
    invt = ph.sbuf("invt", [1, 1], F32)
    t2c = ph.sbuf("t2c", [128, 1], F32)
    invtc = ph.sbuf("invtc", [128, 1], F32)
    lnt_rows = ph.sbuf("lnt_rows", [1, H * DH], F32)
    lncol_t = ph.sbuf("lncol_t", [NL, H], F32)
    lnbT = ph.sbuf("lnbT", [128, H * DH], F32)
    usb = ph.sbuf("usb", [128, 2 * TTW], F32)
    qnb2 = [ph.sbuf(f"qnb{i}", [128, T], F32) for i in range(2)]
    trqn = ph.sbuf("trqn", [128, 256], F32)
    phiq = ph.sbuf("phiq", [NL, T], BF16)
    nrow2 = [ph.sbuf(f"nrow{i}", [1, TTW], F32) for i in range(2)]
    rbb2 = [ph.sbuf(f"rbb{i}", [128, TTW], F32) for i in range(2)]
    wbuf = [ph.sbuf(f"wbuf{i}", [128, D], BF16) for i in range(2)]
    bo_sb = ph.sbuf("bo_sb", [128, IC], F32)
    xoc = [ph.sbuf(f"xoc{i}", [128, SLAB], F32) for i in range(2)]
    x3slab = ph.sbuf("x3slab", [128, IC * SLAB], F32)
    row_sc3 = ph.sbuf("row_sc3", [1, 3 * SLAB + 1], F32)
    bc_sb3 = ph.sbuf("bc_sb3", [128, 2 * SLAB], F32)

    ps_tpb = ph.psum("ps_tpb", [128, 1024], BF16)
    ps_o = ph.psum("ps_o", [128, 512], F32)
    ps_den = ph.psum("ps_den", [128, 512], F32)
    ps_ph = ph.psum("ps_ph", [128, 512], F32)
    ps_bc = ph.psum("ps_bc", [128, 2048], F32)
    ps_a = ps_o
    ps_row = ps_den

    plan.dma("gpsimd", qbf[:, :], qbf_d)
    plan.dma("gpsimd", qn[:, :], qn_d)
    plan.dma("gpsimd", winv[:, :], winv_d)
    plan.wait("gpsimd", "cc_z", 1)
    plan.dma("gpsimd", zsb[:, :], C.z_r.ap())
    plan.wait("gpsimd", "dma", plan.cur("dma"))
    emit_lm_common(plan, C, lm_r_d, temp_d, lmp, lmt, lbf, lsq, eye_f, eye_b,
                   ones64b, ones1f, invt, t2c, invtc, lnt_rows, lncol_t,
                   lnbT, ps_a, ps_row, ps_tpb)

    for h in range(H):
        slot = h % 2
        t1 = plan.op("tensor", lambda e, h=h, slot=slot: e.matmul(
            ps_a[0:NL, slot * 128:slot * 128 + 65] if False else
            ps_a[0:NL, 0:65],
            winv[:, h * DH:(h + 1) * DH], zsb[:, h * 65:(h + 1) * 65],
            start=True, stop=True))
        plan.wait_eng("vector", "tensor", t1)
        v1 = plan.op("vector", lambda e, h=h: e.tensor_copy(
            ybf[:, h * 65:(h + 1) * 65], ps_a[0:NL, 0:65]))
        plan.wait_eng("tensor", "vector", v1)
    plan.join()

    for blk in range(2):
        tb = plan.op("tensor", lambda e, blk=blk: e.transpose(
            ps_bc[0:128, 0:128], qn[:, blk * 128:(blk + 1) * 128],
            eye_f[:, :]))
        plan.wait_eng("vector", "tensor", tb)
        vb = plan.op("vector", lambda e, blk=blk: e.tensor_scalar(
            trqn[:, blk * 128:(blk + 1) * 128], ps_bc[0:128, 0:128],
            invtc[:, 0:1], None, ALU.mult))
        plan.wait_eng("tensor", "vector", vb)
    plan.wait_eng("sync", "vector", plan.cur("d_vector"))
    for blk in range(2):
        plan.dma("sync",
                 C.qn_d[blk * 8:(blk + 1) * 8, :]
                 .rearrange("h (t i) -> (h t) i", t=16),
                 trqn[:, blk * 128:(blk + 1) * 128], "wvs")
    plan.wait("gpsimd", "wvs", plan.cur("wvs"))
    qndma = {}
    for h0 in range(2):
        qndma[h0] = plan.dma("gpsimd", qnb2[h0][:, :],
                             C.qn_d[h0:h0 + 1, :].broadcast_to((128, T)),
                             "nbq")

    pso_s3 = [ps_o[0:DH, 0:TTW], ps_bc[0:DH, TTW:2 * TTW]]
    psd_s3 = [ps_den[0:1, 0:TTW], ps_bc[0:1, 2 * TTW:3 * TTW]]
    bc_s3 = [ps_bc[:, 0:TTW], ps_bc[:, 3 * TTW:4 * TTW]]
    nst = {}
    for h in range(H):
        oc = h // 2
        rb = 64 * (h % 2)
        qnb = qnb2[h % 2]
        dv = plan.cur("d_vector")
        plan.wait_eng("tensor", "vector", dv)
        if h + 1 < H:
            plan.wait_eng("gpsimd", "vector", dv)
            qndma[h + 1] = plan.dma(
                "gpsimd", qnb2[(h + 1) % 2][:, :],
                C.qn_d[h + 1:h + 2, :].broadcast_to((128, T)), "nbq")
        plan.wait("vector", "nbq", qndma[h])
        pst_ev = {}
        for tt in range(NT):
            par = tt % 2
            psX = [ps_ph, ps_den][par]
            if pst_ev.get(("s", par)) is not None:
                plan.wait_eng("tensor", "scalar", pst_ev[("s", par)])
            t4 = plan.op("tensor", lambda e, h=h, rb=rb, oc=oc, tt=tt,
                         psX=psX: e.matmul(
                psX[0:NL, 0:TTW],
                lbf[rb:rb + DH, h * DH:(h + 1) * DH],
                qbf[rb:rb + DH, oc * T + tt * TTW: oc * T + (tt + 1) * TTW],
                start=True, stop=True))
            plan.wait_eng("vector", "tensor", t4)
            if pst_ev.get(("s", par)) is not None:
                plan.wait_eng("vector", "scalar", pst_ev[("s", par)])
            v4 = plan.op("vector", lambda e, h=h, tt=tt, par=par, psX=psX:
                         e.scalar_tensor_tensor(
                usb[0:NL, par * TTW:(par + 1) * TTW],
                psX[0:NL, 0:TTW], t2c[0:NL, 0:1],
                qnb[0:NL, tt * TTW:(tt + 1) * TTW], ALU.mult, ALU.subtract))
            plan.wait_eng("vector", "vector", v4)
            v5 = plan.op("vector", lambda e, h=h, par=par: e.tensor_scalar(
                usb[0:NL, par * TTW:(par + 1) * TTW],
                usb[0:NL, par * TTW:(par + 1) * TTW],
                lncol_t[:, h:h + 1], 0.0,
                ALU.subtract, ALU.min))
            plan.wait_eng("scalar", "vector", v5)
            s_phi = plan.op("scalar", lambda e, tt=tt, par=par: e.activation(
                phiq[:, tt * TTW:(tt + 1) * TTW],
                usb[0:NL, par * TTW:(par + 1) * TTW], AF.Exp))
            pst_ev[("s", par)] = s_phi
        phi_done = plan.cur("d_scalar")
        plan.wait_eng("tensor", "scalar", phi_done)
        def ntail(tz, oc=oc, rb=rb):
            pz = tz % 2
            plan.wait_eng("vector", "tensor", nst[("t5", pz)])
            v6 = plan.op("vector", lambda e, pz=pz: e.tensor_copy(
                nrow2[pz][:, :], psd_s3[pz]))
            plan.wait_eng("tensor", "vector", v6)
            if nst.get(("v8", pz)) is not None:
                plan.wait_eng("tensor", "vector", nst[("v8", pz)])
            t6 = plan.op("tensor", lambda e, pz=pz: e.matmul(
                bc_s3[pz], ones1f[0:1, :], nrow2[pz][0:1, :],
                start=True, stop=True))
            plan.wait_eng("vector", "tensor", t6)
            v8 = plan.op("vector", lambda e, pz=pz: e.tensor_scalar(
                rbb2[pz][:, :], bc_s3[pz], 1e-10, None, ALU.max))
            plan.wait_eng("vector", "vector", v8)
            v9 = plan.op("vector", lambda e, pz=pz: e.reciprocal(
                rbb2[pz][:, :], rbb2[pz][:, :]))
            plan.wait_eng("vector", "vector", v9)
            mv = plan.op("vector", lambda e, oc=oc, rb=rb, tz=tz, pz=pz:
                         e.tensor_mul(
                attn[rb:rb + DH, oc * T + tz * TTW: oc * T + (tz + 1) * TTW],
                pso_s3[pz], rbb2[pz][0:DH, :]))
            nst[("v8", pz)] = v8
            nst[("mul", pz)] = mv

        for tt in range(NT):
            par = tt % 2
            if nst.get(("mul", par)) is not None:
                plan.wait_eng("tensor", "vector", nst[("mul", par)])
            plan.op("tensor", lambda e, h=h, tt=tt, par=par: e.matmul(
                pso_s3[par], ybf[:, h * 65:h * 65 + DH],
                phiq[:, tt * TTW:(tt + 1) * TTW], start=True, stop=True))
            t5 = plan.op("tensor", lambda e, h=h, tt=tt, par=par: e.matmul(
                psd_s3[par], ybf[:, h * 65 + DH:h * 65 + 65],
                phiq[:, tt * TTW:(tt + 1) * TTW], start=True, stop=True))
            nst[("t5", par)] = t5
            if tt > 0:
                ntail(tt - 1)
        ntail(NT - 1)
    plan.join()

    plan.dma("gpsimd", bo_sb[:, :], bo_d)
    plan.wait("gpsimd", "dma", plan.cur("dma"))
    plan.op("gpsimd", lambda e: e.memset(lmt[0:1, 0:1], 0.0))
    plan.join()
    last_mm = {}
    prev_ev = {}
    prev_half = {}
    for sb in range(2):
        for ocx in range(IC):
            buf = wbuf[ocx % 2]
            psem = "pc%d" % (ocx % 2)
            xsem = "xs%d" % (ocx % 2)
            idx = sb * IC + ocx
            if last_mm.get(idx % 2) is not None:
                plan.wait_eng("gpsimd", "tensor", last_mm[idx % 2])
            plan.dma("gpsimd", buf[:, :].rearrange("p (i o) -> p i o", i=IC),
                     Wo_d[:, ocx * 128:(ocx + 1) * 128]
                     .rearrange("(i p) o -> p i o", p=128), psem)
            if (sb * IC + ocx) >= 2:
                plan.wait_eng("sync", "vector", prev_ev[ocx % 2])
            plan.dma("sync", xoc[ocx % 2][:, :],
                     x2_d[ocx * 128:(ocx + 1) * 128,
                          sb * SLAB:(sb + 1) * SLAB], xsem)
            plan.wait("tensor", psem, plan.cur(psem))
            plan.wait("vector", xsem, plan.cur(xsem))
            for tt2 in range(2):
                tt = sb * 2 + tt2
                half = tt2
                if half in prev_half:
                    plan.wait_eng("tensor", "vector", prev_half[half])
                for icx in range(IC):
                    tgt = plan.op("tensor", lambda e, buf=buf, icx=icx,
                                  tt=tt, half=half: e.matmul(
                        ps_bc[:, half * TTW:(half + 1) * TTW],
                        buf[:, icx * 128:(icx + 1) * 128],
                        attn[:, icx * T + tt * TTW: icx * T + (tt + 1) * TTW],
                        start=(icx == 0), stop=(icx == IC - 1)))
                last_mm[idx % 2] = tgt
                plan.wait_eng("vector", "tensor", tgt)
                prev_ev[ocx % 2] = plan.op(
                    "vector", lambda e, ocx=ocx, tt2=tt2, half=half:
                    e.scalar_tensor_tensor(
                        x3slab[:, ocx * SLAB + tt2 * TTW:
                               ocx * SLAB + (tt2 + 1) * TTW],
                        ps_bc[:, half * TTW:(half + 1) * TTW],
                        bo_sb[:, ocx:ocx + 1],
                        xoc[ocx % 2][:, tt2 * TTW:(tt2 + 1) * TTW],
                        ALU.add, ALU.add))
                prev_half[half] = prev_ev[ocx % 2]
        plan.join()
        emit_ln(plan, x3slab, qbf[:, 0:IC * SLAB],
                qbf[:, IC * SLAB:2 * IC * SLAB], C.ones128b, row_sc3,
                bc_sb3, C.ones1f, ps_den, ps_bc, tlen=SLAB)
        plan.join()
        plan.dma("sync",
                 C.x3_i[:, sb * SLAB:(sb + 1) * SLAB]
                 .rearrange("(i p) t -> p i t", p=128),
                 x3slab[:, :].rearrange("p (i t) -> p i t", i=IC), "st")
        if sb == 0:
            plan.wait("vector", "st", plan.cur("st"))
            plan.op("vector", lambda e: e.memset(row_sc3[0:1, 0:1], 0.0))
            plan.join()


SLAB = 1024
NSL = T // SLAB
NJ = INTER // 128
NG = 4
PG = 2
JG = 4
OCG = 4


def emit_l1ffn(plan, C, ph):
    x3_d = C.x3_i
    gw_d = C.gw2
    w13_d = C.w13
    w2_d = C.w2w
    o_xf = C.xf
    eye_f = C.eye_f
    ones1f = C.ones1f
    ones128b = C.ones128b

    xbf = ph.sbuf("xbf", [128, IC * T], BF16)
    act = ph.sbuf("act", [128, NJ * SLAB], BF16)
    out_acc = ph.sbuf("out_acc", [128, IC * SLAB], F32)
    xoc = [ph.sbuf(f"xoc{i}", [128, T], F32) for i in range(2)]
    gw_sb = ph.sbuf("gw_sb", [128, IC * NE], F32)
    logit = ph.sbuf("logit", [NE, T], F32)
    sc_tm = ph.sbuf("sc_tm", [128, NE * NC], F32)
    comb = ph.sbuf("comb", [128, NE * NC], F32)
    m1c = ph.sbuf("m1c", [128, 1], F32)
    m2c = ph.sbuf("m2c", [128, 1], F32)
    g1c = ph.sbuf("g1c", [128, 1], F32)
    g2c = ph.sbuf("g2c", [128, 1], F32)
    gsc = ph.sbuf("gsc", [128, NG], F32)
    geq = ph.sbuf("geq", [128, NG], F32)
    gsel = ph.sbuf("gsel", [128, NG], F32)
    smask = ph.sbuf("smask", [128, NE], F32)
    msked = ph.sbuf("msked", [128, NE], F32)
    meq = ph.sbuf("meq", [128, NE], F32)
    sel = ph.sbuf("sel", [128, NE], F32)
    wsum = ph.sbuf("wsum", [128, 1], F32)
    sgl = ph.sbuf("sgl", [128, 4 * TTW], F32)
    tmp5 = ph.sbuf("tmp5", [128, TTW], F32)
    cbb2 = [ph.sbuf(f"cbb{i}", [128, SLAB], F32) for i in range(2)]
    trcb = ph.sbuf("trcb", [128, 128], F32)
    wbuf = [ph.sbuf(f"wbuf{i}", [128, 2 * IC * 128], BF16)
            for i in range(4)]
    w2buf = [ph.sbuf(f"w2buf{i}", [128, NJ * 128], BF16)
             for i in range(4)]
    row_sc = ph.sbuf("row_sc", [1, 3 * SLAB + 1], F32)
    bc_sb = ph.sbuf("bc_sb", [128, 2 * SLAB], F32)

    ps_g = ph.psum("ps_g", [128, 2048], F32)
    ps_t = ph.psum("ps_t", [128, 512], F32)
    ps_o = ph.psum("ps_o", [128, 512], F32)
    ps_h = ph.psum("ps_h", [128, 1024], F32)
    gu_slots = [
        [ps_g[:, 0:512], ps_g[:, 512:1024],
         ps_g[:, 1024:1536], ps_g[:, 1536:2048]],
        [ps_t[:, 0:512], ps_o[:, 0:512],
         ps_h[:, 0:512], ps_h[:, 512:1024]],
    ]
    w2_slots = [ps_g[:, 0:1024], ps_g[:, 1024:2048]]

    plan._wj = {}
    plan._wsem2 = {}
    plan._wiss2 = 0
    plan._wuse2 = 0
    plan._w2j = {}
    plan._w2sem = {}
    plan._w2iss = 0
    plan._w2use = 0
    plan._actv = {}
    plan._oev = {}
    plan._guv = {}
    plan._gus = {}
    plan._sglv = {}
    plan._tmpv = None

    def issue_w13(e2, j2):
        key = plan._wiss2 % 4
        plan._wiss2 += 1
        psem = "pu%d" % key
        if plan._wj.get(key) is not None:
            plan.wait_eng("gpsimd", "tensor", plan._wj[key])
        plan.dma("gpsimd",
                 wbuf[key][:, 0:1024].rearrange("p (i o) -> p i o", i=IC),
                 w13_d[e2, :, j2 * 128:(j2 + 1) * 128]
                 .rearrange("(i p) o -> p i o", p=128), psem)
        plan._wsem2[key] = plan.dma(
            "gpsimd",
            wbuf[key][:, 1024:2048].rearrange("p (i o) -> p i o", i=IC),
            w13_d[e2, :, INTER + j2 * 128: INTER + (j2 + 1) * 128]
            .rearrange("(i p) o -> p i o", p=128), psem)

    def issue_w2(e2, oc2):
        key = plan._w2iss % 4
        plan._w2iss += 1
        w2s = "pv%d" % key
        if plan._w2j.get(key) is not None:
            plan.wait_eng("gpsimd", "tensor", plan._w2j[key])
        plan._w2sem[key] = plan.dma(
            "gpsimd", w2buf[key][:, :].rearrange("p (j o) -> p j o", j=NJ),
            w2_d[e2, :, oc2 * 128:(oc2 + 1) * 128]
            .rearrange("(j p) o -> p j o", p=128), w2s)
    plan.dma("gpsimd", gw_sb[:, :].rearrange("p (i en) -> p i en", i=IC),
             gw_d.rearrange("(i p) en -> p i en", p=128))
    plan.wait("gpsimd", "dma", plan.cur("dma"))
    plan.op("gpsimd", lambda e: e.memset(row_sc[0:1, 0:1], 0.0))
    plan.join()

    prev_cast = {}
    prev_gmm = {}
    for icx in range(IC):
        xsem = "xs%d" % (icx % 2)
        if icx >= 2:
            plan.wait_eng("sync", "vector", prev_cast[icx - 2])
            plan.wait_eng("sync", "tensor", prev_gmm[icx - 2])
        plan.dma("sync", xoc[icx % 2][:, :],
                 x3_d[icx * 128:(icx + 1) * 128, :], xsem)
        plan.wait("tensor", xsem, plan.cur(xsem))
        plan.wait("vector", xsem, plan.cur(xsem))
        for tt in range(NT):
            tgt = plan.op("tensor", lambda e, icx=icx, tt=tt: e.matmul(
                ps_g[0:NE, tt * TTW:(tt + 1) * TTW],
                gw_sb[:, icx * NE:(icx + 1) * NE],
                xoc[icx % 2][:, tt * TTW:(tt + 1) * TTW],
                start=(icx == 0), stop=(icx == IC - 1)))
        prev_gmm[icx] = tgt
        prev_cast[icx] = plan.op("vector", lambda e, icx=icx: e.tensor_copy(
            xbf[:, icx * T:(icx + 1) * T], xoc[icx % 2][:, :]))
    gmm = plan.cur("d_tensor")
    plan.wait_eng("vector", "tensor", gmm)
    vl = plan.op("vector", lambda e: e.tensor_copy(logit[:, :],
                                                   ps_g[0:NE, 0:T]))
    plan.join()

    for tc in range(NC):
        t1 = plan.op("tensor", lambda e, tc=tc: e.transpose(
            ps_t[0:128, 0:NE], logit[:, tc * 128:(tc + 1) * 128],
            eye_f[0:NE, 0:NE]))
        plan.wait_eng("vector", "tensor", t1)
        va = plan.op("vector", lambda e, tc=tc: e.tensor_copy(
            sc_tm[:, tc * NE:(tc + 1) * NE], ps_t[0:128, 0:NE]))
        plan.wait_eng("tensor", "vector", va)
        plan.wait_eng("vector", "vector", va)
        sl = slice(tc * NE, (tc + 1) * NE)
        vb = plan.op("vector", lambda e, sl=sl: e.reduce_max(
            m1c[:, :], sc_tm[:, sl], axis=AX.X))
        plan.wait_eng("vector", "vector", vb)
        vc = plan.op("vector", lambda e, sl=sl: e.tensor_scalar(
            sc_tm[:, sl], sc_tm[:, sl], m1c[:, 0:1], None, ALU.subtract))
        plan.wait_eng("scalar", "vector", vc)
        s1 = plan.op("scalar", lambda e, sl=sl: e.activation(
            sc_tm[:, sl], sc_tm[:, sl], AF.Exp))
        plan.wait_eng("vector", "scalar", s1)
        vd = plan.op("vector", lambda e, sl=sl: e.reduce_sum(
            m1c[:, :], sc_tm[:, sl], axis=AX.X))
        plan.wait_eng("vector", "vector", vd)
        ve = plan.op("vector", lambda e: e.reciprocal(m1c[:, :], m1c[:, :]))
        plan.wait_eng("vector", "vector", ve)
        vf = plan.op("vector", lambda e, sl=sl: e.tensor_scalar(
            sc_tm[:, sl], sc_tm[:, sl], m1c[:, 0:1], None, ALU.mult))
        plan.wait_eng("vector", "vector", vf)
        vg = plan.op("vector", lambda e, sl=sl: e.reduce_max(
            gsc[:, :], sc_tm[:, sl].rearrange("p (g k) -> p g k", g=NG),
            axis=AX.X))
        plan.wait_eng("vector", "vector", vg)
        vh = plan.op("vector", lambda e: e.reduce_max(g1c[:, :], gsc[:, :],
                                                      axis=AX.X))
        plan.wait_eng("vector", "vector", vh)
        vi = plan.op("vector", lambda e: e.tensor_scalar(
            geq[:, :], gsc[:, :], g1c[:, 0:1], -1e30, ALU.is_equal,
            ALU.mult))
        plan.wait_eng("vector", "vector", vi)
        vj = plan.op("vector", lambda e: e.tensor_add(geq[:, :], geq[:, :],
                                                      gsc[:, :]))
        plan.wait_eng("vector", "vector", vj)
        vk = plan.op("vector", lambda e: e.reduce_max(g2c[:, :], geq[:, :],
                                                      axis=AX.X))
        plan.wait_eng("vector", "vector", vk)
        vm = plan.op("vector", lambda e: e.tensor_scalar(
            gsel[:, :], gsc[:, :], g2c[:, 0:1], None, ALU.is_ge))
        plan.wait_eng("vector", "vector", vm)
        vn = plan.op("vector", lambda e: e.tensor_copy(
            smask[:, :].rearrange("p (g k) -> p g k", g=NG),
            gsel[:, :].rearrange("p (g o) -> p g o", o=1)
            .broadcast_to((128, NG, PG))))
        plan.wait_eng("vector", "vector", vn)
        vo = plan.op("vector", lambda e, sl=sl: e.tensor_mul(
            msked[:, :], sc_tm[:, sl], smask[:, :]))
        plan.wait_eng("vector", "vector", vo)
        vp = plan.op("vector", lambda e: e.reduce_max(m1c[:, :], msked[:, :],
                                                      axis=AX.X))
        plan.wait_eng("vector", "vector", vp)
        vq = plan.op("vector", lambda e: e.tensor_scalar(
            meq[:, :], msked[:, :], m1c[:, 0:1], -1e30, ALU.is_equal,
            ALU.mult))
        plan.wait_eng("vector", "vector", vq)
        vr = plan.op("vector", lambda e: e.tensor_add(meq[:, :], meq[:, :],
                                                      msked[:, :]))
        plan.wait_eng("vector", "vector", vr)
        vs = plan.op("vector", lambda e: e.reduce_max(m2c[:, :], meq[:, :],
                                                      axis=AX.X))
        plan.wait_eng("vector", "vector", vs)
        vt_ = plan.op("vector", lambda e: e.tensor_scalar(
            sel[:, :], msked[:, :], m2c[:, 0:1], None, ALU.is_ge))
        plan.wait_eng("vector", "vector", vt_)
        vu = plan.op("vector", lambda e: e.tensor_mul(sel[:, :], sel[:, :],
                                                      msked[:, :]))
        plan.wait_eng("vector", "vector", vu)
        vv = plan.op("vector", lambda e: e.reduce_sum(wsum[:, :], sel[:, :],
                                                      axis=AX.X))
        plan.wait_eng("vector", "vector", vv)
        vw = plan.op("vector", lambda e: e.reciprocal(wsum[:, :],
                                                      wsum[:, :]))
        plan.wait_eng("vector", "vector", vw)
        vx = plan.op("vector", lambda e, tc=tc: e.tensor_scalar(
            comb[:, tc: NE * NC: NC], sel[:, :], wsum[:, 0:1], None,
            ALU.mult))
    plan.join()

    tcb = plan.op("tensor", lambda e: e.transpose(
        ps_t[0:128, 0:128], comb[:, :], eye_f[:, :]))
    plan.wait_eng("vector", "tensor", tcb)
    vcb = plan.op("vector", lambda e: e.tensor_copy(
        trcb[:, :], ps_t[0:128, 0:128]))
    plan.wait_eng("sync", "vector", vcb)
    plan.wait_eng("tensor", "vector", vcb)
    plan.dma("sync",
             C.cb_d[0:NE, :].rearrange("h (t i) -> (h t) i", t=NC),
             trcb[:, :], "wvs")
    plan.wait("gpsimd", "wvs", plan.cur("wvs"))
    cbdma = {0: plan.dma("gpsimd", cbb2[0][:, :],
                         C.cb_d[0:1, 0:SLAB].broadcast_to((128, SLAB)),
                         "nbq")}

    for sb in range(NSL):
        sl_tt = [sb * 2, sb * 2 + 1]
        rhs0 = sb * SLAB
        first = True
        issue_w13(0, 0)
        issue_w13(0, 1)
        issue_w2(0, 0)
        issue_w2(0, 1)
        for e_i in range(NE):
            kse = sb * NE + e_i
            cbb = cbb2[kse % 2]
            dv = plan.cur("d_vector")
            plan.wait_eng("tensor", "vector", dv)
            if kse + 1 < 2 * NE:
                sb2, e2 = divmod(kse + 1, NE)
                plan.wait_eng("gpsimd", "vector", dv)
                cbdma[kse + 1] = plan.dma(
                    "gpsimd", cbb2[(kse + 1) % 2][:, :],
                    C.cb_d[e2:e2 + 1, sb2 * SLAB:(sb2 + 1) * SLAB]
                    .broadcast_to((128, SLAB)), "nbq")
            plan.wait("vector", "nbq", cbdma[kse])
            for j in range(NJ):
                k = e_i * NJ + j
                if k + 2 < NE * NJ:
                    e2, j2 = divmod(k + 2, NJ)
                    issue_w13(e2, j2)
                key = plan._wuse2 % 4
                plan._wuse2 += 1
                buf = wbuf[key]
                plan.wait("tensor", "pu%d" % key, plan._wsem2[key])
                par = j % 2
                slots = gu_slots[par]
                if plan._guv.get(par) is not None:
                    plan.wait_eng("tensor", "vector", plan._guv[par])
                    plan.wait_eng("tensor", "scalar", plan._gus[par])
                for icx in range(IC):
                    go = icx * 128
                    for qq in range(2):
                        tt = sl_tt[qq]
                        plan.op("tensor", lambda e, buf=buf, go=go,
                                icx=icx, tt=tt, qq=qq, slots=slots:
                                e.matmul(
                            slots[qq],
                            buf[:, go:go + 128],
                            xbf[:, icx * T + tt * TTW:
                                icx * T + (tt + 1) * TTW],
                            start=(icx == 0), stop=(icx == IC - 1),
                            skip_group_check=True))
                for icx in range(IC):
                    uo = 1024 + icx * 128
                    for qq in range(2):
                        tt = sl_tt[qq]
                        tgt = plan.op("tensor", lambda e, buf=buf, uo=uo,
                                      icx=icx, tt=tt, qq=qq, slots=slots:
                                      e.matmul(
                            slots[2 + qq],
                            buf[:, uo:uo + 128],
                            xbf[:, icx * T + tt * TTW:
                                icx * T + (tt + 1) * TTW],
                            start=(icx == 0), stop=(icx == IC - 1),
                            skip_group_check=True))
                plan._wj[key] = tgt
                gu_done = plan.cur("d_tensor")
                for qq in range(2):
                    plan.wait_eng("scalar", "tensor", gu_done)
                    if plan._sglv.get((par, qq)) is not None:
                        plan.wait_eng("scalar", "vector",
                                      plan._sglv[(par, qq)])
                    qs = slice((par * 2 + qq) * TTW,
                               (par * 2 + qq + 1) * TTW)
                    s1 = plan.op("scalar", lambda e, qs=qs, qq=qq,
                                 slots=slots: e.activation(
                        sgl[:, qs], slots[qq], AF.Silu))
                    plan.wait_eng("vector", "scalar", s1)
                    v1 = plan.op("vector", lambda e, j=j, qq=qq, qs=qs,
                                 slots=slots: e.tensor_mul(
                        act[:, j * SLAB + qq * TTW:
                            j * SLAB + (qq + 1) * TTW],
                        sgl[:, qs], slots[2 + qq]))
                    plan._sglv[(par, qq)] = v1
                    plan._gus[par] = s1
                plan._guv[par] = v1
            act_done = plan.cur("d_vector")
            plan.wait_eng("tensor", "vector", act_done)
            plan.wait_eng("tensor", "scalar", plan.cur("d_scalar"))
            for oc in range(IC):
                kw = e_i * IC + oc
                if kw + 2 < NE * IC:
                    e2w, oc2 = divmod(kw + 2, IC)
                    issue_w2(e2w, oc2)
                key2 = plan._w2use % 4
                plan._w2use += 1
                b2 = w2buf[key2]
                plan.wait("tensor", "pv%d" % key2, plan._w2sem[key2])
                if True:
                    par = oc % 2
                    pso = w2_slots[par]
                    if plan._oev.get(par) is not None:
                        plan.wait_eng("tensor", "vector", plan._oev[par])
                    for j in range(NJ):
                        lo = j * 128
                        for qq in range(2):
                            tgt = plan.op("tensor", lambda e, b2=b2, lo=lo,
                                          j=j, qq=qq, pso=pso: e.matmul(
                                pso[:, qq * TTW:(qq + 1) * TTW],
                                b2[:, lo:lo + 128],
                                act[:, j * SLAB + qq * TTW:
                                    j * SLAB + (qq + 1) * TTW],
                                start=(j == 0), stop=(j == NJ - 1),
                                skip_group_check=True))
                    plan._w2j[key2] = tgt
                    plan.wait_eng("vector", "tensor", tgt)
                    for qq in range(2):
                        if first:
                            ov = plan.op("vector", lambda e, oc=oc, qq=qq,
                                         pso=pso, cbb=cbb: e.tensor_mul(
                                out_acc[:, oc * SLAB + qq * TTW:
                                        oc * SLAB + (qq + 1) * TTW],
                                pso[:, qq * TTW:(qq + 1) * TTW],
                                cbb[:, qq * TTW:(qq + 1) * TTW]))
                        else:
                            if plan._tmpv is not None:
                                plan.wait_eng("vector", "vector", plan._tmpv)
                            vm = plan.op("vector", lambda e, qq=qq, pso=pso,
                                         cbb=cbb: e.tensor_mul(
                                tmp5[:, :],
                                pso[:, qq * TTW:(qq + 1) * TTW],
                                cbb[:, qq * TTW:(qq + 1) * TTW]))
                            plan.wait_eng("vector", "vector", vm)
                            ov = plan.op("vector", lambda e, oc=oc, qq=qq:
                                         e.tensor_add(
                                out_acc[:, oc * SLAB + qq * TTW:
                                        oc * SLAB + (qq + 1) * TTW],
                                out_acc[:, oc * SLAB + qq * TTW:
                                        oc * SLAB + (qq + 1) * TTW],
                                tmp5[:, :]))
                            plan._tmpv = ov
                    plan._oev[par] = ov
            first = False
        plan.wait_eng("gpsimd", "vector", plan.cur("d_vector"))
        tgt = plan.bump("res", 16)

        def _acc(e, sb=sb, s=plan.sems["res"]):
            e.dma_start(out=out_acc[:, :].rearrange("p (i t) -> p i t", i=IC),
                        in_=x3_d[:, sb * SLAB:(sb + 1) * SLAB]
                        .rearrange("(i p) t -> p i t", p=128),
                        accum_op=ALU.add).then_inc(s, 16)
        plan.add("gpsimd", _acc)
        plan.wait("gpsimd", "res", tgt)
        plan.op("gpsimd", lambda e: e.memset(trcb[0:1, 0:1], 0.0))
        plan.join()
        ln_xbf = act[:, 0:IC * SLAB]
        ln_sqb = act[:, IC * SLAB:2 * IC * SLAB]
        emit_ln(plan, out_acc, ln_xbf, ln_sqb, ones128b, row_sc, bc_sb,
                ones1f, ps_t, ps_g, tlen=SLAB)
        plan.join()
        plan.dma("sync",
                 o_xf[:, sb * SLAB:(sb + 1) * SLAB]
                 .rearrange("(i p) t -> p i t", p=128),
                 out_acc[:, :].rearrange("p (i t) -> p i t", i=IC),
                 "sto%d" % sb)
        if sb == 0:
            plan.wait("vector", "sto0", plan.cur("sto0"))
            plan.op("vector", lambda e: e.memset(m1c[:, :], 0.0))
            plan.join()


def build_fused():
    nc = bass.Bass("TRN2", target_bir_lowering=False)
    C = SimpleNamespace()

    def ein(name, shape, dtype=F32):
        return nc.dram_tensor(name, shape, dtype, kind="ExternalInput").ap()

    C.xT = ein("xT", [D, T])
    C.Wq = ein("Wq", [D, D])
    C.Wk = ein("Wk", [D, D])
    C.Wv = ein("Wv", [D, D])
    C.Wo = ein("Wo", [D, D])
    C.bo = ein("bo", [128, IC])
    C.PT = ein("PT", [DH, M])
    C.gw = ein("gw", [D, NE])
    C.gb = ein("gb", [NE, 1])
    C.w1 = ein("w1", [NE, D, D])
    C.w2 = ein("w2", [NE, D, D])
    C.w3 = ein("w3", [NE, D, D])
    C.b1 = ein("b1", [128, NE * IC])
    C.b2 = ein("b2", [128, NE * IC])
    C.b3 = ein("b3", [128, NE * IC])
    C.Wq2 = ein("Wq2", [D, D])
    C.Wk2 = ein("Wk2", [D, D])
    C.Wv2 = ein("Wv2", [D, D])
    C.bq2 = ein("bq2", [128, IC])
    C.bk2 = ein("bk2", [128, IC])
    C.bv2 = ein("bv2", [1, D])
    C.lidx = ein("lidx", [NL, 1], I32)
    C.t0c = ein("t0c", [NL, 1], I32)
    C.Wo2 = ein("Wo2", [D, D])
    C.bo2 = ein("bo2", [128, IC])
    C.temp = ein("temp", [1, 1])
    C.gw2 = ein("gw2", [D, NE])
    C.w13 = ein("w13", [NE, D, 2 * INTER])
    C.w2w = ein("w2w", [NE, INTER, D])
    C.xf = nc.dram_tensor("xf", [D, T], F32, kind="ExternalOutput").ap()

    def din(name, shape, dtype=F32):
        return nc.dram_tensor(name, shape, dtype).ap()

    C.qbf_i = din("qbf_i", [128, IC * T], BF16)
    C.kbf_i = din("kbf_i", [128, IC * T], BF16)
    C.vt_i = din("vt_i", [128, NC * D], BF16)
    C.negbq_i = din("negbq_i", [128, H * NC])
    C.diagk_i = din("diagk_i", [128, H * NC])
    C.x1p_i = din("x1p_i", [D, T])
    C.x1_i = din("x1_i", [D, T])
    C.logits_i = din("logits_i", [NE, T])
    C.ind_i = din("ind_i", [NE, T], BF16)
    C.x2_i = din("x2_i", [D, T])
    C.qbf2_i = din("qbf2_i", [128, IC * T], BF16)
    C.kbf2_i = din("kbf2_i", [128, IC * T], BF16)
    C.vt2_i = din("vt2_i", [128, NC * D], BF16)
    C.qn_i = din("qn_i", [128, H * NC])
    C.kn_i = din("kn_i", [128, H * NC])
    C.ktm_i = din("ktm_i", [T, D], BF16)
    C.winv_i = din("winv_i", [NL, H * DH])
    C.x3_i = din("x3_i", [D, T])
    C.nb_d = din("nb_d", [H, T])
    C.qn_d = din("qn_d", [H, T])
    C.wt_d = din("wt_d", [NE, T])
    C.cb_d = din("cb_d", [NE, T])

    C.kmax_w = nc.dram_tensor("kmax_w", [1, 8], F32)
    C.kmax_r = cc_dram(nc, "kmax_r", [1, 8], F32, "AllReduce", G8)
    C.ctx_w = nc.dram_tensor("ctx_w", [128, H * 2 * 65], F32)
    C.ctx_r = cc_dram(nc, "ctx_r", [128, H * 2 * 65], F32, "AllReduce", G4)
    C.lmax_w = nc.dram_tensor("lmax_w", [NE, 1], F32)
    C.gmax_r = cc_dram(nc, "gmax_r", [NE, 1], F32, "AllReduce", G4)
    C.t_w = nc.dram_tensor("t_w", [NE, 1], F32)
    C.rden_r = cc_dram(nc, "rden_r", [NE, 1], F32, "AllReduce", G4)
    C.v_w = nc.dram_tensor("v_w", [NE * IC, 128], F32)
    C.v_r = cc_dram(nc, "v_r", [NE * IC, 128], F32, "AllReduce", G8)
    C.lm_w = nc.dram_tensor("lm_w", [NL, D], BF16)
    C.lm_r = cc_dram(nc, "lm_r", [NL, D], BF16, "AllReduce", G4)
    C.z_w = nc.dram_tensor("z_w", [NL, H * 65], F32)
    C.z_r = cc_dram(nc, "z_r", [NL, H * 65], F32, "AllReduce", G4)

    C.eye_f = nc.alloc_sbuf_tensor("eye_f", [128, 128], F32)
    C.eye_b = nc.alloc_sbuf_tensor("eye_b", [128, 128], BF16)
    C.ones1f = nc.alloc_sbuf_tensor("ones1f", [1, 128], F32)
    C.ones128b = nc.alloc_sbuf_tensor("ones128b", [128, 1], BF16)

    plan = Plan(nc)
    g1 = plan.op("gpsimd", lambda e: e.iota(
        C.eye_f[:, :], pattern=[[1, 128]], base=0, channel_multiplier=-1,
        allow_small_or_imprecise_dtypes=True))
    plan.wait_eng("vector", "gpsimd", g1)
    veq = plan.op("vector", lambda e: e.tensor_scalar(
        C.eye_f[:, :], C.eye_f[:, :], 0.0, None, ALU.is_equal))
    plan.wait_eng("vector", "vector", veq)
    plan.op("vector", lambda e: e.tensor_scalar(
        C.eye_b[:, :], C.eye_f[:, :], 0.5, None, ALU.is_gt))
    plan.op("vector", lambda e: e.memset(C.ones1f[:, :], 1.0))
    plan.op("vector", lambda e: e.memset(C.ones128b[:, :], 1.0))
    plan.join()

    def phase(name, emitter, *args):
        ph = Phase(nc, name + "_")
        emitter(plan, C, ph, *args)
        phase_barrier(plan)
        ph.close()

    phase("a1", emit_l0attn_p1)
    emit_collective(plan, "cc_kmax", "AllReduce", ALU.max, G8,
                    C.kmax_w, C.kmax_r)
    phase("a2", emit_l0attn_p2)
    emit_collective(plan, "cc_ctx", "AllReduce", ALU.add, G4,
                    C.ctx_w, C.ctx_r)
    phase("a3", emit_l0attn_p3)
    phase("f2", emit_l0ffn_p2)
    emit_collective(plan, "cc_v", "AllReduce", ALU.add, G8,
                    C.v_w, C.v_r)
    phase("f3", emit_l0ffn_p3)
    phase("b1", emit_l1attn_p1)
    emit_collective(plan, "cc_lm", "AllReduce", ALU.add, G4,
                    C.lm_w, C.lm_r)
    phase("b2", emit_l1attn_p2)
    emit_collective(plan, "cc_z", "AllReduce", ALU.add, G4,
                    C.z_w, C.z_r)
    phase("b3", emit_l1attn_p3)
    phase("m1", emit_l1ffn)
    finish(plan)
    return nc


def _shard_tokens(x):
    out = []
    for c in range(8):
        b, t0 = c // 4, (c % 4) * T
        out.append(np.ascontiguousarray(x[b, t0:t0 + T, :].T))
    return out


def _bias_cols(b):
    return np.ascontiguousarray(b.reshape(IC, 128).T).astype(np.float32)


def _bias_cols_e(b):
    return np.ascontiguousarray(
        b.reshape(NE, IC, 128).transpose(2, 0, 1).reshape(128, NE * IC)
    ).astype(np.float32)


def kernel(**inp):
    f32 = np.float32
    x = np.asarray(inp['L0_attn_x'], f32)
    xs = _shard_tokens(x)
    PT = np.ascontiguousarray(np.asarray(inp['L0_attn_P'], f32).T)
    Wq0 = np.asarray(inp['L0_attn_Wq'], f32)
    Wk0 = np.asarray(inp['L0_attn_Wk'], f32)
    Wv0 = np.asarray(inp['L0_attn_Wv'], f32)
    Wo0 = np.asarray(inp['L0_attn_Wo'], f32)
    bo0 = _bias_cols(np.asarray(inp['L0_attn_bo'], f32))
    gw0 = np.asarray(inp['L0_ffn_gate_w'], f32)
    gb0 = np.ascontiguousarray(np.asarray(inp['L0_ffn_gate_b'],
                                          f32).reshape(NE, 1))
    w1 = np.asarray(inp['L0_ffn_w1'], f32)
    w2 = np.asarray(inp['L0_ffn_w2'], f32)
    w3 = np.asarray(inp['L0_ffn_w3'], f32)
    b1 = _bias_cols_e(np.asarray(inp['L0_ffn_b1'], f32))
    b2 = _bias_cols_e(np.asarray(inp['L0_ffn_b2'], f32))
    b3 = _bias_cols_e(np.asarray(inp['L0_ffn_b3'], f32))
    Wq1T = np.ascontiguousarray(np.asarray(inp['L1_attn_Wq'], f32).T)
    Wk1T = np.ascontiguousarray(np.asarray(inp['L1_attn_Wk'], f32).T)
    Wv1T = np.ascontiguousarray(np.asarray(inp['L1_attn_Wv'], f32).T)
    Wo1T = np.ascontiguousarray(np.asarray(inp['L1_attn_Wo'], f32).T)
    bq1 = _bias_cols(np.asarray(inp['L1_attn_bq'], f32))
    bk1 = _bias_cols(np.asarray(inp['L1_attn_bk'], f32))
    bv1 = np.ascontiguousarray(np.asarray(inp['L1_attn_bv'],
                                          f32).reshape(1, D))
    bo1 = _bias_cols(np.asarray(inp['L1_attn_bo'], f32))
    temp = np.asarray(inp['L1_attn_temperature'], f32).reshape(1, 1)
    lidx = np.ascontiguousarray(np.asarray(inp['L1_attn_landmark_idx'],
                                           np.int32).reshape(NL, 1))
    gw1T = np.ascontiguousarray(np.asarray(inp['L1_ffn_gate_w'], f32).T)
    w13T = np.ascontiguousarray(
        np.asarray(inp['L1_ffn_w13_weight'], f32).transpose(0, 2, 1))
    w2T = np.ascontiguousarray(
        np.asarray(inp['L1_ffn_w2_weight'], f32).transpose(0, 2, 1))

    cores = list(range(8))
    t0c = [np.full((NL, 1), (c % 4) * T, np.int32) for c in cores]

    nc = build_fused()
    maps = [dict(
        xT=xs[c], Wq=Wq0, Wk=Wk0, Wv=Wv0, Wo=Wo0, bo=bo0, PT=PT,
        gw=gw0, gb=gb0, w1=w1, w2=w2, w3=w3, b1=b1, b2=b2, b3=b3,
        Wq2=Wq1T, Wk2=Wk1T, Wv2=Wv1T, bq2=bq1, bk2=bk1, bv2=bv1,
        lidx=lidx, t0c=t0c[c], Wo2=Wo1T, bo2=bo1, temp=temp,
        gw2=gw1T, w13=w13T, w2w=w2T) for c in cores]
    rf = run_bass_kernel_spmd(nc, maps, core_ids=cores).results
    out = np.zeros((2, 8192, D), f32)
    for c in cores:
        b, t0 = c // 4, (c % 4) * T
        out[b, t0:t0 + T, :] = np.asarray(rf[c]['xf'], f32).T
    return out


if __name__ == "__main__":
    build_fused()
    print("build ok")
